# version 04
# baseline (speedup 1.0000x reference)
import contextlib
import os
import tempfile

os.environ["NEURON_COMPILE_CACHE_URL"] = tempfile.mkdtemp(prefix="ncc_")

import numpy as np
import ml_dtypes
import concourse.bass as bass
import concourse.mybir as mybir
from concourse.bass_utils import run_bass_kernel_spmd

F32 = mybir.dt.float32
BF16 = mybir.dt.bfloat16
I32 = mybir.dt.int32
AF = mybir.ActivationFunctionType
ALU = mybir.AluOpType
AX = mybir.AxisListType.X

D = 2048
DK = 512
HD = 128
TOWN = 1024
TKV = 2048
MLP = 16384
NEG = -1.0e30
GELU_C = 0.044715
GELU_S = 0.7978845608028654
ISQ = 0.08838834764831845
EPS = 1e-5

DEQ_SLICES = 1
STAGE = 5
LNKV_SUB = 7
DBG = ""


class Ctr:
    def __init__(self, es, nc, name):
        self.h = es.enter_context(nc.semaphore(name))
        self.n = 0

    def inc(self, instr, k=1):
        instr.then_inc(self.h, k)
        self.n += k
        return self.n


def build():
    nc = bass.Bass("TRN2", target_bir_lowering=False)

    def din(name, shape, dt):
        return nc.dram_tensor(name, shape, dt, kind="ExternalInput").ap()

    xT = din("xT", [D, TKV], F32)
    xr = din("xr", [TOWN, D], F32)
    wqT = din("wqT", [D, D], F32)
    wkT = din("wkT", [D, DK], F32)
    wvT = din("wvT", [D, DK], F32)
    woT = din("woT", [D, D], F32)
    bq = din("bq", [128, 16], F32)
    bk = din("bk", [128, 4], F32)
    bv = din("bv", [128, 4], F32)
    bo = din("bo", [128, 16], F32)
    gp = din("gp", [2048 * DEQ_SLICES, 1024], I32)
    gs = din("gs", [2048 * DEQ_SLICES, 64], F32)
    upk = din("up", [2048 * DEQ_SLICES, 1024], I32)
    ups = din("us", [2048 * DEQ_SLICES, 64], F32)
    dp = din("dp", [256 * DEQ_SLICES, 8192], I32)
    dsc = din("ds", [256 * DEQ_SLICES, 512], F32)
    qcos = din("qcos", [HD, TOWN], F32)
    qsin = din("qsin", [HD, TOWN], F32)
    kcos = din("kcos", [HD, TKV], F32)
    ksin = din("ksin", [HD, TKV], F32)
    posq = din("posq", [128, 8], F32)
    poskk = din("poskk", [1, TKV], F32)
    iden = din("iden", [128, 128], BF16)
    idenf = din("idenf", [128, 128], F32)
    onescol = din("onescol", [128, 1], BF16)
    onesf32 = din("onesf32", [1, 128], F32)

    out = nc.dram_tensor("out", [TOWN, D], F32, kind="ExternalOutput").ap()

    wk_ = "ExternalOutput" if "w" in DBG else "Internal"
    if "k" in DBG:
        dbg_xn = nc.dram_tensor("dbg_xn", [128, 16, TOWN], BF16, kind="ExternalOutput").ap()
        dbg_kt = nc.dram_tensor("dbg_kt", [128, 4, TKV], BF16, kind="ExternalOutput").ap()
        dbg_vh = nc.dram_tensor("dbg_vh", [128, 4, 16, 128], BF16, kind="ExternalOutput").ap()
    if "a" in DBG:
        nc._dbg_a = dict(
            qt=nc.dram_tensor("dbg_qt", [128, TOWN], BF16, kind="ExternalOutput").ap(),
            pb=nc.dram_tensor("dbg_pb", [128, TKV], BF16, kind="ExternalOutput").ap(),
            ell=nc.dram_tensor("dbg_ell", [128, 1], F32, kind="ExternalOutput").ap(),
            s=nc.dram_tensor("dbg_s", [128, TKV], F32, kind="ExternalOutput").ap(),
            pb2=nc.dram_tensor("dbg_pb2", [128, TKV], BF16, kind="ExternalOutput").ap(),
            msk=nc.dram_tensor("dbg_msk", [128, 8, TKV], BF16, kind="ExternalOutput").ap(),
            yt=nc.dram_tensor("dbg_yt", [128, 16, TOWN], BF16, kind="ExternalOutput").ap(),
        )
    sk_ = "ExternalOutput" if "s" in DBG else "Internal"
    ggcF = nc.dram_tensor("ggc", [D, 2048], BF16, kind="Internal").ap()
    gucF = nc.dram_tensor("guc", [D, 2048], BF16, kind="Internal").ap()
    gdcF = nc.dram_tensor("gdc", [MLP, 256], BF16, kind="Internal").ap()
    ggc = ggcF.rearrange("(a k) n -> a k n", a=1)
    guc = gucF.rearrange("(a k) n -> a k n", a=1)
    gdc = gdcF.rearrange("(a k) n -> a k n", a=1)
    GgF = nc.dram_tensor("Gg", [8 * D, 2048], BF16, kind=wk_).ap()
    GuF = nc.dram_tensor("Gu", [8 * D, 2048], BF16, kind=wk_).ap()
    GdF = nc.dram_tensor("Gd", [8 * MLP, 256], BF16, kind=wk_).ap()
    Gg = GgF.rearrange("(s k) n -> s k n", s=8)
    Gu = GuF.rearrange("(s k) n -> s k n", s=8)
    Gd = GdF.rearrange("(s k) n -> s k n", s=8)
    sDram = nc.dram_tensor("sDram", [D, TOWN], BF16, kind=sk_).ap()

    es = contextlib.ExitStack()
    with es:
        E = es.enter_context
        xnT = E(nc.sbuf_tensor("xnT", [128, 16, TOWN], BF16))
        identt = E(nc.sbuf_tensor("identt", [128, 128], BF16))
        idenft = E(nc.sbuf_tensor("idenft", [128, 128], F32))
        onescolt = E(nc.sbuf_tensor("onescolt", [128, 1], BF16))
        onesf32t = E(nc.sbuf_tensor("onesf32t", [1, 128], F32))
        posqt = E(nc.sbuf_tensor("posqt", [128, 8], F32))
        bqt = E(nc.sbuf_tensor("bqt", [128, 16], F32))
        bot = E(nc.sbuf_tensor("bot", [128, 16], F32))
        bkt = E(nc.sbuf_tensor("bkt", [128, 4], F32))
        bvt = E(nc.sbuf_tensor("bvt", [128, 4], F32))
        cneg4 = E(nc.sbuf_tensor("cneg4", [128, 1], F32))

        C = {}
        for name in ["gd", "pe", "dv", "ac", "ou", "cn", "x", "ms", "st"]:
            C[name] = Ctr(es, nc, name)
        for name in ["pk", "ws", "gg", "gt", "xr"]:
            C[name] = [Ctr(es, nc, name + "0"), Ctr(es, nc, name + "1")]

        gd, pe, dv, ac, ou = C["gd"], C["pe"], C["dv"], C["ac"], C["ou"]

        blk = es.enter_context(nc.Block())
        SQ = nc.sync
        WQ = nc.scalar
        V = nc.vector
        A = nc.scalar
        T = nc.tensor
        ALL_E = (SQ, V, T, WQ, nc.gpsimd)

        def flat_ctrs():
            r = []
            for v in C.values():
                r.extend(v if isinstance(v, list) else [v])
            return r

        def barrier():
            vals = [(c, c.n) for c in flat_ctrs()]
            for e in ALL_E:
                for c, v in vals:
                    if v > 0:
                        e.wait_ge(c.h, v)

        cn = C["cn"]
        cn.inc(SQ.dma_start(out=identt[:], in_=iden), 16)
        cn.inc(SQ.dma_start(out=idenft[:], in_=idenf), 16)
        cn.inc(SQ.dma_start(out=onescolt[:], in_=onescol), 16)
        cn.inc(SQ.dma_start(out=onesf32t[:], in_=onesf32), 16)
        cn.inc(SQ.dma_start(out=posqt[:], in_=posq), 16)
        cn.inc(SQ.dma_start(out=bqt[:], in_=bq), 16)
        cn.inc(SQ.dma_start(out=bot[:], in_=bo), 16)
        cn.inc(SQ.dma_start(out=bkt[:], in_=bk), 16)
        cn.inc(SQ.dma_start(out=bvt[:], in_=bv), 16)
        dv_ = C["dv"]
        dv_.inc(V.memset(cneg4[:], -4.0))
        for e in (V, T, WQ, nc.gpsimd):
            e.wait_ge(cn.h, cn.n)
            e.wait_ge(dv_.h, 1)

        cc = Ctr(es, nc, "ccs")
        nc._cc = cc
        if STAGE >= 1:
            G_ = nc.gpsimd
            rg = [[0, 1, 2, 3, 4, 5, 6, 7]]
            _dequant(nc, SQ, WQ, V, T, C, identt, gp, gs, ggc, 1.0, "g")
            G_.wait_ge(gd.h, gd.n)
            cc.inc(G_.collective_compute("AllGather", ALU.bypass,
                                         replica_groups=rg,
                                         ins=[ggcF.opt()], outs=[GgF.opt()]), 1)
            _dequant(nc, SQ, WQ, V, T, C, identt, upk, ups, guc, 1.0, "u")
            G_.wait_ge(gd.h, gd.n)
            cc.inc(G_.collective_compute("AllGather", ALU.bypass,
                                         replica_groups=rg,
                                         ins=[gucF.opt()], outs=[GuF.opt()]), 1)
            _dequant(nc, SQ, WQ, V, T, C, identt, dp, dsc, gdc, 0.5, "d")
            G_.wait_ge(gd.h, gd.n)
            cc.inc(G_.collective_compute("AllGather", ALU.bypass,
                                         replica_groups=rg,
                                         ins=[gdcF.opt()], outs=[GdF.opt()]), 1)
        SQ.wait_ge(gd.h, gd.n)

        if STAGE >= 2:
            with contextlib.ExitStack() as es23:
                E23 = es23.enter_context
                poskkb = E23(nc.sbuf_tensor("poskkb", [128, TKV], F32))
                qcost = E23(nc.sbuf_tensor("qcost", [128, TOWN], F32))
                qsint = E23(nc.sbuf_tensor("qsint", [128, TOWN], F32))
                ms = C["ms"]
                ms.inc(SQ.dma_start(out=poskkb[:], in_=poskk.partition_broadcast(128)), 16)
                ms.inc(SQ.dma_start(out=qcost[:], in_=qcos), 16)
                ms.inc(SQ.dma_start(out=qsint[:], in_=qsin), 16)
                kvt = _ln_kv(nc, es23, SQ, V, A, T, C,
                             xT, wkT, wvT, bkt, bvt, kcos, ksin,
                             xnT, onescolt, onesf32t, identt)
                barrier()
                if "k" in DBG:
                    ou.inc(WQ.dma_start(out=dbg_xn, in_=xnT[:]), 16)
                    ou.inc(WQ.dma_start(out=dbg_kt, in_=kvt["KT"][:]), 16)
                    ou.inc(WQ.dma_start(out=dbg_vh, in_=kvt["Vh"][:]), 16)
                    for e in ALL_E:
                        e.wait_ge(ou.h, ou.n)
                if STAGE >= 3:
                    _attention(nc, es23, SQ, WQ, V, A, T, C,
                               wqT, woT, bqt, bot, qcost, qsint, posqt, poskkb,
                               xnT, sDram, kvt, identt, cneg4)
                barrier()

        if STAGE >= 4:
            with contextlib.ExitStack() as es45:
                gacc = es45.enter_context(nc.sbuf_tensor("gacc", [128, 16, TOWN], F32))
                _ffn(nc, SQ, V, A, T, C, Gg, Gu, Gd, xnT, gacc, nc._cc)
                barrier()
                if STAGE >= 5:
                    _combine(nc, SQ, WQ, V, A, T, C, xr, out, sDram, gacc, identt, idenft)

        SQ.wait_ge(ou.h, ou.n)

    return nc


def _dequant(nc, SQ, WQ, V, T, C, identt, packed, scales, Gout, smul, tag):
    gd, pe, dv = C["gd"], C["pe"], C["dv"]
    pk = C["pk"]
    KK = packed.shape[1]
    NB = scales.shape[1]
    n_loc = Gout.shape[2]
    n_rtiles = n_loc // 128
    n_cchunk = KK // 512
    nslices = packed.shape[0] // n_loc
    K_win = Gout.shape[1] // n_cchunk

    with contextlib.ExitStack() as es:
        SQ.wait_ge(gd.h, gd.n)
        E = es.enter_context
        pt = [E(nc.sbuf_tensor(f"dqp{tag}{i}", [128, 512], I32)) for i in range(2)]
        sct = [E(nc.sbuf_tensor(f"dqs{tag}{i}", [128, NB], F32)) for i in range(2)]
        sce = [E(nc.sbuf_tensor(f"dqe{tag}{i}", [128, NB], F32)) for i in range(2)]
        mtile = E(nc.sbuf_tensor(f"dqm{tag}", [128, 1024], I32))
        stile = E(nc.sbuf_tensor(f"dqt{tag}", [128, 1024], I32))
        btile = E(nc.sbuf_tensor(f"dqb{tag}", [128, 1024], I32))
        wtile = E(nc.sbuf_tensor(f"dqw{tag}", [128, 1024], BF16))
        rb = E(nc.sbuf_tensor(f"dqr{tag}", [128, 8, n_loc], BF16))
        tp = [E(nc.psum_tensor(f"dqx{tag}{i}", [128, 512], BF16)) for i in range(2)]

        dv_decode = {}
        dv_evac = {}
        pe_tp = {}
        gd_grp = {}

        u = 0
        for sl in range(nslices):
            for c in range(n_cchunk):
                for r in range(n_rtiles):
                    b = u % 2
                    if u >= 2:
                        SQ.wait_ge(dv.h, dv_decode[u - 2])
                    row0 = sl * n_loc + r * 128
                    pk[b].inc(SQ.dma_start(out=pt[b][:], in_=packed[row0:row0 + 128, c * 512:(c + 1) * 512]), 16)
                    pk[b].inc(SQ.dma_start(out=sct[b][:], in_=scales[row0:row0 + 128, :]), 16)

                    V.wait_ge(pk[b].h, pk[b].n)
                    if u >= 1:
                        V.wait_ge(pe.h, pe_tp[u - 1])
                    if r == 0:
                        g_prev = (sl * n_cchunk + c) - 1
                        if g_prev >= 0:
                            V.wait_ge(gd.h, gd_grp[g_prev])
                    V.tensor_scalar(sce[b][:], sct[b][:], float(smul * (2.0 ** 126)), None, ALU.mult)
                    p = pt[b][:]
                    V.tensor_scalar(mtile[:, 0:512], p, 7, 22, ALU.bitwise_and, ALU.logical_shift_left)
                    V.tensor_scalar(stile[:, 0:512], p, 8, 28, ALU.bitwise_and, ALU.logical_shift_left)
                    V.tensor_scalar(mtile[:, 512:1024], p, 0x70, 18, ALU.bitwise_and, ALU.logical_shift_left)
                    V.tensor_scalar(stile[:, 512:1024], p, 0x80, 24, ALU.bitwise_and, ALU.logical_shift_left)
                    V.tensor_tensor(btile[:], mtile[:], stile[:], ALU.bitwise_or)
                    scb = sce[b][:, 32 * c:32 * c + 32].unsqueeze(2).broadcast_to([128, 32, 16])
                    bl = btile[:, 0:512].bitcast(F32).rearrange("p (b j) -> p b j", b=32)
                    bh = btile[:, 512:1024].bitcast(F32).rearrange("p (b j) -> p b j", b=32)
                    V.tensor_tensor(wtile[:, 0:512].rearrange("p (b j) -> p b j", b=32), bl, scb, ALU.mult)
                    dv.inc(V.tensor_tensor(wtile[:, 512:1024].rearrange("p (b j) -> p b j", b=32), bh, scb, ALU.mult))
                    dv_decode[u] = dv.n

                    T.wait_ge(dv.h, dv_decode[u])
                    if u >= 1:
                        T.wait_ge(dv.h, dv_evac[u - 1])
                    for plane in range(2):
                        for s4 in range(4):
                            ins = T.transpose(tp[plane][:, 128 * s4:128 * (s4 + 1)],
                                              wtile[:, 512 * plane + 128 * s4:512 * plane + 128 * (s4 + 1)],
                                              identt[:])
                        pe.inc(ins)
                    pe_tp[u] = pe.n

                    V.wait_ge(pe.h, pe_tp[u])
                    V.tensor_copy(rb[:, 0:4, r * 128:(r + 1) * 128],
                                  tp[0][:].rearrange("p (s j) -> p s j", s=4))
                    dv.inc(V.tensor_copy(rb[:, 4:8, r * 128:(r + 1) * 128],
                                         tp[1][:].rearrange("p (s j) -> p s j", s=4)))
                    dv_evac[u] = dv.n
                    u += 1

                WQ.wait_ge(dv.h, dv_evac[u - 1])
                for plane in range(2):
                    for s4 in range(4):
                        base = K_win * c + 256 * s4 + plane
                        gd.inc(WQ.dma_start(out=Gout[sl, base:base + 255:2, :],
                                            in_=rb[:, plane * 4 + s4, :]), 16)
                gd_grp[sl * n_cchunk + c] = gd.n


def _ln_kv(nc, es, SQ, V, A, T, C, xT, wkT, wvT, bkt, bvt, kcos, ksin,
           xnT, onescolt, onesf32t, identt):
    pe, dv, ac, ms, sx, ws = C["pe"], C["dv"], C["ac"], C["ms"], C["x"], C["ws"]
    E = es.enter_context
    KT = E(nc.sbuf_tensor("KT", [128, 4, TKV], BF16))
    Vh = E(nc.sbuf_tensor("Vh", [128, 4, 16, 128], BF16))

    with contextlib.ExitStack() as es2:
        E2 = es2.enter_context
        kcost = E2(nc.sbuf_tensor("kcost", [128, TKV], F32))
        ksint = E2(nc.sbuf_tensor("ksint", [128, TKV], F32))
        wk16 = E2(nc.sbuf_tensor("wk16", [128, 16, DK], BF16))
        wv16 = E2(nc.sbuf_tensor("wv16", [128, 16, DK], BF16))
        xtc = E2(nc.sbuf_tensor("xtc", [128, 16, 512], F32))
        xn_t = E2(nc.sbuf_tensor("xn_t", [128, 16, 512], BF16))
        x16 = [E2(nc.sbuf_tensor(f"x16{i}", [128, 512], BF16)) for i in range(2)]
        x2 = [E2(nc.sbuf_tensor(f"x2{i}", [128, 512], BF16)) for i in range(2)]
        wstage = [E2(nc.sbuf_tensor(f"wst{i}", [128, 512], F32)) for i in range(2)]
        srow = E2(nc.sbuf_tensor("srow", [1, 512], F32))
        qrow = E2(nc.sbuf_tensor("qrow", [1, 512], F32))
        trow = E2(nc.sbuf_tensor("trow", [1, 512], F32))
        vrow = E2(nc.sbuf_tensor("vrow", [1, 512], F32))
        t1 = E2(nc.sbuf_tensor("lnt1", [128, 512], F32))
        kraw = E2(nc.sbuf_tensor("kraw", [128, 512], F32))
        krot = E2(nc.sbuf_tensor("krot", [128, 512], F32))
        kcs = E2(nc.sbuf_tensor("kcs", [128, 512], F32))
        vtmp = E2(nc.sbuf_tensor("vtmp", [128, 512], BF16))
        sum_p = E2(nc.psum_tensor("sum_p", [1, 512], F32))
        sq_p = E2(nc.psum_tensor("sq_p", [1, 512], F32))
        mb_p = E2(nc.psum_tensor("mb_p", [128, 512], F32))
        rb_p = E2(nc.psum_tensor("rb_p", [128, 512], F32))
        kv_p = E2(nc.psum_tensor("kv_p", [128, 512], F32))
        vv_p = E2(nc.psum_tensor("vv_p", [128, 512], F32))
        vt_p = E2(nc.psum_tensor("vt_p", [128, 512], BF16))

        ms.inc(SQ.dma_start(out=kcost[:], in_=kcos), 16)
        ms.inc(SQ.dma_start(out=ksint[:], in_=ksin), 16)
        MS_ALL = ms.n

        casts = {}
        for w in range(2):
            src = wkT if w == 0 else wvT
            dst = wk16 if w == 0 else wv16
            for kc in range(16):
                i = w * 16 + kc
                b = i % 2
                if i >= 2:
                    SQ.wait_ge(dv.h, casts[i - 2])
                ws[b].inc(SQ.dma_start(out=wstage[b][:], in_=src[128 * kc:128 * (kc + 1), :]), 16)
                V.wait_ge(ws[b].h, ws[b].n)
                dv.inc(V.tensor_copy(dst[:, kc, :], wstage[b][:]))
                casts[i] = dv.n
        V.wait_ge(ms.h, MS_ALL)

        dv_norm = {}
        pe_kv = {}
        dv_kv = dv.n
        for tc in range(4 if LNKV_SUB >= 2 else 0):
            own = tc < 2
            if tc >= 1:
                SQ.wait_ge(dv.h, dv_norm[tc - 1])
            for kc in range(16):
                sx.inc(SQ.dma_start(out=xtc[:, kc, :],
                                    in_=xT[128 * kc:128 * (kc + 1), 512 * tc:512 * (tc + 1)]), 16)
            sx_all = sx.n

            pe_stat = {}
            for kc in range(16):
                bb = kc % 2
                V.wait_ge(sx.h, sx_all)
                A.wait_ge(sx.h, sx_all)
                if kc >= 2:
                    V.wait_ge(pe.h, pe_stat[kc - 2])
                    A.wait_ge(pe.h, pe_stat[kc - 2])
                dv.inc(V.tensor_copy(x16[bb][:], xtc[:, kc, :]))
                ac.inc(A.activation(out=x2[bb][:], in_=xtc[:, kc, :], func=AF.Square))
                T.wait_ge(dv.h, dv.n)
                T.wait_ge(ac.h, ac.n)
                T.matmul(sum_p[:], onescolt[:], x16[bb][:], start=(kc == 0), stop=(kc == 15))
                pe.inc(T.matmul(sq_p[:], onescolt[:], x2[bb][:], start=(kc == 0), stop=(kc == 15)))
                pe_stat[kc] = pe.n
            pe_stats = pe.n

            if LNKV_SUB < 3:
                dv_norm[tc] = dv.n
                pe_kv[tc] = pe.n
                continue
            V.wait_ge(pe.h, pe_stats)
            V.tensor_scalar(srow[:], sum_p[:], 1.0 / D, None, ALU.mult)
            V.tensor_scalar(qrow[:], sq_p[:], 1.0 / D, None, ALU.mult)
            V.tensor_copy(t1[0:1, :], qrow[:])
            V.tensor_tensor(trow[:], srow[:], srow[:], ALU.mult)
            V.tensor_copy(t1[0:1, :], qrow[:])
            V.scalar_tensor_tensor(vrow[:], qrow[:], EPS, trow[:], ALU.add, ALU.subtract)
            dv.inc(V.tensor_copy(t1[0:1, :], qrow[:]))
            A.wait_ge(dv.h, dv.n)
            ac.inc(A.activation(out=trow[:], in_=vrow[:], func=AF.Sqrt))
            V.wait_ge(ac.h, ac.n)
            V.reciprocal(vrow[:], trow[:])
            dv.inc(V.tensor_copy(t1[0:1, :], qrow[:]))
            T.wait_ge(dv.h, dv.n)
            T.matmul(mb_p[:], onesf32t[:], srow[:], start=True, stop=True)
            pe.inc(T.matmul(rb_p[:], onesf32t[:], vrow[:], start=True, stop=True))
            pe_bc = pe.n

            if LNKV_SUB < 4:
                dv_norm[tc] = dv.n
                pe_kv[tc] = pe.n
                continue
            V.wait_ge(pe.h, pe_bc)
            if tc >= 3:
                V.wait_ge(pe.h, pe_kv[tc - 1])
            for kc in range(16):
                V.tensor_tensor(t1[:], xtc[:, kc, :], mb_p[:], ALU.subtract)
                dst = xnT[:, kc, 512 * tc:512 * (tc + 1)] if own else xn_t[:, kc, :]
                dv.inc(V.tensor_tensor(dst, t1[:], rb_p[:], ALU.mult))
            dv_norm[tc] = dv.n

            def xsrc(kc, tc=tc, own=own):
                return xnT[:, kc, 512 * tc:512 * (tc + 1)] if own else xn_t[:, kc, :]

            for kvh in range(4 if LNKV_SUB >= 5 else 0):
                T.wait_ge(dv.h, dv_norm[tc])
                T.wait_ge(dv.h, dv_kv)
                for kc in range(16):
                    T.matmul(kv_p[:], wk16[:, kc, 128 * kvh:128 * (kvh + 1)], xsrc(kc),
                             start=(kc == 0), stop=(kc == 15))
                for kc in range(16):
                    ins = T.matmul(vv_p[:], wv16[:, kc, 128 * kvh:128 * (kvh + 1)], xsrc(kc),
                                   start=(kc == 0), stop=(kc == 15))
                pe.inc(ins)
                pe_proj = pe.n
                if LNKV_SUB < 6:
                    dv_kv = dv.n
                    continue

                V.wait_ge(pe.h, pe_proj)
                cs = slice(512 * tc, 512 * (tc + 1))
                nops = (LNKV_SUB - 60) if 60 <= LNKV_SUB < 70 else 6
                ops = [
                    lambda: V.tensor_scalar(kraw[:], kv_p[:], bkt[:, kvh:kvh + 1], None, ALU.add),
                    lambda: V.tensor_tensor(krot[0:64, :], kraw[64:128, :], ksint[64:128, cs], ALU.mult),
                    lambda: V.tensor_tensor(krot[64:128, :], kraw[0:64, :], ksint[0:64, cs], ALU.mult),
                    lambda: V.tensor_tensor(kcs[:], kraw[:], kcost[:, cs], ALU.mult),
                    lambda: V.tensor_tensor(KT[:, kvh, cs], kcs[:], krot[:], ALU.add),
                    lambda: V.tensor_scalar(vtmp[:], vv_p[:], bvt[:, kvh:kvh + 1], None, ALU.add),
                ]
                last = None
                for _i in range(nops):
                    last = ops[_i]()
                if last is None:
                    last = V.engine_nop() if hasattr(V, "engine_nop") else V.nop()
                dv.inc(last)
                dv_kv = dv.n
                if LNKV_SUB < 7:
                    continue

                T.wait_ge(dv.h, dv_kv)
                for j in range(4):
                    ins = T.transpose(vt_p[:, 128 * j:128 * (j + 1)],
                                      vtmp[:, 128 * j:128 * (j + 1)], identt[:])
                pe.inc(ins)
                V.wait_ge(pe.h, pe.n)
                dv.inc(V.tensor_copy(Vh[:, kvh, 4 * tc:4 * tc + 4, :],
                                     vt_p[:].rearrange("p (s j) -> p s j", s=4)))
                dv_kv = dv.n
            pe_kv[tc] = pe.n
    return dict(KT=KT, Vh=Vh)


def _attention(nc, es, SQ, WQ, V, A, T, C, wqT, woT, bqt, bot,
               qcost, qsint, posqt, poskkb, xnT, sDram, kvt, identt, cneg4):
    gd, pe, dv, ac, ws = C["gd"], C["pe"], C["dv"], C["ac"], C["ws"]
    KT, Vh = kvt["KT"], kvt["Vh"]
    E = es.enter_context
    YT = E(nc.sbuf_tensor("YT", [128, 16, TOWN], BF16))
    masks = E(nc.sbuf_tensor("masks", [128, 8, TKV], BF16))

    with contextlib.ExitStack() as es2:
        E2 = es2.enter_context
        wstage = [E2(nc.sbuf_tensor(f"aws{i}", [128, 16, 128], F32)) for i in range(2)]
        w16 = [E2(nc.sbuf_tensor(f"aw16{i}", [128, 16, 128], BF16)) for i in range(2)]
        qraw = E2(nc.sbuf_tensor("qraw", [128, TOWN], F32))
        qrot = E2(nc.sbuf_tensor("qrot", [128, TOWN], F32))
        qcs = E2(nc.sbuf_tensor("qcs", [128, TOWN], F32))
        QT = E2(nc.sbuf_tensor("QT", [128, TOWN], BF16))
        Pb = E2(nc.sbuf_tensor("Pb", [128, TKV], BF16))
        PT = E2(nc.sbuf_tensor("PT", [128, 16, 512], BF16))
        stmp = E2(nc.sbuf_tensor("stmp", [128, TOWN], BF16))
        nmax = E2(nc.sbuf_tensor("nmax", [128, 1], F32))
        nmaxs = E2(nc.sbuf_tensor("nmaxs", [128, 1], F32))
        ell = E2(nc.sbuf_tensor("ell", [128, 1], F32))
        elli = E2(nc.sbuf_tensor("elli", [128, 1], F32))
        scr = E2(nc.sbuf_tensor("ascr", [128, 256], BF16))
        S_p = E2(nc.psum_tensor("S_p", [128, 2048], F32))
        qp_p = E2(nc.psum_tensor("qp_p", [128, 1024], F32))
        pt_p = [E2(nc.psum_tensor(f"pt_p{i}", [128, 512], BF16)) for i in range(2)]

        for qt in range(8):
            dv.inc(V.tensor_scalar(masks[:, qt, :], poskkb[:], posqt[:, qt:qt + 1], NEG,
                                   ALU.is_gt, ALU.mult))

        ac_exp = 0
        dv_pn = 0
        pe_ptp = 0
        dv_av = dv.n
        dv_qe = 0
        pe_w = {}
        ptev_bank = {0: 0, 1: 0}

        for qh in range(16):
            kvh = qh // 4
            b = qh % 2
            if qh >= 2:
                SQ.wait_ge(pe.h, pe_w[qh - 2])
            ws[b].inc(SQ.dma_start(out=wstage[b][:],
                                   in_=wqT[:, 128 * qh:128 * (qh + 1)].rearrange("(c p) e -> p c e", p=128)), 16)
            V.wait_ge(ws[b].h, ws[b].n)
            if qh >= 2:
                V.wait_ge(pe.h, pe_w[qh - 2])
            dv.inc(V.tensor_copy(w16[b][:], wstage[b][:]))
            dv_w = dv.n

            T.wait_ge(dv.h, dv_w)
            T.wait_ge(dv.h, max(dv_qe, dv_av))
            for th in range(2):
                for kc in range(16):
                    ins = T.matmul(qp_p[:, 512 * th:512 * (th + 1)], w16[b][:, kc, :],
                                   xnT[:, kc, 512 * th:512 * (th + 1)], start=(kc == 0), stop=(kc == 15))
            pe.inc(ins)
            pe_w[qh] = pe.n

            V.wait_ge(pe.h, pe_w[qh])
            V.tensor_scalar(qraw[:], qp_p[:], bqt[:, qh:qh + 1], None, ALU.add)
            V.tensor_tensor(qrot[0:64, :], qraw[64:128, :], qsint[64:128, :], ALU.mult)
            V.tensor_tensor(qrot[64:128, :], qraw[0:64, :], qsint[0:64, :], ALU.mult)
            V.tensor_tensor(qcs[:], qraw[:], qcost[:], ALU.mult)
            dv.inc(V.tensor_tensor(QT[:], qcs[:], qrot[:], ALU.add))
            dv_qe = dv.n
            if qh == 0 and hasattr(nc, "_dbg_a"):
                WQ.wait_ge(dv.h, dv_qe)
                C["ou"].inc(WQ.dma_start(out=nc._dbg_a["qt"], in_=QT[:]), 16)
                C["ou"].inc(WQ.dma_start(out=nc._dbg_a["msk"], in_=masks[:]), 16)

            for qt in range(8):
                T.wait_ge(dv.h, dv_qe)
                if ac_exp:
                    T.wait_ge(ac.h, ac_exp)
                T.wait_ge(dv.h, dv.n)
                for cc in range(4):
                    ins = T.matmul(S_p[:, 512 * cc:512 * (cc + 1)], QT[:, 128 * qt:128 * (qt + 1)],
                                   KT[:, kvh, 512 * cc:512 * (cc + 1)], start=True, stop=True)
                pe.inc(ins)
                pe_s = pe.n

                V.wait_ge(pe.h, pe_s)
                if qh == 0 and qt == 0 and hasattr(nc, "_dbg_a"):
                    V.tensor_copy(qraw[:], S_p[:, 0:1024])
                    dv.inc(V.tensor_copy(qcs[:], S_p[:, 1024:2048]))
                    WQ.wait_ge(dv.h, dv.n)
                    C["ou"].inc(WQ.dma_start(out=nc._dbg_a["s"][:, 0:1024], in_=qraw[:]), 16)
                    C["ou"].inc(WQ.dma_start(out=nc._dbg_a["s"][:, 1024:2048], in_=qcs[:]), 16)
                    V.wait_ge(C["ou"].h, C["ou"].n)
                dv.inc(V.tensor_tensor(S_p[:], S_p[:], masks[:, qt, :], ALU.add))
                dv_m = dv.n

                A.wait_ge(dv.h, dv_m)
                if dv_pn:
                    A.wait_ge(dv.h, dv_pn)
                if pe_ptp:
                    A.wait_ge(pe.h, pe_ptp)
                ac.inc(A.activation(out=Pb[:], in_=S_p[:], func=AF.Exp, scale=ISQ, bias=cneg4[:]))
                ac_exp = ac.n

                V.wait_ge(ac.h, ac_exp)
                if qh == 0 and qt == 0 and hasattr(nc, "_dbg_a"):
                    WQ.wait_ge(ac.h, ac_exp)
                    C["ou"].inc(WQ.dma_start(out=nc._dbg_a["pb2"], in_=Pb[:]), 16)
                    V.wait_ge(C["ou"].h, C["ou"].n)
                V.reduce_sum(ell[:], Pb[:], axis=AX)
                V.tensor_copy(scr[:], Pb[:, 0:256])
                V.reciprocal(elli[:], ell[:])
                V.tensor_copy(scr[:], Pb[:, 256:512])
                dv.inc(V.tensor_tensor(Pb[:], Pb[:],
                                       elli[:].broadcast_to([128, TKV]), ALU.mult))
                dv_pn = dv.n
                if qh == 0 and qt == 0 and hasattr(nc, "_dbg_a"):
                    WQ.wait_ge(dv.h, dv_pn)
                    C["ou"].inc(WQ.dma_start(out=nc._dbg_a["pb"], in_=Pb[:]), 16)
                    C["ou"].inc(WQ.dma_start(out=nc._dbg_a["ell"], in_=ell[:]), 16)
                    A.wait_ge(C["ou"].h, C["ou"].n)

                T.wait_ge(dv.h, dv_pn)
                q4 = qt % 4
                for g in range(4):
                    bank = g % 2
                    if ptev_bank[bank]:
                        T.wait_ge(dv.h, ptev_bank[bank])
                    if q4 == 0 and g < 2:
                        T.wait_ge(dv.h, dv_av)
                    for j4 in range(4):
                        j = 4 * g + j4
                        ins = T.transpose(pt_p[bank][:, 128 * j4:128 * (j4 + 1)],
                                          Pb[:, 128 * j:128 * (j + 1)], identt[:])
                    pe.inc(ins)
                    V.wait_ge(pe.h, pe.n)
                    dv.inc(V.tensor_copy(PT[:, 4 * g:4 * g + 4, 128 * q4:128 * (q4 + 1)],
                                         pt_p[bank][:].rearrange("p (s j) -> p s j", s=4)))
                    ptev_bank[bank] = dv.n
                pe_ptp = pe.n
                dv_ptev = dv.n

                if q4 == 3:
                    T.wait_ge(dv.h, dv_ptev)
                    for j in range(16):
                        ins = T.matmul(qp_p[:, 0:512], Vh[:, kvh, j, :], PT[:, j, :],
                                       start=(j == 0), stop=(j == 15))
                    pe.inc(ins)
                    V.wait_ge(pe.h, pe.n)
                    dv.inc(V.tensor_copy(YT[:, qh, 512 * (qt // 4):512 * (qt // 4 + 1)],
                                         qp_p[:, 0:512]))
                    dv_av = dv.n

        dv_yt = dv.n
        if hasattr(nc, "_dbg_a"):
            WQ.wait_ge(dv.h, dv_yt)
            C["ou"].inc(WQ.dma_start(out=nc._dbg_a["yt"], in_=YT[:]), 16)
        gd_s = 0
        dv_se = 0
        for ec in range(16):
            b = ec % 2
            last_use = pe_w[16 + ec - 2] if ec >= 2 else pe_w[14 + ec]
            SQ.wait_ge(pe.h, last_use)
            ws[b].inc(SQ.dma_start(out=wstage[b][:],
                                   in_=woT[:, 128 * ec:128 * (ec + 1)].rearrange("(c p) e -> p c e", p=128)), 16)
            V.wait_ge(ws[b].h, ws[b].n)
            V.wait_ge(pe.h, last_use)
            dv.inc(V.tensor_copy(w16[b][:], wstage[b][:]))
            dv_w2 = dv.n

            T.wait_ge(dv.h, dv_w2)
            T.wait_ge(dv.h, dv_yt)
            if dv_se:
                T.wait_ge(dv.h, dv_se)
            for th in range(2):
                for kc in range(16):
                    ins = T.matmul(qp_p[:, 512 * th:512 * (th + 1)], w16[b][:, kc, :],
                                   YT[:, kc, 512 * th:512 * (th + 1)], start=(kc == 0), stop=(kc == 15))
            pe.inc(ins)
            pe_w[16 + ec] = pe.n

            V.wait_ge(pe.h, pe_w[16 + ec])
            if gd_s:
                V.wait_ge(gd.h, gd_s)
            dv.inc(V.tensor_scalar(stmp[:], qp_p[:], bot[:, ec:ec + 1], None, ALU.add))
            dv_se = dv.n
            WQ.wait_ge(dv.h, dv_se)
            gd_s = gd.inc(WQ.dma_start(out=sDram[128 * ec:128 * (ec + 1), :], in_=stmp[:]), 16)


def _ffn(nc, SQ, V, A, T, C, Gg, Gu, Gd, xnT, gacc, cc):
    pe, dv, ac = C["pe"], C["dv"], C["ac"]
    sgg, sgt = C["gg"], C["gt"]
    NRING = 18
    with contextlib.ExitStack() as es2:
        E2 = es2.enter_context
        grng = E2(nc.sbuf_tensor("grng", [128, NRING, 512], BF16))
        urng = E2(nc.sbuf_tensor("urng", [128, NRING, 512], BF16))
        gdt = [E2(nc.sbuf_tensor(f"gdt{i}", [128, 16, 256], BF16)) for i in range(2)]
        interT = E2(nc.sbuf_tensor("interT", [128, 16, TOWN], BF16))
        ga = E2(nc.sbuf_tensor("ffa", [128, 1024], F32))
        gb = E2(nc.sbuf_tensor("ffb", [128, 1024], F32))
        gth = E2(nc.sbuf_tensor("ffth", [128, 1024], F32))
        gp_p = [E2(nc.psum_tensor(f"gp_p{i}", [128, 1024], F32)) for i in range(2)]
        up_p = [E2(nc.psum_tensor(f"up_p{i}", [128, 1024], F32)) for i in range(2)]

        dv.inc(V.memset(gacc[:], 0.0))

        group_pe = {}
        emitted = [0]
        n_strips = 8 * 4 * 16

        def strip_idx(seg, ng, kc):
            return (seg * 4 + ng) * 16 + kc

        SQ.wait_ge(cc.h, 2)
        cc_d = [False]

        def emit_loads(upto):
            while emitted[0] < min(upto, n_strips):
                s = emitted[0]
                seg, rem = divmod(s, 64)
                ng, kc = divmod(rem, 16)
                slot = s % NRING
                gidx = s // 16
                if s >= NRING:
                    SQ.wait_ge(pe.h, group_pe[(s - NRING) // 16])
                sgg[gidx % 2].inc(SQ.dma_start(out=grng[:, slot, :],
                                               in_=Gg[seg, 128 * kc:128 * (kc + 1), 512 * ng:512 * (ng + 1)]), 16)
                sgg[gidx % 2].inc(SQ.dma_start(out=urng[:, slot, :],
                                               in_=Gu[seg, 128 * kc:128 * (kc + 1), 512 * ng:512 * (ng + 1)]), 16)
                emitted[0] += 1

        dv_ep = {}
        dv_dacc = 0
        pe_dw = {}
        for seg in range(8):
            for ng in range(4):
                g = seg * 4 + ng
                emit_loads(strip_idx(seg, ng, 0) + NRING)
                sg_all = 32 * 16 * (g // 2 + 1)
                for ns in range(4):
                    nsub = 4 * ng + ns
                    gidx2 = seg * 16 + nsub
                    par = nsub % 2
                    T.wait_ge(sgg[g % 2].h, sg_all)
                    if gidx2 >= 2 and (gidx2 - 2) in dv_ep:
                        T.wait_ge(dv.h, dv_ep[gidx2 - 2])
                    if ng == 0 and ns < 2 and dv_dacc:
                        T.wait_ge(dv.h, dv_dacc)
                    for kc in range(16):
                        s = strip_idx(seg, ng, kc)
                        slot = s % NRING
                        for th in range(2):
                            T.matmul(gp_p[par][:, 512 * th:512 * (th + 1)],
                                     grng[:, slot, 128 * ns:128 * (ns + 1)],
                                     xnT[:, kc, 512 * th:512 * (th + 1)],
                                     start=(kc == 0), stop=(kc == 15))
                        for th in range(2):
                            ins = T.matmul(up_p[par][:, 512 * th:512 * (th + 1)],
                                           urng[:, slot, 128 * ns:128 * (ns + 1)],
                                           xnT[:, kc, 512 * th:512 * (th + 1)],
                                           start=(kc == 0), stop=(kc == 15))
                    pe.inc(ins)
                    pe_gu = pe.n

                    gpp = gp_p[par][:]
                    upp = up_p[par][:]
                    A.wait_ge(pe.h, pe_gu)
                    ac.inc(A.activation(out=ga[:], in_=gpp, func=AF.Square))
                    V.wait_ge(ac.h, ac.n)
                    V.tensor_scalar(ga[:], ga[:], GELU_C, 1.0, ALU.mult, ALU.add)
                    dv.inc(V.tensor_tensor(gb[:], ga[:], gpp, ALU.mult))
                    A.wait_ge(dv.h, dv.n)
                    ac.inc(A.activation(out=gth[:], in_=gb[:], func=AF.Tanh, scale=GELU_S))
                    V.wait_ge(ac.h, ac.n)
                    V.scalar_tensor_tensor(gb[:], gth[:], 1.0, gpp, ALU.add, ALU.mult)
                    dv.inc(V.tensor_tensor(interT[:, nsub, :], gb[:], upp, ALU.mult))
                    dv_ep[gidx2] = dv.n
                group_pe[g] = pe.n

            for sl in range(8):
                b = sl % 2
                di = seg * 8 + sl
                if di >= 2:
                    SQ.wait_ge(pe.h, pe_dw[di - 2])
                if not cc_d[0]:
                    SQ.wait_ge(cc.h, 3)
                    cc_d[0] = True
                sgt[b].inc(SQ.dma_start(out=gdt[b][:],
                                        in_=Gd[sl, 2048 * seg:2048 * (seg + 1), :].rearrange("(c p) h -> p c h", p=128)), 16)
                T.wait_ge(sgt[b].h, sgt[b].n)
                T.wait_ge(dv.h, dv_ep[seg * 16 + 15])
                for j in range(2):
                    if dv_dacc:
                        T.wait_ge(dv.h, dv_dacc)
                    for nci in range(16):
                        for th in range(2):
                            ins = T.matmul(gp_p[j][:, 512 * th:512 * (th + 1)],
                                           gdt[b][:, nci, 128 * j:128 * (j + 1)],
                                           interT[:, nci, 512 * th:512 * (th + 1)],
                                           start=(nci == 0), stop=(nci == 15))
                    pe.inc(ins)
                    V.wait_ge(pe.h, pe.n)
                    dv.inc(V.tensor_tensor(gacc[:, 2 * sl + j, :], gp_p[j][:],
                                           gacc[:, 2 * sl + j, :], ALU.add))
                    dv_dacc = dv.n
                pe_dw[di] = pe.n


def _combine(nc, SQ, WQ, V, A, T, C, xr, out, sDram, gacc, identt, idenft):
    pe, dv, ac, ou, st, sxr = C["pe"], C["dv"], C["ac"], C["ou"], C["st"], C["xr"]
    with contextlib.ExitStack() as es2:
        E2 = es2.enter_context
        outst = E2(nc.sbuf_tensor("outst", [128, 8, D], F32))
        stile = [E2(nc.sbuf_tensor(f"cst{i}", [128, TOWN], BF16)) for i in range(2)]
        silt = E2(nc.sbuf_tensor("silt", [128, TOWN], F32))
        xrt = [E2(nc.sbuf_tensor(f"xrt{i}", [128, D], F32)) for i in range(2)]
        cpa = E2(nc.sbuf_tensor("cpa", [128, 128], BF16))
        ta_p = E2(nc.psum_tensor("ta_p", [128, 128], BF16))
        tb_p = E2(nc.psum_tensor("tb_p", [128, 128], F32))

        dv_c = 0
        pe_tp2 = 0
        pe_ec = {}
        for ec in range(16):
            b = ec % 2
            if ec >= 2:
                SQ.wait_ge(pe.h, pe_ec[ec - 2])
            st.inc(SQ.dma_start(out=stile[b][:], in_=sDram[128 * ec:128 * (ec + 1), :]), 16)
            st_all = st.n
            if dv_c:
                A.wait_ge(dv.h, dv_c)
            if pe_tp2:
                A.wait_ge(pe.h, pe_tp2)
            ac.inc(A.activation(out=silt[:], in_=gacc[:, ec, :], func=AF.Silu))
            ac_sil = ac.n
            for t in range(8):
                T.wait_ge(st.h, st_all)
                T.wait_ge(ac.h, ac_sil)
                if dv_c:
                    T.wait_ge(dv.h, dv_c)
                T.transpose(ta_p[:], stile[b][:, 128 * t:128 * (t + 1)], identt[:])
                pe.inc(T.transpose(tb_p[:], silt[:, 128 * t:128 * (t + 1)], idenft[:]))
                pe_tp2 = pe.n
                V.wait_ge(pe.h, pe_tp2)
                V.tensor_copy(cpa[:], ta_p[:])
                dv.inc(V.tensor_tensor(outst[:, t, 128 * ec:128 * (ec + 1)], tb_p[:], cpa[:], ALU.mult))
                dv_c = dv.n
            pe_ec[ec] = pe.n

        dv_xr = {}
        for t in range(8):
            b = t % 2
            if t >= 2:
                SQ.wait_ge(dv.h, dv_xr[t - 2])
            sxr[b].inc(SQ.dma_start(out=xrt[b][:], in_=xr[128 * t:128 * (t + 1), :]), 16)
            V.wait_ge(sxr[b].h, sxr[b].n)
            dv.inc(V.tensor_tensor(outst[:, t, :], outst[:, t, :], xrt[b][:], ALU.add))
            dv_xr[t] = dv.n
            WQ.wait_ge(dv.h, dv.n)
            C["ou"].inc(WQ.dma_start(out=out[128 * t:128 * (t + 1), :], in_=outst[:, t, :]), 16)


_NC_CACHE = {}


def _get_nc():
    key = (DEQ_SLICES, STAGE, LNKV_SUB, DBG)
    if key not in _NC_CACHE:
        _NC_CACHE[key] = build()
    return _NC_CACHE[key]


def _rope_tables():
    i = np.arange(1, 65, dtype=np.float32)
    thetas = (np.float32(10000.0) ** (np.float32(-2.0) * (i - np.float32(1.0)) / np.float32(128.0))).astype(np.float32)
    pos = np.arange(1, TKV + 1, dtype=np.float32)
    ang = pos[:, None] * thetas[None, :]
    c = np.cos(ang).astype(np.float32)
    s = np.sin(ang).astype(np.float32)
    cosF = np.ascontiguousarray(np.concatenate([c, c], axis=1).T)
    sinH = np.ascontiguousarray(np.concatenate([s, -s], axis=1).T)
    return cosF, sinH


def prepare_in_maps(L0_attn_x, L0_attn_Wq, L0_attn_bq, L0_attn_Wk, L0_attn_bk, L0_attn_Wv,
                    L0_attn_bv, L0_attn_Wo, L0_attn_bo, L0_ffn_gate_W_packed, L0_ffn_gate_scales,
                    L0_ffn_up_W_packed, L0_ffn_up_scales, L0_ffn_down_W_packed, L0_ffn_down_scales):
    x = np.asarray(L0_attn_x, dtype=np.float32)
    wq = np.ascontiguousarray(np.asarray(L0_attn_Wq, np.float32).T)
    wk = np.ascontiguousarray(np.asarray(L0_attn_Wk, np.float32).T)
    wv = np.ascontiguousarray(np.asarray(L0_attn_Wv, np.float32).T)
    wo = np.ascontiguousarray(np.asarray(L0_attn_Wo, np.float32).T)
    bqv = np.ascontiguousarray(np.asarray(L0_attn_bq, np.float32).reshape(16, 128).T)
    bkv = np.ascontiguousarray(np.asarray(L0_attn_bk, np.float32).reshape(4, 128).T)
    bvv = np.ascontiguousarray(np.asarray(L0_attn_bv, np.float32).reshape(4, 128).T)
    bov = np.ascontiguousarray(np.asarray(L0_attn_bo, np.float32).reshape(16, 128).T)
    gpk = np.ascontiguousarray(np.asarray(L0_ffn_gate_W_packed, np.int32))
    gsc = np.ascontiguousarray(np.asarray(L0_ffn_gate_scales, np.float32))
    upkk = np.ascontiguousarray(np.asarray(L0_ffn_up_W_packed, np.int32))
    upsc = np.ascontiguousarray(np.asarray(L0_ffn_up_scales, np.float32))
    dpk = np.ascontiguousarray(np.asarray(L0_ffn_down_W_packed, np.int32))
    dssc = np.ascontiguousarray(np.asarray(L0_ffn_down_scales, np.float32))

    cosF, sinH = _rope_tables()
    iden = np.eye(128, dtype=np.float32).astype(ml_dtypes.bfloat16)
    idenf32 = np.eye(128, dtype=np.float32)
    onescol = np.ones((128, 1), np.float32).astype(ml_dtypes.bfloat16)
    onesf32 = np.ones((1, 128), np.float32)

    in_maps = []
    for c in range(8):
        bsel = c // 2
        h = c % 2
        own0 = 1024 * h
        oth0 = 1024 * (1 - h)
        idx = np.concatenate([np.arange(own0, own0 + 1024), np.arange(oth0, oth0 + 1024)])
        xb = x[bsel]
        in_maps.append({
            "xT": np.ascontiguousarray(xb.T[:, idx]),
            "xr": np.ascontiguousarray(xb[own0:own0 + 1024]),
            "wqT": wq, "wkT": wk, "wvT": wv, "woT": wo,
            "bq": bqv, "bk": bkv, "bv": bvv, "bo": bov,
            "gp": np.ascontiguousarray(gpk[2048 * c:2048 * (c + 1)]),
            "gs": np.ascontiguousarray(gsc[2048 * c:2048 * (c + 1)]),
            "up": np.ascontiguousarray(upkk[2048 * c:2048 * (c + 1)]),
            "us": np.ascontiguousarray(upsc[2048 * c:2048 * (c + 1)]),
            "dp": np.ascontiguousarray(dpk[256 * c:256 * (c + 1)]),
            "ds": np.ascontiguousarray(dssc[256 * c:256 * (c + 1)]),
            "qcos": np.ascontiguousarray(cosF[:, own0:own0 + 1024]),
            "qsin": np.ascontiguousarray(sinH[:, own0:own0 + 1024]),
            "kcos": np.ascontiguousarray(cosF[:, idx]),
            "ksin": np.ascontiguousarray(sinH[:, idx]),
            "posq": np.ascontiguousarray(np.arange(own0, own0 + 1024, dtype=np.float32).reshape(8, 128).T),
            "poskk": idx.astype(np.float32).reshape(1, TKV),
            "iden": iden, "idenf": idenf32, "onescol": onescol, "onesf32": onesf32,
        })
    return in_maps


def kernel(**inputs):
    in_maps = prepare_in_maps(**inputs)
    nc = _get_nc()
    res = run_bass_kernel_spmd(nc, in_maps, core_ids=list(range(8)))
    outp = np.empty((4, 2048, 2048), np.float32)
    for c in range(8):
        bsel = c // 2
        h = c % 2
        outp[bsel, 1024 * h:1024 * (h + 1)] = res.results[c]["out"]
    return outp


# version 05
# speedup vs baseline: 1.0289x; 1.0289x over previous
import contextlib
import os
import tempfile

os.environ["NEURON_COMPILE_CACHE_URL"] = tempfile.mkdtemp(prefix="ncc_")

import numpy as np
import ml_dtypes
import concourse.bass as bass
import concourse.mybir as mybir
from concourse.bass_utils import run_bass_kernel_spmd

F32 = mybir.dt.float32
BF16 = mybir.dt.bfloat16
I32 = mybir.dt.int32
AF = mybir.ActivationFunctionType
ALU = mybir.AluOpType
AX = mybir.AxisListType.X

D = 2048
DK = 512
HD = 128
TOWN = 1024
TKV = 2048
MLP = 16384
NEG = -1.0e30
GELU_C = 0.044715
GELU_S = 0.7978845608028654
ISQ = 0.08838834764831845
EPS = 1e-5

DEQ_SLICES = 1
STAGE = 5
LNKV_SUB = 7
DBG = ""


class Ctr:
    def __init__(self, es, nc, name):
        self.h = es.enter_context(nc.semaphore(name))
        self.n = 0

    def inc(self, instr, k=1):
        instr.then_inc(self.h, k)
        self.n += k
        return self.n


def build():
    nc = bass.Bass("TRN2", target_bir_lowering=False)

    def din(name, shape, dt):
        return nc.dram_tensor(name, shape, dt, kind="ExternalInput").ap()

    xT = din("xT", [D, TKV], F32)
    xr = din("xr", [TOWN, D], F32)
    wqT = din("wqT", [D, D], F32)
    wkT = din("wkT", [D, DK], F32)
    wvT = din("wvT", [D, DK], F32)
    woT = din("woT", [D, D], F32)
    bq = din("bq", [128, 16], F32)
    bk = din("bk", [128, 4], F32)
    bv = din("bv", [128, 4], F32)
    bo = din("bo", [128, 16], F32)
    gp = din("gp", [2048 * DEQ_SLICES, 1024], I32)
    gs = din("gs", [2048 * DEQ_SLICES, 64], F32)
    upk = din("up", [2048 * DEQ_SLICES, 1024], I32)
    ups = din("us", [2048 * DEQ_SLICES, 64], F32)
    dp = din("dp", [256 * DEQ_SLICES, 8192], I32)
    dsc = din("ds", [256 * DEQ_SLICES, 512], F32)
    qcos = din("qcos", [HD, TOWN], F32)
    qsin = din("qsin", [HD, TOWN], F32)
    kcos = din("kcos", [HD, TKV], F32)
    ksin = din("ksin", [HD, TKV], F32)
    posq = din("posq", [128, 8], F32)
    poskk = din("poskk", [1, TKV], F32)
    iden = din("iden", [128, 128], BF16)
    idenf = din("idenf", [128, 128], F32)
    onescol = din("onescol", [128, 1], BF16)
    onesf32 = din("onesf32", [1, 128], F32)

    out = nc.dram_tensor("out", [TOWN, D], F32, kind="ExternalOutput").ap()

    wk_ = "ExternalOutput" if "w" in DBG else "Internal"
    if "k" in DBG:
        dbg_xn = nc.dram_tensor("dbg_xn", [128, 16, TOWN], BF16, kind="ExternalOutput").ap()
        dbg_kt = nc.dram_tensor("dbg_kt", [128, 4, TKV], BF16, kind="ExternalOutput").ap()
        dbg_vh = nc.dram_tensor("dbg_vh", [128, 4, 16, 128], BF16, kind="ExternalOutput").ap()
    if "a" in DBG:
        nc._dbg_a = dict(
            qt=nc.dram_tensor("dbg_qt", [128, TOWN], BF16, kind="ExternalOutput").ap(),
            pb=nc.dram_tensor("dbg_pb", [128, TKV], BF16, kind="ExternalOutput").ap(),
            ell=nc.dram_tensor("dbg_ell", [128, 1], F32, kind="ExternalOutput").ap(),
            s=nc.dram_tensor("dbg_s", [128, TKV], F32, kind="ExternalOutput").ap(),
            pb2=nc.dram_tensor("dbg_pb2", [128, TKV], BF16, kind="ExternalOutput").ap(),
            msk=nc.dram_tensor("dbg_msk", [128, 8, TKV], BF16, kind="ExternalOutput").ap(),
            yt=nc.dram_tensor("dbg_yt", [128, 16, TOWN], BF16, kind="ExternalOutput").ap(),
        )
    sk_ = "ExternalOutput" if "s" in DBG else "Internal"
    ggcF = nc.dram_tensor("ggc", [D, 2048], BF16, kind="Internal").ap()
    gucF = nc.dram_tensor("guc", [D, 2048], BF16, kind="Internal").ap()
    gdcF = nc.dram_tensor("gdc", [MLP, 256], BF16, kind="Internal").ap()
    ggc = ggcF.rearrange("(a k) n -> a k n", a=1)
    guc = gucF.rearrange("(a k) n -> a k n", a=1)
    gdc = gdcF.rearrange("(a k) n -> a k n", a=1)
    GgF = nc.dram_tensor("Gg", [8 * D, 2048], BF16, kind=wk_).ap()
    GuF = nc.dram_tensor("Gu", [8 * D, 2048], BF16, kind=wk_).ap()
    GdF = nc.dram_tensor("Gd", [8 * MLP, 256], BF16, kind=wk_).ap()
    Gg = GgF.rearrange("(s k) n -> s k n", s=8)
    Gu = GuF.rearrange("(s k) n -> s k n", s=8)
    Gd = GdF.rearrange("(s k) n -> s k n", s=8)
    sDram = nc.dram_tensor("sDram", [D, TOWN], BF16, kind=sk_).ap()

    es = contextlib.ExitStack()
    with es:
        E = es.enter_context
        xnT = E(nc.sbuf_tensor("xnT", [128, 16, TOWN], BF16))
        identt = E(nc.sbuf_tensor("identt", [128, 128], BF16))
        idenft = E(nc.sbuf_tensor("idenft", [128, 128], F32))
        onescolt = E(nc.sbuf_tensor("onescolt", [128, 1], BF16))
        onesf32t = E(nc.sbuf_tensor("onesf32t", [1, 128], F32))
        posqt = E(nc.sbuf_tensor("posqt", [128, 8], F32))
        bqt = E(nc.sbuf_tensor("bqt", [128, 16], F32))
        bot = E(nc.sbuf_tensor("bot", [128, 16], F32))
        bkt = E(nc.sbuf_tensor("bkt", [128, 4], F32))
        bvt = E(nc.sbuf_tensor("bvt", [128, 4], F32))
        cneg4 = E(nc.sbuf_tensor("cneg4", [128, 1], F32))

        C = {}
        for name in ["gd", "pe", "dv", "ac", "ou", "cn", "x", "ms", "st"]:
            C[name] = Ctr(es, nc, name)
        for name in ["pk", "ws", "gg", "gt", "xr"]:
            C[name] = [Ctr(es, nc, name + "0"), Ctr(es, nc, name + "1")]

        gd, pe, dv, ac, ou = C["gd"], C["pe"], C["dv"], C["ac"], C["ou"]

        blk = es.enter_context(nc.Block())
        SQ = nc.sync
        WQ = nc.scalar
        V = nc.vector
        A = nc.scalar
        T = nc.tensor
        ALL_E = (SQ, V, T, WQ, nc.gpsimd)

        def flat_ctrs():
            r = []
            for v in C.values():
                r.extend(v if isinstance(v, list) else [v])
            return r

        def barrier():
            vals = [(c, c.n) for c in flat_ctrs()]
            for e in ALL_E:
                for c, v in vals:
                    if v > 0:
                        e.wait_ge(c.h, v)

        cn = C["cn"]
        cn.inc(SQ.dma_start(out=identt[:], in_=iden), 16)
        cn.inc(SQ.dma_start(out=idenft[:], in_=idenf), 16)
        cn.inc(SQ.dma_start(out=onescolt[:], in_=onescol), 16)
        cn.inc(SQ.dma_start(out=onesf32t[:], in_=onesf32), 16)
        cn.inc(SQ.dma_start(out=posqt[:], in_=posq), 16)
        cn.inc(SQ.dma_start(out=bqt[:], in_=bq), 16)
        cn.inc(SQ.dma_start(out=bot[:], in_=bo), 16)
        cn.inc(SQ.dma_start(out=bkt[:], in_=bk), 16)
        cn.inc(SQ.dma_start(out=bvt[:], in_=bv), 16)
        dv_ = C["dv"]
        dv_.inc(V.memset(cneg4[:], -4.0))
        for e in (V, T, WQ, nc.gpsimd):
            e.wait_ge(cn.h, cn.n)
            e.wait_ge(dv_.h, 1)

        cc = Ctr(es, nc, "ccs")
        nc._cc = cc
        if STAGE >= 1:
            G_ = nc.gpsimd
            rg = [[0, 1, 2, 3, 4, 5, 6, 7]]
            _dequant(nc, SQ, WQ, V, T, C, identt, gp, gs, ggc, 1.0, "g")
            _dequant(nc, SQ, WQ, V, T, C, identt, upk, ups, guc, 1.0, "u")
            _dequant(nc, SQ, WQ, V, T, C, identt, dp, dsc, gdc, 0.5, "d")
            G_.wait_ge(gd.h, gd.n)
            cc.inc(G_.collective_compute("AllGather", ALU.bypass,
                                         replica_groups=rg,
                                         ins=[ggcF.opt()], outs=[GgF.opt()]), 1)
            cc.inc(G_.collective_compute("AllGather", ALU.bypass,
                                         replica_groups=rg,
                                         ins=[gucF.opt()], outs=[GuF.opt()]), 1)
            cc.inc(G_.collective_compute("AllGather", ALU.bypass,
                                         replica_groups=rg,
                                         ins=[gdcF.opt()], outs=[GdF.opt()]), 1)
        SQ.wait_ge(gd.h, gd.n)

        if STAGE >= 2:
            with contextlib.ExitStack() as es23:
                E23 = es23.enter_context
                poskkb = E23(nc.sbuf_tensor("poskkb", [128, TKV], F32))
                qcost = E23(nc.sbuf_tensor("qcost", [128, TOWN], F32))
                qsint = E23(nc.sbuf_tensor("qsint", [128, TOWN], F32))
                ms = C["ms"]
                ms.inc(SQ.dma_start(out=poskkb[:], in_=poskk.partition_broadcast(128)), 16)
                ms.inc(SQ.dma_start(out=qcost[:], in_=qcos), 16)
                ms.inc(SQ.dma_start(out=qsint[:], in_=qsin), 16)
                kvt = _ln_kv(nc, es23, SQ, V, A, T, C,
                             xT, wkT, wvT, bkt, bvt, kcos, ksin,
                             xnT, onescolt, onesf32t, identt)
                barrier()
                if "k" in DBG:
                    ou.inc(WQ.dma_start(out=dbg_xn, in_=xnT[:]), 16)
                    ou.inc(WQ.dma_start(out=dbg_kt, in_=kvt["KT"][:]), 16)
                    ou.inc(WQ.dma_start(out=dbg_vh, in_=kvt["Vh"][:]), 16)
                    for e in ALL_E:
                        e.wait_ge(ou.h, ou.n)
                if STAGE >= 3:
                    _attention(nc, es23, SQ, WQ, V, A, T, C,
                               wqT, woT, bqt, bot, qcost, qsint, posqt, poskkb,
                               xnT, sDram, kvt, identt, cneg4)
                barrier()

        if STAGE >= 4:
            with contextlib.ExitStack() as es45:
                gacc = es45.enter_context(nc.sbuf_tensor("gacc", [128, 16, TOWN], F32))
                _ffn(nc, SQ, V, A, T, C, Gg, Gu, Gd, xnT, gacc, nc._cc)
                barrier()
                if STAGE >= 5:
                    _combine(nc, SQ, WQ, V, A, T, C, xr, out, sDram, gacc, identt, idenft)

        SQ.wait_ge(ou.h, ou.n)

    return nc


def _dequant(nc, SQ, WQ, V, T, C, identt, packed, scales, Gout, smul, tag):
    gd, pe, dv = C["gd"], C["pe"], C["dv"]
    pk = C["pk"]
    KK = packed.shape[1]
    NB = scales.shape[1]
    n_loc = Gout.shape[2]
    n_rtiles = n_loc // 128
    n_cchunk = KK // 512
    nslices = packed.shape[0] // n_loc
    K_win = Gout.shape[1] // n_cchunk

    with contextlib.ExitStack() as es:
        SQ.wait_ge(gd.h, gd.n)
        E = es.enter_context
        pt = [E(nc.sbuf_tensor(f"dqp{tag}{i}", [128, 512], I32)) for i in range(2)]
        sct = [E(nc.sbuf_tensor(f"dqs{tag}{i}", [128, NB], F32)) for i in range(2)]
        sce = [E(nc.sbuf_tensor(f"dqe{tag}{i}", [128, NB], F32)) for i in range(2)]
        mtile = E(nc.sbuf_tensor(f"dqm{tag}", [128, 1024], I32))
        stile = E(nc.sbuf_tensor(f"dqt{tag}", [128, 1024], I32))
        btile = E(nc.sbuf_tensor(f"dqb{tag}", [128, 1024], I32))
        wtile = E(nc.sbuf_tensor(f"dqw{tag}", [128, 1024], BF16))
        rb = E(nc.sbuf_tensor(f"dqr{tag}", [128, 8, n_loc], BF16))
        tp = [E(nc.psum_tensor(f"dqx{tag}{i}", [128, 512], BF16)) for i in range(2)]

        dv_decode = {}
        dv_evac = {}
        pe_tp = {}
        gd_grp = {}

        u = 0
        for sl in range(nslices):
            for c in range(n_cchunk):
                for r in range(n_rtiles):
                    b = u % 2
                    if u >= 2:
                        SQ.wait_ge(dv.h, dv_decode[u - 2])
                    row0 = sl * n_loc + r * 128
                    pk[b].inc(SQ.dma_start(out=pt[b][:], in_=packed[row0:row0 + 128, c * 512:(c + 1) * 512]), 16)
                    pk[b].inc(SQ.dma_start(out=sct[b][:], in_=scales[row0:row0 + 128, :]), 16)

                    V.wait_ge(pk[b].h, pk[b].n)
                    if u >= 1:
                        V.wait_ge(pe.h, pe_tp[u - 1])
                    if r == 0:
                        g_prev = (sl * n_cchunk + c) - 1
                        if g_prev >= 0:
                            V.wait_ge(gd.h, gd_grp[g_prev])
                    V.tensor_scalar(sce[b][:], sct[b][:], float(smul * (2.0 ** 126)), None, ALU.mult)
                    p = pt[b][:]
                    V.tensor_scalar(mtile[:, 0:512], p, 7, 22, ALU.bitwise_and, ALU.logical_shift_left)
                    V.tensor_scalar(stile[:, 0:512], p, 8, 28, ALU.bitwise_and, ALU.logical_shift_left)
                    V.tensor_scalar(mtile[:, 512:1024], p, 0x70, 18, ALU.bitwise_and, ALU.logical_shift_left)
                    V.tensor_scalar(stile[:, 512:1024], p, 0x80, 24, ALU.bitwise_and, ALU.logical_shift_left)
                    V.tensor_tensor(btile[:], mtile[:], stile[:], ALU.bitwise_or)
                    scb = sce[b][:, 32 * c:32 * c + 32].unsqueeze(2).broadcast_to([128, 32, 16])
                    bl = btile[:, 0:512].bitcast(F32).rearrange("p (b j) -> p b j", b=32)
                    bh = btile[:, 512:1024].bitcast(F32).rearrange("p (b j) -> p b j", b=32)
                    V.tensor_tensor(wtile[:, 0:512].rearrange("p (b j) -> p b j", b=32), bl, scb, ALU.mult)
                    dv.inc(V.tensor_tensor(wtile[:, 512:1024].rearrange("p (b j) -> p b j", b=32), bh, scb, ALU.mult))
                    dv_decode[u] = dv.n

                    T.wait_ge(dv.h, dv_decode[u])
                    if u >= 1:
                        T.wait_ge(dv.h, dv_evac[u - 1])
                    for plane in range(2):
                        for s4 in range(4):
                            ins = T.transpose(tp[plane][:, 128 * s4:128 * (s4 + 1)],
                                              wtile[:, 512 * plane + 128 * s4:512 * plane + 128 * (s4 + 1)],
                                              identt[:])
                        pe.inc(ins)
                    pe_tp[u] = pe.n

                    V.wait_ge(pe.h, pe_tp[u])
                    V.tensor_copy(rb[:, 0:4, r * 128:(r + 1) * 128],
                                  tp[0][:].rearrange("p (s j) -> p s j", s=4))
                    dv.inc(V.tensor_copy(rb[:, 4:8, r * 128:(r + 1) * 128],
                                         tp[1][:].rearrange("p (s j) -> p s j", s=4)))
                    dv_evac[u] = dv.n
                    u += 1

                WQ.wait_ge(dv.h, dv_evac[u - 1])
                for plane in range(2):
                    for s4 in range(4):
                        base = K_win * c + 256 * s4 + plane
                        gd.inc(WQ.dma_start(out=Gout[sl, base:base + 255:2, :],
                                            in_=rb[:, plane * 4 + s4, :]), 16)
                gd_grp[sl * n_cchunk + c] = gd.n


def _ln_kv(nc, es, SQ, V, A, T, C, xT, wkT, wvT, bkt, bvt, kcos, ksin,
           xnT, onescolt, onesf32t, identt):
    pe, dv, ac, ms, sx, ws = C["pe"], C["dv"], C["ac"], C["ms"], C["x"], C["ws"]
    E = es.enter_context
    KT = E(nc.sbuf_tensor("KT", [128, 4, TKV], BF16))
    Vh = E(nc.sbuf_tensor("Vh", [128, 4, 16, 128], BF16))

    with contextlib.ExitStack() as es2:
        E2 = es2.enter_context
        kcost = E2(nc.sbuf_tensor("kcost", [128, TKV], F32))
        ksint = E2(nc.sbuf_tensor("ksint", [128, TKV], F32))
        wk16 = E2(nc.sbuf_tensor("wk16", [128, 16, DK], BF16))
        wv16 = E2(nc.sbuf_tensor("wv16", [128, 16, DK], BF16))
        xtc = E2(nc.sbuf_tensor("xtc", [128, 16, 512], F32))
        xn_t = E2(nc.sbuf_tensor("xn_t", [128, 16, 512], BF16))
        x16 = [E2(nc.sbuf_tensor(f"x16{i}", [128, 512], BF16)) for i in range(2)]
        x2 = [E2(nc.sbuf_tensor(f"x2{i}", [128, 512], BF16)) for i in range(2)]
        wstage = [E2(nc.sbuf_tensor(f"wst{i}", [128, 512], F32)) for i in range(2)]
        srow = E2(nc.sbuf_tensor("srow", [1, 512], F32))
        qrow = E2(nc.sbuf_tensor("qrow", [1, 512], F32))
        trow = E2(nc.sbuf_tensor("trow", [1, 512], F32))
        vrow = E2(nc.sbuf_tensor("vrow", [1, 512], F32))
        t1 = E2(nc.sbuf_tensor("lnt1", [128, 512], F32))
        kraw = E2(nc.sbuf_tensor("kraw", [128, 512], F32))
        krot = E2(nc.sbuf_tensor("krot", [128, 512], F32))
        kcs = E2(nc.sbuf_tensor("kcs", [128, 512], F32))
        vtmp = E2(nc.sbuf_tensor("vtmp", [128, 512], BF16))
        sum_p = E2(nc.psum_tensor("sum_p", [1, 512], F32))
        sq_p = E2(nc.psum_tensor("sq_p", [1, 512], F32))
        mb_p = E2(nc.psum_tensor("mb_p", [128, 512], F32))
        rb_p = E2(nc.psum_tensor("rb_p", [128, 512], F32))
        kv_p = E2(nc.psum_tensor("kv_p", [128, 512], F32))
        vv_p = E2(nc.psum_tensor("vv_p", [128, 512], F32))
        vt_p = E2(nc.psum_tensor("vt_p", [128, 512], BF16))

        ms.inc(SQ.dma_start(out=kcost[:], in_=kcos), 16)
        ms.inc(SQ.dma_start(out=ksint[:], in_=ksin), 16)
        MS_ALL = ms.n

        casts = {}
        for w in range(2):
            src = wkT if w == 0 else wvT
            dst = wk16 if w == 0 else wv16
            for kc in range(16):
                i = w * 16 + kc
                b = i % 2
                if i >= 2:
                    SQ.wait_ge(dv.h, casts[i - 2])
                ws[b].inc(SQ.dma_start(out=wstage[b][:], in_=src[128 * kc:128 * (kc + 1), :]), 16)
                V.wait_ge(ws[b].h, ws[b].n)
                dv.inc(V.tensor_copy(dst[:, kc, :], wstage[b][:]))
                casts[i] = dv.n
        V.wait_ge(ms.h, MS_ALL)

        dv_norm = {}
        pe_kv = {}
        dv_kv = dv.n
        for tc in range(4 if LNKV_SUB >= 2 else 0):
            own = tc < 2
            if tc >= 1:
                SQ.wait_ge(dv.h, dv_norm[tc - 1])
            for kc in range(16):
                sx.inc(SQ.dma_start(out=xtc[:, kc, :],
                                    in_=xT[128 * kc:128 * (kc + 1), 512 * tc:512 * (tc + 1)]), 16)
            sx_all = sx.n

            pe_stat = {}
            for kc in range(16):
                bb = kc % 2
                V.wait_ge(sx.h, sx_all)
                A.wait_ge(sx.h, sx_all)
                if kc >= 2:
                    V.wait_ge(pe.h, pe_stat[kc - 2])
                    A.wait_ge(pe.h, pe_stat[kc - 2])
                dv.inc(V.tensor_copy(x16[bb][:], xtc[:, kc, :]))
                ac.inc(A.activation(out=x2[bb][:], in_=xtc[:, kc, :], func=AF.Square))
                T.wait_ge(dv.h, dv.n)
                T.wait_ge(ac.h, ac.n)
                T.matmul(sum_p[:], onescolt[:], x16[bb][:], start=(kc == 0), stop=(kc == 15))
                pe.inc(T.matmul(sq_p[:], onescolt[:], x2[bb][:], start=(kc == 0), stop=(kc == 15)))
                pe_stat[kc] = pe.n
            pe_stats = pe.n

            if LNKV_SUB < 3:
                dv_norm[tc] = dv.n
                pe_kv[tc] = pe.n
                continue
            V.wait_ge(pe.h, pe_stats)
            V.tensor_scalar(srow[:], sum_p[:], 1.0 / D, None, ALU.mult)
            V.tensor_scalar(qrow[:], sq_p[:], 1.0 / D, None, ALU.mult)
            V.tensor_copy(t1[0:1, :], qrow[:])
            V.tensor_tensor(trow[:], srow[:], srow[:], ALU.mult)
            V.tensor_copy(t1[0:1, :], qrow[:])
            V.scalar_tensor_tensor(vrow[:], qrow[:], EPS, trow[:], ALU.add, ALU.subtract)
            dv.inc(V.tensor_copy(t1[0:1, :], qrow[:]))
            A.wait_ge(dv.h, dv.n)
            ac.inc(A.activation(out=trow[:], in_=vrow[:], func=AF.Sqrt))
            V.wait_ge(ac.h, ac.n)
            V.reciprocal(vrow[:], trow[:])
            dv.inc(V.tensor_copy(t1[0:1, :], qrow[:]))
            T.wait_ge(dv.h, dv.n)
            T.matmul(mb_p[:], onesf32t[:], srow[:], start=True, stop=True)
            pe.inc(T.matmul(rb_p[:], onesf32t[:], vrow[:], start=True, stop=True))
            pe_bc = pe.n

            if LNKV_SUB < 4:
                dv_norm[tc] = dv.n
                pe_kv[tc] = pe.n
                continue
            V.wait_ge(pe.h, pe_bc)
            if tc >= 3:
                V.wait_ge(pe.h, pe_kv[tc - 1])
            for kc in range(16):
                V.tensor_tensor(t1[:], xtc[:, kc, :], mb_p[:], ALU.subtract)
                dst = xnT[:, kc, 512 * tc:512 * (tc + 1)] if own else xn_t[:, kc, :]
                dv.inc(V.tensor_tensor(dst, t1[:], rb_p[:], ALU.mult))
            dv_norm[tc] = dv.n

            def xsrc(kc, tc=tc, own=own):
                return xnT[:, kc, 512 * tc:512 * (tc + 1)] if own else xn_t[:, kc, :]

            for kvh in range(4 if LNKV_SUB >= 5 else 0):
                T.wait_ge(dv.h, dv_norm[tc])
                T.wait_ge(dv.h, dv_kv)
                for kc in range(16):
                    T.matmul(kv_p[:], wk16[:, kc, 128 * kvh:128 * (kvh + 1)], xsrc(kc),
                             start=(kc == 0), stop=(kc == 15))
                for kc in range(16):
                    ins = T.matmul(vv_p[:], wv16[:, kc, 128 * kvh:128 * (kvh + 1)], xsrc(kc),
                                   start=(kc == 0), stop=(kc == 15))
                pe.inc(ins)
                pe_proj = pe.n
                if LNKV_SUB < 6:
                    dv_kv = dv.n
                    continue

                V.wait_ge(pe.h, pe_proj)
                cs = slice(512 * tc, 512 * (tc + 1))
                nops = (LNKV_SUB - 60) if 60 <= LNKV_SUB < 70 else 6
                ops = [
                    lambda: V.tensor_scalar(kraw[:], kv_p[:], bkt[:, kvh:kvh + 1], None, ALU.add),
                    lambda: V.tensor_tensor(krot[0:64, :], kraw[64:128, :], ksint[64:128, cs], ALU.mult),
                    lambda: V.tensor_tensor(krot[64:128, :], kraw[0:64, :], ksint[0:64, cs], ALU.mult),
                    lambda: V.tensor_tensor(kcs[:], kraw[:], kcost[:, cs], ALU.mult),
                    lambda: V.tensor_tensor(KT[:, kvh, cs], kcs[:], krot[:], ALU.add),
                    lambda: V.tensor_scalar(vtmp[:], vv_p[:], bvt[:, kvh:kvh + 1], None, ALU.add),
                ]
                last = None
                for _i in range(nops):
                    last = ops[_i]()
                if last is None:
                    last = V.engine_nop() if hasattr(V, "engine_nop") else V.nop()
                dv.inc(last)
                dv_kv = dv.n
                if LNKV_SUB < 7:
                    continue

                T.wait_ge(dv.h, dv_kv)
                for j in range(4):
                    ins = T.transpose(vt_p[:, 128 * j:128 * (j + 1)],
                                      vtmp[:, 128 * j:128 * (j + 1)], identt[:])
                pe.inc(ins)
                V.wait_ge(pe.h, pe.n)
                dv.inc(V.tensor_copy(Vh[:, kvh, 4 * tc:4 * tc + 4, :],
                                     vt_p[:].rearrange("p (s j) -> p s j", s=4)))
                dv_kv = dv.n
            pe_kv[tc] = pe.n
    return dict(KT=KT, Vh=Vh)


def _attention(nc, es, SQ, WQ, V, A, T, C, wqT, woT, bqt, bot,
               qcost, qsint, posqt, poskkb, xnT, sDram, kvt, identt, cneg4):
    gd, pe, dv, ac, ws = C["gd"], C["pe"], C["dv"], C["ac"], C["ws"]
    KT, Vh = kvt["KT"], kvt["Vh"]
    E = es.enter_context
    YT = E(nc.sbuf_tensor("YT", [128, 16, TOWN], BF16))
    masks = E(nc.sbuf_tensor("masks", [128, 8, TKV], BF16))

    with contextlib.ExitStack() as es2:
        E2 = es2.enter_context
        wstage = [E2(nc.sbuf_tensor(f"aws{i}", [128, 16, 128], F32)) for i in range(2)]
        w16 = [E2(nc.sbuf_tensor(f"aw16{i}", [128, 16, 128], BF16)) for i in range(2)]
        qraw = E2(nc.sbuf_tensor("qraw", [128, TOWN], F32))
        qrot = E2(nc.sbuf_tensor("qrot", [128, TOWN], F32))
        qcs = E2(nc.sbuf_tensor("qcs", [128, TOWN], F32))
        QT = E2(nc.sbuf_tensor("QT", [128, TOWN], BF16))
        Pb = E2(nc.sbuf_tensor("Pb", [128, TKV], BF16))
        PT = E2(nc.sbuf_tensor("PT", [128, 16, 512], BF16))
        stmp = E2(nc.sbuf_tensor("stmp", [128, TOWN], BF16))
        nmax = E2(nc.sbuf_tensor("nmax", [128, 1], F32))
        nmaxs = E2(nc.sbuf_tensor("nmaxs", [128, 1], F32))
        ell = E2(nc.sbuf_tensor("ell", [128, 1], F32))
        elli = E2(nc.sbuf_tensor("elli", [128, 1], F32))
        scr = E2(nc.sbuf_tensor("ascr", [128, 256], BF16))
        S_p = E2(nc.psum_tensor("S_p", [128, 2048], F32))
        qp_p = E2(nc.psum_tensor("qp_p", [128, 1024], F32))
        pt_p = [E2(nc.psum_tensor(f"pt_p{i}", [128, 512], BF16)) for i in range(2)]

        for qt in range(8):
            dv.inc(V.tensor_scalar(masks[:, qt, :], poskkb[:], posqt[:, qt:qt + 1], NEG,
                                   ALU.is_gt, ALU.mult))

        ac_exp = 0
        dv_pn = 0
        pe_ptp = 0
        dv_av = dv.n
        dv_qe = 0
        pe_w = {}
        ptev_bank = {0: 0, 1: 0}

        for qh in range(16):
            kvh = qh // 4
            b = qh % 2
            if qh >= 2:
                SQ.wait_ge(pe.h, pe_w[qh - 2])
            ws[b].inc(SQ.dma_start(out=wstage[b][:],
                                   in_=wqT[:, 128 * qh:128 * (qh + 1)].rearrange("(c p) e -> p c e", p=128)), 16)
            V.wait_ge(ws[b].h, ws[b].n)
            if qh >= 2:
                V.wait_ge(pe.h, pe_w[qh - 2])
            dv.inc(V.tensor_copy(w16[b][:], wstage[b][:]))
            dv_w = dv.n

            T.wait_ge(dv.h, dv_w)
            T.wait_ge(dv.h, max(dv_qe, dv_av))
            for th in range(2):
                for kc in range(16):
                    ins = T.matmul(qp_p[:, 512 * th:512 * (th + 1)], w16[b][:, kc, :],
                                   xnT[:, kc, 512 * th:512 * (th + 1)], start=(kc == 0), stop=(kc == 15))
            pe.inc(ins)
            pe_w[qh] = pe.n

            V.wait_ge(pe.h, pe_w[qh])
            V.tensor_scalar(qraw[:], qp_p[:], bqt[:, qh:qh + 1], None, ALU.add)
            V.tensor_tensor(qrot[0:64, :], qraw[64:128, :], qsint[64:128, :], ALU.mult)
            V.tensor_tensor(qrot[64:128, :], qraw[0:64, :], qsint[0:64, :], ALU.mult)
            V.tensor_tensor(qcs[:], qraw[:], qcost[:], ALU.mult)
            dv.inc(V.tensor_tensor(QT[:], qcs[:], qrot[:], ALU.add))
            dv_qe = dv.n
            if qh == 0 and hasattr(nc, "_dbg_a"):
                WQ.wait_ge(dv.h, dv_qe)
                C["ou"].inc(WQ.dma_start(out=nc._dbg_a["qt"], in_=QT[:]), 16)
                C["ou"].inc(WQ.dma_start(out=nc._dbg_a["msk"], in_=masks[:]), 16)

            for qt in range(8):
                T.wait_ge(dv.h, dv_qe)
                if ac_exp:
                    T.wait_ge(ac.h, ac_exp)
                T.wait_ge(dv.h, dv.n)
                for cc in range(4):
                    ins = T.matmul(S_p[:, 512 * cc:512 * (cc + 1)], QT[:, 128 * qt:128 * (qt + 1)],
                                   KT[:, kvh, 512 * cc:512 * (cc + 1)], start=True, stop=True)
                pe.inc(ins)
                pe_s = pe.n

                V.wait_ge(pe.h, pe_s)
                if qh == 0 and qt == 0 and hasattr(nc, "_dbg_a"):
                    V.tensor_copy(qraw[:], S_p[:, 0:1024])
                    dv.inc(V.tensor_copy(qcs[:], S_p[:, 1024:2048]))
                    WQ.wait_ge(dv.h, dv.n)
                    C["ou"].inc(WQ.dma_start(out=nc._dbg_a["s"][:, 0:1024], in_=qraw[:]), 16)
                    C["ou"].inc(WQ.dma_start(out=nc._dbg_a["s"][:, 1024:2048], in_=qcs[:]), 16)
                    V.wait_ge(C["ou"].h, C["ou"].n)
                dv.inc(V.tensor_tensor(S_p[:], S_p[:], masks[:, qt, :], ALU.add))
                dv_m = dv.n

                A.wait_ge(dv.h, dv_m)
                if dv_pn:
                    A.wait_ge(dv.h, dv_pn)
                if pe_ptp:
                    A.wait_ge(pe.h, pe_ptp)
                ac.inc(A.activation(out=Pb[:], in_=S_p[:], func=AF.Exp, scale=ISQ, bias=cneg4[:]))
                ac_exp = ac.n

                V.wait_ge(ac.h, ac_exp)
                if qh == 0 and qt == 0 and hasattr(nc, "_dbg_a"):
                    WQ.wait_ge(ac.h, ac_exp)
                    C["ou"].inc(WQ.dma_start(out=nc._dbg_a["pb2"], in_=Pb[:]), 16)
                    V.wait_ge(C["ou"].h, C["ou"].n)
                V.reduce_sum(ell[:], Pb[:], axis=AX)
                V.tensor_copy(scr[:], Pb[:, 0:256])
                V.reciprocal(elli[:], ell[:])
                V.tensor_copy(scr[:], Pb[:, 256:512])
                dv.inc(V.tensor_tensor(Pb[:], Pb[:],
                                       elli[:].broadcast_to([128, TKV]), ALU.mult))
                dv_pn = dv.n
                if qh == 0 and qt == 0 and hasattr(nc, "_dbg_a"):
                    WQ.wait_ge(dv.h, dv_pn)
                    C["ou"].inc(WQ.dma_start(out=nc._dbg_a["pb"], in_=Pb[:]), 16)
                    C["ou"].inc(WQ.dma_start(out=nc._dbg_a["ell"], in_=ell[:]), 16)
                    A.wait_ge(C["ou"].h, C["ou"].n)

                T.wait_ge(dv.h, dv_pn)
                q4 = qt % 4
                for g in range(4):
                    bank = g % 2
                    if ptev_bank[bank]:
                        T.wait_ge(dv.h, ptev_bank[bank])
                    if q4 == 0 and g < 2:
                        T.wait_ge(dv.h, dv_av)
                    for j4 in range(4):
                        j = 4 * g + j4
                        ins = T.transpose(pt_p[bank][:, 128 * j4:128 * (j4 + 1)],
                                          Pb[:, 128 * j:128 * (j + 1)], identt[:])
                    pe.inc(ins)
                    V.wait_ge(pe.h, pe.n)
                    dv.inc(V.tensor_copy(PT[:, 4 * g:4 * g + 4, 128 * q4:128 * (q4 + 1)],
                                         pt_p[bank][:].rearrange("p (s j) -> p s j", s=4)))
                    ptev_bank[bank] = dv.n
                pe_ptp = pe.n
                dv_ptev = dv.n

                if q4 == 3:
                    T.wait_ge(dv.h, dv_ptev)
                    for j in range(16):
                        ins = T.matmul(qp_p[:, 0:512], Vh[:, kvh, j, :], PT[:, j, :],
                                       start=(j == 0), stop=(j == 15))
                    pe.inc(ins)
                    V.wait_ge(pe.h, pe.n)
                    dv.inc(V.tensor_copy(YT[:, qh, 512 * (qt // 4):512 * (qt // 4 + 1)],
                                         qp_p[:, 0:512]))
                    dv_av = dv.n

        dv_yt = dv.n
        if hasattr(nc, "_dbg_a"):
            WQ.wait_ge(dv.h, dv_yt)
            C["ou"].inc(WQ.dma_start(out=nc._dbg_a["yt"], in_=YT[:]), 16)
        gd_s = 0
        dv_se = 0
        for ec in range(16):
            b = ec % 2
            last_use = pe_w[16 + ec - 2] if ec >= 2 else pe_w[14 + ec]
            SQ.wait_ge(pe.h, last_use)
            ws[b].inc(SQ.dma_start(out=wstage[b][:],
                                   in_=woT[:, 128 * ec:128 * (ec + 1)].rearrange("(c p) e -> p c e", p=128)), 16)
            V.wait_ge(ws[b].h, ws[b].n)
            V.wait_ge(pe.h, last_use)
            dv.inc(V.tensor_copy(w16[b][:], wstage[b][:]))
            dv_w2 = dv.n

            T.wait_ge(dv.h, dv_w2)
            T.wait_ge(dv.h, dv_yt)
            if dv_se:
                T.wait_ge(dv.h, dv_se)
            for th in range(2):
                for kc in range(16):
                    ins = T.matmul(qp_p[:, 512 * th:512 * (th + 1)], w16[b][:, kc, :],
                                   YT[:, kc, 512 * th:512 * (th + 1)], start=(kc == 0), stop=(kc == 15))
            pe.inc(ins)
            pe_w[16 + ec] = pe.n

            V.wait_ge(pe.h, pe_w[16 + ec])
            if gd_s:
                V.wait_ge(gd.h, gd_s)
            dv.inc(V.tensor_scalar(stmp[:], qp_p[:], bot[:, ec:ec + 1], None, ALU.add))
            dv_se = dv.n
            WQ.wait_ge(dv.h, dv_se)
            gd_s = gd.inc(WQ.dma_start(out=sDram[128 * ec:128 * (ec + 1), :], in_=stmp[:]), 16)


def _ffn(nc, SQ, V, A, T, C, Gg, Gu, Gd, xnT, gacc, cc):
    pe, dv, ac = C["pe"], C["dv"], C["ac"]
    sgg, sgt = C["gg"], C["gt"]
    NRING = 18
    with contextlib.ExitStack() as es2:
        E2 = es2.enter_context
        grng = E2(nc.sbuf_tensor("grng", [128, NRING, 512], BF16))
        urng = E2(nc.sbuf_tensor("urng", [128, NRING, 512], BF16))
        gdt = [E2(nc.sbuf_tensor(f"gdt{i}", [128, 16, 256], BF16)) for i in range(2)]
        interT = E2(nc.sbuf_tensor("interT", [128, 16, TOWN], BF16))
        ga = E2(nc.sbuf_tensor("ffa", [128, 1024], F32))
        gb = E2(nc.sbuf_tensor("ffb", [128, 1024], F32))
        gth = E2(nc.sbuf_tensor("ffth", [128, 1024], F32))
        gp_p = [E2(nc.psum_tensor(f"gp_p{i}", [128, 1024], F32)) for i in range(2)]
        up_p = [E2(nc.psum_tensor(f"up_p{i}", [128, 1024], F32)) for i in range(2)]

        dv.inc(V.memset(gacc[:], 0.0))

        group_pe = {}
        emitted = [0]
        n_strips = 8 * 4 * 16

        def strip_idx(seg, ng, kc):
            return (seg * 4 + ng) * 16 + kc

        SQ.wait_ge(cc.h, 2)
        cc_d = [False]

        def emit_loads(upto):
            while emitted[0] < min(upto, n_strips):
                s = emitted[0]
                seg, rem = divmod(s, 64)
                ng, kc = divmod(rem, 16)
                slot = s % NRING
                gidx = s // 16
                if s >= NRING:
                    SQ.wait_ge(pe.h, group_pe[(s - NRING) // 16])
                sgg[gidx % 2].inc(SQ.dma_start(out=grng[:, slot, :],
                                               in_=Gg[seg, 128 * kc:128 * (kc + 1), 512 * ng:512 * (ng + 1)]), 16)
                sgg[gidx % 2].inc(SQ.dma_start(out=urng[:, slot, :],
                                               in_=Gu[seg, 128 * kc:128 * (kc + 1), 512 * ng:512 * (ng + 1)]), 16)
                emitted[0] += 1

        dv_ep = {}
        dv_dacc = 0
        pe_dw = {}
        for seg in range(8):
            for ng in range(4):
                g = seg * 4 + ng
                emit_loads(strip_idx(seg, ng, 0) + NRING)
                sg_all = 32 * 16 * (g // 2 + 1)
                for ns in range(4):
                    nsub = 4 * ng + ns
                    gidx2 = seg * 16 + nsub
                    par = nsub % 2
                    T.wait_ge(sgg[g % 2].h, sg_all)
                    if gidx2 >= 2 and (gidx2 - 2) in dv_ep:
                        T.wait_ge(dv.h, dv_ep[gidx2 - 2])
                    if ng == 0 and ns < 2 and dv_dacc:
                        T.wait_ge(dv.h, dv_dacc)
                    for kc in range(16):
                        s = strip_idx(seg, ng, kc)
                        slot = s % NRING
                        for th in range(2):
                            T.matmul(gp_p[par][:, 512 * th:512 * (th + 1)],
                                     grng[:, slot, 128 * ns:128 * (ns + 1)],
                                     xnT[:, kc, 512 * th:512 * (th + 1)],
                                     start=(kc == 0), stop=(kc == 15))
                        for th in range(2):
                            ins = T.matmul(up_p[par][:, 512 * th:512 * (th + 1)],
                                           urng[:, slot, 128 * ns:128 * (ns + 1)],
                                           xnT[:, kc, 512 * th:512 * (th + 1)],
                                           start=(kc == 0), stop=(kc == 15))
                    pe.inc(ins)
                    pe_gu = pe.n

                    gpp = gp_p[par][:]
                    upp = up_p[par][:]
                    A.wait_ge(pe.h, pe_gu)
                    ac.inc(A.activation(out=ga[:], in_=gpp, func=AF.Square))
                    V.wait_ge(ac.h, ac.n)
                    V.tensor_scalar(ga[:], ga[:], GELU_C, 1.0, ALU.mult, ALU.add)
                    dv.inc(V.tensor_tensor(gb[:], ga[:], gpp, ALU.mult))
                    A.wait_ge(dv.h, dv.n)
                    ac.inc(A.activation(out=gth[:], in_=gb[:], func=AF.Tanh, scale=GELU_S))
                    V.wait_ge(ac.h, ac.n)
                    V.scalar_tensor_tensor(gb[:], gth[:], 1.0, gpp, ALU.add, ALU.mult)
                    dv.inc(V.tensor_tensor(interT[:, nsub, :], gb[:], upp, ALU.mult))
                    dv_ep[gidx2] = dv.n
                group_pe[g] = pe.n

            for sl in range(8):
                b = sl % 2
                di = seg * 8 + sl
                if di >= 2:
                    SQ.wait_ge(pe.h, pe_dw[di - 2])
                if not cc_d[0]:
                    SQ.wait_ge(cc.h, 3)
                    cc_d[0] = True
                sgt[b].inc(SQ.dma_start(out=gdt[b][:],
                                        in_=Gd[sl, 2048 * seg:2048 * (seg + 1), :].rearrange("(c p) h -> p c h", p=128)), 16)
                T.wait_ge(sgt[b].h, sgt[b].n)
                T.wait_ge(dv.h, dv_ep[seg * 16 + 15])
                for j in range(2):
                    if dv_dacc:
                        T.wait_ge(dv.h, dv_dacc)
                    for nci in range(16):
                        for th in range(2):
                            ins = T.matmul(gp_p[j][:, 512 * th:512 * (th + 1)],
                                           gdt[b][:, nci, 128 * j:128 * (j + 1)],
                                           interT[:, nci, 512 * th:512 * (th + 1)],
                                           start=(nci == 0), stop=(nci == 15))
                    pe.inc(ins)
                    V.wait_ge(pe.h, pe.n)
                    dv.inc(V.tensor_tensor(gacc[:, 2 * sl + j, :], gp_p[j][:],
                                           gacc[:, 2 * sl + j, :], ALU.add))
                    dv_dacc = dv.n
                pe_dw[di] = pe.n


def _combine(nc, SQ, WQ, V, A, T, C, xr, out, sDram, gacc, identt, idenft):
    pe, dv, ac, ou, st, sxr = C["pe"], C["dv"], C["ac"], C["ou"], C["st"], C["xr"]
    with contextlib.ExitStack() as es2:
        E2 = es2.enter_context
        outst = E2(nc.sbuf_tensor("outst", [128, 8, D], F32))
        stile = [E2(nc.sbuf_tensor(f"cst{i}", [128, TOWN], BF16)) for i in range(2)]
        silt = E2(nc.sbuf_tensor("silt", [128, TOWN], F32))
        xrt = [E2(nc.sbuf_tensor(f"xrt{i}", [128, D], F32)) for i in range(2)]
        cpa = E2(nc.sbuf_tensor("cpa", [128, 128], BF16))
        ta_p = E2(nc.psum_tensor("ta_p", [128, 128], BF16))
        tb_p = E2(nc.psum_tensor("tb_p", [128, 128], F32))

        dv_c = 0
        pe_tp2 = 0
        pe_ec = {}
        for ec in range(16):
            b = ec % 2
            if ec >= 2:
                SQ.wait_ge(pe.h, pe_ec[ec - 2])
            st.inc(SQ.dma_start(out=stile[b][:], in_=sDram[128 * ec:128 * (ec + 1), :]), 16)
            st_all = st.n
            if dv_c:
                A.wait_ge(dv.h, dv_c)
            if pe_tp2:
                A.wait_ge(pe.h, pe_tp2)
            ac.inc(A.activation(out=silt[:], in_=gacc[:, ec, :], func=AF.Silu))
            ac_sil = ac.n
            for t in range(8):
                T.wait_ge(st.h, st_all)
                T.wait_ge(ac.h, ac_sil)
                if dv_c:
                    T.wait_ge(dv.h, dv_c)
                T.transpose(ta_p[:], stile[b][:, 128 * t:128 * (t + 1)], identt[:])
                pe.inc(T.transpose(tb_p[:], silt[:, 128 * t:128 * (t + 1)], idenft[:]))
                pe_tp2 = pe.n
                V.wait_ge(pe.h, pe_tp2)
                V.tensor_copy(cpa[:], ta_p[:])
                dv.inc(V.tensor_tensor(outst[:, t, 128 * ec:128 * (ec + 1)], tb_p[:], cpa[:], ALU.mult))
                dv_c = dv.n
            pe_ec[ec] = pe.n

        dv_xr = {}
        for t in range(8):
            b = t % 2
            if t >= 2:
                SQ.wait_ge(dv.h, dv_xr[t - 2])
            sxr[b].inc(SQ.dma_start(out=xrt[b][:], in_=xr[128 * t:128 * (t + 1), :]), 16)
            V.wait_ge(sxr[b].h, sxr[b].n)
            dv.inc(V.tensor_tensor(outst[:, t, :], outst[:, t, :], xrt[b][:], ALU.add))
            dv_xr[t] = dv.n
            WQ.wait_ge(dv.h, dv.n)
            C["ou"].inc(WQ.dma_start(out=out[128 * t:128 * (t + 1), :], in_=outst[:, t, :]), 16)


_NC_CACHE = {}


def _get_nc():
    key = (DEQ_SLICES, STAGE, LNKV_SUB, DBG)
    if key not in _NC_CACHE:
        _NC_CACHE[key] = build()
    return _NC_CACHE[key]


def _rope_tables():
    i = np.arange(1, 65, dtype=np.float32)
    thetas = (np.float32(10000.0) ** (np.float32(-2.0) * (i - np.float32(1.0)) / np.float32(128.0))).astype(np.float32)
    pos = np.arange(1, TKV + 1, dtype=np.float32)
    ang = pos[:, None] * thetas[None, :]
    c = np.cos(ang).astype(np.float32)
    s = np.sin(ang).astype(np.float32)
    cosF = np.ascontiguousarray(np.concatenate([c, c], axis=1).T)
    sinH = np.ascontiguousarray(np.concatenate([s, -s], axis=1).T)
    return cosF, sinH


def prepare_in_maps(L0_attn_x, L0_attn_Wq, L0_attn_bq, L0_attn_Wk, L0_attn_bk, L0_attn_Wv,
                    L0_attn_bv, L0_attn_Wo, L0_attn_bo, L0_ffn_gate_W_packed, L0_ffn_gate_scales,
                    L0_ffn_up_W_packed, L0_ffn_up_scales, L0_ffn_down_W_packed, L0_ffn_down_scales):
    x = np.asarray(L0_attn_x, dtype=np.float32)
    wq = np.ascontiguousarray(np.asarray(L0_attn_Wq, np.float32).T)
    wk = np.ascontiguousarray(np.asarray(L0_attn_Wk, np.float32).T)
    wv = np.ascontiguousarray(np.asarray(L0_attn_Wv, np.float32).T)
    wo = np.ascontiguousarray(np.asarray(L0_attn_Wo, np.float32).T)
    bqv = np.ascontiguousarray(np.asarray(L0_attn_bq, np.float32).reshape(16, 128).T)
    bkv = np.ascontiguousarray(np.asarray(L0_attn_bk, np.float32).reshape(4, 128).T)
    bvv = np.ascontiguousarray(np.asarray(L0_attn_bv, np.float32).reshape(4, 128).T)
    bov = np.ascontiguousarray(np.asarray(L0_attn_bo, np.float32).reshape(16, 128).T)
    gpk = np.ascontiguousarray(np.asarray(L0_ffn_gate_W_packed, np.int32))
    gsc = np.ascontiguousarray(np.asarray(L0_ffn_gate_scales, np.float32))
    upkk = np.ascontiguousarray(np.asarray(L0_ffn_up_W_packed, np.int32))
    upsc = np.ascontiguousarray(np.asarray(L0_ffn_up_scales, np.float32))
    dpk = np.ascontiguousarray(np.asarray(L0_ffn_down_W_packed, np.int32))
    dssc = np.ascontiguousarray(np.asarray(L0_ffn_down_scales, np.float32))

    cosF, sinH = _rope_tables()
    iden = np.eye(128, dtype=np.float32).astype(ml_dtypes.bfloat16)
    idenf32 = np.eye(128, dtype=np.float32)
    onescol = np.ones((128, 1), np.float32).astype(ml_dtypes.bfloat16)
    onesf32 = np.ones((1, 128), np.float32)

    in_maps = []
    for c in range(8):
        bsel = c // 2
        h = c % 2
        own0 = 1024 * h
        oth0 = 1024 * (1 - h)
        idx = np.concatenate([np.arange(own0, own0 + 1024), np.arange(oth0, oth0 + 1024)])
        xb = x[bsel]
        in_maps.append({
            "xT": np.ascontiguousarray(xb.T[:, idx]),
            "xr": np.ascontiguousarray(xb[own0:own0 + 1024]),
            "wqT": wq, "wkT": wk, "wvT": wv, "woT": wo,
            "bq": bqv, "bk": bkv, "bv": bvv, "bo": bov,
            "gp": np.ascontiguousarray(gpk[2048 * c:2048 * (c + 1)]),
            "gs": np.ascontiguousarray(gsc[2048 * c:2048 * (c + 1)]),
            "up": np.ascontiguousarray(upkk[2048 * c:2048 * (c + 1)]),
            "us": np.ascontiguousarray(upsc[2048 * c:2048 * (c + 1)]),
            "dp": np.ascontiguousarray(dpk[256 * c:256 * (c + 1)]),
            "ds": np.ascontiguousarray(dssc[256 * c:256 * (c + 1)]),
            "qcos": np.ascontiguousarray(cosF[:, own0:own0 + 1024]),
            "qsin": np.ascontiguousarray(sinH[:, own0:own0 + 1024]),
            "kcos": np.ascontiguousarray(cosF[:, idx]),
            "ksin": np.ascontiguousarray(sinH[:, idx]),
            "posq": np.ascontiguousarray(np.arange(own0, own0 + 1024, dtype=np.float32).reshape(8, 128).T),
            "poskk": idx.astype(np.float32).reshape(1, TKV),
            "iden": iden, "idenf": idenf32, "onescol": onescol, "onesf32": onesf32,
        })
    return in_maps


def kernel(**inputs):
    in_maps = prepare_in_maps(**inputs)
    nc = _get_nc()
    res = run_bass_kernel_spmd(nc, in_maps, core_ids=list(range(8)))
    outp = np.empty((4, 2048, 2048), np.float32)
    for c in range(8):
        bsel = c // 2
        h = c % 2
        outp[bsel, 1024 * h:1024 * (h + 1)] = res.results[c]["out"]
    return outp


# version 06
# speedup vs baseline: 1.1766x; 1.1435x over previous
import contextlib
import os
import tempfile

os.environ["NEURON_COMPILE_CACHE_URL"] = tempfile.mkdtemp(prefix="ncc_")

import numpy as np
import ml_dtypes
import concourse.bass as bass
import concourse.mybir as mybir
from concourse.bass_utils import run_bass_kernel_spmd

F32 = mybir.dt.float32
BF16 = mybir.dt.bfloat16
I32 = mybir.dt.int32
AF = mybir.ActivationFunctionType
ALU = mybir.AluOpType
AX = mybir.AxisListType.X

D = 2048
DK = 512
HD = 128
TOWN = 1024
TKV = 2048
MLP = 16384
NEG = -1.0e30
GELU_C = 0.044715
GELU_S = 0.7978845608028654
ISQ = 0.08838834764831845
EPS = 1e-5

DEQ_SLICES = 1
STAGE = 5
LNKV_SUB = 7
DBG = ""


class Ctr:
    def __init__(self, es, nc, name):
        self.h = es.enter_context(nc.semaphore(name))
        self.n = 0

    def inc(self, instr, k=1):
        instr.then_inc(self.h, k)
        self.n += k
        return self.n


def build():
    nc = bass.Bass("TRN2", target_bir_lowering=False)

    def din(name, shape, dt):
        return nc.dram_tensor(name, shape, dt, kind="ExternalInput").ap()

    xT = din("xT", [D, TKV], F32)
    xr = din("xr", [TOWN, D], F32)
    wqT = din("wqT", [D, D], F32)
    wkT = din("wkT", [D, DK], F32)
    wvT = din("wvT", [D, DK], F32)
    woT = din("woT", [D, D], F32)
    bq = din("bq", [128, 16], F32)
    bk = din("bk", [128, 4], F32)
    bv = din("bv", [128, 4], F32)
    bo = din("bo", [128, 16], F32)
    gp = din("gp", [2048 * DEQ_SLICES, 1024], I32)
    gs = din("gs", [2048 * DEQ_SLICES, 64], F32)
    upk = din("up", [2048 * DEQ_SLICES, 1024], I32)
    ups = din("us", [2048 * DEQ_SLICES, 64], F32)
    dp = din("dp", [256 * DEQ_SLICES, 8192], I32)
    dsc = din("ds", [256 * DEQ_SLICES, 512], F32)
    qcos = din("qcos", [HD, TOWN], F32)
    qsin = din("qsin", [HD, TOWN], F32)
    kcos = din("kcos", [HD, TKV], F32)
    ksin = din("ksin", [HD, TKV], F32)
    posq = din("posq", [128, 8], F32)
    poskcol = din("poskcol", [128, 16], F32)
    poskk = din("poskk", [1, TKV], F32)
    iden = din("iden", [128, 128], BF16)
    idenf = din("idenf", [128, 128], F32)
    onescol = din("onescol", [128, 1], BF16)
    onesf32 = din("onesf32", [1, 128], F32)

    out = nc.dram_tensor("out", [TOWN, D], F32, kind="ExternalOutput").ap()

    wk_ = "ExternalOutput" if "w" in DBG else "Internal"
    if "k" in DBG:
        dbg_xn = nc.dram_tensor("dbg_xn", [128, 16, TOWN], BF16, kind="ExternalOutput").ap()
        dbg_kt = nc.dram_tensor("dbg_kt", [128, 4, TKV], BF16, kind="ExternalOutput").ap()
        dbg_vh = nc.dram_tensor("dbg_vh", [128, 4, 16, 128], BF16, kind="ExternalOutput").ap()
    if "a" in DBG:
        nc._dbg_a = dict(
            qt=nc.dram_tensor("dbg_qt", [128, TOWN], BF16, kind="ExternalOutput").ap(),
            pb=nc.dram_tensor("dbg_pb", [128, TKV], BF16, kind="ExternalOutput").ap(),
            ell=nc.dram_tensor("dbg_ell", [128, 1], F32, kind="ExternalOutput").ap(),
            s=nc.dram_tensor("dbg_s", [128, TKV], F32, kind="ExternalOutput").ap(),
            pb2=nc.dram_tensor("dbg_pb2", [128, TKV], BF16, kind="ExternalOutput").ap(),
            msk=nc.dram_tensor("dbg_msk", [128, 8, TKV], BF16, kind="ExternalOutput").ap(),
            yt=nc.dram_tensor("dbg_yt", [128, 16, TOWN], BF16, kind="ExternalOutput").ap(),
        )
    sk_ = "ExternalOutput" if "s" in DBG else "Internal"
    ggcF = nc.dram_tensor("ggc", [D, 2048], BF16, kind="Internal").ap()
    gucF = nc.dram_tensor("guc", [D, 2048], BF16, kind="Internal").ap()
    gdcF = nc.dram_tensor("gdc", [MLP, 256], BF16, kind="Internal").ap()
    ggc = ggcF.rearrange("(a k) n -> a k n", a=1)
    guc = gucF.rearrange("(a k) n -> a k n", a=1)
    gdc = gdcF.rearrange("(a k) n -> a k n", a=1)
    GgF = nc.dram_tensor("Gg", [8 * D, 2048], BF16, kind=wk_, addr_space="Shared").ap()
    GuF = nc.dram_tensor("Gu", [8 * D, 2048], BF16, kind=wk_, addr_space="Shared").ap()
    GdF = nc.dram_tensor("Gd", [8 * MLP, 256], BF16, kind=wk_, addr_space="Shared").ap()
    Gg = GgF.rearrange("(s k) n -> s k n", s=8)
    Gu = GuF.rearrange("(s k) n -> s k n", s=8)
    Gd = GdF.rearrange("(s k) n -> s k n", s=8)
    sDram = nc.dram_tensor("sDram", [D, TOWN], BF16, kind=sk_).ap()

    es = contextlib.ExitStack()
    with es:
        E = es.enter_context
        xnT = E(nc.sbuf_tensor("xnT", [128, 16, TOWN], BF16))
        identt = E(nc.sbuf_tensor("identt", [128, 128], BF16))
        idenft = E(nc.sbuf_tensor("idenft", [128, 128], F32))
        onescolt = E(nc.sbuf_tensor("onescolt", [128, 1], BF16))
        onesf32t = E(nc.sbuf_tensor("onesf32t", [1, 128], F32))
        posqt = E(nc.sbuf_tensor("posqt", [128, 8], F32))
        poskcolt = E(nc.sbuf_tensor("poskcolt", [128, 16], F32))
        bqt = E(nc.sbuf_tensor("bqt", [128, 16], F32))
        bot = E(nc.sbuf_tensor("bot", [128, 16], F32))
        bkt = E(nc.sbuf_tensor("bkt", [128, 4], F32))
        bvt = E(nc.sbuf_tensor("bvt", [128, 4], F32))
        cneg4 = E(nc.sbuf_tensor("cneg4", [128, 1], F32))

        C = {}
        for name in ["gd", "pe", "dv", "ac", "ou", "cn", "x", "ms", "st"]:
            C[name] = Ctr(es, nc, name)
        for name in ["pk", "ws", "gg", "gt", "xr"]:
            C[name] = [Ctr(es, nc, name + "0"), Ctr(es, nc, name + "1")]

        gd, pe, dv, ac, ou = C["gd"], C["pe"], C["dv"], C["ac"], C["ou"]

        blk = es.enter_context(nc.Block())
        SQ = nc.sync
        WQ = nc.scalar
        V = nc.vector
        A = nc.scalar
        T = nc.tensor
        ALL_E = (SQ, V, T, WQ, nc.gpsimd)

        def flat_ctrs():
            r = []
            for v in C.values():
                r.extend(v if isinstance(v, list) else [v])
            return r

        def barrier():
            vals = [(c, c.n) for c in flat_ctrs()]
            for e in ALL_E:
                for c, v in vals:
                    if v > 0:
                        e.wait_ge(c.h, v)

        cn = C["cn"]
        cn.inc(SQ.dma_start(out=identt[:], in_=iden), 16)
        cn.inc(SQ.dma_start(out=idenft[:], in_=idenf), 16)
        cn.inc(SQ.dma_start(out=onescolt[:], in_=onescol), 16)
        cn.inc(SQ.dma_start(out=onesf32t[:], in_=onesf32), 16)
        cn.inc(SQ.dma_start(out=posqt[:], in_=posq), 16)
        cn.inc(SQ.dma_start(out=poskcolt[:], in_=poskcol), 16)
        cn.inc(SQ.dma_start(out=bqt[:], in_=bq), 16)
        cn.inc(SQ.dma_start(out=bot[:], in_=bo), 16)
        cn.inc(SQ.dma_start(out=bkt[:], in_=bk), 16)
        cn.inc(SQ.dma_start(out=bvt[:], in_=bv), 16)
        dv_ = C["dv"]
        dv_.inc(V.memset(cneg4[:], -4.0))
        for e in (V, T, WQ, nc.gpsimd):
            e.wait_ge(cn.h, cn.n)
            e.wait_ge(dv_.h, 1)

        cc = Ctr(es, nc, "ccs")
        nc._cc = cc
        if STAGE >= 1:
            G_ = nc.gpsimd
            rg = [[0, 1, 2, 3, 4, 5, 6, 7]]
            _dequant(nc, SQ, WQ, V, T, C, identt, gp, gs, ggc, 1.0, "g")
            _dequant(nc, SQ, WQ, V, T, C, identt, upk, ups, guc, 1.0, "u")
            _dequant(nc, SQ, WQ, V, T, C, identt, dp, dsc, gdc, 0.5, "d")
            G_.wait_ge(gd.h, gd.n)
            cc.inc(G_.collective_compute("AllGather", ALU.bypass,
                                         replica_groups=rg,
                                         ins=[ggcF.opt()], outs=[GgF.opt()]), 1)
            cc.inc(G_.collective_compute("AllGather", ALU.bypass,
                                         replica_groups=rg,
                                         ins=[gucF.opt()], outs=[GuF.opt()]), 1)
            cc.inc(G_.collective_compute("AllGather", ALU.bypass,
                                         replica_groups=rg,
                                         ins=[gdcF.opt()], outs=[GdF.opt()]), 1)
        SQ.wait_ge(gd.h, gd.n)

        if STAGE >= 2:
            with contextlib.ExitStack() as es23:
                E23 = es23.enter_context
                poskkb = E23(nc.sbuf_tensor("poskkb", [128, TKV], F32))
                qcost = E23(nc.sbuf_tensor("qcost", [128, TOWN], F32))
                qsint = E23(nc.sbuf_tensor("qsint", [128, TOWN], F32))
                ms = C["ms"]
                ms.inc(SQ.dma_start(out=poskkb[:], in_=poskk.partition_broadcast(128)), 16)
                ms.inc(SQ.dma_start(out=qcost[:], in_=qcos), 16)
                ms.inc(SQ.dma_start(out=qsint[:], in_=qsin), 16)
                kvt = _ln_kv(nc, es23, SQ, V, A, T, C,
                             xT, wkT, wvT, bkt, bvt, kcos, ksin,
                             xnT, onescolt, onesf32t, identt)
                barrier()
                if "k" in DBG:
                    ou.inc(WQ.dma_start(out=dbg_xn, in_=xnT[:]), 16)
                    ou.inc(WQ.dma_start(out=dbg_kt, in_=kvt["KT"][:]), 16)
                    ou.inc(WQ.dma_start(out=dbg_vh, in_=kvt["Vh"][:]), 16)
                    for e in ALL_E:
                        e.wait_ge(ou.h, ou.n)
                if STAGE >= 3:
                    _attention(nc, es23, SQ, WQ, V, A, T, C,
                               wqT, woT, bqt, bot, qcost, qsint, poskcolt, poskkb,
                               xnT, sDram, kvt, identt, cneg4, onescolt, onesf32t)
                barrier()

        if STAGE >= 4:
            with contextlib.ExitStack() as es45:
                gacc = es45.enter_context(nc.sbuf_tensor("gacc", [128, 16, TOWN], F32))
                _ffn(nc, SQ, V, A, T, C, Gg, Gu, Gd, xnT, gacc, nc._cc)
                barrier()
                if STAGE >= 5:
                    _combine(nc, SQ, WQ, V, A, T, C, xr, out, sDram, gacc, identt, idenft)

        SQ.wait_ge(ou.h, ou.n)

    return nc


def _dequant(nc, SQ, WQ, V, T, C, identt, packed, scales, Gout, smul, tag):
    gd, pe, dv = C["gd"], C["pe"], C["dv"]
    pk = C["pk"]
    KK = packed.shape[1]
    NB = scales.shape[1]
    n_loc = Gout.shape[2]
    n_rtiles = n_loc // 128
    n_cchunk = KK // 512
    nslices = packed.shape[0] // n_loc
    K_win = Gout.shape[1] // n_cchunk

    with contextlib.ExitStack() as es:
        SQ.wait_ge(gd.h, gd.n)
        E = es.enter_context
        pt = [E(nc.sbuf_tensor(f"dqp{tag}{i}", [128, 512], I32)) for i in range(2)]
        sct = [E(nc.sbuf_tensor(f"dqs{tag}{i}", [128, NB], F32)) for i in range(2)]
        sce = [E(nc.sbuf_tensor(f"dqe{tag}{i}", [128, NB], F32)) for i in range(2)]
        mtile = E(nc.sbuf_tensor(f"dqm{tag}", [128, 1024], I32))
        stile = E(nc.sbuf_tensor(f"dqt{tag}", [128, 1024], I32))
        btile = E(nc.sbuf_tensor(f"dqb{tag}", [128, 1024], I32))
        wtile = E(nc.sbuf_tensor(f"dqw{tag}", [128, 1024], BF16))
        rb = E(nc.sbuf_tensor(f"dqr{tag}", [128, 8, n_loc], BF16))
        tp = [E(nc.psum_tensor(f"dqx{tag}{i}", [128, 512], BF16)) for i in range(2)]

        dv_decode = {}
        dv_evac = {}
        pe_tp = {}
        gd_grp = {}

        u = 0
        for sl in range(nslices):
            for c in range(n_cchunk):
                for r in range(n_rtiles):
                    b = u % 2
                    if u >= 2:
                        SQ.wait_ge(dv.h, dv_decode[u - 2])
                    row0 = sl * n_loc + r * 128
                    pk[b].inc(SQ.dma_start(out=pt[b][:], in_=packed[row0:row0 + 128, c * 512:(c + 1) * 512]), 16)
                    pk[b].inc(SQ.dma_start(out=sct[b][:], in_=scales[row0:row0 + 128, :]), 16)

                    V.wait_ge(pk[b].h, pk[b].n)
                    if u >= 1:
                        V.wait_ge(pe.h, pe_tp[u - 1])
                    if r == 0:
                        g_prev = (sl * n_cchunk + c) - 1
                        if g_prev >= 0:
                            V.wait_ge(gd.h, gd_grp[g_prev])
                    V.tensor_scalar(sce[b][:], sct[b][:], float(smul * (2.0 ** 126)), None, ALU.mult)
                    p = pt[b][:]
                    V.tensor_scalar(mtile[:, 0:512], p, 7, 22, ALU.bitwise_and, ALU.logical_shift_left)
                    V.tensor_scalar(stile[:, 0:512], p, 8, 28, ALU.bitwise_and, ALU.logical_shift_left)
                    V.tensor_scalar(mtile[:, 512:1024], p, 0x70, 18, ALU.bitwise_and, ALU.logical_shift_left)
                    V.tensor_scalar(stile[:, 512:1024], p, 0x80, 24, ALU.bitwise_and, ALU.logical_shift_left)
                    V.tensor_tensor(btile[:], mtile[:], stile[:], ALU.bitwise_or)
                    scb = sce[b][:, 32 * c:32 * c + 32].unsqueeze(2).broadcast_to([128, 32, 16])
                    bl = btile[:, 0:512].bitcast(F32).rearrange("p (b j) -> p b j", b=32)
                    bh = btile[:, 512:1024].bitcast(F32).rearrange("p (b j) -> p b j", b=32)
                    V.tensor_tensor(wtile[:, 0:512].rearrange("p (b j) -> p b j", b=32), bl, scb, ALU.mult)
                    dv.inc(V.tensor_tensor(wtile[:, 512:1024].rearrange("p (b j) -> p b j", b=32), bh, scb, ALU.mult))
                    dv_decode[u] = dv.n

                    T.wait_ge(dv.h, dv_decode[u])
                    if u >= 1:
                        T.wait_ge(dv.h, dv_evac[u - 1])
                    for plane in range(2):
                        for s4 in range(4):
                            ins = T.transpose(tp[plane][:, 128 * s4:128 * (s4 + 1)],
                                              wtile[:, 512 * plane + 128 * s4:512 * plane + 128 * (s4 + 1)],
                                              identt[:])
                        pe.inc(ins)
                    pe_tp[u] = pe.n

                    V.wait_ge(pe.h, pe_tp[u])
                    V.tensor_copy(rb[:, 0:4, r * 128:(r + 1) * 128],
                                  tp[0][:].rearrange("p (s j) -> p s j", s=4))
                    dv.inc(V.tensor_copy(rb[:, 4:8, r * 128:(r + 1) * 128],
                                         tp[1][:].rearrange("p (s j) -> p s j", s=4)))
                    dv_evac[u] = dv.n
                    u += 1

                WQ.wait_ge(dv.h, dv_evac[u - 1])
                for plane in range(2):
                    for s4 in range(4):
                        base = K_win * c + 256 * s4 + plane
                        gd.inc(WQ.dma_start(out=Gout[sl, base:base + 255:2, :],
                                            in_=rb[:, plane * 4 + s4, :]), 16)
                gd_grp[sl * n_cchunk + c] = gd.n


def _ln_kv(nc, es, SQ, V, A, T, C, xT, wkT, wvT, bkt, bvt, kcos, ksin,
           xnT, onescolt, onesf32t, identt):
    pe, dv, ac, ms, sx, ws = C["pe"], C["dv"], C["ac"], C["ms"], C["x"], C["ws"]
    E = es.enter_context
    KT = E(nc.sbuf_tensor("KT", [128, 4, TKV], BF16))
    Vh = E(nc.sbuf_tensor("Vh", [128, 4, 16, 128], BF16))

    with contextlib.ExitStack() as es2:
        E2 = es2.enter_context
        kcost = E2(nc.sbuf_tensor("kcost", [128, TKV], F32))
        ksint = E2(nc.sbuf_tensor("ksint", [128, TKV], F32))
        wk16 = E2(nc.sbuf_tensor("wk16", [128, 16, DK], BF16))
        wv16 = E2(nc.sbuf_tensor("wv16", [128, 16, DK], BF16))
        xtc = E2(nc.sbuf_tensor("xtc", [128, 16, 512], F32))
        xn_t = E2(nc.sbuf_tensor("xn_t", [128, 16, 512], BF16))
        x16 = [E2(nc.sbuf_tensor(f"x16{i}", [128, 512], BF16)) for i in range(2)]
        x2 = [E2(nc.sbuf_tensor(f"x2{i}", [128, 512], BF16)) for i in range(2)]
        wstage = [E2(nc.sbuf_tensor(f"wst{i}", [128, 512], F32)) for i in range(2)]
        srow = E2(nc.sbuf_tensor("srow", [1, 512], F32))
        qrow = E2(nc.sbuf_tensor("qrow", [1, 512], F32))
        trow = E2(nc.sbuf_tensor("trow", [1, 512], F32))
        vrow = E2(nc.sbuf_tensor("vrow", [1, 512], F32))
        t1 = E2(nc.sbuf_tensor("lnt1", [128, 512], F32))
        kraw = E2(nc.sbuf_tensor("kraw", [128, 512], F32))
        krot = E2(nc.sbuf_tensor("krot", [128, 512], F32))
        kcs = E2(nc.sbuf_tensor("kcs", [128, 512], F32))
        vtmp = E2(nc.sbuf_tensor("vtmp", [128, 512], BF16))
        sum_p = E2(nc.psum_tensor("sum_p", [1, 512], F32))
        sq_p = E2(nc.psum_tensor("sq_p", [1, 512], F32))
        mb_p = E2(nc.psum_tensor("mb_p", [128, 512], F32))
        rb_p = E2(nc.psum_tensor("rb_p", [128, 512], F32))
        kv_p = E2(nc.psum_tensor("kv_p", [128, 512], F32))
        vv_p = E2(nc.psum_tensor("vv_p", [128, 512], F32))
        vt_p = E2(nc.psum_tensor("vt_p", [128, 512], BF16))

        ms.inc(SQ.dma_start(out=kcost[:], in_=kcos), 16)
        ms.inc(SQ.dma_start(out=ksint[:], in_=ksin), 16)
        MS_ALL = ms.n

        casts = {}
        for w in range(2):
            src = wkT if w == 0 else wvT
            dst = wk16 if w == 0 else wv16
            for kc in range(16):
                i = w * 16 + kc
                b = i % 2
                if i >= 2:
                    SQ.wait_ge(dv.h, casts[i - 2])
                ws[b].inc(SQ.dma_start(out=wstage[b][:], in_=src[128 * kc:128 * (kc + 1), :]), 16)
                V.wait_ge(ws[b].h, ws[b].n)
                dv.inc(V.tensor_copy(dst[:, kc, :], wstage[b][:]))
                casts[i] = dv.n
        V.wait_ge(ms.h, MS_ALL)

        dv_norm = {}
        pe_kv = {}
        dv_kv = dv.n
        for tc in range(4 if LNKV_SUB >= 2 else 0):
            own = tc < 2
            if tc >= 1:
                SQ.wait_ge(dv.h, dv_norm[tc - 1])
            for kc in range(16):
                sx.inc(SQ.dma_start(out=xtc[:, kc, :],
                                    in_=xT[128 * kc:128 * (kc + 1), 512 * tc:512 * (tc + 1)]), 16)
            sx_all = sx.n

            pe_stat = {}
            for kc in range(16):
                bb = kc % 2
                V.wait_ge(sx.h, sx_all)
                A.wait_ge(sx.h, sx_all)
                if kc >= 2:
                    V.wait_ge(pe.h, pe_stat[kc - 2])
                    A.wait_ge(pe.h, pe_stat[kc - 2])
                dv.inc(V.tensor_copy(x16[bb][:], xtc[:, kc, :]))
                ac.inc(A.activation(out=x2[bb][:], in_=xtc[:, kc, :], func=AF.Square))
                T.wait_ge(dv.h, dv.n)
                T.wait_ge(ac.h, ac.n)
                T.matmul(sum_p[:], onescolt[:], x16[bb][:], start=(kc == 0), stop=(kc == 15))
                pe.inc(T.matmul(sq_p[:], onescolt[:], x2[bb][:], start=(kc == 0), stop=(kc == 15)))
                pe_stat[kc] = pe.n
            pe_stats = pe.n

            if LNKV_SUB < 3:
                dv_norm[tc] = dv.n
                pe_kv[tc] = pe.n
                continue
            V.wait_ge(pe.h, pe_stats)
            V.tensor_scalar(srow[:], sum_p[:], 1.0 / D, None, ALU.mult)
            V.tensor_scalar(qrow[:], sq_p[:], 1.0 / D, None, ALU.mult)
            V.tensor_copy(t1[0:1, :], qrow[:])
            V.tensor_tensor(trow[:], srow[:], srow[:], ALU.mult)
            V.tensor_copy(t1[0:1, :], qrow[:])
            V.scalar_tensor_tensor(vrow[:], qrow[:], EPS, trow[:], ALU.add, ALU.subtract)
            dv.inc(V.tensor_copy(t1[0:1, :], qrow[:]))
            A.wait_ge(dv.h, dv.n)
            ac.inc(A.activation(out=trow[:], in_=vrow[:], func=AF.Sqrt))
            V.wait_ge(ac.h, ac.n)
            V.reciprocal(vrow[:], trow[:])
            dv.inc(V.tensor_copy(t1[0:1, :], qrow[:]))
            T.wait_ge(dv.h, dv.n)
            T.matmul(mb_p[:], onesf32t[:], srow[:], start=True, stop=True)
            pe.inc(T.matmul(rb_p[:], onesf32t[:], vrow[:], start=True, stop=True))
            pe_bc = pe.n

            if LNKV_SUB < 4:
                dv_norm[tc] = dv.n
                pe_kv[tc] = pe.n
                continue
            V.wait_ge(pe.h, pe_bc)
            if tc >= 3:
                V.wait_ge(pe.h, pe_kv[tc - 1])
            for kc in range(16):
                V.tensor_tensor(t1[:], xtc[:, kc, :], mb_p[:], ALU.subtract)
                dst = xnT[:, kc, 512 * tc:512 * (tc + 1)] if own else xn_t[:, kc, :]
                dv.inc(V.tensor_tensor(dst, t1[:], rb_p[:], ALU.mult))
            dv_norm[tc] = dv.n

            def xsrc(kc, tc=tc, own=own):
                return xnT[:, kc, 512 * tc:512 * (tc + 1)] if own else xn_t[:, kc, :]

            for kvh in range(4 if LNKV_SUB >= 5 else 0):
                T.wait_ge(dv.h, dv_norm[tc])
                T.wait_ge(dv.h, dv_kv)
                for kc in range(16):
                    T.matmul(kv_p[:], wk16[:, kc, 128 * kvh:128 * (kvh + 1)], xsrc(kc),
                             start=(kc == 0), stop=(kc == 15))
                for kc in range(16):
                    ins = T.matmul(vv_p[:], wv16[:, kc, 128 * kvh:128 * (kvh + 1)], xsrc(kc),
                                   start=(kc == 0), stop=(kc == 15))
                pe.inc(ins)
                pe_proj = pe.n
                if LNKV_SUB < 6:
                    dv_kv = dv.n
                    continue

                V.wait_ge(pe.h, pe_proj)
                cs = slice(512 * tc, 512 * (tc + 1))
                nops = (LNKV_SUB - 60) if 60 <= LNKV_SUB < 70 else 6
                ops = [
                    lambda: V.tensor_scalar(kraw[:], kv_p[:], bkt[:, kvh:kvh + 1], None, ALU.add),
                    lambda: V.tensor_tensor(krot[0:64, :], kraw[64:128, :], ksint[64:128, cs], ALU.mult),
                    lambda: V.tensor_tensor(krot[64:128, :], kraw[0:64, :], ksint[0:64, cs], ALU.mult),
                    lambda: V.tensor_tensor(kcs[:], kraw[:], kcost[:, cs], ALU.mult),
                    lambda: V.tensor_tensor(KT[:, kvh, cs], kcs[:], krot[:], ALU.add),
                    lambda: V.tensor_scalar(vtmp[:], vv_p[:], bvt[:, kvh:kvh + 1], None, ALU.add),
                ]
                last = None
                for _i in range(nops):
                    last = ops[_i]()
                if last is None:
                    last = V.engine_nop() if hasattr(V, "engine_nop") else V.nop()
                dv.inc(last)
                dv_kv = dv.n
                if LNKV_SUB < 7:
                    continue

                T.wait_ge(dv.h, dv_kv)
                for j in range(4):
                    ins = T.transpose(vt_p[:, 128 * j:128 * (j + 1)],
                                      vtmp[:, 128 * j:128 * (j + 1)], identt[:])
                pe.inc(ins)
                V.wait_ge(pe.h, pe.n)
                dv.inc(V.tensor_copy(Vh[:, kvh, 4 * tc:4 * tc + 4, :],
                                     vt_p[:].rearrange("p (s j) -> p s j", s=4)))
                dv_kv = dv.n
            pe_kv[tc] = pe.n
    return dict(KT=KT, Vh=Vh)


def _attention(nc, es, SQ, WQ, V, A, T, C, wqT, woT, bqt, bot,
               qcost, qsint, poskcolt, poskkb, xnT, sDram, kvt, identt, cneg4,
               onescolt, onesf32t):
    gd, pe, dv, ac, ws = C["gd"], C["pe"], C["dv"], C["ac"], C["ws"]
    KT, Vh = kvt["KT"], kvt["Vh"]
    E = es.enter_context
    YT = E(nc.sbuf_tensor("YT", [128, 16, TOWN], BF16))
    mask01 = E(nc.sbuf_tensor("mask01", [128, 16, TOWN], BF16))

    with contextlib.ExitStack() as es2:
        E2 = es2.enter_context
        wstage = [E2(nc.sbuf_tensor(f"aws{i}", [128, 16, 128], F32)) for i in range(2)]
        w16 = [E2(nc.sbuf_tensor(f"aw16{i}", [128, 16, 128], BF16)) for i in range(2)]
        qraw = E2(nc.sbuf_tensor("qraw", [128, TOWN], F32))
        qrot = E2(nc.sbuf_tensor("qrot", [128, TOWN], F32))
        qcs = E2(nc.sbuf_tensor("qcs", [128, TOWN], F32))
        QT = E2(nc.sbuf_tensor("QT", [128, TOWN], BF16))
        Pe = E2(nc.sbuf_tensor("Pe", [128, 4, 512], BF16))
        Pm = E2(nc.sbuf_tensor("Pm", [128, 4, 512], BF16))
        elrec = E2(nc.sbuf_tensor("elrec", [1, TOWN], F32))
        ebs = E2(nc.sbuf_tensor("ebs", [128, 2, 512], F32))
        stmp = E2(nc.sbuf_tensor("stmp", [128, TOWN], BF16))
        st_p = [E2(nc.psum_tensor(f"st_p{i}", [128, 512], F32)) for i in range(2)]
        y_p = [E2(nc.psum_tensor(f"y_p{i}", [128, 512], F32)) for i in range(2)]
        eb_p = [E2(nc.psum_tensor(f"eb_p{i}", [128, 512], F32)) for i in range(2)]
        qp_p = E2(nc.psum_tensor("qp_p", [128, 1024], F32))

        for s in range(16):
            dv.inc(V.tensor_scalar(mask01[:, s, :], poskkb[:, 0:TOWN],
                                   poskcolt[:, s:s + 1], None, ALU.is_ge))
        dv_mask_built = dv.n

        ac_exp = {}
        dv_mask = {}
        pe_st = {}
        pe_yell = {}
        dv_qe = 0
        pe_stlast = 0
        dv_ynorm = {0: 0, 1: 0}
        pe_rank1 = {0: 0, 1: 0}
        pe_w = {}
        pending = None

        def flush_pending():
            pg, pqh, phalf, ps, pslot = pending
            pqs = slice(512 * phalf, 512 * (phalf + 1))
            pkvh = pqh // 4
            T.wait_ge(dv.h, dv_mask[pg])
            T.matmul(y_p[phalf][:], Vh[:, pkvh, ps, :], Pm[:, pslot, :],
                     start=(ps == 0), stop=(ps == 15))
            pe.inc(T.matmul(eb_p[phalf][0:1, :], onescolt[:], Pm[:, pslot, :],
                            start=(ps == 0), stop=(ps == 15)))
            pe_yell[pg] = pe.n
            if ps == 15:
                V.wait_ge(pe.h, pe_yell[pg])
                V.wait_ge(pe.h, pe_rank1[phalf])
                dv.inc(V.reciprocal(elrec[0:1, pqs], eb_p[phalf][0:1, :]))
                dv_rec = dv.n
                T.wait_ge(dv.h, dv_rec)
                pe.inc(T.matmul(eb_p[phalf][:], onesf32t[:], elrec[0:1, pqs],
                                start=True, stop=True))
                pe_rank1[phalf] = pe.n
                V.wait_ge(pe.h, pe_rank1[phalf])
                V.tensor_copy(ebs[:, phalf, :], eb_p[phalf][:])
                dv.inc(V.tensor_tensor(YT[:, pqh, pqs], y_p[phalf][:],
                                       ebs[:, phalf, :], ALU.mult))
                dv_ynorm[phalf] = dv.n

        g = 0
        for qh in range(16):
            kvh = qh // 4
            b = qh % 2
            if qh >= 2:
                SQ.wait_ge(pe.h, pe_w[qh - 2])
            ws[b].inc(SQ.dma_start(out=wstage[b][:],
                                   in_=wqT[:, 128 * qh:128 * (qh + 1)].rearrange("(c p) e -> p c e", p=128)), 16)
            V.wait_ge(ws[b].h, ws[b].n)
            if qh >= 2:
                V.wait_ge(pe.h, pe_w[qh - 2])
            dv.inc(V.tensor_copy(w16[b][:], wstage[b][:]))
            dv_w = dv.n

            T.wait_ge(dv.h, dv_w)
            T.wait_ge(dv.h, dv_qe)
            for th in range(2):
                for kc in range(16):
                    ins = T.matmul(qp_p[:, 512 * th:512 * (th + 1)], w16[b][:, kc, :],
                                   xnT[:, kc, 512 * th:512 * (th + 1)], start=(kc == 0), stop=(kc == 15))
            pe.inc(ins)
            pe_w[qh] = pe.n

            V.wait_ge(pe.h, pe_w[qh])
            V.wait_ge(pe.h, pe_stlast)
            V.tensor_scalar(qraw[:], qp_p[:], bqt[:, qh:qh + 1], None, ALU.add)
            V.tensor_tensor(qrot[0:64, :], qraw[64:128, :], qsint[64:128, :], ALU.mult)
            V.tensor_tensor(qrot[64:128, :], qraw[0:64, :], qsint[0:64, :], ALU.mult)
            V.tensor_tensor(qcs[:], qraw[:], qcost[:], ALU.mult)
            dv.inc(V.tensor_tensor(QT[:], qcs[:], qrot[:], ALU.add))
            dv_qe = dv.n

            for half in range(2):
                qs = slice(512 * half, 512 * (half + 1))
                for s in range(16):
                    par = g % 2
                    slot = g % 4
                    T.wait_ge(dv.h, dv_qe)
                    if s == 0:
                        T.wait_ge(dv.h, max(dv_mask_built, dv_ynorm[half]))
                    if g >= 2:
                        T.wait_ge(ac.h, ac_exp[g - 2])
                    pe.inc(T.matmul(st_p[par][:], KT[:, kvh, 128 * s:128 * (s + 1)],
                                    QT[:, qs], start=True, stop=True))
                    pe_st[g] = pe.n
                    if half == 1 and s == 15:
                        pe_stlast = pe.n

                    A.wait_ge(pe.h, pe_st[g])
                    if g >= 4:
                        A.wait_ge(dv.h, dv_mask[g - 4])
                    ac.inc(A.activation(out=Pe[:, slot, :], in_=st_p[par][:],
                                        func=AF.Exp, scale=ISQ, bias=cneg4[:]))
                    ac_exp[g] = ac.n

                    V.wait_ge(ac.h, ac_exp[g])
                    if g >= 4:
                        V.wait_ge(pe.h, pe_yell[g - 4])
                    dv.inc(V.tensor_tensor(Pm[:, slot, :], Pe[:, slot, :],
                                           mask01[:, s, qs], ALU.mult))
                    dv_mask[g] = dv.n

                    if pending is not None:
                        flush_pending()
                    pending = (g, qh, half, s, slot)
                    g += 1
        flush_pending()
        pending = None

        dv_yt = dv.n
        gd_s = 0
        dv_se = 0
        for ec in range(16):
            b = ec % 2
            last_use = pe_w[16 + ec - 2] if ec >= 2 else pe_w[14 + ec]
            SQ.wait_ge(pe.h, last_use)
            ws[b].inc(SQ.dma_start(out=wstage[b][:],
                                   in_=woT[:, 128 * ec:128 * (ec + 1)].rearrange("(c p) e -> p c e", p=128)), 16)
            V.wait_ge(ws[b].h, ws[b].n)
            V.wait_ge(pe.h, last_use)
            dv.inc(V.tensor_copy(w16[b][:], wstage[b][:]))
            dv_w2 = dv.n

            T.wait_ge(dv.h, dv_w2)
            T.wait_ge(dv.h, dv_yt)
            if dv_se:
                T.wait_ge(dv.h, dv_se)
            for th in range(2):
                for kc in range(16):
                    ins = T.matmul(qp_p[:, 512 * th:512 * (th + 1)], w16[b][:, kc, :],
                                   YT[:, kc, 512 * th:512 * (th + 1)], start=(kc == 0), stop=(kc == 15))
            pe.inc(ins)
            pe_w[16 + ec] = pe.n

            V.wait_ge(pe.h, pe_w[16 + ec])
            if gd_s:
                V.wait_ge(gd.h, gd_s)
            dv.inc(V.tensor_scalar(stmp[:], qp_p[:], bot[:, ec:ec + 1], None, ALU.add))
            dv_se = dv.n
            WQ.wait_ge(dv.h, dv_se)
            gd_s = gd.inc(WQ.dma_start(out=sDram[128 * ec:128 * (ec + 1), :], in_=stmp[:]), 16)


def _ffn(nc, SQ, V, A, T, C, Gg, Gu, Gd, xnT, gacc, cc):
    pe, dv, ac = C["pe"], C["dv"], C["ac"]
    sgg, sgt = C["gg"], C["gt"]
    NRING = 18
    with contextlib.ExitStack() as es2:
        E2 = es2.enter_context
        grng = E2(nc.sbuf_tensor("grng", [128, NRING, 512], BF16))
        urng = E2(nc.sbuf_tensor("urng", [128, NRING, 512], BF16))
        gdt = [E2(nc.sbuf_tensor(f"gdt{i}", [128, 16, 256], BF16)) for i in range(2)]
        interT = E2(nc.sbuf_tensor("interT", [128, 16, TOWN], BF16))
        ga = E2(nc.sbuf_tensor("ffa", [128, 1024], F32))
        gb = E2(nc.sbuf_tensor("ffb", [128, 1024], F32))
        gth = E2(nc.sbuf_tensor("ffth", [128, 1024], F32))
        gp_p = [E2(nc.psum_tensor(f"gp_p{i}", [128, 1024], F32)) for i in range(2)]
        up_p = [E2(nc.psum_tensor(f"up_p{i}", [128, 1024], F32)) for i in range(2)]

        dv.inc(V.memset(gacc[:], 0.0))

        group_pe = {}
        emitted = [0]
        n_strips = 8 * 4 * 16

        def strip_idx(seg, ng, kc):
            return (seg * 4 + ng) * 16 + kc

        SQ.wait_ge(cc.h, 2)
        cc_d = [False]

        def emit_loads(upto):
            while emitted[0] < min(upto, n_strips):
                s = emitted[0]
                seg, rem = divmod(s, 64)
                ng, kc = divmod(rem, 16)
                slot = s % NRING
                gidx = s // 16
                if s >= NRING:
                    SQ.wait_ge(pe.h, group_pe[(s - NRING) // 16])
                sgg[gidx % 2].inc(SQ.dma_start(out=grng[:, slot, :],
                                               in_=Gg[seg, 128 * kc:128 * (kc + 1), 512 * ng:512 * (ng + 1)]), 16)
                sgg[gidx % 2].inc(SQ.dma_start(out=urng[:, slot, :],
                                               in_=Gu[seg, 128 * kc:128 * (kc + 1), 512 * ng:512 * (ng + 1)]), 16)
                emitted[0] += 1

        dv_ep = {}
        dv_dacc = 0
        pe_dw = {}
        for seg in range(8):
            for ng in range(4):
                g = seg * 4 + ng
                emit_loads(strip_idx(seg, ng, 0) + NRING)
                sg_all = 32 * 16 * (g // 2 + 1)
                for ns in range(4):
                    nsub = 4 * ng + ns
                    gidx2 = seg * 16 + nsub
                    par = nsub % 2
                    T.wait_ge(sgg[g % 2].h, sg_all)
                    if gidx2 >= 2 and (gidx2 - 2) in dv_ep:
                        T.wait_ge(dv.h, dv_ep[gidx2 - 2])
                    if ng == 0 and ns < 2 and dv_dacc:
                        T.wait_ge(dv.h, dv_dacc)
                    for kc in range(16):
                        s = strip_idx(seg, ng, kc)
                        slot = s % NRING
                        for th in range(2):
                            T.matmul(gp_p[par][:, 512 * th:512 * (th + 1)],
                                     grng[:, slot, 128 * ns:128 * (ns + 1)],
                                     xnT[:, kc, 512 * th:512 * (th + 1)],
                                     start=(kc == 0), stop=(kc == 15))
                        for th in range(2):
                            ins = T.matmul(up_p[par][:, 512 * th:512 * (th + 1)],
                                           urng[:, slot, 128 * ns:128 * (ns + 1)],
                                           xnT[:, kc, 512 * th:512 * (th + 1)],
                                           start=(kc == 0), stop=(kc == 15))
                    pe.inc(ins)
                    pe_gu = pe.n

                    gpp = gp_p[par][:]
                    upp = up_p[par][:]
                    A.wait_ge(pe.h, pe_gu)
                    ac.inc(A.activation(out=ga[:], in_=gpp, func=AF.Square))
                    V.wait_ge(ac.h, ac.n)
                    V.tensor_scalar(ga[:], ga[:], GELU_C, 1.0, ALU.mult, ALU.add)
                    dv.inc(V.tensor_tensor(gb[:], ga[:], gpp, ALU.mult))
                    A.wait_ge(dv.h, dv.n)
                    ac.inc(A.activation(out=gth[:], in_=gb[:], func=AF.Tanh, scale=GELU_S))
                    V.wait_ge(ac.h, ac.n)
                    V.scalar_tensor_tensor(gb[:], gth[:], 1.0, gpp, ALU.add, ALU.mult)
                    dv.inc(V.tensor_tensor(interT[:, nsub, :], gb[:], upp, ALU.mult))
                    dv_ep[gidx2] = dv.n
                group_pe[g] = pe.n

            for sl in range(8):
                b = sl % 2
                di = seg * 8 + sl
                if di >= 2:
                    SQ.wait_ge(pe.h, pe_dw[di - 2])
                if not cc_d[0]:
                    SQ.wait_ge(cc.h, 3)
                    cc_d[0] = True
                sgt[b].inc(SQ.dma_start(out=gdt[b][:],
                                        in_=Gd[sl, 2048 * seg:2048 * (seg + 1), :].rearrange("(c p) h -> p c h", p=128)), 16)
                T.wait_ge(sgt[b].h, sgt[b].n)
                T.wait_ge(dv.h, dv_ep[seg * 16 + 15])
                for j in range(2):
                    if dv_dacc:
                        T.wait_ge(dv.h, dv_dacc)
                    for nci in range(16):
                        for th in range(2):
                            ins = T.matmul(gp_p[j][:, 512 * th:512 * (th + 1)],
                                           gdt[b][:, nci, 128 * j:128 * (j + 1)],
                                           interT[:, nci, 512 * th:512 * (th + 1)],
                                           start=(nci == 0), stop=(nci == 15))
                    pe.inc(ins)
                    V.wait_ge(pe.h, pe.n)
                    dv.inc(V.tensor_tensor(gacc[:, 2 * sl + j, :], gp_p[j][:],
                                           gacc[:, 2 * sl + j, :], ALU.add))
                    dv_dacc = dv.n
                pe_dw[di] = pe.n


def _combine(nc, SQ, WQ, V, A, T, C, xr, out, sDram, gacc, identt, idenft):
    pe, dv, ac, ou, st, sxr = C["pe"], C["dv"], C["ac"], C["ou"], C["st"], C["xr"]
    with contextlib.ExitStack() as es2:
        E2 = es2.enter_context
        outst = E2(nc.sbuf_tensor("outst", [128, 8, D], F32))
        stile = [E2(nc.sbuf_tensor(f"cst{i}", [128, TOWN], BF16)) for i in range(2)]
        silt = E2(nc.sbuf_tensor("silt", [128, TOWN], F32))
        xrt = [E2(nc.sbuf_tensor(f"xrt{i}", [128, D], F32)) for i in range(2)]
        cpa = E2(nc.sbuf_tensor("cpa", [128, 128], BF16))
        ta_p = E2(nc.psum_tensor("ta_p", [128, 128], BF16))
        tb_p = E2(nc.psum_tensor("tb_p", [128, 128], F32))

        dv_c = 0
        pe_tp2 = 0
        pe_ec = {}
        for ec in range(16):
            b = ec % 2
            if ec >= 2:
                SQ.wait_ge(pe.h, pe_ec[ec - 2])
            st.inc(SQ.dma_start(out=stile[b][:], in_=sDram[128 * ec:128 * (ec + 1), :]), 16)
            st_all = st.n
            if dv_c:
                A.wait_ge(dv.h, dv_c)
            if pe_tp2:
                A.wait_ge(pe.h, pe_tp2)
            ac.inc(A.activation(out=silt[:], in_=gacc[:, ec, :], func=AF.Silu))
            ac_sil = ac.n
            for t in range(8):
                T.wait_ge(st.h, st_all)
                T.wait_ge(ac.h, ac_sil)
                if dv_c:
                    T.wait_ge(dv.h, dv_c)
                T.transpose(ta_p[:], stile[b][:, 128 * t:128 * (t + 1)], identt[:])
                pe.inc(T.transpose(tb_p[:], silt[:, 128 * t:128 * (t + 1)], idenft[:]))
                pe_tp2 = pe.n
                V.wait_ge(pe.h, pe_tp2)
                V.tensor_copy(cpa[:], ta_p[:])
                dv.inc(V.tensor_tensor(outst[:, t, 128 * ec:128 * (ec + 1)], tb_p[:], cpa[:], ALU.mult))
                dv_c = dv.n
            pe_ec[ec] = pe.n

        dv_xr = {}
        for t in range(8):
            b = t % 2
            if t >= 2:
                SQ.wait_ge(dv.h, dv_xr[t - 2])
            sxr[b].inc(SQ.dma_start(out=xrt[b][:], in_=xr[128 * t:128 * (t + 1), :]), 16)
            V.wait_ge(sxr[b].h, sxr[b].n)
            dv.inc(V.tensor_tensor(outst[:, t, :], outst[:, t, :], xrt[b][:], ALU.add))
            dv_xr[t] = dv.n
            WQ.wait_ge(dv.h, dv.n)
            C["ou"].inc(WQ.dma_start(out=out[128 * t:128 * (t + 1), :], in_=outst[:, t, :]), 16)


_NC_CACHE = {}


def _get_nc():
    key = (DEQ_SLICES, STAGE, LNKV_SUB, DBG)
    if key not in _NC_CACHE:
        _NC_CACHE[key] = build()
    return _NC_CACHE[key]


def _rope_tables():
    i = np.arange(1, 65, dtype=np.float32)
    thetas = (np.float32(10000.0) ** (np.float32(-2.0) * (i - np.float32(1.0)) / np.float32(128.0))).astype(np.float32)
    pos = np.arange(1, TKV + 1, dtype=np.float32)
    ang = pos[:, None] * thetas[None, :]
    c = np.cos(ang).astype(np.float32)
    s = np.sin(ang).astype(np.float32)
    cosF = np.ascontiguousarray(np.concatenate([c, c], axis=1).T)
    sinH = np.ascontiguousarray(np.concatenate([s, -s], axis=1).T)
    return cosF, sinH


def prepare_in_maps(L0_attn_x, L0_attn_Wq, L0_attn_bq, L0_attn_Wk, L0_attn_bk, L0_attn_Wv,
                    L0_attn_bv, L0_attn_Wo, L0_attn_bo, L0_ffn_gate_W_packed, L0_ffn_gate_scales,
                    L0_ffn_up_W_packed, L0_ffn_up_scales, L0_ffn_down_W_packed, L0_ffn_down_scales):
    x = np.asarray(L0_attn_x, dtype=np.float32)
    wq = np.ascontiguousarray(np.asarray(L0_attn_Wq, np.float32).T)
    wk = np.ascontiguousarray(np.asarray(L0_attn_Wk, np.float32).T)
    wv = np.ascontiguousarray(np.asarray(L0_attn_Wv, np.float32).T)
    wo = np.ascontiguousarray(np.asarray(L0_attn_Wo, np.float32).T)
    bqv = np.ascontiguousarray(np.asarray(L0_attn_bq, np.float32).reshape(16, 128).T)
    bkv = np.ascontiguousarray(np.asarray(L0_attn_bk, np.float32).reshape(4, 128).T)
    bvv = np.ascontiguousarray(np.asarray(L0_attn_bv, np.float32).reshape(4, 128).T)
    bov = np.ascontiguousarray(np.asarray(L0_attn_bo, np.float32).reshape(16, 128).T)
    gpk = np.ascontiguousarray(np.asarray(L0_ffn_gate_W_packed, np.int32))
    gsc = np.ascontiguousarray(np.asarray(L0_ffn_gate_scales, np.float32))
    upkk = np.ascontiguousarray(np.asarray(L0_ffn_up_W_packed, np.int32))
    upsc = np.ascontiguousarray(np.asarray(L0_ffn_up_scales, np.float32))
    dpk = np.ascontiguousarray(np.asarray(L0_ffn_down_W_packed, np.int32))
    dssc = np.ascontiguousarray(np.asarray(L0_ffn_down_scales, np.float32))

    cosF, sinH = _rope_tables()
    iden = np.eye(128, dtype=np.float32).astype(ml_dtypes.bfloat16)
    idenf32 = np.eye(128, dtype=np.float32)
    onescol = np.ones((128, 1), np.float32).astype(ml_dtypes.bfloat16)
    onesf32 = np.ones((1, 128), np.float32)

    in_maps = []
    for c in range(8):
        bsel = c // 2
        h = c % 2
        own0 = 1024 * h
        oth0 = 1024 * (1 - h)
        idx = np.concatenate([np.arange(own0, own0 + 1024), np.arange(oth0, oth0 + 1024)])
        xb = x[bsel]
        in_maps.append({
            "xT": np.ascontiguousarray(xb.T[:, idx]),
            "xr": np.ascontiguousarray(xb[own0:own0 + 1024]),
            "wqT": wq, "wkT": wk, "wvT": wv, "woT": wo,
            "bq": bqv, "bk": bkv, "bv": bvv, "bo": bov,
            "gp": np.ascontiguousarray(gpk[2048 * c:2048 * (c + 1)]),
            "gs": np.ascontiguousarray(gsc[2048 * c:2048 * (c + 1)]),
            "up": np.ascontiguousarray(upkk[2048 * c:2048 * (c + 1)]),
            "us": np.ascontiguousarray(upsc[2048 * c:2048 * (c + 1)]),
            "dp": np.ascontiguousarray(dpk[256 * c:256 * (c + 1)]),
            "ds": np.ascontiguousarray(dssc[256 * c:256 * (c + 1)]),
            "qcos": np.ascontiguousarray(cosF[:, own0:own0 + 1024]),
            "qsin": np.ascontiguousarray(sinH[:, own0:own0 + 1024]),
            "kcos": np.ascontiguousarray(cosF[:, idx]),
            "ksin": np.ascontiguousarray(sinH[:, idx]),
            "posq": np.ascontiguousarray(np.arange(own0, own0 + 1024, dtype=np.float32).reshape(8, 128).T),
            "poskcol": np.ascontiguousarray(idx.astype(np.float32).reshape(16, 128).T),
            "poskk": idx.astype(np.float32).reshape(1, TKV),
            "iden": iden, "idenf": idenf32, "onescol": onescol, "onesf32": onesf32,
        })
    return in_maps


def kernel(**inputs):
    in_maps = prepare_in_maps(**inputs)
    nc = _get_nc()
    res = run_bass_kernel_spmd(nc, in_maps, core_ids=list(range(8)))
    outp = np.empty((4, 2048, 2048), np.float32)
    for c in range(8):
        bsel = c // 2
        h = c % 2
        outp[bsel, 1024 * h:1024 * (h + 1)] = res.results[c]["out"]
    return outp


# version 08
# speedup vs baseline: 1.2528x; 1.0648x over previous
import contextlib
import os
import tempfile

os.environ["NEURON_COMPILE_CACHE_URL"] = tempfile.mkdtemp(prefix="ncc_")

import numpy as np
import ml_dtypes
import concourse.bass as bass
import concourse.mybir as mybir
from concourse.bass_utils import run_bass_kernel_spmd

F32 = mybir.dt.float32
BF16 = mybir.dt.bfloat16
I32 = mybir.dt.int32
AF = mybir.ActivationFunctionType
ALU = mybir.AluOpType
AX = mybir.AxisListType.X

D = 2048
DK = 512
HD = 128
TOWN = 1024
TKV = 2048
MLP = 16384
NEG = -1.0e30
GELU_C = 0.044715
GELU_S = 0.7978845608028654
ISQ = 0.08838834764831845
EPS = 1e-5

DEQ_SLICES = 1
STAGE = 5
LNKV_SUB = 7
DBG = ""


class Ctr:
    def __init__(self, es, nc, name):
        self.h = es.enter_context(nc.semaphore(name))
        self.n = 0

    def inc(self, instr, k=1):
        instr.then_inc(self.h, k)
        self.n += k
        return self.n


def build():
    nc = bass.Bass("TRN2", target_bir_lowering=False)

    def din(name, shape, dt):
        return nc.dram_tensor(name, shape, dt, kind="ExternalInput").ap()

    xT = din("xT", [D, TKV], F32)
    xr = din("xr", [TOWN, D], F32)
    wqT = din("wqT", [D, D], F32)
    wkT = din("wkT", [D, DK], F32)
    wvT = din("wvT", [D, DK], F32)
    woT = din("woT", [D, D], F32)
    bq = din("bq", [128, 16], F32)
    bk = din("bk", [128, 4], F32)
    bv = din("bv", [128, 4], F32)
    bo = din("bo", [128, 16], F32)
    gp = din("gp", [2048 * DEQ_SLICES, 1024], I32)
    gs = din("gs", [2048 * DEQ_SLICES, 64], F32)
    upk = din("up", [2048 * DEQ_SLICES, 1024], I32)
    ups = din("us", [2048 * DEQ_SLICES, 64], F32)
    dp = din("dp", [256 * DEQ_SLICES, 8192], I32)
    dsc = din("ds", [256 * DEQ_SLICES, 512], F32)
    qcos = din("qcos", [HD, TOWN], F32)
    qsin = din("qsin", [HD, TOWN], F32)
    kcos = din("kcos", [HD, TKV], F32)
    ksin = din("ksin", [HD, TKV], F32)
    posq = din("posq", [128, 8], F32)
    poskcol = din("poskcol", [128, 16], F32)
    poskk = din("poskk", [1, TKV], F32)
    iden = din("iden", [128, 128], BF16)
    idenf = din("idenf", [128, 128], F32)
    onescol = din("onescol", [128, 1], BF16)
    onesf32 = din("onesf32", [1, 128], F32)

    out = nc.dram_tensor("out", [TOWN, D], F32, kind="ExternalOutput").ap()

    wk_ = "ExternalOutput" if "w" in DBG else "Internal"
    if "k" in DBG:
        dbg_xn = nc.dram_tensor("dbg_xn", [128, 16, TOWN], BF16, kind="ExternalOutput").ap()
        dbg_kt = nc.dram_tensor("dbg_kt", [128, 4, TKV], BF16, kind="ExternalOutput").ap()
        dbg_vh = nc.dram_tensor("dbg_vh", [128, 4, 16, 128], BF16, kind="ExternalOutput").ap()
    if "a" in DBG:
        nc._dbg_a = dict(
            qt=nc.dram_tensor("dbg_qt", [128, TOWN], BF16, kind="ExternalOutput").ap(),
            pb=nc.dram_tensor("dbg_pb", [128, TKV], BF16, kind="ExternalOutput").ap(),
            ell=nc.dram_tensor("dbg_ell", [128, 1], F32, kind="ExternalOutput").ap(),
            s=nc.dram_tensor("dbg_s", [128, TKV], F32, kind="ExternalOutput").ap(),
            pb2=nc.dram_tensor("dbg_pb2", [128, TKV], BF16, kind="ExternalOutput").ap(),
            msk=nc.dram_tensor("dbg_msk", [128, 8, TKV], BF16, kind="ExternalOutput").ap(),
            yt=nc.dram_tensor("dbg_yt", [128, 16, TOWN], BF16, kind="ExternalOutput").ap(),
        )
    sk_ = "ExternalOutput" if "s" in DBG else "Internal"
    ggcF = nc.dram_tensor("ggc", [D, 2048], BF16, kind="Internal").ap()
    gucF = nc.dram_tensor("guc", [D, 2048], BF16, kind="Internal").ap()
    gdcF = nc.dram_tensor("gdc", [MLP, 256], BF16, kind="Internal").ap()
    ggc = ggcF.rearrange("(a k) n -> a k n", a=1)
    guc = gucF.rearrange("(a k) n -> a k n", a=1)
    gdc = gdcF.rearrange("(a k) n -> a k n", a=1)
    GgF = nc.dram_tensor("Gg", [8 * D, 2048], BF16, kind=wk_, addr_space="Shared").ap()
    GuF = nc.dram_tensor("Gu", [8 * D, 2048], BF16, kind=wk_, addr_space="Shared").ap()
    GdF = nc.dram_tensor("Gd", [8 * MLP, 256], BF16, kind=wk_, addr_space="Shared").ap()
    Gg = GgF.rearrange("(s k) n -> s k n", s=8)
    Gu = GuF.rearrange("(s k) n -> s k n", s=8)
    Gd = GdF.rearrange("(s k) n -> s k n", s=8)
    sDram = nc.dram_tensor("sDram", [D, TOWN], BF16, kind=sk_).ap()

    es = contextlib.ExitStack()
    with es:
        E = es.enter_context
        xnT = E(nc.sbuf_tensor("xnT", [128, 16, TOWN], BF16))
        identt = E(nc.sbuf_tensor("identt", [128, 128], BF16))
        idenft = E(nc.sbuf_tensor("idenft", [128, 128], F32))
        onescolt = E(nc.sbuf_tensor("onescolt", [128, 1], BF16))
        onesf32t = E(nc.sbuf_tensor("onesf32t", [1, 128], F32))
        posqt = E(nc.sbuf_tensor("posqt", [128, 8], F32))
        poskcolt = E(nc.sbuf_tensor("poskcolt", [128, 16], F32))
        bqt = E(nc.sbuf_tensor("bqt", [128, 16], F32))
        bot = E(nc.sbuf_tensor("bot", [128, 16], F32))
        bkt = E(nc.sbuf_tensor("bkt", [128, 4], F32))
        bvt = E(nc.sbuf_tensor("bvt", [128, 4], F32))
        cneg4 = E(nc.sbuf_tensor("cneg4", [128, 1], F32))

        C = {}
        for name in ["gd", "pe", "dv", "ac", "ou", "cn", "x", "ms", "st"]:
            C[name] = Ctr(es, nc, name)
        for name in ["pk", "ws", "gg", "gt", "xr"]:
            C[name] = [Ctr(es, nc, name + "0"), Ctr(es, nc, name + "1")]

        gd, pe, dv, ac, ou = C["gd"], C["pe"], C["dv"], C["ac"], C["ou"]

        blk = es.enter_context(nc.Block())
        SQ = nc.sync
        WQ = nc.scalar
        V = nc.vector
        A = nc.scalar
        T = nc.tensor
        ALL_E = (SQ, V, T, WQ, nc.gpsimd)

        def flat_ctrs():
            r = []
            for v in C.values():
                r.extend(v if isinstance(v, list) else [v])
            return r

        def barrier():
            vals = [(c, c.n) for c in flat_ctrs()]
            for e in ALL_E:
                for c, v in vals:
                    if v > 0:
                        e.wait_ge(c.h, v)

        cn = C["cn"]
        cn.inc(SQ.dma_start(out=identt[:], in_=iden), 16)
        cn.inc(SQ.dma_start(out=idenft[:], in_=idenf), 16)
        cn.inc(SQ.dma_start(out=onescolt[:], in_=onescol), 16)
        cn.inc(SQ.dma_start(out=onesf32t[:], in_=onesf32), 16)
        cn.inc(SQ.dma_start(out=posqt[:], in_=posq), 16)
        cn.inc(SQ.dma_start(out=poskcolt[:], in_=poskcol), 16)
        cn.inc(SQ.dma_start(out=bqt[:], in_=bq), 16)
        cn.inc(SQ.dma_start(out=bot[:], in_=bo), 16)
        cn.inc(SQ.dma_start(out=bkt[:], in_=bk), 16)
        cn.inc(SQ.dma_start(out=bvt[:], in_=bv), 16)
        dv_ = C["dv"]
        dv_.inc(V.memset(cneg4[:], -4.0))
        for e in (V, T, WQ, nc.gpsimd):
            e.wait_ge(cn.h, cn.n)
            e.wait_ge(dv_.h, 1)

        cc = Ctr(es, nc, "ccs")
        nc._cc = cc
        if STAGE >= 1:
            G_ = nc.gpsimd
            rg = [[0, 1, 2, 3, 4, 5, 6, 7]]
            _dequant(nc, SQ, WQ, V, T, C, identt, gp, gs, ggc, 1.0, "g")
            _dequant(nc, SQ, WQ, V, T, C, identt, upk, ups, guc, 1.0, "u")
            _dequant(nc, SQ, WQ, V, T, C, identt, dp, dsc, gdc, 0.5, "d")

            def issue_gathers():
                G_.wait_ge(gd.h, gd.n)
                G_.wait_ge(C["x"].h, C["x"].n)
                cc.inc(G_.collective_compute("AllGather", ALU.bypass,
                                             replica_groups=rg,
                                             ins=[ggcF.opt()], outs=[GgF.opt()]), 1)
                cc.inc(G_.collective_compute("AllGather", ALU.bypass,
                                             replica_groups=rg,
                                             ins=[gucF.opt()], outs=[GuF.opt()]), 1)
                cc.inc(G_.collective_compute("AllGather", ALU.bypass,
                                             replica_groups=rg,
                                             ins=[gdcF.opt()], outs=[GdF.opt()]), 1)
        else:
            def issue_gathers():
                pass
        SQ.wait_ge(gd.h, gd.n)

        if STAGE >= 2:
            with contextlib.ExitStack() as es23:
                E23 = es23.enter_context
                poskkb = E23(nc.sbuf_tensor("poskkb", [128, TKV], F32))
                qcost = E23(nc.sbuf_tensor("qcost", [128, TOWN], F32))
                qsint = E23(nc.sbuf_tensor("qsint", [128, TOWN], F32))
                ms = C["ms"]
                ms.inc(SQ.dma_start(out=poskkb[:], in_=poskk.partition_broadcast(128)), 16)
                ms.inc(SQ.dma_start(out=qcost[:], in_=qcos), 16)
                ms.inc(SQ.dma_start(out=qsint[:], in_=qsin), 16)
                kvt = _ln_kv(nc, es23, SQ, V, A, T, C,
                             xT, wkT, wvT, bkt, bvt, kcos, ksin,
                             xnT, onescolt, onesf32t, identt)
                issue_gathers()
                barrier()
                if "k" in DBG:
                    ou.inc(WQ.dma_start(out=dbg_xn, in_=xnT[:]), 16)
                    ou.inc(WQ.dma_start(out=dbg_kt, in_=kvt["KT"][:]), 16)
                    ou.inc(WQ.dma_start(out=dbg_vh, in_=kvt["Vh"][:]), 16)
                    for e in ALL_E:
                        e.wait_ge(ou.h, ou.n)
                if STAGE >= 3:
                    _attention(nc, es23, SQ, WQ, V, A, T, C,
                               wqT, woT, bqt, bot, qcost, qsint, poskcolt, poskkb,
                               xnT, sDram, kvt, identt, cneg4, onescolt, onesf32t)
                barrier()

        if STAGE >= 4:
            with contextlib.ExitStack() as es45:
                gacc = es45.enter_context(nc.sbuf_tensor("gacc", [128, 16, TOWN], F32))
                _ffn(nc, SQ, V, A, T, C, Gg, Gu, Gd, xnT, gacc, nc._cc)
                barrier()
                if STAGE >= 5:
                    _combine(nc, SQ, WQ, V, A, T, C, xr, out, sDram, gacc, identt, idenft)

        SQ.wait_ge(ou.h, ou.n)

    return nc


def _dequant(nc, SQ, WQ, V, T, C, identt, packed, scales, Gout, smul, tag):
    gd, pe, dv = C["gd"], C["pe"], C["dv"]
    pk = C["pk"]
    KK = packed.shape[1]
    NB = scales.shape[1]
    n_loc = Gout.shape[2]
    n_rtiles = n_loc // 128
    n_cchunk = KK // 512
    nslices = packed.shape[0] // n_loc
    K_win = Gout.shape[1] // n_cchunk

    with contextlib.ExitStack() as es:
        SQ.wait_ge(gd.h, gd.n)
        E = es.enter_context
        pt = [E(nc.sbuf_tensor(f"dqp{tag}{i}", [128, 512], I32)) for i in range(4)]
        sct = [E(nc.sbuf_tensor(f"dqs{tag}{i}", [128, NB], F32)) for i in range(4)]
        sce = [E(nc.sbuf_tensor(f"dqe{tag}{i}", [128, NB], F32)) for i in range(4)]
        mtile = E(nc.sbuf_tensor(f"dqm{tag}", [128, 1024], I32))
        stile = E(nc.sbuf_tensor(f"dqt{tag}", [128, 1024], I32))
        btile = E(nc.sbuf_tensor(f"dqb{tag}", [128, 1024], I32))
        wtile = E(nc.sbuf_tensor(f"dqw{tag}", [128, 1024], BF16))
        rb = E(nc.sbuf_tensor(f"dqr{tag}", [128, 8, n_loc], BF16))
        tp = [E(nc.psum_tensor(f"dqx{tag}{i}", [128, 512], BF16)) for i in range(2)]

        dv_decode = {}
        dv_evac = {}
        pe_tp = {}
        gd_grp = {}

        u = 0
        for sl in range(nslices):
            for c in range(n_cchunk):
                for r in range(n_rtiles):
                    b = u % 4
                    bc = u % 2
                    if u >= 4:
                        SQ.wait_ge(dv.h, dv_decode[u - 4])
                    row0 = sl * n_loc + r * 128
                    pk[bc].inc(SQ.dma_start(out=pt[b][:], in_=packed[row0:row0 + 128, c * 512:(c + 1) * 512]), 16)
                    pk[bc].inc(SQ.dma_start(out=sct[b][:], in_=scales[row0:row0 + 128, :]), 16)

                    V.wait_ge(pk[bc].h, pk[bc].n)
                    if u >= 1:
                        V.wait_ge(pe.h, pe_tp[u - 1])
                    if r == 0:
                        g_prev = (sl * n_cchunk + c) - 1
                        if g_prev >= 0:
                            V.wait_ge(gd.h, gd_grp[g_prev])
                    V.tensor_scalar(sce[b][:], sct[b][:], float(smul * (2.0 ** 126)), None, ALU.mult)
                    p = pt[b][:]
                    V.tensor_scalar(mtile[:, 0:512], p, 7, 22, ALU.bitwise_and, ALU.logical_shift_left)
                    V.tensor_scalar(stile[:, 0:512], p, 8, 28, ALU.bitwise_and, ALU.logical_shift_left)
                    V.tensor_scalar(mtile[:, 512:1024], p, 0x70, 18, ALU.bitwise_and, ALU.logical_shift_left)
                    V.tensor_scalar(stile[:, 512:1024], p, 0x80, 24, ALU.bitwise_and, ALU.logical_shift_left)
                    V.tensor_tensor(btile[:], mtile[:], stile[:], ALU.bitwise_or)
                    scb = sce[b][:, 32 * c:32 * c + 32].unsqueeze(2).broadcast_to([128, 32, 16])
                    bl = btile[:, 0:512].bitcast(F32).rearrange("p (b j) -> p b j", b=32)
                    bh = btile[:, 512:1024].bitcast(F32).rearrange("p (b j) -> p b j", b=32)
                    V.tensor_tensor(wtile[:, 0:512].rearrange("p (b j) -> p b j", b=32), bl, scb, ALU.mult)
                    dv.inc(V.tensor_tensor(wtile[:, 512:1024].rearrange("p (b j) -> p b j", b=32), bh, scb, ALU.mult))
                    dv_decode[u] = dv.n

                    T.wait_ge(dv.h, dv_decode[u])
                    if u >= 1:
                        T.wait_ge(dv.h, dv_evac[u - 1])
                    for plane in range(2):
                        for s4 in range(4):
                            ins = T.transpose(tp[plane][:, 128 * s4:128 * (s4 + 1)],
                                              wtile[:, 512 * plane + 128 * s4:512 * plane + 128 * (s4 + 1)],
                                              identt[:])
                        pe.inc(ins)
                    pe_tp[u] = pe.n

                    V.wait_ge(pe.h, pe_tp[u])
                    V.tensor_copy(rb[:, 0:4, r * 128:(r + 1) * 128],
                                  tp[0][:].rearrange("p (s j) -> p s j", s=4))
                    dv.inc(V.tensor_copy(rb[:, 4:8, r * 128:(r + 1) * 128],
                                         tp[1][:].rearrange("p (s j) -> p s j", s=4)))
                    dv_evac[u] = dv.n
                    u += 1

                WQ.wait_ge(dv.h, dv_evac[u - 1])
                for plane in range(2):
                    for s4 in range(4):
                        base = K_win * c + 256 * s4 + plane
                        gd.inc(WQ.dma_start(out=Gout[sl, base:base + 255:2, :],
                                            in_=rb[:, plane * 4 + s4, :]), 16)
                gd_grp[sl * n_cchunk + c] = gd.n


def _ln_kv(nc, es, SQ, V, A, T, C, xT, wkT, wvT, bkt, bvt, kcos, ksin,
           xnT, onescolt, onesf32t, identt):
    pe, dv, ac, ms, sx, ws = C["pe"], C["dv"], C["ac"], C["ms"], C["x"], C["ws"]
    E = es.enter_context
    KT = E(nc.sbuf_tensor("KT", [128, 4, TKV], BF16))
    Vh = E(nc.sbuf_tensor("Vh", [128, 4, 16, 128], BF16))

    with contextlib.ExitStack() as es2:
        E2 = es2.enter_context
        kcost = E2(nc.sbuf_tensor("kcost", [128, TKV], F32))
        ksint = E2(nc.sbuf_tensor("ksint", [128, TKV], F32))
        wk16 = E2(nc.sbuf_tensor("wk16", [128, 16, DK], BF16))
        wv16 = E2(nc.sbuf_tensor("wv16", [128, 16, DK], BF16))
        xtc = E2(nc.sbuf_tensor("xtc", [128, 16, 512], F32))
        xn_t = E2(nc.sbuf_tensor("xn_t", [128, 16, 512], BF16))
        x16 = [E2(nc.sbuf_tensor(f"x16{i}", [128, 512], BF16)) for i in range(2)]
        x2 = [E2(nc.sbuf_tensor(f"x2{i}", [128, 512], BF16)) for i in range(2)]
        wstage = [E2(nc.sbuf_tensor(f"wst{i}", [128, 512], F32)) for i in range(2)]
        srow = E2(nc.sbuf_tensor("srow", [1, 512], F32))
        qrow = E2(nc.sbuf_tensor("qrow", [1, 512], F32))
        trow = E2(nc.sbuf_tensor("trow", [1, 512], F32))
        vrow = E2(nc.sbuf_tensor("vrow", [1, 512], F32))
        t1 = E2(nc.sbuf_tensor("lnt1", [128, 512], F32))
        kraw = E2(nc.sbuf_tensor("kraw", [128, 512], F32))
        krot = E2(nc.sbuf_tensor("krot", [128, 512], F32))
        kcs = E2(nc.sbuf_tensor("kcs", [128, 512], F32))
        vtmp = E2(nc.sbuf_tensor("vtmp", [128, 512], BF16))
        sum_p = E2(nc.psum_tensor("sum_p", [1, 512], F32))
        sq_p = E2(nc.psum_tensor("sq_p", [1, 512], F32))
        mb_p = E2(nc.psum_tensor("mb_p", [128, 512], F32))
        rb_p = E2(nc.psum_tensor("rb_p", [128, 512], F32))
        kv_p = E2(nc.psum_tensor("kv_p", [128, 512], F32))
        vv_p = E2(nc.psum_tensor("vv_p", [128, 512], F32))
        vt_p = E2(nc.psum_tensor("vt_p", [128, 512], BF16))

        ms.inc(SQ.dma_start(out=kcost[:], in_=kcos), 16)
        ms.inc(SQ.dma_start(out=ksint[:], in_=ksin), 16)
        MS_ALL = ms.n

        casts = {}
        for w in range(2):
            src = wkT if w == 0 else wvT
            dst = wk16 if w == 0 else wv16
            for kc in range(16):
                i = w * 16 + kc
                b = i % 2
                if i >= 2:
                    SQ.wait_ge(dv.h, casts[i - 2])
                ws[b].inc(SQ.dma_start(out=wstage[b][:], in_=src[128 * kc:128 * (kc + 1), :]), 16)
                V.wait_ge(ws[b].h, ws[b].n)
                dv.inc(V.tensor_copy(dst[:, kc, :], wstage[b][:]))
                casts[i] = dv.n
        V.wait_ge(ms.h, MS_ALL)

        dv_norm = {}
        pe_kv = {}
        dv_kv = dv.n
        for tc in range(4 if LNKV_SUB >= 2 else 0):
            own = tc < 2
            if tc >= 1:
                SQ.wait_ge(dv.h, dv_norm[tc - 1])
            for kc in range(16):
                sx.inc(SQ.dma_start(out=xtc[:, kc, :],
                                    in_=xT[128 * kc:128 * (kc + 1), 512 * tc:512 * (tc + 1)]), 16)
            sx_all = sx.n

            pe_stat = {}
            for kc in range(16):
                bb = kc % 2
                V.wait_ge(sx.h, sx_all)
                A.wait_ge(sx.h, sx_all)
                if kc >= 2:
                    V.wait_ge(pe.h, pe_stat[kc - 2])
                    A.wait_ge(pe.h, pe_stat[kc - 2])
                dv.inc(V.tensor_copy(x16[bb][:], xtc[:, kc, :]))
                ac.inc(A.activation(out=x2[bb][:], in_=xtc[:, kc, :], func=AF.Square))
                T.wait_ge(dv.h, dv.n)
                T.wait_ge(ac.h, ac.n)
                T.matmul(sum_p[:], onescolt[:], x16[bb][:], start=(kc == 0), stop=(kc == 15))
                pe.inc(T.matmul(sq_p[:], onescolt[:], x2[bb][:], start=(kc == 0), stop=(kc == 15)))
                pe_stat[kc] = pe.n
            pe_stats = pe.n

            if LNKV_SUB < 3:
                dv_norm[tc] = dv.n
                pe_kv[tc] = pe.n
                continue
            V.wait_ge(pe.h, pe_stats)
            V.tensor_scalar(srow[:], sum_p[:], 1.0 / D, None, ALU.mult)
            V.tensor_scalar(qrow[:], sq_p[:], 1.0 / D, None, ALU.mult)
            V.tensor_copy(t1[0:1, :], qrow[:])
            V.tensor_tensor(trow[:], srow[:], srow[:], ALU.mult)
            V.tensor_copy(t1[0:1, :], qrow[:])
            V.scalar_tensor_tensor(vrow[:], qrow[:], EPS, trow[:], ALU.add, ALU.subtract)
            dv.inc(V.tensor_copy(t1[0:1, :], qrow[:]))
            A.wait_ge(dv.h, dv.n)
            ac.inc(A.activation(out=trow[:], in_=vrow[:], func=AF.Sqrt))
            V.wait_ge(ac.h, ac.n)
            V.reciprocal(vrow[:], trow[:])
            dv.inc(V.tensor_copy(t1[0:1, :], qrow[:]))
            T.wait_ge(dv.h, dv.n)
            T.matmul(mb_p[:], onesf32t[:], srow[:], start=True, stop=True)
            pe.inc(T.matmul(rb_p[:], onesf32t[:], vrow[:], start=True, stop=True))
            pe_bc = pe.n

            if LNKV_SUB < 4:
                dv_norm[tc] = dv.n
                pe_kv[tc] = pe.n
                continue
            V.wait_ge(pe.h, pe_bc)
            if tc >= 3:
                V.wait_ge(pe.h, pe_kv[tc - 1])
            for kc in range(16):
                V.tensor_tensor(t1[:], xtc[:, kc, :], mb_p[:], ALU.subtract)
                dst = xnT[:, kc, 512 * tc:512 * (tc + 1)] if own else xn_t[:, kc, :]
                dv.inc(V.tensor_tensor(dst, t1[:], rb_p[:], ALU.mult))
            dv_norm[tc] = dv.n

            def xsrc(kc, tc=tc, own=own):
                return xnT[:, kc, 512 * tc:512 * (tc + 1)] if own else xn_t[:, kc, :]

            for kvh in range(4 if LNKV_SUB >= 5 else 0):
                T.wait_ge(dv.h, dv_norm[tc])
                T.wait_ge(dv.h, dv_kv)
                for kc in range(16):
                    T.matmul(kv_p[:], wk16[:, kc, 128 * kvh:128 * (kvh + 1)], xsrc(kc),
                             start=(kc == 0), stop=(kc == 15))
                for kc in range(16):
                    ins = T.matmul(vv_p[:], wv16[:, kc, 128 * kvh:128 * (kvh + 1)], xsrc(kc),
                                   start=(kc == 0), stop=(kc == 15))
                pe.inc(ins)
                pe_proj = pe.n
                if LNKV_SUB < 6:
                    dv_kv = dv.n
                    continue

                V.wait_ge(pe.h, pe_proj)
                cs = slice(512 * tc, 512 * (tc + 1))
                nops = (LNKV_SUB - 60) if 60 <= LNKV_SUB < 70 else 6
                ops = [
                    lambda: V.tensor_scalar(kraw[:], kv_p[:], bkt[:, kvh:kvh + 1], None, ALU.add),
                    lambda: V.tensor_tensor(krot[0:64, :], kraw[64:128, :], ksint[64:128, cs], ALU.mult),
                    lambda: V.tensor_tensor(krot[64:128, :], kraw[0:64, :], ksint[0:64, cs], ALU.mult),
                    lambda: V.tensor_tensor(kcs[:], kraw[:], kcost[:, cs], ALU.mult),
                    lambda: V.tensor_tensor(KT[:, kvh, cs], kcs[:], krot[:], ALU.add),
                    lambda: V.tensor_scalar(vtmp[:], vv_p[:], bvt[:, kvh:kvh + 1], None, ALU.add),
                ]
                last = None
                for _i in range(nops):
                    last = ops[_i]()
                if last is None:
                    last = V.engine_nop() if hasattr(V, "engine_nop") else V.nop()
                dv.inc(last)
                dv_kv = dv.n
                if LNKV_SUB < 7:
                    continue

                T.wait_ge(dv.h, dv_kv)
                for j in range(4):
                    ins = T.transpose(vt_p[:, 128 * j:128 * (j + 1)],
                                      vtmp[:, 128 * j:128 * (j + 1)], identt[:])
                pe.inc(ins)
                V.wait_ge(pe.h, pe.n)
                dv.inc(V.tensor_copy(Vh[:, kvh, 4 * tc:4 * tc + 4, :],
                                     vt_p[:].rearrange("p (s j) -> p s j", s=4)))
                dv_kv = dv.n
            pe_kv[tc] = pe.n
    return dict(KT=KT, Vh=Vh)


def _attention(nc, es, SQ, WQ, V, A, T, C, wqT, woT, bqt, bot,
               qcost, qsint, poskcolt, poskkb, xnT, sDram, kvt, identt, cneg4,
               onescolt, onesf32t):
    gd, pe, dv, ac, ws = C["gd"], C["pe"], C["dv"], C["ac"], C["ws"]
    KT, Vh = kvt["KT"], kvt["Vh"]
    E = es.enter_context
    YT = E(nc.sbuf_tensor("YT", [128, 16, TOWN], BF16))
    mask01 = E(nc.sbuf_tensor("mask01", [128, 16, TOWN], BF16))

    with contextlib.ExitStack() as es2:
        E2 = es2.enter_context
        wstage = [E2(nc.sbuf_tensor(f"aws{i}", [128, 16, 128], F32)) for i in range(2)]
        w16 = [E2(nc.sbuf_tensor(f"aw16{i}", [128, 16, 128], BF16)) for i in range(2)]
        qraw = E2(nc.sbuf_tensor("qraw", [128, TOWN], F32))
        qrot = E2(nc.sbuf_tensor("qrot", [128, TOWN], F32))
        qcs = E2(nc.sbuf_tensor("qcs", [128, TOWN], F32))
        QT = E2(nc.sbuf_tensor("QT", [128, 2, TOWN], BF16))
        NSL = 6
        Pe = E2(nc.sbuf_tensor("Pe", [128, NSL, 512], BF16))
        Pm = E2(nc.sbuf_tensor("Pm", [128, NSL, 512], BF16))
        elrec = E2(nc.sbuf_tensor("elrec", [1, TOWN], F32))
        ebs = E2(nc.sbuf_tensor("ebs", [128, 2, 512], F32))
        stmp = E2(nc.sbuf_tensor("stmp", [128, TOWN], BF16))
        st_p = [E2(nc.psum_tensor(f"st_p{i}", [128, 512], F32)) for i in range(3)]
        y_p = [E2(nc.psum_tensor(f"y_p{i}", [128, 512], F32)) for i in range(2)]
        eb_p = E2(nc.psum_tensor("eb_p", [128, 512], F32))
        qp_p = E2(nc.psum_tensor("qp_p", [128, 1024], F32))

        for s in range(16):
            dv.inc(V.tensor_scalar(mask01[:, s, :], poskkb[:, 0:TOWN],
                                   poskcolt[:, s:s + 1], None, ALU.is_ge))
        dv_mask_built = dv.n

        ac_exp = {}
        dv_mask = {}
        pe_st = {}
        pe_yell = {}
        dv_qe = {}
        st_last_of_qh = {}
        dv_ynorm = {0: 0, 1: 0}
        dv_tail = [0]
        pe_rank1 = {0: 0, 1: 0}
        pe_w = {}
        from collections import deque
        pending = deque()

        def flush_pending():
            pg, pqh, phalf, ps, pslot = pending.popleft()
            pqs = slice(512 * phalf, 512 * (phalf + 1))
            pkvh = pqh // 4
            T.wait_ge(dv.h, dv_mask[pg])
            if ps == 0:
                T.wait_ge(dv.h, dv_tail[0])
            T.matmul(y_p[phalf][:], Vh[:, pkvh, ps, :], Pm[:, pslot, :],
                     start=(ps == 0), stop=(ps == 15))
            pe.inc(T.matmul(eb_p[0:1, :], onescolt[:], Pm[:, pslot, :],
                            start=(ps == 0), stop=(ps == 15)))
            pe_yell[pg] = pe.n
            if ps == 15:
                V.wait_ge(pe.h, pe_yell[pg])
                V.wait_ge(pe.h, pe_rank1[phalf])
                dv.inc(V.reciprocal(elrec[0:1, pqs], eb_p[0:1, :]))
                dv_rec = dv.n
                T.wait_ge(dv.h, dv_rec)
                pe.inc(T.matmul(eb_p[:], onesf32t[:], elrec[0:1, pqs],
                                start=True, stop=True))
                pe_rank1[phalf] = pe.n
                V.wait_ge(pe.h, pe_rank1[phalf])
                V.tensor_copy(ebs[:, phalf, :], eb_p[:])
                dv.inc(V.tensor_tensor(YT[:, pqh, pqs], y_p[phalf][:],
                                       ebs[:, phalf, :], ALU.mult))
                dv_ynorm[phalf] = dv.n
                dv_tail[0] = dv.n

        g = 0
        for qh in range(16):
            kvh = qh // 4
            b = qh % 2
            if qh >= 2:
                SQ.wait_ge(pe.h, pe_w[qh - 2])
            ws[b].inc(SQ.dma_start(out=wstage[b][:],
                                   in_=wqT[:, 128 * qh:128 * (qh + 1)].rearrange("(c p) e -> p c e", p=128)), 16)
            V.wait_ge(ws[b].h, ws[b].n)
            if qh >= 2:
                V.wait_ge(pe.h, pe_w[qh - 2])
            dv.inc(V.tensor_copy(w16[b][:], wstage[b][:]))
            dv_w = dv.n

            T.wait_ge(dv.h, dv_w)
            T.wait_ge(dv.h, dv_qe.get(qh - 1, 0))
            for th in range(2):
                for kc in range(16):
                    ins = T.matmul(qp_p[:, 512 * th:512 * (th + 1)], w16[b][:, kc, :],
                                   xnT[:, kc, 512 * th:512 * (th + 1)], start=(kc == 0), stop=(kc == 15))
            pe.inc(ins)
            pe_w[qh] = pe.n

            qb = qh % 2
            V.wait_ge(pe.h, pe_w[qh])
            V.wait_ge(pe.h, st_last_of_qh.get(qh - 2, 0))
            V.tensor_scalar(qraw[:], qp_p[:], bqt[:, qh:qh + 1], None, ALU.add)
            V.tensor_tensor(qrot[0:64, :], qraw[64:128, :], qsint[64:128, :], ALU.mult)
            V.tensor_tensor(qrot[64:128, :], qraw[0:64, :], qsint[0:64, :], ALU.mult)
            V.tensor_tensor(qcs[:], qraw[:], qcost[:], ALU.mult)
            dv.inc(V.tensor_tensor(QT[:, qb, :], qcs[:], qrot[:], ALU.add))
            dv_qe[qh] = dv.n

            for half in range(2):
                qs = slice(512 * half, 512 * (half + 1))
                for s in range(16):
                    par = g % 3
                    slot = g % NSL
                    T.wait_ge(dv.h, dv_qe[qh])
                    if s == 0:
                        T.wait_ge(dv.h, max(dv_mask_built, dv_ynorm[half]))
                    if g >= 3:
                        T.wait_ge(ac.h, ac_exp[g - 3])
                    pe.inc(T.matmul(st_p[par][:], KT[:, kvh, 128 * s:128 * (s + 1)],
                                    QT[:, qb, qs], start=True, stop=True))
                    pe_st[g] = pe.n
                    if half == 1 and s == 15:
                        st_last_of_qh[qh] = pe.n

                    A.wait_ge(pe.h, pe_st[g])
                    if g >= NSL:
                        A.wait_ge(dv.h, dv_mask[g - NSL])
                    ac.inc(A.activation(out=Pe[:, slot, :], in_=st_p[par][:],
                                        func=AF.Exp, scale=ISQ, bias=cneg4[:]))
                    ac_exp[g] = ac.n

                    V.wait_ge(ac.h, ac_exp[g])
                    if g >= NSL:
                        V.wait_ge(pe.h, pe_yell.get(g - NSL, 0))
                    dv.inc(V.tensor_tensor(Pm[:, slot, :], Pe[:, slot, :],
                                           mask01[:, s, qs], ALU.mult))
                    dv_mask[g] = dv.n

                    if len(pending) >= 3:
                        flush_pending()
                    pending.append((g, qh, half, s, slot))
                    g += 1
        while pending:
            flush_pending()

        dv_yt = dv.n
        gd_s = 0
        dv_se = 0
        for ec in range(16):
            b = ec % 2
            last_use = pe_w[16 + ec - 2] if ec >= 2 else pe_w[14 + ec]
            SQ.wait_ge(pe.h, last_use)
            ws[b].inc(SQ.dma_start(out=wstage[b][:],
                                   in_=woT[:, 128 * ec:128 * (ec + 1)].rearrange("(c p) e -> p c e", p=128)), 16)
            V.wait_ge(ws[b].h, ws[b].n)
            V.wait_ge(pe.h, last_use)
            dv.inc(V.tensor_copy(w16[b][:], wstage[b][:]))
            dv_w2 = dv.n

            T.wait_ge(dv.h, dv_w2)
            T.wait_ge(dv.h, dv_yt)
            if dv_se:
                T.wait_ge(dv.h, dv_se)
            for th in range(2):
                for kc in range(16):
                    ins = T.matmul(qp_p[:, 512 * th:512 * (th + 1)], w16[b][:, kc, :],
                                   YT[:, kc, 512 * th:512 * (th + 1)], start=(kc == 0), stop=(kc == 15))
            pe.inc(ins)
            pe_w[16 + ec] = pe.n

            V.wait_ge(pe.h, pe_w[16 + ec])
            if gd_s:
                V.wait_ge(gd.h, gd_s)
            dv.inc(V.tensor_scalar(stmp[:], qp_p[:], bot[:, ec:ec + 1], None, ALU.add))
            dv_se = dv.n
            WQ.wait_ge(dv.h, dv_se)
            gd_s = gd.inc(WQ.dma_start(out=sDram[128 * ec:128 * (ec + 1), :], in_=stmp[:]), 16)


def _ffn(nc, SQ, V, A, T, C, Gg, Gu, Gd, xnT, gacc, cc):
    pe, dv, ac = C["pe"], C["dv"], C["ac"]
    sgg, sgt = C["gg"], C["gt"]
    NRING = 18
    with contextlib.ExitStack() as es2:
        E2 = es2.enter_context
        grng = E2(nc.sbuf_tensor("grng", [128, NRING, 512], BF16))
        urng = E2(nc.sbuf_tensor("urng", [128, NRING, 512], BF16))
        gdt = [E2(nc.sbuf_tensor(f"gdt{i}", [128, 16, 256], BF16)) for i in range(2)]
        interT = E2(nc.sbuf_tensor("interT", [128, 16, TOWN], BF16))
        ga = E2(nc.sbuf_tensor("ffa", [128, 1024], F32))
        gb = E2(nc.sbuf_tensor("ffb", [128, 1024], F32))
        gth = E2(nc.sbuf_tensor("ffth", [128, 1024], F32))
        gp_p = [E2(nc.psum_tensor(f"gp_p{i}", [128, 1024], F32)) for i in range(2)]
        up_p = [E2(nc.psum_tensor(f"up_p{i}", [128, 1024], F32)) for i in range(2)]

        dv.inc(V.memset(gacc[:], 0.0))

        group_pe = {}
        emitted = [0]
        n_strips = 8 * 4 * 16

        def strip_idx(seg, ng, kc):
            return (seg * 4 + ng) * 16 + kc

        SQ.wait_ge(cc.h, 2)
        cc_d = [False]

        def emit_loads(upto):
            while emitted[0] < min(upto, n_strips):
                s = emitted[0]
                seg, rem = divmod(s, 64)
                ng, kc = divmod(rem, 16)
                slot = s % NRING
                gidx = s // 16
                if s >= NRING:
                    SQ.wait_ge(pe.h, group_pe[(s - NRING) // 16])
                sgg[gidx % 2].inc(SQ.dma_start(out=grng[:, slot, :],
                                               in_=Gg[seg, 128 * kc:128 * (kc + 1), 512 * ng:512 * (ng + 1)]), 16)
                sgg[gidx % 2].inc(SQ.dma_start(out=urng[:, slot, :],
                                               in_=Gu[seg, 128 * kc:128 * (kc + 1), 512 * ng:512 * (ng + 1)]), 16)
                emitted[0] += 1

        dv_ep = {}
        dv_dacc = 0
        pe_dw = {}
        for seg in range(8):
            for ng in range(4):
                g = seg * 4 + ng
                emit_loads(strip_idx(seg, ng, 0) + NRING)
                sg_all = 32 * 16 * (g // 2 + 1)
                for ns in range(4):
                    nsub = 4 * ng + ns
                    gidx2 = seg * 16 + nsub
                    par = nsub % 2
                    T.wait_ge(sgg[g % 2].h, sg_all)
                    if gidx2 >= 2 and (gidx2 - 2) in dv_ep:
                        T.wait_ge(dv.h, dv_ep[gidx2 - 2])
                    if ng == 0 and ns < 2 and dv_dacc:
                        T.wait_ge(dv.h, dv_dacc)
                    for kc in range(16):
                        s = strip_idx(seg, ng, kc)
                        slot = s % NRING
                        for th in range(2):
                            T.matmul(gp_p[par][:, 512 * th:512 * (th + 1)],
                                     grng[:, slot, 128 * ns:128 * (ns + 1)],
                                     xnT[:, kc, 512 * th:512 * (th + 1)],
                                     start=(kc == 0), stop=(kc == 15))
                        for th in range(2):
                            ins = T.matmul(up_p[par][:, 512 * th:512 * (th + 1)],
                                           urng[:, slot, 128 * ns:128 * (ns + 1)],
                                           xnT[:, kc, 512 * th:512 * (th + 1)],
                                           start=(kc == 0), stop=(kc == 15))
                    pe.inc(ins)
                    pe_gu = pe.n

                    gpp = gp_p[par][:]
                    upp = up_p[par][:]
                    A.wait_ge(pe.h, pe_gu)
                    ac.inc(A.activation(out=ga[:], in_=gpp, func=AF.Square))
                    V.wait_ge(ac.h, ac.n)
                    V.tensor_scalar(ga[:], ga[:], GELU_C, 1.0, ALU.mult, ALU.add)
                    dv.inc(V.tensor_tensor(gb[:], ga[:], gpp, ALU.mult))
                    A.wait_ge(dv.h, dv.n)
                    ac.inc(A.activation(out=gth[:], in_=gb[:], func=AF.Tanh, scale=GELU_S))
                    V.wait_ge(ac.h, ac.n)
                    V.scalar_tensor_tensor(gb[:], gth[:], 1.0, gpp, ALU.add, ALU.mult)
                    dv.inc(V.tensor_tensor(interT[:, nsub, :], gb[:], upp, ALU.mult))
                    dv_ep[gidx2] = dv.n
                group_pe[g] = pe.n

            for sl in range(8):
                b = sl % 2
                di = seg * 8 + sl
                if di >= 2:
                    SQ.wait_ge(pe.h, pe_dw[di - 2])
                if not cc_d[0]:
                    SQ.wait_ge(cc.h, 3)
                    cc_d[0] = True
                sgt[b].inc(SQ.dma_start(out=gdt[b][:],
                                        in_=Gd[sl, 2048 * seg:2048 * (seg + 1), :].rearrange("(c p) h -> p c h", p=128)), 16)
                T.wait_ge(sgt[b].h, sgt[b].n)
                T.wait_ge(dv.h, dv_ep[seg * 16 + 15])
                for j in range(2):
                    if dv_dacc:
                        T.wait_ge(dv.h, dv_dacc)
                    for nci in range(16):
                        for th in range(2):
                            ins = T.matmul(gp_p[j][:, 512 * th:512 * (th + 1)],
                                           gdt[b][:, nci, 128 * j:128 * (j + 1)],
                                           interT[:, nci, 512 * th:512 * (th + 1)],
                                           start=(nci == 0), stop=(nci == 15))
                    pe.inc(ins)
                    V.wait_ge(pe.h, pe.n)
                    dv.inc(V.tensor_tensor(gacc[:, 2 * sl + j, :], gp_p[j][:],
                                           gacc[:, 2 * sl + j, :], ALU.add))
                    dv_dacc = dv.n
                pe_dw[di] = pe.n


def _combine(nc, SQ, WQ, V, A, T, C, xr, out, sDram, gacc, identt, idenft):
    pe, dv, ac, ou, st, sxr = C["pe"], C["dv"], C["ac"], C["ou"], C["st"], C["xr"]
    with contextlib.ExitStack() as es2:
        E2 = es2.enter_context
        outst = E2(nc.sbuf_tensor("outst", [128, 8, D], F32))
        stile = [E2(nc.sbuf_tensor(f"cst{i}", [128, TOWN], BF16)) for i in range(2)]
        silt = E2(nc.sbuf_tensor("silt", [128, TOWN], F32))
        xrt = [E2(nc.sbuf_tensor(f"xrt{i}", [128, D], F32)) for i in range(2)]
        cpa = E2(nc.sbuf_tensor("cpa", [128, 128], BF16))
        ta_p = E2(nc.psum_tensor("ta_p", [128, 128], BF16))
        tb_p = E2(nc.psum_tensor("tb_p", [128, 128], F32))

        dv_c = 0
        pe_tp2 = 0
        pe_ec = {}
        for ec in range(16):
            b = ec % 2
            if ec >= 2:
                SQ.wait_ge(pe.h, pe_ec[ec - 2])
            st.inc(SQ.dma_start(out=stile[b][:], in_=sDram[128 * ec:128 * (ec + 1), :]), 16)
            st_all = st.n
            if dv_c:
                A.wait_ge(dv.h, dv_c)
            if pe_tp2:
                A.wait_ge(pe.h, pe_tp2)
            ac.inc(A.activation(out=silt[:], in_=gacc[:, ec, :], func=AF.Silu))
            ac_sil = ac.n
            for t in range(8):
                T.wait_ge(st.h, st_all)
                T.wait_ge(ac.h, ac_sil)
                if dv_c:
                    T.wait_ge(dv.h, dv_c)
                T.transpose(ta_p[:], stile[b][:, 128 * t:128 * (t + 1)], identt[:])
                pe.inc(T.transpose(tb_p[:], silt[:, 128 * t:128 * (t + 1)], idenft[:]))
                pe_tp2 = pe.n
                V.wait_ge(pe.h, pe_tp2)
                V.tensor_copy(cpa[:], ta_p[:])
                dv.inc(V.tensor_tensor(outst[:, t, 128 * ec:128 * (ec + 1)], tb_p[:], cpa[:], ALU.mult))
                dv_c = dv.n
            pe_ec[ec] = pe.n

        dv_xr = {}
        for t in range(8):
            b = t % 2
            if t >= 2:
                SQ.wait_ge(dv.h, dv_xr[t - 2])
            sxr[b].inc(SQ.dma_start(out=xrt[b][:], in_=xr[128 * t:128 * (t + 1), :]), 16)
            V.wait_ge(sxr[b].h, sxr[b].n)
            dv.inc(V.tensor_tensor(outst[:, t, :], outst[:, t, :], xrt[b][:], ALU.add))
            dv_xr[t] = dv.n
            WQ.wait_ge(dv.h, dv.n)
            C["ou"].inc(WQ.dma_start(out=out[128 * t:128 * (t + 1), :], in_=outst[:, t, :]), 16)


_NC_CACHE = {}


def _get_nc():
    key = (DEQ_SLICES, STAGE, LNKV_SUB, DBG)
    if key not in _NC_CACHE:
        _NC_CACHE[key] = build()
    return _NC_CACHE[key]


def _rope_tables():
    i = np.arange(1, 65, dtype=np.float32)
    thetas = (np.float32(10000.0) ** (np.float32(-2.0) * (i - np.float32(1.0)) / np.float32(128.0))).astype(np.float32)
    pos = np.arange(1, TKV + 1, dtype=np.float32)
    ang = pos[:, None] * thetas[None, :]
    c = np.cos(ang).astype(np.float32)
    s = np.sin(ang).astype(np.float32)
    cosF = np.ascontiguousarray(np.concatenate([c, c], axis=1).T)
    sinH = np.ascontiguousarray(np.concatenate([s, -s], axis=1).T)
    return cosF, sinH


def prepare_in_maps(L0_attn_x, L0_attn_Wq, L0_attn_bq, L0_attn_Wk, L0_attn_bk, L0_attn_Wv,
                    L0_attn_bv, L0_attn_Wo, L0_attn_bo, L0_ffn_gate_W_packed, L0_ffn_gate_scales,
                    L0_ffn_up_W_packed, L0_ffn_up_scales, L0_ffn_down_W_packed, L0_ffn_down_scales):
    x = np.asarray(L0_attn_x, dtype=np.float32)
    wq = np.ascontiguousarray(np.asarray(L0_attn_Wq, np.float32).T)
    wk = np.ascontiguousarray(np.asarray(L0_attn_Wk, np.float32).T)
    wv = np.ascontiguousarray(np.asarray(L0_attn_Wv, np.float32).T)
    wo = np.ascontiguousarray(np.asarray(L0_attn_Wo, np.float32).T)
    bqv = np.ascontiguousarray(np.asarray(L0_attn_bq, np.float32).reshape(16, 128).T)
    bkv = np.ascontiguousarray(np.asarray(L0_attn_bk, np.float32).reshape(4, 128).T)
    bvv = np.ascontiguousarray(np.asarray(L0_attn_bv, np.float32).reshape(4, 128).T)
    bov = np.ascontiguousarray(np.asarray(L0_attn_bo, np.float32).reshape(16, 128).T)
    gpk = np.ascontiguousarray(np.asarray(L0_ffn_gate_W_packed, np.int32))
    gsc = np.ascontiguousarray(np.asarray(L0_ffn_gate_scales, np.float32))
    upkk = np.ascontiguousarray(np.asarray(L0_ffn_up_W_packed, np.int32))
    upsc = np.ascontiguousarray(np.asarray(L0_ffn_up_scales, np.float32))
    dpk = np.ascontiguousarray(np.asarray(L0_ffn_down_W_packed, np.int32))
    dssc = np.ascontiguousarray(np.asarray(L0_ffn_down_scales, np.float32))

    cosF, sinH = _rope_tables()
    iden = np.eye(128, dtype=np.float32).astype(ml_dtypes.bfloat16)
    idenf32 = np.eye(128, dtype=np.float32)
    onescol = np.ones((128, 1), np.float32).astype(ml_dtypes.bfloat16)
    onesf32 = np.ones((1, 128), np.float32)

    in_maps = []
    for c in range(8):
        bsel = c // 2
        h = c % 2
        own0 = 1024 * h
        oth0 = 1024 * (1 - h)
        idx = np.concatenate([np.arange(own0, own0 + 1024), np.arange(oth0, oth0 + 1024)])
        xb = x[bsel]
        in_maps.append({
            "xT": np.ascontiguousarray(xb.T[:, idx]),
            "xr": np.ascontiguousarray(xb[own0:own0 + 1024]),
            "wqT": wq, "wkT": wk, "wvT": wv, "woT": wo,
            "bq": bqv, "bk": bkv, "bv": bvv, "bo": bov,
            "gp": np.ascontiguousarray(gpk[2048 * c:2048 * (c + 1)]),
            "gs": np.ascontiguousarray(gsc[2048 * c:2048 * (c + 1)]),
            "up": np.ascontiguousarray(upkk[2048 * c:2048 * (c + 1)]),
            "us": np.ascontiguousarray(upsc[2048 * c:2048 * (c + 1)]),
            "dp": np.ascontiguousarray(dpk[256 * c:256 * (c + 1)]),
            "ds": np.ascontiguousarray(dssc[256 * c:256 * (c + 1)]),
            "qcos": np.ascontiguousarray(cosF[:, own0:own0 + 1024]),
            "qsin": np.ascontiguousarray(sinH[:, own0:own0 + 1024]),
            "kcos": np.ascontiguousarray(cosF[:, idx]),
            "ksin": np.ascontiguousarray(sinH[:, idx]),
            "posq": np.ascontiguousarray(np.arange(own0, own0 + 1024, dtype=np.float32).reshape(8, 128).T),
            "poskcol": np.ascontiguousarray(idx.astype(np.float32).reshape(16, 128).T),
            "poskk": idx.astype(np.float32).reshape(1, TKV),
            "iden": iden, "idenf": idenf32, "onescol": onescol, "onesf32": onesf32,
        })
    return in_maps


def kernel(**inputs):
    in_maps = prepare_in_maps(**inputs)
    nc = _get_nc()
    res = run_bass_kernel_spmd(nc, in_maps, core_ids=list(range(8)))
    outp = np.empty((4, 2048, 2048), np.float32)
    for c in range(8):
        bsel = c // 2
        h = c % 2
        outp[bsel, 1024 * h:1024 * (h + 1)] = res.results[c]["out"]
    return outp


# version 09
# speedup vs baseline: 1.3478x; 1.0758x over previous
import contextlib
import os
import tempfile

os.environ["NEURON_COMPILE_CACHE_URL"] = tempfile.mkdtemp(prefix="ncc_")

import numpy as np
import ml_dtypes
import concourse.bass as bass
import concourse.mybir as mybir
from concourse.bass_utils import run_bass_kernel_spmd

F32 = mybir.dt.float32
BF16 = mybir.dt.bfloat16
I32 = mybir.dt.int32
AF = mybir.ActivationFunctionType
ALU = mybir.AluOpType
AX = mybir.AxisListType.X

D = 2048
DK = 512
HD = 128
TOWN = 1024
TKV = 2048
MLP = 16384
NEG = -1.0e30
GELU_C = 0.044715
GELU_S = 0.7978845608028654
ISQ = 0.08838834764831845
EPS = 1e-5

DEQ_SLICES = 1
STAGE = 5
LNKV_SUB = 7
DBG = ""


class Ctr:
    def __init__(self, es, nc, name):
        self.h = es.enter_context(nc.semaphore(name))
        self.n = 0

    def inc(self, instr, k=1):
        instr.then_inc(self.h, k)
        self.n += k
        return self.n


def build():
    nc = bass.Bass("TRN2", target_bir_lowering=False)

    def din(name, shape, dt):
        return nc.dram_tensor(name, shape, dt, kind="ExternalInput").ap()

    xT = din("xT", [D, TKV], F32)
    xr = din("xr", [TOWN, D], F32)
    wqT = din("wqT", [D, D], F32)
    wkT = din("wkT", [D, DK], F32)
    wvT = din("wvT", [D, DK], F32)
    woT = din("woT", [D, D], F32)
    bq = din("bq", [128, 16], F32)
    bk = din("bk", [128, 4], F32)
    bv = din("bv", [128, 4], F32)
    bo = din("bo", [128, 16], F32)
    gp = din("gp", [2048 * DEQ_SLICES, 1024], I32)
    gs = din("gs", [2048 * DEQ_SLICES, 64], F32)
    upk = din("up", [2048 * DEQ_SLICES, 1024], I32)
    ups = din("us", [2048 * DEQ_SLICES, 64], F32)
    dp = din("dp", [256 * DEQ_SLICES, 8192], I32)
    dsc = din("ds", [256 * DEQ_SLICES, 512], F32)
    qcos = din("qcos", [HD, TOWN], F32)
    qsin = din("qsin", [HD, TOWN], F32)
    kcos = din("kcos", [HD, TKV], F32)
    ksin = din("ksin", [HD, TKV], F32)
    posq = din("posq", [128, 8], F32)
    poskcol = din("poskcol", [128, 16], F32)
    poskk = din("poskk", [1, TKV], F32)
    iden = din("iden", [128, 128], BF16)
    idenf = din("idenf", [128, 128], F32)
    onescol = din("onescol", [128, 1], BF16)
    onesf32 = din("onesf32", [1, 128], F32)

    out = nc.dram_tensor("out", [TOWN, D], F32, kind="ExternalOutput").ap()

    wk_ = "ExternalOutput" if "w" in DBG else "Internal"
    if "k" in DBG:
        dbg_xn = nc.dram_tensor("dbg_xn", [128, 16, TOWN], BF16, kind="ExternalOutput").ap()
        dbg_kt = nc.dram_tensor("dbg_kt", [128, 4, TKV], BF16, kind="ExternalOutput").ap()
        dbg_vh = nc.dram_tensor("dbg_vh", [128, 4, 16, 128], BF16, kind="ExternalOutput").ap()
    if "a" in DBG:
        nc._dbg_a = dict(
            qt=nc.dram_tensor("dbg_qt", [128, TOWN], BF16, kind="ExternalOutput").ap(),
            pb=nc.dram_tensor("dbg_pb", [128, TKV], BF16, kind="ExternalOutput").ap(),
            ell=nc.dram_tensor("dbg_ell", [128, 1], F32, kind="ExternalOutput").ap(),
            s=nc.dram_tensor("dbg_s", [128, TKV], F32, kind="ExternalOutput").ap(),
            pb2=nc.dram_tensor("dbg_pb2", [128, TKV], BF16, kind="ExternalOutput").ap(),
            msk=nc.dram_tensor("dbg_msk", [128, 8, TKV], BF16, kind="ExternalOutput").ap(),
            yt=nc.dram_tensor("dbg_yt", [128, 16, TOWN], BF16, kind="ExternalOutput").ap(),
        )
    sk_ = "ExternalOutput" if "s" in DBG else "Internal"
    ggcF = nc.dram_tensor("ggc", [D, 2048], BF16, kind="Internal").ap()
    gucF = nc.dram_tensor("guc", [D, 2048], BF16, kind="Internal").ap()
    gdcF = nc.dram_tensor("gdc", [MLP, 256], BF16, kind="Internal").ap()
    ggc = ggcF.rearrange("(a k) n -> a k n", a=1)
    guc = gucF.rearrange("(a k) n -> a k n", a=1)
    gdc = gdcF.rearrange("(a k) n -> a k n", a=1)
    GgF = nc.dram_tensor("Gg", [8 * D, 2048], BF16, kind=wk_, addr_space="Shared").ap()
    GuF = nc.dram_tensor("Gu", [8 * D, 2048], BF16, kind=wk_, addr_space="Shared").ap()
    GdF = nc.dram_tensor("Gd", [8 * MLP, 256], BF16, kind=wk_, addr_space="Shared").ap()
    Gg = GgF.rearrange("(s k) n -> s k n", s=8)
    Gu = GuF.rearrange("(s k) n -> s k n", s=8)
    Gd = GdF.rearrange("(s k) n -> s k n", s=8)
    sDram = nc.dram_tensor("sDram", [D, TOWN], BF16, kind=sk_).ap()

    es = contextlib.ExitStack()
    with es:
        E = es.enter_context
        xnT = E(nc.sbuf_tensor("xnT", [128, 16, TOWN], BF16))
        identt = E(nc.sbuf_tensor("identt", [128, 128], BF16))
        idenft = E(nc.sbuf_tensor("idenft", [128, 128], F32))
        onescolt = E(nc.sbuf_tensor("onescolt", [128, 1], BF16))
        onesf32t = E(nc.sbuf_tensor("onesf32t", [1, 128], F32))
        posqt = E(nc.sbuf_tensor("posqt", [128, 8], F32))
        poskcolt = E(nc.sbuf_tensor("poskcolt", [128, 16], F32))
        bqt = E(nc.sbuf_tensor("bqt", [128, 16], F32))
        bot = E(nc.sbuf_tensor("bot", [128, 16], F32))
        bkt = E(nc.sbuf_tensor("bkt", [128, 4], F32))
        bvt = E(nc.sbuf_tensor("bvt", [128, 4], F32))
        cneg4 = E(nc.sbuf_tensor("cneg4", [128, 1], F32))

        C = {}
        for name in ["gd", "pe", "dv", "ac", "ou", "cn", "x", "ms", "st"]:
            C[name] = Ctr(es, nc, name)
        for name in ["pk", "ws", "gg", "gt", "xr"]:
            C[name] = [Ctr(es, nc, name + "0"), Ctr(es, nc, name + "1")]

        gd, pe, dv, ac, ou = C["gd"], C["pe"], C["dv"], C["ac"], C["ou"]

        blk = es.enter_context(nc.Block())
        SQ = nc.sync
        WQ = nc.scalar
        V = nc.vector
        A = nc.scalar
        T = nc.tensor
        ALL_E = (SQ, V, T, WQ, nc.gpsimd)

        def flat_ctrs():
            r = []
            for v in C.values():
                r.extend(v if isinstance(v, list) else [v])
            return r

        def barrier():
            vals = [(c, c.n) for c in flat_ctrs()]
            for e in ALL_E:
                for c, v in vals:
                    if v > 0:
                        e.wait_ge(c.h, v)

        cn = C["cn"]
        cn.inc(SQ.dma_start(out=identt[:], in_=iden), 16)
        cn.inc(SQ.dma_start(out=idenft[:], in_=idenf), 16)
        cn.inc(SQ.dma_start(out=onescolt[:], in_=onescol), 16)
        cn.inc(SQ.dma_start(out=onesf32t[:], in_=onesf32), 16)
        cn.inc(SQ.dma_start(out=posqt[:], in_=posq), 16)
        cn.inc(SQ.dma_start(out=poskcolt[:], in_=poskcol), 16)
        cn.inc(SQ.dma_start(out=bqt[:], in_=bq), 16)
        cn.inc(SQ.dma_start(out=bot[:], in_=bo), 16)
        cn.inc(SQ.dma_start(out=bkt[:], in_=bk), 16)
        cn.inc(SQ.dma_start(out=bvt[:], in_=bv), 16)
        dv_ = C["dv"]
        dv_.inc(V.memset(cneg4[:], -4.0))
        for e in (V, T, WQ, nc.gpsimd):
            e.wait_ge(cn.h, cn.n)
            e.wait_ge(dv_.h, 1)

        cc = Ctr(es, nc, "ccs")
        nc._cc = cc
        if STAGE >= 1:
            G_ = nc.gpsimd
            rg = [[0, 1, 2, 3, 4, 5, 6, 7]]
            _dequant(nc, SQ, WQ, V, T, C, identt, gp, gs, ggc, 1.0, "g")
            _dequant(nc, SQ, WQ, V, T, C, identt, upk, ups, guc, 1.0, "u")
            _dequant(nc, SQ, WQ, V, T, C, identt, dp, dsc, gdc, 0.5, "d")

            def issue_gathers():
                G_.wait_ge(gd.h, gd.n)
                G_.wait_ge(C["x"].h, C["x"].n)
                cc.inc(G_.collective_compute("AllGather", ALU.bypass,
                                             replica_groups=rg,
                                             ins=[ggcF.opt()], outs=[GgF.opt()]), 1)
                cc.inc(G_.collective_compute("AllGather", ALU.bypass,
                                             replica_groups=rg,
                                             ins=[gucF.opt()], outs=[GuF.opt()]), 1)
                cc.inc(G_.collective_compute("AllGather", ALU.bypass,
                                             replica_groups=rg,
                                             ins=[gdcF.opt()], outs=[GdF.opt()]), 1)
        else:
            def issue_gathers():
                pass
        SQ.wait_ge(gd.h, gd.n)

        if STAGE >= 2:
            with contextlib.ExitStack() as es23:
                E23 = es23.enter_context
                poskkb = E23(nc.sbuf_tensor("poskkb", [128, TKV], F32))
                qcost = E23(nc.sbuf_tensor("qcost", [128, TOWN], F32))
                qsint = E23(nc.sbuf_tensor("qsint", [128, TOWN], F32))
                ms = C["ms"]
                ms.inc(SQ.dma_start(out=poskkb[:], in_=poskk.partition_broadcast(128)), 16)
                ms.inc(SQ.dma_start(out=qcost[:], in_=qcos), 16)
                ms.inc(SQ.dma_start(out=qsint[:], in_=qsin), 16)
                kvt = _ln_kv(nc, es23, SQ, V, A, T, C,
                             xT, wkT, wvT, bkt, bvt, kcos, ksin,
                             xnT, onescolt, onesf32t, identt)
                issue_gathers()
                barrier()
                if "k" in DBG:
                    ou.inc(WQ.dma_start(out=dbg_xn, in_=xnT[:]), 16)
                    ou.inc(WQ.dma_start(out=dbg_kt, in_=kvt["KT"][:]), 16)
                    ou.inc(WQ.dma_start(out=dbg_vh, in_=kvt["Vh"][:]), 16)
                    for e in ALL_E:
                        e.wait_ge(ou.h, ou.n)
                if STAGE >= 3:
                    _attention(nc, es23, SQ, WQ, V, A, T, C,
                               wqT, woT, bqt, bot, qcost, qsint, poskcolt, poskkb,
                               xnT, sDram, kvt, identt, cneg4, onescolt, onesf32t)
                barrier()

        if STAGE >= 4:
            with contextlib.ExitStack() as es45:
                gacc = es45.enter_context(nc.sbuf_tensor("gacc", [128, 16, TOWN], F32))
                _ffn(nc, SQ, V, A, T, C, Gg, Gu, Gd, xnT, gacc, nc._cc)
                barrier()
                if STAGE >= 5:
                    _combine(nc, SQ, WQ, V, A, T, C, xr, out, sDram, gacc, identt, idenft)

        SQ.wait_ge(ou.h, ou.n)

    return nc


def _dequant(nc, SQ, WQ, V, T, C, identt, packed, scales, Gout, smul, tag):
    gd, pe, dv = C["gd"], C["pe"], C["dv"]
    pk = C["pk"]
    KK = packed.shape[1]
    NB = scales.shape[1]
    n_loc = Gout.shape[2]
    n_rtiles = n_loc // 128
    n_cchunk = KK // 512
    nslices = packed.shape[0] // n_loc
    K_win = Gout.shape[1] // n_cchunk

    with contextlib.ExitStack() as es:
        SQ.wait_ge(gd.h, gd.n)
        E = es.enter_context
        pt = [E(nc.sbuf_tensor(f"dqp{tag}{i}", [128, 512], I32)) for i in range(4)]
        sct = [E(nc.sbuf_tensor(f"dqs{tag}{i}", [128, NB], F32)) for i in range(4)]
        sce = [E(nc.sbuf_tensor(f"dqe{tag}{i}", [128, NB], F32)) for i in range(4)]
        mtile = E(nc.sbuf_tensor(f"dqm{tag}", [128, 1024], I32))
        stile = E(nc.sbuf_tensor(f"dqt{tag}", [128, 1024], I32))
        btile = E(nc.sbuf_tensor(f"dqb{tag}", [128, 1024], I32))
        wtile = [E(nc.sbuf_tensor(f"dqw{tag}{i}", [128, 1024], BF16)) for i in range(2)]
        rb = E(nc.sbuf_tensor(f"dqr{tag}", [128, 8, n_loc], BF16))
        tp = [E(nc.psum_tensor(f"dqx{tag}{i}", [128, 512], BF16)) for i in range(2)]

        dv_decode = {}
        dv_evac = {}
        pe_tp = {}
        gd_grp = {}

        u = 0
        for sl in range(nslices):
            for c in range(n_cchunk):
                for r in range(n_rtiles):
                    b = u % 4
                    bc = u % 2
                    if u >= 4:
                        SQ.wait_ge(dv.h, dv_decode[u - 4])
                    row0 = sl * n_loc + r * 128
                    pk[bc].inc(SQ.dma_start(out=pt[b][:], in_=packed[row0:row0 + 128, c * 512:(c + 1) * 512]), 16)
                    pk[bc].inc(SQ.dma_start(out=sct[b][:], in_=scales[row0:row0 + 128, :]), 16)

                    V.wait_ge(pk[bc].h, pk[bc].n)
                    if u >= 2:
                        V.wait_ge(pe.h, pe_tp[u - 2])
                    if r == 0:
                        g_prev = (sl * n_cchunk + c) - 1
                        if g_prev >= 0:
                            V.wait_ge(gd.h, gd_grp[g_prev])
                    wb = u % 2
                    V.tensor_scalar(sce[b][:], sct[b][:], float(smul * (2.0 ** 126)), None, ALU.mult)
                    p = pt[b][:]
                    V.tensor_scalar(mtile[:, 0:512], p, 7, 22, ALU.bitwise_and, ALU.logical_shift_left)
                    V.tensor_scalar(stile[:, 0:512], p, 8, 28, ALU.bitwise_and, ALU.logical_shift_left)
                    V.tensor_scalar(mtile[:, 512:1024], p, 0x70, 18, ALU.bitwise_and, ALU.logical_shift_left)
                    V.tensor_scalar(stile[:, 512:1024], p, 0x80, 24, ALU.bitwise_and, ALU.logical_shift_left)
                    V.tensor_tensor(btile[:], mtile[:], stile[:], ALU.bitwise_or)
                    scb = sce[b][:, 32 * c:32 * c + 32].unsqueeze(2).broadcast_to([128, 32, 16])
                    bl = btile[:, 0:512].bitcast(F32).rearrange("p (b j) -> p b j", b=32)
                    bh = btile[:, 512:1024].bitcast(F32).rearrange("p (b j) -> p b j", b=32)
                    V.tensor_tensor(wtile[wb][:, 0:512].rearrange("p (b j) -> p b j", b=32), bl, scb, ALU.mult)
                    dv.inc(V.tensor_tensor(wtile[wb][:, 512:1024].rearrange("p (b j) -> p b j", b=32), bh, scb, ALU.mult))
                    dv_decode[u] = dv.n

                    T.wait_ge(dv.h, dv_decode[u])
                    if u >= 1:
                        T.wait_ge(dv.h, dv_evac[u - 1])
                    for plane in range(2):
                        for s4 in range(4):
                            ins = T.transpose(tp[plane][:, 128 * s4:128 * (s4 + 1)],
                                              wtile[wb][:, 512 * plane + 128 * s4:512 * plane + 128 * (s4 + 1)],
                                              identt[:])
                        pe.inc(ins)
                    pe_tp[u] = pe.n

                    V.wait_ge(pe.h, pe_tp[u])
                    V.tensor_copy(rb[:, 0:4, r * 128:(r + 1) * 128],
                                  tp[0][:].rearrange("p (s j) -> p s j", s=4))
                    dv.inc(V.tensor_copy(rb[:, 4:8, r * 128:(r + 1) * 128],
                                         tp[1][:].rearrange("p (s j) -> p s j", s=4)))
                    dv_evac[u] = dv.n
                    u += 1

                WQ.wait_ge(dv.h, dv_evac[u - 1])
                for plane in range(2):
                    for s4 in range(4):
                        base = K_win * c + 256 * s4 + plane
                        gd.inc(WQ.dma_start(out=Gout[sl, base:base + 255:2, :],
                                            in_=rb[:, plane * 4 + s4, :]), 16)
                gd_grp[sl * n_cchunk + c] = gd.n


def _ln_kv(nc, es, SQ, V, A, T, C, xT, wkT, wvT, bkt, bvt, kcos, ksin,
           xnT, onescolt, onesf32t, identt):
    pe, dv, ac, ms, sx, ws = C["pe"], C["dv"], C["ac"], C["ms"], C["x"], C["ws"]
    E = es.enter_context
    KT = E(nc.sbuf_tensor("KT", [128, 4, TKV], BF16))
    Vh = E(nc.sbuf_tensor("Vh", [128, 4, 16, 128], BF16))

    with contextlib.ExitStack() as es2:
        E2 = es2.enter_context
        kcost = E2(nc.sbuf_tensor("kcost", [128, TKV], F32))
        ksint = E2(nc.sbuf_tensor("ksint", [128, TKV], F32))
        wk16 = E2(nc.sbuf_tensor("wk16", [128, 16, DK], BF16))
        wv16 = E2(nc.sbuf_tensor("wv16", [128, 16, DK], BF16))
        xtc = E2(nc.sbuf_tensor("xtc", [128, 16, 512], F32))
        xn_t = E2(nc.sbuf_tensor("xn_t", [128, 16, 512], BF16))
        x16 = [E2(nc.sbuf_tensor(f"x16{i}", [128, 512], BF16)) for i in range(2)]
        x2 = [E2(nc.sbuf_tensor(f"x2{i}", [128, 512], BF16)) for i in range(2)]
        wstage = [E2(nc.sbuf_tensor(f"wst{i}", [128, 512], F32)) for i in range(2)]
        srow = E2(nc.sbuf_tensor("srow", [1, 512], F32))
        qrow = E2(nc.sbuf_tensor("qrow", [1, 512], F32))
        trow = E2(nc.sbuf_tensor("trow", [1, 512], F32))
        vrow = E2(nc.sbuf_tensor("vrow", [1, 512], F32))
        t1 = E2(nc.sbuf_tensor("lnt1", [128, 512], F32))
        kraw = E2(nc.sbuf_tensor("kraw", [128, 512], F32))
        krot = E2(nc.sbuf_tensor("krot", [128, 512], F32))
        kcs = E2(nc.sbuf_tensor("kcs", [128, 512], F32))
        vtmp = E2(nc.sbuf_tensor("vtmp", [128, 512], BF16))
        sum_p = E2(nc.psum_tensor("sum_p", [1, 512], F32))
        sq_p = E2(nc.psum_tensor("sq_p", [1, 512], F32))
        mb_p = E2(nc.psum_tensor("mb_p", [128, 512], F32))
        rb_p = E2(nc.psum_tensor("rb_p", [128, 512], F32))
        kv_p = E2(nc.psum_tensor("kv_p", [128, 512], F32))
        vv_p = E2(nc.psum_tensor("vv_p", [128, 512], F32))
        vt_p = E2(nc.psum_tensor("vt_p", [128, 512], BF16))

        ms.inc(SQ.dma_start(out=kcost[:], in_=kcos), 16)
        ms.inc(SQ.dma_start(out=ksint[:], in_=ksin), 16)
        MS_ALL = ms.n

        casts = {}
        for w in range(2):
            src = wkT if w == 0 else wvT
            dst = wk16 if w == 0 else wv16
            for kc in range(16):
                i = w * 16 + kc
                b = i % 2
                if i >= 2:
                    SQ.wait_ge(dv.h, casts[i - 2])
                ws[b].inc(SQ.dma_start(out=wstage[b][:], in_=src[128 * kc:128 * (kc + 1), :]), 16)
                V.wait_ge(ws[b].h, ws[b].n)
                dv.inc(V.tensor_copy(dst[:, kc, :], wstage[b][:]))
                casts[i] = dv.n
        V.wait_ge(ms.h, MS_ALL)

        dv_norm = {}
        pe_kv = {}
        dv_kv = dv.n
        for tc in range(4 if LNKV_SUB >= 2 else 0):
            own = tc < 2
            if tc >= 1:
                SQ.wait_ge(dv.h, dv_norm[tc - 1])
            for kc in range(16):
                sx.inc(SQ.dma_start(out=xtc[:, kc, :],
                                    in_=xT[128 * kc:128 * (kc + 1), 512 * tc:512 * (tc + 1)]), 16)
            sx_all = sx.n

            pe_stat = {}
            for kc in range(16):
                bb = kc % 2
                V.wait_ge(sx.h, sx_all)
                A.wait_ge(sx.h, sx_all)
                if kc >= 2:
                    V.wait_ge(pe.h, pe_stat[kc - 2])
                    A.wait_ge(pe.h, pe_stat[kc - 2])
                dv.inc(V.tensor_copy(x16[bb][:], xtc[:, kc, :]))
                ac.inc(A.activation(out=x2[bb][:], in_=xtc[:, kc, :], func=AF.Square))
                T.wait_ge(dv.h, dv.n)
                T.wait_ge(ac.h, ac.n)
                T.matmul(sum_p[:], onescolt[:], x16[bb][:], start=(kc == 0), stop=(kc == 15))
                pe.inc(T.matmul(sq_p[:], onescolt[:], x2[bb][:], start=(kc == 0), stop=(kc == 15)))
                pe_stat[kc] = pe.n
            pe_stats = pe.n

            if LNKV_SUB < 3:
                dv_norm[tc] = dv.n
                pe_kv[tc] = pe.n
                continue
            V.wait_ge(pe.h, pe_stats)
            V.tensor_scalar(srow[:], sum_p[:], 1.0 / D, None, ALU.mult)
            V.tensor_scalar(qrow[:], sq_p[:], 1.0 / D, None, ALU.mult)
            V.tensor_copy(t1[0:1, :], qrow[:])
            V.tensor_tensor(trow[:], srow[:], srow[:], ALU.mult)
            V.tensor_copy(t1[0:1, :], qrow[:])
            V.scalar_tensor_tensor(vrow[:], qrow[:], EPS, trow[:], ALU.add, ALU.subtract)
            dv.inc(V.tensor_copy(t1[0:1, :], qrow[:]))
            A.wait_ge(dv.h, dv.n)
            ac.inc(A.activation(out=trow[:], in_=vrow[:], func=AF.Sqrt))
            V.wait_ge(ac.h, ac.n)
            V.reciprocal(vrow[:], trow[:])
            dv.inc(V.tensor_copy(t1[0:1, :], qrow[:]))
            T.wait_ge(dv.h, dv.n)
            T.matmul(mb_p[:], onesf32t[:], srow[:], start=True, stop=True)
            pe.inc(T.matmul(rb_p[:], onesf32t[:], vrow[:], start=True, stop=True))
            pe_bc = pe.n

            if LNKV_SUB < 4:
                dv_norm[tc] = dv.n
                pe_kv[tc] = pe.n
                continue
            V.wait_ge(pe.h, pe_bc)
            if tc >= 3:
                V.wait_ge(pe.h, pe_kv[tc - 1])
            for kc in range(16):
                V.tensor_tensor(t1[:], xtc[:, kc, :], mb_p[:], ALU.subtract)
                dst = xnT[:, kc, 512 * tc:512 * (tc + 1)] if own else xn_t[:, kc, :]
                dv.inc(V.tensor_tensor(dst, t1[:], rb_p[:], ALU.mult))
            dv_norm[tc] = dv.n

            def xsrc(kc, tc=tc, own=own):
                return xnT[:, kc, 512 * tc:512 * (tc + 1)] if own else xn_t[:, kc, :]

            for kvh in range(4 if LNKV_SUB >= 5 else 0):
                T.wait_ge(dv.h, dv_norm[tc])
                T.wait_ge(dv.h, dv_kv)
                for kc in range(16):
                    T.matmul(kv_p[:], wk16[:, kc, 128 * kvh:128 * (kvh + 1)], xsrc(kc),
                             start=(kc == 0), stop=(kc == 15))
                for kc in range(16):
                    ins = T.matmul(vv_p[:], wv16[:, kc, 128 * kvh:128 * (kvh + 1)], xsrc(kc),
                                   start=(kc == 0), stop=(kc == 15))
                pe.inc(ins)
                pe_proj = pe.n
                if LNKV_SUB < 6:
                    dv_kv = dv.n
                    continue

                V.wait_ge(pe.h, pe_proj)
                cs = slice(512 * tc, 512 * (tc + 1))
                nops = (LNKV_SUB - 60) if 60 <= LNKV_SUB < 70 else 6
                ops = [
                    lambda: V.tensor_scalar(kraw[:], kv_p[:], bkt[:, kvh:kvh + 1], None, ALU.add),
                    lambda: V.tensor_tensor(krot[0:64, :], kraw[64:128, :], ksint[64:128, cs], ALU.mult),
                    lambda: V.tensor_tensor(krot[64:128, :], kraw[0:64, :], ksint[0:64, cs], ALU.mult),
                    lambda: V.tensor_tensor(kcs[:], kraw[:], kcost[:, cs], ALU.mult),
                    lambda: V.tensor_tensor(KT[:, kvh, cs], kcs[:], krot[:], ALU.add),
                    lambda: V.tensor_scalar(vtmp[:], vv_p[:], bvt[:, kvh:kvh + 1], None, ALU.add),
                ]
                last = None
                for _i in range(nops):
                    last = ops[_i]()
                if last is None:
                    last = V.engine_nop() if hasattr(V, "engine_nop") else V.nop()
                dv.inc(last)
                dv_kv = dv.n
                if LNKV_SUB < 7:
                    continue

                T.wait_ge(dv.h, dv_kv)
                for j in range(4):
                    ins = T.transpose(vt_p[:, 128 * j:128 * (j + 1)],
                                      vtmp[:, 128 * j:128 * (j + 1)], identt[:])
                pe.inc(ins)
                V.wait_ge(pe.h, pe.n)
                dv.inc(V.tensor_copy(Vh[:, kvh, 4 * tc:4 * tc + 4, :],
                                     vt_p[:].rearrange("p (s j) -> p s j", s=4)))
                dv_kv = dv.n
            pe_kv[tc] = pe.n
    return dict(KT=KT, Vh=Vh)


def _attention(nc, es, SQ, WQ, V, A, T, C, wqT, woT, bqt, bot,
               qcost, qsint, poskcolt, poskkb, xnT, sDram, kvt, identt, cneg4,
               onescolt, onesf32t):
    gd, pe, dv, ac, ws = C["gd"], C["pe"], C["dv"], C["ac"], C["ws"]
    KT, Vh = kvt["KT"], kvt["Vh"]
    E = es.enter_context
    YT = E(nc.sbuf_tensor("YT", [128, 16, TOWN], BF16))
    mask01 = E(nc.sbuf_tensor("mask01", [128, 16, TOWN], BF16))

    with contextlib.ExitStack() as es2:
        E2 = es2.enter_context
        wstage = [E2(nc.sbuf_tensor(f"aws{i}", [128, 16, 128], F32)) for i in range(2)]
        w16 = [E2(nc.sbuf_tensor(f"aw16{i}", [128, 16, 128], BF16)) for i in range(2)]
        qraw = E2(nc.sbuf_tensor("qraw", [128, TOWN], F32))
        qrot = E2(nc.sbuf_tensor("qrot", [128, TOWN], F32))
        qcs = E2(nc.sbuf_tensor("qcs", [128, TOWN], F32))
        QT = E2(nc.sbuf_tensor("QT", [128, 2, TOWN], BF16))
        NSL = 6
        Pe = E2(nc.sbuf_tensor("Pe", [128, NSL, 512], BF16))
        Pm = E2(nc.sbuf_tensor("Pm", [128, NSL, 512], BF16))
        elrec = E2(nc.sbuf_tensor("elrec", [1, TOWN], F32))
        ebs = E2(nc.sbuf_tensor("ebs", [128, 2, 512], F32))
        stmp = E2(nc.sbuf_tensor("stmp", [128, TOWN], BF16))
        st_p = [E2(nc.psum_tensor(f"st_p{i}", [128, 512], F32)) for i in range(3)]
        y_p = [E2(nc.psum_tensor(f"y_p{i}", [128, 512], F32)) for i in range(2)]
        eb_p = E2(nc.psum_tensor("eb_p", [128, 512], F32))
        qp_p = E2(nc.psum_tensor("qp_p", [128, 1024], F32))

        for s in range(16):
            dv.inc(V.tensor_scalar(mask01[:, s, :], poskkb[:, 0:TOWN],
                                   poskcolt[:, s:s + 1], None, ALU.is_ge))
        dv_mask_built = dv.n

        ac_exp = {}
        dv_mask = {}
        pe_st = {}
        pe_yell = {}
        dv_qe = {}
        st_last_of_qh = {}
        dv_ynorm = {0: 0, 1: 0}
        dv_tail = [0]
        pe_rank1 = {0: 0, 1: 0}
        pe_w = {}
        from collections import deque
        pending = deque()

        def flush_pending():
            pg, pqh, phalf, ps, pslot = pending.popleft()
            pqs = slice(512 * phalf, 512 * (phalf + 1))
            pkvh = pqh // 4
            T.wait_ge(dv.h, dv_mask[pg])
            if ps == 0:
                T.wait_ge(dv.h, dv_tail[0])
            T.matmul(y_p[phalf][:], Vh[:, pkvh, ps, :], Pm[:, pslot, :],
                     start=(ps == 0), stop=(ps == 15))
            pe.inc(T.matmul(eb_p[0:1, :], onescolt[:], Pm[:, pslot, :],
                            start=(ps == 0), stop=(ps == 15)))
            pe_yell[pg] = pe.n
            if ps == 15:
                V.wait_ge(pe.h, pe_yell[pg])
                V.wait_ge(pe.h, pe_rank1[phalf])
                dv.inc(V.reciprocal(elrec[0:1, pqs], eb_p[0:1, :]))
                dv_rec = dv.n
                T.wait_ge(dv.h, dv_rec)
                pe.inc(T.matmul(eb_p[:], onesf32t[:], elrec[0:1, pqs],
                                start=True, stop=True))
                pe_rank1[phalf] = pe.n
                V.wait_ge(pe.h, pe_rank1[phalf])
                V.tensor_copy(ebs[:, phalf, :], eb_p[:])
                dv.inc(V.tensor_tensor(YT[:, pqh, pqs], y_p[phalf][:],
                                       ebs[:, phalf, :], ALU.mult))
                dv_ynorm[phalf] = dv.n
                dv_tail[0] = dv.n

        g = 0
        for qh in range(16):
            kvh = qh // 4
            b = qh % 2
            if qh >= 2:
                SQ.wait_ge(pe.h, pe_w[qh - 2])
            ws[b].inc(SQ.dma_start(out=wstage[b][:],
                                   in_=wqT[:, 128 * qh:128 * (qh + 1)].rearrange("(c p) e -> p c e", p=128)), 16)
            V.wait_ge(ws[b].h, ws[b].n)
            if qh >= 2:
                V.wait_ge(pe.h, pe_w[qh - 2])
            dv.inc(V.tensor_copy(w16[b][:], wstage[b][:]))
            dv_w = dv.n

            T.wait_ge(dv.h, dv_w)
            T.wait_ge(dv.h, dv_qe.get(qh - 1, 0))
            for th in range(2):
                for kc in range(16):
                    ins = T.matmul(qp_p[:, 512 * th:512 * (th + 1)], w16[b][:, kc, :],
                                   xnT[:, kc, 512 * th:512 * (th + 1)], start=(kc == 0), stop=(kc == 15))
            pe.inc(ins)
            pe_w[qh] = pe.n

            qb = qh % 2
            V.wait_ge(pe.h, pe_w[qh])
            V.wait_ge(pe.h, st_last_of_qh.get(qh - 2, 0))
            V.tensor_scalar(qraw[:], qp_p[:], bqt[:, qh:qh + 1], None, ALU.add)
            V.tensor_tensor(qrot[0:64, :], qraw[64:128, :], qsint[64:128, :], ALU.mult)
            V.tensor_tensor(qrot[64:128, :], qraw[0:64, :], qsint[0:64, :], ALU.mult)
            V.tensor_tensor(qcs[:], qraw[:], qcost[:], ALU.mult)
            dv.inc(V.tensor_tensor(QT[:, qb, :], qcs[:], qrot[:], ALU.add))
            dv_qe[qh] = dv.n

            for half in range(2):
                qs = slice(512 * half, 512 * (half + 1))
                for s in range(16):
                    par = g % 3
                    slot = g % NSL
                    T.wait_ge(dv.h, dv_qe[qh])
                    if s == 0:
                        T.wait_ge(dv.h, max(dv_mask_built, dv_ynorm[half]))
                    if g >= 3:
                        T.wait_ge(ac.h, ac_exp[g - 3])
                    pe.inc(T.matmul(st_p[par][:], KT[:, kvh, 128 * s:128 * (s + 1)],
                                    QT[:, qb, qs], start=True, stop=True))
                    pe_st[g] = pe.n
                    if half == 1 and s == 15:
                        st_last_of_qh[qh] = pe.n

                    A.wait_ge(pe.h, pe_st[g])
                    if g >= NSL:
                        A.wait_ge(dv.h, dv_mask[g - NSL])
                    ac.inc(A.activation(out=Pe[:, slot, :], in_=st_p[par][:],
                                        func=AF.Exp, scale=ISQ, bias=cneg4[:]))
                    ac_exp[g] = ac.n

                    V.wait_ge(ac.h, ac_exp[g])
                    if g >= NSL:
                        V.wait_ge(pe.h, pe_yell.get(g - NSL, 0))
                    dv.inc(V.tensor_tensor(Pm[:, slot, :], Pe[:, slot, :],
                                           mask01[:, s, qs], ALU.mult))
                    dv_mask[g] = dv.n

                    if len(pending) >= 3:
                        flush_pending()
                    pending.append((g, qh, half, s, slot))
                    g += 1
        while pending:
            flush_pending()

        dv_yt = dv.n
        gd_s = 0
        dv_se = 0
        for ec in range(16):
            b = ec % 2
            last_use = pe_w[16 + ec - 2] if ec >= 2 else pe_w[14 + ec]
            SQ.wait_ge(pe.h, last_use)
            ws[b].inc(SQ.dma_start(out=wstage[b][:],
                                   in_=woT[:, 128 * ec:128 * (ec + 1)].rearrange("(c p) e -> p c e", p=128)), 16)
            V.wait_ge(ws[b].h, ws[b].n)
            V.wait_ge(pe.h, last_use)
            dv.inc(V.tensor_copy(w16[b][:], wstage[b][:]))
            dv_w2 = dv.n

            T.wait_ge(dv.h, dv_w2)
            T.wait_ge(dv.h, dv_yt)
            if dv_se:
                T.wait_ge(dv.h, dv_se)
            for th in range(2):
                for kc in range(16):
                    ins = T.matmul(qp_p[:, 512 * th:512 * (th + 1)], w16[b][:, kc, :],
                                   YT[:, kc, 512 * th:512 * (th + 1)], start=(kc == 0), stop=(kc == 15))
            pe.inc(ins)
            pe_w[16 + ec] = pe.n

            V.wait_ge(pe.h, pe_w[16 + ec])
            if gd_s:
                V.wait_ge(gd.h, gd_s)
            dv.inc(V.tensor_scalar(stmp[:], qp_p[:], bot[:, ec:ec + 1], None, ALU.add))
            dv_se = dv.n
            WQ.wait_ge(dv.h, dv_se)
            gd_s = gd.inc(WQ.dma_start(out=sDram[128 * ec:128 * (ec + 1), :], in_=stmp[:]), 16)


def _ffn(nc, SQ, V, A, T, C, Gg, Gu, Gd, xnT, gacc, cc):
    pe, dv, ac = C["pe"], C["dv"], C["ac"]
    sgg, sgt = C["gg"], C["gt"]
    NRING = 18
    with contextlib.ExitStack() as es2:
        E2 = es2.enter_context
        grng = E2(nc.sbuf_tensor("grng", [128, NRING, 512], BF16))
        urng = E2(nc.sbuf_tensor("urng", [128, NRING, 512], BF16))
        gdt = [E2(nc.sbuf_tensor(f"gdt{i}", [128, 16, 256], BF16)) for i in range(2)]
        interT = E2(nc.sbuf_tensor("interT", [128, 16, TOWN], BF16))
        ga = E2(nc.sbuf_tensor("ffa", [128, 1024], F32))
        gb = E2(nc.sbuf_tensor("ffb", [128, 1024], F32))
        gth = E2(nc.sbuf_tensor("ffth", [128, 1024], F32))
        gp_p = [E2(nc.psum_tensor(f"gp_p{i}", [128, 1024], F32)) for i in range(2)]
        up_p = [E2(nc.psum_tensor(f"up_p{i}", [128, 1024], F32)) for i in range(2)]

        dv.inc(V.memset(gacc[:], 0.0))

        group_pe = {}
        emitted = [0]
        n_strips = 8 * 4 * 16

        def strip_idx(seg, ng, kc):
            return (seg * 4 + ng) * 16 + kc

        SQ.wait_ge(cc.h, 2)
        cc_d = [False]
        sgt_cnt = {}
        pe_dw = {}

        def emit_dload(di2):
            if di2 >= 64:
                return
            seg2, sl2 = divmod(di2, 8)
            b2 = di2 % 2
            if not cc_d[0]:
                SQ.wait_ge(cc.h, 3)
                cc_d[0] = True
            if di2 >= 2:
                SQ.wait_ge(pe.h, pe_dw[di2 - 2])
            sgt[b2].inc(SQ.dma_start(out=gdt[b2][:],
                                     in_=Gd[sl2, 2048 * seg2:2048 * (seg2 + 1), :].rearrange("(c p) h -> p c h", p=128)), 16)
            sgt_cnt[di2] = sgt[b2].n

        def emit_loads(upto):
            while emitted[0] < min(upto, n_strips):
                s = emitted[0]
                seg, rem = divmod(s, 64)
                ng, kc = divmod(rem, 16)
                slot = s % NRING
                gidx = s // 16
                if s >= NRING:
                    SQ.wait_ge(pe.h, group_pe[(s - NRING) // 16])
                sgg[gidx % 2].inc(SQ.dma_start(out=grng[:, slot, :],
                                               in_=Gg[seg, 128 * kc:128 * (kc + 1), 512 * ng:512 * (ng + 1)]), 16)
                sgg[gidx % 2].inc(SQ.dma_start(out=urng[:, slot, :],
                                               in_=Gu[seg, 128 * kc:128 * (kc + 1), 512 * ng:512 * (ng + 1)]), 16)
                emitted[0] += 1

        dv_ep = {}
        dv_dacc = 0
        for seg in range(8):
            for ng in range(4):
                g = seg * 4 + ng
                emit_loads(strip_idx(seg, ng, 0) + NRING)
                sg_all = 32 * 16 * (g // 2 + 1)
                for ns in range(4):
                    nsub = 4 * ng + ns
                    gidx2 = seg * 16 + nsub
                    par = nsub % 2
                    T.wait_ge(sgg[g % 2].h, sg_all)
                    if gidx2 >= 2 and (gidx2 - 2) in dv_ep:
                        T.wait_ge(dv.h, dv_ep[gidx2 - 2])
                    if ng == 0 and ns < 2 and dv_dacc:
                        T.wait_ge(dv.h, dv_dacc)
                    for kc in range(16):
                        s = strip_idx(seg, ng, kc)
                        slot = s % NRING
                        for th in range(2):
                            T.matmul(gp_p[par][:, 512 * th:512 * (th + 1)],
                                     grng[:, slot, 128 * ns:128 * (ns + 1)],
                                     xnT[:, kc, 512 * th:512 * (th + 1)],
                                     start=(kc == 0), stop=(kc == 15))
                        for th in range(2):
                            ins = T.matmul(up_p[par][:, 512 * th:512 * (th + 1)],
                                           urng[:, slot, 128 * ns:128 * (ns + 1)],
                                           xnT[:, kc, 512 * th:512 * (th + 1)],
                                           start=(kc == 0), stop=(kc == 15))
                    pe.inc(ins)
                    pe_gu = pe.n

                    gpp = gp_p[par][:]
                    upp = up_p[par][:]
                    A.wait_ge(pe.h, pe_gu)
                    ac.inc(A.activation(out=ga[:], in_=gpp, func=AF.Square))
                    V.wait_ge(ac.h, ac.n)
                    V.tensor_scalar(ga[:], ga[:], GELU_C, 1.0, ALU.mult, ALU.add)
                    dv.inc(V.tensor_tensor(gb[:], ga[:], gpp, ALU.mult))
                    A.wait_ge(dv.h, dv.n)
                    ac.inc(A.activation(out=gth[:], in_=gb[:], func=AF.Tanh, scale=GELU_S))
                    V.wait_ge(ac.h, ac.n)
                    V.scalar_tensor_tensor(gb[:], gth[:], 1.0, gpp, ALU.add, ALU.mult)
                    dv.inc(V.tensor_tensor(interT[:, nsub, :], gb[:], upp, ALU.mult))
                    dv_ep[gidx2] = dv.n
                group_pe[g] = pe.n

            for sl in range(8):
                b = sl % 2
                di = seg * 8 + sl
                if di == 0:
                    emit_dload(0)
                emit_dload(di + 1)
                T.wait_ge(sgt[b].h, sgt_cnt[di])
                T.wait_ge(dv.h, dv_ep[seg * 16 + 15])
                for j in range(2):
                    if dv_dacc:
                        T.wait_ge(dv.h, dv_dacc)
                    for nci in range(16):
                        for th in range(2):
                            ins = T.matmul(gp_p[j][:, 512 * th:512 * (th + 1)],
                                           gdt[b][:, nci, 128 * j:128 * (j + 1)],
                                           interT[:, nci, 512 * th:512 * (th + 1)],
                                           start=(nci == 0), stop=(nci == 15))
                    pe.inc(ins)
                    V.wait_ge(pe.h, pe.n)
                    dv.inc(V.tensor_tensor(gacc[:, 2 * sl + j, :], gp_p[j][:],
                                           gacc[:, 2 * sl + j, :], ALU.add))
                    dv_dacc = dv.n
                pe_dw[di] = pe.n


def _combine(nc, SQ, WQ, V, A, T, C, xr, out, sDram, gacc, identt, idenft):
    pe, dv, ac, ou, st, sxr = C["pe"], C["dv"], C["ac"], C["ou"], C["st"], C["xr"]
    with contextlib.ExitStack() as es2:
        E2 = es2.enter_context
        outst = E2(nc.sbuf_tensor("outst", [128, 8, D], F32))
        stile = [E2(nc.sbuf_tensor(f"cst{i}", [128, TOWN], BF16)) for i in range(2)]
        silt = E2(nc.sbuf_tensor("silt", [128, TOWN], F32))
        xrt = [E2(nc.sbuf_tensor(f"xrt{i}", [128, D], F32)) for i in range(2)]
        cpa = E2(nc.sbuf_tensor("cpa", [128, 128], BF16))
        ta_p = E2(nc.psum_tensor("ta_p", [128, 128], BF16))
        tb_p = E2(nc.psum_tensor("tb_p", [128, 128], F32))

        dv_c = 0
        pe_tp2 = 0
        pe_ec = {}
        for ec in range(16):
            b = ec % 2
            if ec >= 2:
                SQ.wait_ge(pe.h, pe_ec[ec - 2])
            st.inc(SQ.dma_start(out=stile[b][:], in_=sDram[128 * ec:128 * (ec + 1), :]), 16)
            st_all = st.n
            if dv_c:
                A.wait_ge(dv.h, dv_c)
            if pe_tp2:
                A.wait_ge(pe.h, pe_tp2)
            ac.inc(A.activation(out=silt[:], in_=gacc[:, ec, :], func=AF.Silu))
            ac_sil = ac.n
            for t in range(8):
                T.wait_ge(st.h, st_all)
                T.wait_ge(ac.h, ac_sil)
                if dv_c:
                    T.wait_ge(dv.h, dv_c)
                T.transpose(ta_p[:], stile[b][:, 128 * t:128 * (t + 1)], identt[:])
                pe.inc(T.transpose(tb_p[:], silt[:, 128 * t:128 * (t + 1)], idenft[:]))
                pe_tp2 = pe.n
                V.wait_ge(pe.h, pe_tp2)
                V.tensor_copy(cpa[:], ta_p[:])
                dv.inc(V.tensor_tensor(outst[:, t, 128 * ec:128 * (ec + 1)], tb_p[:], cpa[:], ALU.mult))
                dv_c = dv.n
            pe_ec[ec] = pe.n

        dv_xr = {}
        for t in range(8):
            b = t % 2
            if t >= 2:
                SQ.wait_ge(dv.h, dv_xr[t - 2])
            sxr[b].inc(SQ.dma_start(out=xrt[b][:], in_=xr[128 * t:128 * (t + 1), :]), 16)
            V.wait_ge(sxr[b].h, sxr[b].n)
            dv.inc(V.tensor_tensor(outst[:, t, :], outst[:, t, :], xrt[b][:], ALU.add))
            dv_xr[t] = dv.n
            WQ.wait_ge(dv.h, dv.n)
            C["ou"].inc(WQ.dma_start(out=out[128 * t:128 * (t + 1), :], in_=outst[:, t, :]), 16)


_NC_CACHE = {}


def _get_nc():
    key = (DEQ_SLICES, STAGE, LNKV_SUB, DBG)
    if key not in _NC_CACHE:
        _NC_CACHE[key] = build()
    return _NC_CACHE[key]


def _rope_tables():
    i = np.arange(1, 65, dtype=np.float32)
    thetas = (np.float32(10000.0) ** (np.float32(-2.0) * (i - np.float32(1.0)) / np.float32(128.0))).astype(np.float32)
    pos = np.arange(1, TKV + 1, dtype=np.float32)
    ang = pos[:, None] * thetas[None, :]
    c = np.cos(ang).astype(np.float32)
    s = np.sin(ang).astype(np.float32)
    cosF = np.ascontiguousarray(np.concatenate([c, c], axis=1).T)
    sinH = np.ascontiguousarray(np.concatenate([s, -s], axis=1).T)
    return cosF, sinH


def prepare_in_maps(L0_attn_x, L0_attn_Wq, L0_attn_bq, L0_attn_Wk, L0_attn_bk, L0_attn_Wv,
                    L0_attn_bv, L0_attn_Wo, L0_attn_bo, L0_ffn_gate_W_packed, L0_ffn_gate_scales,
                    L0_ffn_up_W_packed, L0_ffn_up_scales, L0_ffn_down_W_packed, L0_ffn_down_scales):
    x = np.asarray(L0_attn_x, dtype=np.float32)
    wq = np.ascontiguousarray(np.asarray(L0_attn_Wq, np.float32).T)
    wk = np.ascontiguousarray(np.asarray(L0_attn_Wk, np.float32).T)
    wv = np.ascontiguousarray(np.asarray(L0_attn_Wv, np.float32).T)
    wo = np.ascontiguousarray(np.asarray(L0_attn_Wo, np.float32).T)
    bqv = np.ascontiguousarray(np.asarray(L0_attn_bq, np.float32).reshape(16, 128).T)
    bkv = np.ascontiguousarray(np.asarray(L0_attn_bk, np.float32).reshape(4, 128).T)
    bvv = np.ascontiguousarray(np.asarray(L0_attn_bv, np.float32).reshape(4, 128).T)
    bov = np.ascontiguousarray(np.asarray(L0_attn_bo, np.float32).reshape(16, 128).T)
    gpk = np.ascontiguousarray(np.asarray(L0_ffn_gate_W_packed, np.int32))
    gsc = np.ascontiguousarray(np.asarray(L0_ffn_gate_scales, np.float32))
    upkk = np.ascontiguousarray(np.asarray(L0_ffn_up_W_packed, np.int32))
    upsc = np.ascontiguousarray(np.asarray(L0_ffn_up_scales, np.float32))
    dpk = np.ascontiguousarray(np.asarray(L0_ffn_down_W_packed, np.int32))
    dssc = np.ascontiguousarray(np.asarray(L0_ffn_down_scales, np.float32))

    cosF, sinH = _rope_tables()
    iden = np.eye(128, dtype=np.float32).astype(ml_dtypes.bfloat16)
    idenf32 = np.eye(128, dtype=np.float32)
    onescol = np.ones((128, 1), np.float32).astype(ml_dtypes.bfloat16)
    onesf32 = np.ones((1, 128), np.float32)

    in_maps = []
    for c in range(8):
        bsel = c // 2
        h = c % 2
        own0 = 1024 * h
        oth0 = 1024 * (1 - h)
        idx = np.concatenate([np.arange(own0, own0 + 1024), np.arange(oth0, oth0 + 1024)])
        xb = x[bsel]
        in_maps.append({
            "xT": np.ascontiguousarray(xb.T[:, idx]),
            "xr": np.ascontiguousarray(xb[own0:own0 + 1024]),
            "wqT": wq, "wkT": wk, "wvT": wv, "woT": wo,
            "bq": bqv, "bk": bkv, "bv": bvv, "bo": bov,
            "gp": np.ascontiguousarray(gpk[2048 * c:2048 * (c + 1)]),
            "gs": np.ascontiguousarray(gsc[2048 * c:2048 * (c + 1)]),
            "up": np.ascontiguousarray(upkk[2048 * c:2048 * (c + 1)]),
            "us": np.ascontiguousarray(upsc[2048 * c:2048 * (c + 1)]),
            "dp": np.ascontiguousarray(dpk[256 * c:256 * (c + 1)]),
            "ds": np.ascontiguousarray(dssc[256 * c:256 * (c + 1)]),
            "qcos": np.ascontiguousarray(cosF[:, own0:own0 + 1024]),
            "qsin": np.ascontiguousarray(sinH[:, own0:own0 + 1024]),
            "kcos": np.ascontiguousarray(cosF[:, idx]),
            "ksin": np.ascontiguousarray(sinH[:, idx]),
            "posq": np.ascontiguousarray(np.arange(own0, own0 + 1024, dtype=np.float32).reshape(8, 128).T),
            "poskcol": np.ascontiguousarray(idx.astype(np.float32).reshape(16, 128).T),
            "poskk": idx.astype(np.float32).reshape(1, TKV),
            "iden": iden, "idenf": idenf32, "onescol": onescol, "onesf32": onesf32,
        })
    return in_maps


def kernel(**inputs):
    in_maps = prepare_in_maps(**inputs)
    nc = _get_nc()
    res = run_bass_kernel_spmd(nc, in_maps, core_ids=list(range(8)))
    outp = np.empty((4, 2048, 2048), np.float32)
    for c in range(8):
        bsel = c // 2
        h = c % 2
        outp[bsel, 1024 * h:1024 * (h + 1)] = res.results[c]["out"]
    return outp


# version 10
# speedup vs baseline: 1.3581x; 1.0077x over previous
import contextlib
import os
import tempfile

os.environ["NEURON_COMPILE_CACHE_URL"] = tempfile.mkdtemp(prefix="ncc_")

import numpy as np
import ml_dtypes
import concourse.bass as bass
import concourse.mybir as mybir
from concourse.bass_utils import run_bass_kernel_spmd

F32 = mybir.dt.float32
BF16 = mybir.dt.bfloat16
I32 = mybir.dt.int32
AF = mybir.ActivationFunctionType
ALU = mybir.AluOpType
AX = mybir.AxisListType.X

D = 2048
DK = 512
HD = 128
TOWN = 1024
TKV = 2048
MLP = 16384
NEG = -1.0e30
GELU_C = 0.044715
GELU_S = 0.7978845608028654
ISQ = 0.08838834764831845
EPS = 1e-5

DEQ_SLICES = 1
STAGE = 5
LNKV_SUB = 7
DBG = ""


class Ctr:
    def __init__(self, es, nc, name):
        self.h = es.enter_context(nc.semaphore(name))
        self.n = 0

    def inc(self, instr, k=1):
        instr.then_inc(self.h, k)
        self.n += k
        return self.n


def build():
    nc = bass.Bass("TRN2", target_bir_lowering=False)

    def din(name, shape, dt):
        return nc.dram_tensor(name, shape, dt, kind="ExternalInput").ap()

    xT = din("xT", [D, TKV], F32)
    xr = din("xr", [TOWN, D], F32)
    wqT = din("wqT", [D, D], F32)
    wkT = din("wkT", [D, DK], F32)
    wvT = din("wvT", [D, DK], F32)
    woT = din("woT", [D, D], F32)
    bq = din("bq", [128, 16], F32)
    bk = din("bk", [128, 4], F32)
    bv = din("bv", [128, 4], F32)
    bo = din("bo", [128, 16], F32)
    gp = din("gp", [2048 * DEQ_SLICES, 1024], I32)
    gs = din("gs", [2048 * DEQ_SLICES, 64], F32)
    upk = din("up", [2048 * DEQ_SLICES, 1024], I32)
    ups = din("us", [2048 * DEQ_SLICES, 64], F32)
    dp = din("dp", [256 * DEQ_SLICES, 8192], I32)
    dsc = din("ds", [256 * DEQ_SLICES, 512], F32)
    qcos = din("qcos", [HD, TOWN], F32)
    qsin = din("qsin", [HD, TOWN], F32)
    kcos = din("kcos", [HD, TKV], F32)
    ksin = din("ksin", [HD, TKV], F32)
    posq = din("posq", [128, 8], F32)
    poskcol = din("poskcol", [128, 16], F32)
    poskk = din("poskk", [1, TKV], F32)
    iden = din("iden", [128, 128], BF16)
    idenf = din("idenf", [128, 128], F32)
    onescol = din("onescol", [128, 1], BF16)
    onesf32 = din("onesf32", [1, 128], F32)

    out = nc.dram_tensor("out", [TOWN, D], F32, kind="ExternalOutput").ap()

    wk_ = "ExternalOutput" if "w" in DBG else "Internal"
    if "k" in DBG:
        dbg_xn = nc.dram_tensor("dbg_xn", [128, 16, TOWN], BF16, kind="ExternalOutput").ap()
        dbg_kt = nc.dram_tensor("dbg_kt", [128, 4, TKV], BF16, kind="ExternalOutput").ap()
        dbg_vh = nc.dram_tensor("dbg_vh", [128, 4, 16, 128], BF16, kind="ExternalOutput").ap()
    if "a" in DBG:
        nc._dbg_a = dict(
            qt=nc.dram_tensor("dbg_qt", [128, TOWN], BF16, kind="ExternalOutput").ap(),
            pb=nc.dram_tensor("dbg_pb", [128, TKV], BF16, kind="ExternalOutput").ap(),
            ell=nc.dram_tensor("dbg_ell", [128, 1], F32, kind="ExternalOutput").ap(),
            s=nc.dram_tensor("dbg_s", [128, TKV], F32, kind="ExternalOutput").ap(),
            pb2=nc.dram_tensor("dbg_pb2", [128, TKV], BF16, kind="ExternalOutput").ap(),
            msk=nc.dram_tensor("dbg_msk", [128, 8, TKV], BF16, kind="ExternalOutput").ap(),
            yt=nc.dram_tensor("dbg_yt", [128, 16, TOWN], BF16, kind="ExternalOutput").ap(),
        )
    sk_ = "ExternalOutput" if "s" in DBG else "Internal"
    ggcF = nc.dram_tensor("ggc", [D, 2048], BF16, kind="Internal").ap()
    gucF = nc.dram_tensor("guc", [D, 2048], BF16, kind="Internal").ap()
    gdcF = nc.dram_tensor("gdc", [MLP, 256], BF16, kind="Internal").ap()
    ggc = ggcF.rearrange("(a k) n -> a k n", a=1)
    guc = gucF.rearrange("(a k) n -> a k n", a=1)
    gdc = gdcF.rearrange("(a k) n -> a k n", a=1)
    GgF = nc.dram_tensor("Gg", [8 * D, 2048], BF16, kind=wk_, addr_space="Shared").ap()
    GuF = nc.dram_tensor("Gu", [8 * D, 2048], BF16, kind=wk_, addr_space="Shared").ap()
    GdF = nc.dram_tensor("Gd", [8 * MLP, 256], BF16, kind=wk_, addr_space="Shared").ap()
    Gg = GgF.rearrange("(s k) n -> s k n", s=8)
    Gu = GuF.rearrange("(s k) n -> s k n", s=8)
    Gd = GdF.rearrange("(s k) n -> s k n", s=8)
    sDram = nc.dram_tensor("sDram", [D, TOWN], BF16, kind=sk_).ap()

    es = contextlib.ExitStack()
    with es:
        E = es.enter_context
        xnT = E(nc.sbuf_tensor("xnT", [128, 16, TOWN], BF16))
        identt = E(nc.sbuf_tensor("identt", [128, 128], BF16))
        idenft = E(nc.sbuf_tensor("idenft", [128, 128], F32))
        onescolt = E(nc.sbuf_tensor("onescolt", [128, 1], BF16))
        onesf32t = E(nc.sbuf_tensor("onesf32t", [1, 128], F32))
        posqt = E(nc.sbuf_tensor("posqt", [128, 8], F32))
        poskcolt = E(nc.sbuf_tensor("poskcolt", [128, 16], F32))
        bqt = E(nc.sbuf_tensor("bqt", [128, 16], F32))
        bot = E(nc.sbuf_tensor("bot", [128, 16], F32))
        bkt = E(nc.sbuf_tensor("bkt", [128, 4], F32))
        bvt = E(nc.sbuf_tensor("bvt", [128, 4], F32))
        cneg4 = E(nc.sbuf_tensor("cneg4", [128, 1], F32))

        C = {}
        for name in ["gd", "pe", "dv", "ac", "ou", "cn", "x", "ms", "st"]:
            C[name] = Ctr(es, nc, name)
        for name in ["ws", "gg", "gt", "xr"]:
            C[name] = [Ctr(es, nc, name + "0"), Ctr(es, nc, name + "1")]
        C["pk"] = [Ctr(es, nc, f"pk{i}") for i in range(4)]

        gd, pe, dv, ac, ou = C["gd"], C["pe"], C["dv"], C["ac"], C["ou"]

        blk = es.enter_context(nc.Block())
        SQ = nc.sync
        WQ = nc.scalar
        V = nc.vector
        A = nc.scalar
        T = nc.tensor
        ALL_E = (SQ, V, T, WQ, nc.gpsimd)

        def flat_ctrs():
            r = []
            for v in C.values():
                r.extend(v if isinstance(v, list) else [v])
            return r

        def barrier():
            vals = [(c, c.n) for c in flat_ctrs()]
            for e in ALL_E:
                for c, v in vals:
                    if v > 0:
                        e.wait_ge(c.h, v)

        cn = C["cn"]
        cn.inc(SQ.dma_start(out=identt[:], in_=iden), 16)
        cn.inc(SQ.dma_start(out=idenft[:], in_=idenf), 16)
        cn.inc(SQ.dma_start(out=onescolt[:], in_=onescol), 16)
        cn.inc(SQ.dma_start(out=onesf32t[:], in_=onesf32), 16)
        cn.inc(SQ.dma_start(out=posqt[:], in_=posq), 16)
        cn.inc(SQ.dma_start(out=poskcolt[:], in_=poskcol), 16)
        cn.inc(SQ.dma_start(out=bqt[:], in_=bq), 16)
        cn.inc(SQ.dma_start(out=bot[:], in_=bo), 16)
        cn.inc(SQ.dma_start(out=bkt[:], in_=bk), 16)
        cn.inc(SQ.dma_start(out=bvt[:], in_=bv), 16)
        dv_ = C["dv"]
        dv_.inc(V.memset(cneg4[:], -4.0))
        for e in (V, T, WQ, nc.gpsimd):
            e.wait_ge(cn.h, cn.n)
            e.wait_ge(dv_.h, 1)

        cc = Ctr(es, nc, "ccs")
        nc._cc = cc
        if STAGE >= 1:
            G_ = nc.gpsimd
            rg = [[0, 1, 2, 3, 4, 5, 6, 7]]
            _dequant(nc, SQ, WQ, V, T, C, identt, gp, gs, ggc, 1.0, "g")
            _dequant(nc, SQ, WQ, V, T, C, identt, upk, ups, guc, 1.0, "u")
            _dequant(nc, SQ, WQ, V, T, C, identt, dp, dsc, gdc, 0.5, "d")

            def issue_gathers():
                G_.wait_ge(gd.h, gd.n)
                G_.wait_ge(C["x"].h, C["x"].n)
                cc.inc(G_.collective_compute("AllGather", ALU.bypass,
                                             replica_groups=rg,
                                             ins=[ggcF.opt()], outs=[GgF.opt()]), 1)
                cc.inc(G_.collective_compute("AllGather", ALU.bypass,
                                             replica_groups=rg,
                                             ins=[gucF.opt()], outs=[GuF.opt()]), 1)
                cc.inc(G_.collective_compute("AllGather", ALU.bypass,
                                             replica_groups=rg,
                                             ins=[gdcF.opt()], outs=[GdF.opt()]), 1)
        else:
            def issue_gathers():
                pass
        SQ.wait_ge(gd.h, gd.n)

        if STAGE >= 2:
            with contextlib.ExitStack() as es23:
                E23 = es23.enter_context
                poskkb = E23(nc.sbuf_tensor("poskkb", [128, TKV], F32))
                qcost = E23(nc.sbuf_tensor("qcost", [128, TOWN], F32))
                qsint = E23(nc.sbuf_tensor("qsint", [128, TOWN], F32))
                ms = C["ms"]
                ms.inc(SQ.dma_start(out=poskkb[:], in_=poskk.partition_broadcast(128)), 16)
                ms.inc(SQ.dma_start(out=qcost[:], in_=qcos), 16)
                ms.inc(SQ.dma_start(out=qsint[:], in_=qsin), 16)
                kvt = _ln_kv(nc, es23, SQ, V, A, T, C,
                             xT, wkT, wvT, bkt, bvt, kcos, ksin,
                             xnT, onescolt, onesf32t, identt)
                issue_gathers()
                barrier()
                if "k" in DBG:
                    ou.inc(WQ.dma_start(out=dbg_xn, in_=xnT[:]), 16)
                    ou.inc(WQ.dma_start(out=dbg_kt, in_=kvt["KT"][:]), 16)
                    ou.inc(WQ.dma_start(out=dbg_vh, in_=kvt["Vh"][:]), 16)
                    for e in ALL_E:
                        e.wait_ge(ou.h, ou.n)
                if STAGE >= 3:
                    _attention(nc, es23, SQ, WQ, V, A, T, C,
                               wqT, woT, bqt, bot, qcost, qsint, poskcolt, poskkb,
                               xnT, sDram, kvt, identt, cneg4, onescolt, onesf32t)
                barrier()

        if STAGE >= 4:
            with contextlib.ExitStack() as es45:
                gacc = es45.enter_context(nc.sbuf_tensor("gacc", [128, 16, TOWN], F32))
                _ffn(nc, SQ, V, A, T, C, Gg, Gu, Gd, xnT, gacc, nc._cc)
                barrier()
                if STAGE >= 5:
                    _combine(nc, SQ, WQ, V, A, T, C, xr, out, sDram, gacc, identt, idenft)

        SQ.wait_ge(ou.h, ou.n)

    return nc


def _dequant(nc, SQ, WQ, V, T, C, identt, packed, scales, Gout, smul, tag):
    gd, pe, dv = C["gd"], C["pe"], C["dv"]
    pk = C["pk"]
    KK = packed.shape[1]
    NB = scales.shape[1]
    n_loc = Gout.shape[2]
    n_rtiles = n_loc // 128
    n_cchunk = KK // 512
    nslices = packed.shape[0] // n_loc
    K_win = Gout.shape[1] // n_cchunk

    with contextlib.ExitStack() as es:
        SQ.wait_ge(gd.h, gd.n)
        E = es.enter_context
        pt = [E(nc.sbuf_tensor(f"dqp{tag}{i}", [128, 512], I32)) for i in range(4)]
        sct = [E(nc.sbuf_tensor(f"dqs{tag}{i}", [128, NB], F32)) for i in range(4)]
        sce = [E(nc.sbuf_tensor(f"dqe{tag}{i}", [128, NB], F32)) for i in range(4)]
        mtile = E(nc.sbuf_tensor(f"dqm{tag}", [128, 1024], I32))
        stile = E(nc.sbuf_tensor(f"dqt{tag}", [128, 1024], I32))
        btile = E(nc.sbuf_tensor(f"dqb{tag}", [128, 1024], I32))
        wtile = E(nc.sbuf_tensor(f"dqw{tag}", [128, 1024], BF16))
        rb = E(nc.sbuf_tensor(f"dqr{tag}", [128, 8, n_loc], BF16))
        tp = [E(nc.psum_tensor(f"dqx{tag}{i}", [128, 512], BF16)) for i in range(2)]

        dv_decode = {}
        dv_evac = {}
        pe_tp = {}
        gd_grp = {}

        u = 0
        for sl in range(nslices):
            for c in range(n_cchunk):
                for r in range(n_rtiles):
                    b = u % 4
                    bc = u % 4
                    if u >= 4:
                        SQ.wait_ge(dv.h, dv_decode[u - 4])
                    row0 = sl * n_loc + r * 128
                    pk[bc].inc(SQ.dma_start(out=pt[b][:], in_=packed[row0:row0 + 128, c * 512:(c + 1) * 512]), 16)
                    pk[bc].inc(SQ.dma_start(out=sct[b][:], in_=scales[row0:row0 + 128, :]), 16)

                    V.wait_ge(pk[bc].h, pk[bc].n)
                    if u >= 1:
                        V.wait_ge(pe.h, pe_tp[u - 1])
                    if r == 0:
                        g_prev = (sl * n_cchunk + c) - 1
                        if g_prev >= 0:
                            V.wait_ge(gd.h, gd_grp[g_prev])
                    V.tensor_scalar(sce[b][:], sct[b][:], float(smul * (2.0 ** 126)), None, ALU.mult)
                    p = pt[b][:]
                    V.tensor_scalar(mtile[:, 0:512], p, 7, 22, ALU.bitwise_and, ALU.logical_shift_left)
                    V.tensor_scalar(stile[:, 0:512], p, 8, 28, ALU.bitwise_and, ALU.logical_shift_left)
                    V.tensor_scalar(mtile[:, 512:1024], p, 0x70, 18, ALU.bitwise_and, ALU.logical_shift_left)
                    V.tensor_scalar(stile[:, 512:1024], p, 0x80, 24, ALU.bitwise_and, ALU.logical_shift_left)
                    V.tensor_tensor(btile[:], mtile[:], stile[:], ALU.bitwise_or)
                    scb = sce[b][:, 32 * c:32 * c + 32].unsqueeze(2).broadcast_to([128, 32, 16])
                    bl = btile[:, 0:512].bitcast(F32).rearrange("p (b j) -> p b j", b=32)
                    bh = btile[:, 512:1024].bitcast(F32).rearrange("p (b j) -> p b j", b=32)
                    V.tensor_tensor(wtile[:, 0:512].rearrange("p (b j) -> p b j", b=32), bl, scb, ALU.mult)
                    dv.inc(V.tensor_tensor(wtile[:, 512:1024].rearrange("p (b j) -> p b j", b=32), bh, scb, ALU.mult))
                    dv_decode[u] = dv.n

                    T.wait_ge(dv.h, dv_decode[u])
                    if u >= 1:
                        T.wait_ge(dv.h, dv_evac[u - 1])
                    for plane in range(2):
                        for s4 in range(4):
                            ins = T.transpose(tp[plane][:, 128 * s4:128 * (s4 + 1)],
                                              wtile[:, 512 * plane + 128 * s4:512 * plane + 128 * (s4 + 1)],
                                              identt[:])
                        pe.inc(ins)
                    pe_tp[u] = pe.n

                    V.wait_ge(pe.h, pe_tp[u])
                    V.tensor_copy(rb[:, 0:4, r * 128:(r + 1) * 128],
                                  tp[0][:].rearrange("p (s j) -> p s j", s=4))
                    dv.inc(V.tensor_copy(rb[:, 4:8, r * 128:(r + 1) * 128],
                                         tp[1][:].rearrange("p (s j) -> p s j", s=4)))
                    dv_evac[u] = dv.n
                    u += 1

                WQ.wait_ge(dv.h, dv_evac[u - 1])
                for plane in range(2):
                    for s4 in range(4):
                        base = K_win * c + 256 * s4 + plane
                        gd.inc(WQ.dma_start(out=Gout[sl, base:base + 255:2, :],
                                            in_=rb[:, plane * 4 + s4, :]), 16)
                gd_grp[sl * n_cchunk + c] = gd.n


def _ln_kv(nc, es, SQ, V, A, T, C, xT, wkT, wvT, bkt, bvt, kcos, ksin,
           xnT, onescolt, onesf32t, identt):
    pe, dv, ac, ms, sx, ws = C["pe"], C["dv"], C["ac"], C["ms"], C["x"], C["ws"]
    E = es.enter_context
    KT = E(nc.sbuf_tensor("KT", [128, 4, TKV], BF16))
    Vh = E(nc.sbuf_tensor("Vh", [128, 4, 16, 128], BF16))

    with contextlib.ExitStack() as es2:
        E2 = es2.enter_context
        kcost = E2(nc.sbuf_tensor("kcost", [128, TKV], F32))
        ksint = E2(nc.sbuf_tensor("ksint", [128, TKV], F32))
        wk16 = E2(nc.sbuf_tensor("wk16", [128, 16, DK], BF16))
        wv16 = E2(nc.sbuf_tensor("wv16", [128, 16, DK], BF16))
        xtc = E2(nc.sbuf_tensor("xtc", [128, 16, 512], F32))
        xn_t = E2(nc.sbuf_tensor("xn_t", [128, 16, 512], BF16))
        x16 = [E2(nc.sbuf_tensor(f"x16{i}", [128, 512], BF16)) for i in range(2)]
        x2 = [E2(nc.sbuf_tensor(f"x2{i}", [128, 512], BF16)) for i in range(2)]
        wstage = [E2(nc.sbuf_tensor(f"wst{i}", [128, 512], F32)) for i in range(2)]
        srow = E2(nc.sbuf_tensor("srow", [1, 512], F32))
        qrow = E2(nc.sbuf_tensor("qrow", [1, 512], F32))
        trow = E2(nc.sbuf_tensor("trow", [1, 512], F32))
        vrow = E2(nc.sbuf_tensor("vrow", [1, 512], F32))
        t1 = E2(nc.sbuf_tensor("lnt1", [128, 512], F32))
        kraw = E2(nc.sbuf_tensor("kraw", [128, 512], F32))
        krot = E2(nc.sbuf_tensor("krot", [128, 512], F32))
        kcs = E2(nc.sbuf_tensor("kcs", [128, 512], F32))
        vtmp = E2(nc.sbuf_tensor("vtmp", [128, 512], BF16))
        sum_p = E2(nc.psum_tensor("sum_p", [1, 512], F32))
        sq_p = E2(nc.psum_tensor("sq_p", [1, 512], F32))
        mb_p = E2(nc.psum_tensor("mb_p", [128, 512], F32))
        rb_p = E2(nc.psum_tensor("rb_p", [128, 512], F32))
        kv_p = E2(nc.psum_tensor("kv_p", [128, 512], F32))
        vv_p = E2(nc.psum_tensor("vv_p", [128, 512], F32))
        vt_p = E2(nc.psum_tensor("vt_p", [128, 512], BF16))

        ms.inc(SQ.dma_start(out=kcost[:], in_=kcos), 16)
        ms.inc(SQ.dma_start(out=ksint[:], in_=ksin), 16)
        MS_ALL = ms.n

        casts = {}
        for w in range(2):
            src = wkT if w == 0 else wvT
            dst = wk16 if w == 0 else wv16
            for kc in range(16):
                i = w * 16 + kc
                b = i % 2
                if i >= 2:
                    SQ.wait_ge(dv.h, casts[i - 2])
                ws[b].inc(SQ.dma_start(out=wstage[b][:], in_=src[128 * kc:128 * (kc + 1), :]), 16)
                V.wait_ge(ws[b].h, ws[b].n)
                dv.inc(V.tensor_copy(dst[:, kc, :], wstage[b][:]))
                casts[i] = dv.n
        V.wait_ge(ms.h, MS_ALL)

        dv_norm = {}
        pe_kv = {}
        dv_kv = dv.n
        for tc in range(4 if LNKV_SUB >= 2 else 0):
            own = tc < 2
            if tc >= 1:
                SQ.wait_ge(dv.h, dv_norm[tc - 1])
            for kc in range(16):
                sx.inc(SQ.dma_start(out=xtc[:, kc, :],
                                    in_=xT[128 * kc:128 * (kc + 1), 512 * tc:512 * (tc + 1)]), 16)
            sx_all = sx.n

            pe_stat = {}
            for kc in range(16):
                bb = kc % 2
                V.wait_ge(sx.h, sx_all)
                A.wait_ge(sx.h, sx_all)
                if kc >= 2:
                    V.wait_ge(pe.h, pe_stat[kc - 2])
                    A.wait_ge(pe.h, pe_stat[kc - 2])
                dv.inc(V.tensor_copy(x16[bb][:], xtc[:, kc, :]))
                ac.inc(A.activation(out=x2[bb][:], in_=xtc[:, kc, :], func=AF.Square))
                T.wait_ge(dv.h, dv.n)
                T.wait_ge(ac.h, ac.n)
                T.matmul(sum_p[:], onescolt[:], x16[bb][:], start=(kc == 0), stop=(kc == 15))
                pe.inc(T.matmul(sq_p[:], onescolt[:], x2[bb][:], start=(kc == 0), stop=(kc == 15)))
                pe_stat[kc] = pe.n
            pe_stats = pe.n

            if LNKV_SUB < 3:
                dv_norm[tc] = dv.n
                pe_kv[tc] = pe.n
                continue
            V.wait_ge(pe.h, pe_stats)
            V.tensor_scalar(srow[:], sum_p[:], 1.0 / D, None, ALU.mult)
            V.tensor_scalar(qrow[:], sq_p[:], 1.0 / D, None, ALU.mult)
            V.tensor_copy(t1[0:1, :], qrow[:])
            V.tensor_tensor(trow[:], srow[:], srow[:], ALU.mult)
            V.tensor_copy(t1[0:1, :], qrow[:])
            V.scalar_tensor_tensor(vrow[:], qrow[:], EPS, trow[:], ALU.add, ALU.subtract)
            dv.inc(V.tensor_copy(t1[0:1, :], qrow[:]))
            A.wait_ge(dv.h, dv.n)
            ac.inc(A.activation(out=trow[:], in_=vrow[:], func=AF.Sqrt))
            V.wait_ge(ac.h, ac.n)
            V.reciprocal(vrow[:], trow[:])
            dv.inc(V.tensor_copy(t1[0:1, :], qrow[:]))
            T.wait_ge(dv.h, dv.n)
            T.matmul(mb_p[:], onesf32t[:], srow[:], start=True, stop=True)
            pe.inc(T.matmul(rb_p[:], onesf32t[:], vrow[:], start=True, stop=True))
            pe_bc = pe.n

            if LNKV_SUB < 4:
                dv_norm[tc] = dv.n
                pe_kv[tc] = pe.n
                continue
            V.wait_ge(pe.h, pe_bc)
            if tc >= 3:
                V.wait_ge(pe.h, pe_kv[tc - 1])
            for kc in range(16):
                V.tensor_tensor(t1[:], xtc[:, kc, :], mb_p[:], ALU.subtract)
                dst = xnT[:, kc, 512 * tc:512 * (tc + 1)] if own else xn_t[:, kc, :]
                dv.inc(V.tensor_tensor(dst, t1[:], rb_p[:], ALU.mult))
            dv_norm[tc] = dv.n

            def xsrc(kc, tc=tc, own=own):
                return xnT[:, kc, 512 * tc:512 * (tc + 1)] if own else xn_t[:, kc, :]

            for kvh in range(4 if LNKV_SUB >= 5 else 0):
                T.wait_ge(dv.h, dv_norm[tc])
                T.wait_ge(dv.h, dv_kv)
                for kc in range(16):
                    T.matmul(kv_p[:], wk16[:, kc, 128 * kvh:128 * (kvh + 1)], xsrc(kc),
                             start=(kc == 0), stop=(kc == 15))
                for kc in range(16):
                    ins = T.matmul(vv_p[:], wv16[:, kc, 128 * kvh:128 * (kvh + 1)], xsrc(kc),
                                   start=(kc == 0), stop=(kc == 15))
                pe.inc(ins)
                pe_proj = pe.n
                if LNKV_SUB < 6:
                    dv_kv = dv.n
                    continue

                V.wait_ge(pe.h, pe_proj)
                cs = slice(512 * tc, 512 * (tc + 1))
                nops = (LNKV_SUB - 60) if 60 <= LNKV_SUB < 70 else 6
                ops = [
                    lambda: V.tensor_scalar(kraw[:], kv_p[:], bkt[:, kvh:kvh + 1], None, ALU.add),
                    lambda: V.tensor_tensor(krot[0:64, :], kraw[64:128, :], ksint[64:128, cs], ALU.mult),
                    lambda: V.tensor_tensor(krot[64:128, :], kraw[0:64, :], ksint[0:64, cs], ALU.mult),
                    lambda: V.tensor_tensor(kcs[:], kraw[:], kcost[:, cs], ALU.mult),
                    lambda: V.tensor_tensor(KT[:, kvh, cs], kcs[:], krot[:], ALU.add),
                    lambda: V.tensor_scalar(vtmp[:], vv_p[:], bvt[:, kvh:kvh + 1], None, ALU.add),
                ]
                last = None
                for _i in range(nops):
                    last = ops[_i]()
                if last is None:
                    last = V.engine_nop() if hasattr(V, "engine_nop") else V.nop()
                dv.inc(last)
                dv_kv = dv.n
                if LNKV_SUB < 7:
                    continue

                T.wait_ge(dv.h, dv_kv)
                for j in range(4):
                    ins = T.transpose(vt_p[:, 128 * j:128 * (j + 1)],
                                      vtmp[:, 128 * j:128 * (j + 1)], identt[:])
                pe.inc(ins)
                V.wait_ge(pe.h, pe.n)
                dv.inc(V.tensor_copy(Vh[:, kvh, 4 * tc:4 * tc + 4, :],
                                     vt_p[:].rearrange("p (s j) -> p s j", s=4)))
                dv_kv = dv.n
            pe_kv[tc] = pe.n
    return dict(KT=KT, Vh=Vh)


def _attention(nc, es, SQ, WQ, V, A, T, C, wqT, woT, bqt, bot,
               qcost, qsint, poskcolt, poskkb, xnT, sDram, kvt, identt, cneg4,
               onescolt, onesf32t):
    gd, pe, dv, ac, ws = C["gd"], C["pe"], C["dv"], C["ac"], C["ws"]
    KT, Vh = kvt["KT"], kvt["Vh"]
    E = es.enter_context
    YT = E(nc.sbuf_tensor("YT", [128, 16, TOWN], BF16))
    mask01 = E(nc.sbuf_tensor("mask01", [128, 16, TOWN], BF16))

    with contextlib.ExitStack() as es2:
        E2 = es2.enter_context
        wstage = [E2(nc.sbuf_tensor(f"aws{i}", [128, 16, 128], F32)) for i in range(2)]
        w16 = [E2(nc.sbuf_tensor(f"aw16{i}", [128, 16, 128], BF16)) for i in range(2)]
        qraw = E2(nc.sbuf_tensor("qraw", [128, TOWN], F32))
        qrot = E2(nc.sbuf_tensor("qrot", [128, TOWN], F32))
        qcs = E2(nc.sbuf_tensor("qcs", [128, TOWN], F32))
        QT = E2(nc.sbuf_tensor("QT", [128, 2, TOWN], BF16))
        NSL = 6
        Pe = E2(nc.sbuf_tensor("Pe", [128, NSL, 512], BF16))
        Pm = E2(nc.sbuf_tensor("Pm", [128, NSL, 512], BF16))
        elrec = E2(nc.sbuf_tensor("elrec", [1, TOWN], F32))
        ebs = E2(nc.sbuf_tensor("ebs", [128, 2, 512], F32))
        stmp = E2(nc.sbuf_tensor("stmp", [128, TOWN], BF16))
        st_p = [E2(nc.psum_tensor(f"st_p{i}", [128, 512], F32)) for i in range(3)]
        y_p = [E2(nc.psum_tensor(f"y_p{i}", [128, 512], F32)) for i in range(2)]
        eb_p = E2(nc.psum_tensor("eb_p", [128, 512], F32))
        qp_p = E2(nc.psum_tensor("qp_p", [128, 1024], F32))

        for s in range(16):
            dv.inc(V.tensor_scalar(mask01[:, s, :], poskkb[:, 0:TOWN],
                                   poskcolt[:, s:s + 1], None, ALU.is_ge))
        dv_mask_built = dv.n

        ac_exp = {}
        dv_mask = {}
        pe_st = {}
        pe_yell = {}
        dv_qe = {}
        st_last_of_qh = {}
        dv_ynorm = {0: 0, 1: 0}
        dv_tail = [0]
        pe_rank1 = {0: 0, 1: 0}
        pe_w = {}
        from collections import deque
        pending = deque()

        def flush_pending():
            pg, pqh, phalf, ps, pslot = pending.popleft()
            pqs = slice(512 * phalf, 512 * (phalf + 1))
            pkvh = pqh // 4
            T.wait_ge(dv.h, dv_mask[pg])
            if ps == 0:
                T.wait_ge(dv.h, dv_tail[0])
            T.matmul(y_p[phalf][:], Vh[:, pkvh, ps, :], Pm[:, pslot, :],
                     start=(ps == 0), stop=(ps == 15))
            pe.inc(T.matmul(eb_p[0:1, :], onescolt[:], Pm[:, pslot, :],
                            start=(ps == 0), stop=(ps == 15)))
            pe_yell[pg] = pe.n
            if ps == 15:
                V.wait_ge(pe.h, pe_yell[pg])
                V.wait_ge(pe.h, pe_rank1[phalf])
                dv.inc(V.reciprocal(elrec[0:1, pqs], eb_p[0:1, :]))
                dv_rec = dv.n
                T.wait_ge(dv.h, dv_rec)
                pe.inc(T.matmul(eb_p[:], onesf32t[:], elrec[0:1, pqs],
                                start=True, stop=True))
                pe_rank1[phalf] = pe.n
                V.wait_ge(pe.h, pe_rank1[phalf])
                V.tensor_copy(ebs[:, phalf, :], eb_p[:])
                dv.inc(V.tensor_tensor(YT[:, pqh, pqs], y_p[phalf][:],
                                       ebs[:, phalf, :], ALU.mult))
                dv_ynorm[phalf] = dv.n
                dv_tail[0] = dv.n

        g = 0
        for qh in range(16):
            kvh = qh // 4
            b = qh % 2
            if qh >= 2:
                SQ.wait_ge(pe.h, pe_w[qh - 2])
            ws[b].inc(SQ.dma_start(out=wstage[b][:],
                                   in_=wqT[:, 128 * qh:128 * (qh + 1)].rearrange("(c p) e -> p c e", p=128)), 16)
            V.wait_ge(ws[b].h, ws[b].n)
            if qh >= 2:
                V.wait_ge(pe.h, pe_w[qh - 2])
            dv.inc(V.tensor_copy(w16[b][:], wstage[b][:]))
            dv_w = dv.n

            T.wait_ge(dv.h, dv_w)
            T.wait_ge(dv.h, dv_qe.get(qh - 1, 0))
            for th in range(2):
                for kc in range(16):
                    ins = T.matmul(qp_p[:, 512 * th:512 * (th + 1)], w16[b][:, kc, :],
                                   xnT[:, kc, 512 * th:512 * (th + 1)], start=(kc == 0), stop=(kc == 15))
            pe.inc(ins)
            pe_w[qh] = pe.n

            qb = qh % 2
            V.wait_ge(pe.h, pe_w[qh])
            V.wait_ge(pe.h, st_last_of_qh.get(qh - 2, 0))
            V.tensor_scalar(qraw[:], qp_p[:], bqt[:, qh:qh + 1], None, ALU.add)
            V.tensor_tensor(qrot[0:64, :], qraw[64:128, :], qsint[64:128, :], ALU.mult)
            V.tensor_tensor(qrot[64:128, :], qraw[0:64, :], qsint[0:64, :], ALU.mult)
            V.tensor_tensor(qcs[:], qraw[:], qcost[:], ALU.mult)
            dv.inc(V.tensor_tensor(QT[:, qb, :], qcs[:], qrot[:], ALU.add))
            dv_qe[qh] = dv.n

            for half in range(2):
                qs = slice(512 * half, 512 * (half + 1))
                for s in range(16):
                    par = g % 3
                    slot = g % NSL
                    T.wait_ge(dv.h, dv_qe[qh])
                    if s == 0:
                        T.wait_ge(dv.h, max(dv_mask_built, dv_ynorm[half]))
                    if g >= 3:
                        T.wait_ge(ac.h, ac_exp[g - 3])
                    pe.inc(T.matmul(st_p[par][:], KT[:, kvh, 128 * s:128 * (s + 1)],
                                    QT[:, qb, qs], start=True, stop=True))
                    pe_st[g] = pe.n
                    if half == 1 and s == 15:
                        st_last_of_qh[qh] = pe.n

                    A.wait_ge(pe.h, pe_st[g])
                    if g >= NSL:
                        A.wait_ge(dv.h, dv_mask[g - NSL])
                    ac.inc(A.activation(out=Pe[:, slot, :], in_=st_p[par][:],
                                        func=AF.Exp, scale=ISQ, bias=cneg4[:]))
                    ac_exp[g] = ac.n

                    V.wait_ge(ac.h, ac_exp[g])
                    if g >= NSL:
                        V.wait_ge(pe.h, pe_yell.get(g - NSL, 0))
                    dv.inc(V.tensor_tensor(Pm[:, slot, :], Pe[:, slot, :],
                                           mask01[:, s, qs], ALU.mult))
                    dv_mask[g] = dv.n

                    if len(pending) >= 3:
                        flush_pending()
                    pending.append((g, qh, half, s, slot))
                    g += 1
        while pending:
            flush_pending()

        dv_yt = dv.n
        gd_s = 0
        dv_se = 0
        for ec in range(16):
            b = ec % 2
            last_use = pe_w[16 + ec - 2] if ec >= 2 else pe_w[14 + ec]
            SQ.wait_ge(pe.h, last_use)
            ws[b].inc(SQ.dma_start(out=wstage[b][:],
                                   in_=woT[:, 128 * ec:128 * (ec + 1)].rearrange("(c p) e -> p c e", p=128)), 16)
            V.wait_ge(ws[b].h, ws[b].n)
            V.wait_ge(pe.h, last_use)
            dv.inc(V.tensor_copy(w16[b][:], wstage[b][:]))
            dv_w2 = dv.n

            T.wait_ge(dv.h, dv_w2)
            T.wait_ge(dv.h, dv_yt)
            if dv_se:
                T.wait_ge(dv.h, dv_se)
            for th in range(2):
                for kc in range(16):
                    ins = T.matmul(qp_p[:, 512 * th:512 * (th + 1)], w16[b][:, kc, :],
                                   YT[:, kc, 512 * th:512 * (th + 1)], start=(kc == 0), stop=(kc == 15))
            pe.inc(ins)
            pe_w[16 + ec] = pe.n

            V.wait_ge(pe.h, pe_w[16 + ec])
            if gd_s:
                V.wait_ge(gd.h, gd_s)
            dv.inc(V.tensor_scalar(stmp[:], qp_p[:], bot[:, ec:ec + 1], None, ALU.add))
            dv_se = dv.n
            WQ.wait_ge(dv.h, dv_se)
            gd_s = gd.inc(WQ.dma_start(out=sDram[128 * ec:128 * (ec + 1), :], in_=stmp[:]), 16)


def _ffn(nc, SQ, V, A, T, C, Gg, Gu, Gd, xnT, gacc, cc):
    pe, dv, ac = C["pe"], C["dv"], C["ac"]
    sgg, sgt = C["gg"], C["gt"]
    NRING = 18
    with contextlib.ExitStack() as es2:
        E2 = es2.enter_context
        grng = E2(nc.sbuf_tensor("grng", [128, NRING, 512], BF16))
        urng = E2(nc.sbuf_tensor("urng", [128, NRING, 512], BF16))
        gdt = [E2(nc.sbuf_tensor(f"gdt{i}", [128, 16, 256], BF16)) for i in range(2)]
        interT = E2(nc.sbuf_tensor("interT", [128, 16, TOWN], BF16))
        ga = E2(nc.sbuf_tensor("ffa", [128, 1024], F32))
        gb = E2(nc.sbuf_tensor("ffb", [128, 1024], F32))
        gth = E2(nc.sbuf_tensor("ffth", [128, 1024], F32))
        gp_p = [E2(nc.psum_tensor(f"gp_p{i}", [128, 1024], F32)) for i in range(2)]
        up_p = [E2(nc.psum_tensor(f"up_p{i}", [128, 1024], F32)) for i in range(2)]

        dv.inc(V.memset(gacc[:], 0.0))

        group_pe = {}
        emitted = [0]
        n_strips = 8 * 4 * 16

        def strip_idx(seg, ng, kc):
            return (seg * 4 + ng) * 16 + kc

        SQ.wait_ge(cc.h, 2)
        cc_d = [False]

        def emit_loads(upto):
            while emitted[0] < min(upto, n_strips):
                s = emitted[0]
                seg, rem = divmod(s, 64)
                ng, kc = divmod(rem, 16)
                slot = s % NRING
                gidx = s // 16
                if s >= NRING:
                    SQ.wait_ge(pe.h, group_pe[(s - NRING) // 16])
                sgg[gidx % 2].inc(SQ.dma_start(out=grng[:, slot, :],
                                               in_=Gg[seg, 128 * kc:128 * (kc + 1), 512 * ng:512 * (ng + 1)]), 16)
                sgg[gidx % 2].inc(SQ.dma_start(out=urng[:, slot, :],
                                               in_=Gu[seg, 128 * kc:128 * (kc + 1), 512 * ng:512 * (ng + 1)]), 16)
                emitted[0] += 1

        dv_ep = {}
        dv_dacc = 0
        pe_dw = {}
        for seg in range(8):
            for ng in range(4):
                g = seg * 4 + ng
                emit_loads(strip_idx(seg, ng, 0) + NRING)
                sg_all = 32 * 16 * (g // 2 + 1)
                for ns in range(4):
                    nsub = 4 * ng + ns
                    gidx2 = seg * 16 + nsub
                    par = nsub % 2
                    T.wait_ge(sgg[g % 2].h, sg_all)
                    if gidx2 >= 2 and (gidx2 - 2) in dv_ep:
                        T.wait_ge(dv.h, dv_ep[gidx2 - 2])
                    if ng == 0 and ns < 2 and dv_dacc:
                        T.wait_ge(dv.h, dv_dacc)
                    for kc in range(16):
                        s = strip_idx(seg, ng, kc)
                        slot = s % NRING
                        for th in range(2):
                            T.matmul(gp_p[par][:, 512 * th:512 * (th + 1)],
                                     grng[:, slot, 128 * ns:128 * (ns + 1)],
                                     xnT[:, kc, 512 * th:512 * (th + 1)],
                                     start=(kc == 0), stop=(kc == 15))
                        for th in range(2):
                            ins = T.matmul(up_p[par][:, 512 * th:512 * (th + 1)],
                                           urng[:, slot, 128 * ns:128 * (ns + 1)],
                                           xnT[:, kc, 512 * th:512 * (th + 1)],
                                           start=(kc == 0), stop=(kc == 15))
                    pe.inc(ins)
                    pe_gu = pe.n

                    gpp = gp_p[par][:]
                    upp = up_p[par][:]
                    A.wait_ge(pe.h, pe_gu)
                    ac.inc(A.activation(out=ga[:], in_=gpp, func=AF.Square))
                    V.wait_ge(ac.h, ac.n)
                    V.tensor_scalar(ga[:], ga[:], GELU_C, 1.0, ALU.mult, ALU.add)
                    dv.inc(V.tensor_tensor(gb[:], ga[:], gpp, ALU.mult))
                    A.wait_ge(dv.h, dv.n)
                    ac.inc(A.activation(out=gth[:], in_=gb[:], func=AF.Tanh, scale=GELU_S))
                    V.wait_ge(ac.h, ac.n)
                    V.scalar_tensor_tensor(gb[:], gth[:], 1.0, gpp, ALU.add, ALU.mult)
                    dv.inc(V.tensor_tensor(interT[:, nsub, :], gb[:], upp, ALU.mult))
                    dv_ep[gidx2] = dv.n
                group_pe[g] = pe.n

            for sl in range(8):
                b = sl % 2
                di = seg * 8 + sl
                if di >= 2:
                    SQ.wait_ge(pe.h, pe_dw[di - 2])
                if not cc_d[0]:
                    SQ.wait_ge(cc.h, 3)
                    cc_d[0] = True
                sgt[b].inc(SQ.dma_start(out=gdt[b][:],
                                        in_=Gd[sl, 2048 * seg:2048 * (seg + 1), :].rearrange("(c p) h -> p c h", p=128)), 16)
                T.wait_ge(sgt[b].h, sgt[b].n)
                T.wait_ge(dv.h, dv_ep[seg * 16 + 15])
                for j in range(2):
                    if dv_dacc:
                        T.wait_ge(dv.h, dv_dacc)
                    for nci in range(16):
                        for th in range(2):
                            ins = T.matmul(gp_p[j][:, 512 * th:512 * (th + 1)],
                                           gdt[b][:, nci, 128 * j:128 * (j + 1)],
                                           interT[:, nci, 512 * th:512 * (th + 1)],
                                           start=(nci == 0), stop=(nci == 15))
                    pe.inc(ins)
                    V.wait_ge(pe.h, pe.n)
                    dv.inc(V.tensor_tensor(gacc[:, 2 * sl + j, :], gp_p[j][:],
                                           gacc[:, 2 * sl + j, :], ALU.add))
                    dv_dacc = dv.n
                pe_dw[di] = pe.n


def _combine(nc, SQ, WQ, V, A, T, C, xr, out, sDram, gacc, identt, idenft):
    pe, dv, ac, ou, st, sxr = C["pe"], C["dv"], C["ac"], C["ou"], C["st"], C["xr"]
    with contextlib.ExitStack() as es2:
        E2 = es2.enter_context
        outst = E2(nc.sbuf_tensor("outst", [128, 8, D], F32))
        stile = [E2(nc.sbuf_tensor(f"cst{i}", [128, TOWN], BF16)) for i in range(2)]
        silt = E2(nc.sbuf_tensor("silt", [128, TOWN], F32))
        xrt = [E2(nc.sbuf_tensor(f"xrt{i}", [128, D], F32)) for i in range(2)]
        cpa = E2(nc.sbuf_tensor("cpa", [128, 128], BF16))
        ta_p = E2(nc.psum_tensor("ta_p", [128, 128], BF16))
        tb_p = E2(nc.psum_tensor("tb_p", [128, 128], F32))

        dv_c = 0
        pe_tp2 = 0
        pe_ec = {}
        for ec in range(16):
            b = ec % 2
            if ec >= 2:
                SQ.wait_ge(pe.h, pe_ec[ec - 2])
            st.inc(SQ.dma_start(out=stile[b][:], in_=sDram[128 * ec:128 * (ec + 1), :]), 16)
            st_all = st.n
            if dv_c:
                A.wait_ge(dv.h, dv_c)
            if pe_tp2:
                A.wait_ge(pe.h, pe_tp2)
            ac.inc(A.activation(out=silt[:], in_=gacc[:, ec, :], func=AF.Silu))
            ac_sil = ac.n
            for t in range(8):
                T.wait_ge(st.h, st_all)
                T.wait_ge(ac.h, ac_sil)
                if dv_c:
                    T.wait_ge(dv.h, dv_c)
                T.transpose(ta_p[:], stile[b][:, 128 * t:128 * (t + 1)], identt[:])
                pe.inc(T.transpose(tb_p[:], silt[:, 128 * t:128 * (t + 1)], idenft[:]))
                pe_tp2 = pe.n
                V.wait_ge(pe.h, pe_tp2)
                V.tensor_copy(cpa[:], ta_p[:])
                dv.inc(V.tensor_tensor(outst[:, t, 128 * ec:128 * (ec + 1)], tb_p[:], cpa[:], ALU.mult))
                dv_c = dv.n
            pe_ec[ec] = pe.n

        dv_xr = {}
        for t in range(8):
            b = t % 2
            if t >= 2:
                SQ.wait_ge(dv.h, dv_xr[t - 2])
            sxr[b].inc(SQ.dma_start(out=xrt[b][:], in_=xr[128 * t:128 * (t + 1), :]), 16)
            V.wait_ge(sxr[b].h, sxr[b].n)
            dv.inc(V.tensor_tensor(outst[:, t, :], outst[:, t, :], xrt[b][:], ALU.add))
            dv_xr[t] = dv.n
            WQ.wait_ge(dv.h, dv.n)
            C["ou"].inc(WQ.dma_start(out=out[128 * t:128 * (t + 1), :], in_=outst[:, t, :]), 16)


_NC_CACHE = {}


def _get_nc():
    key = (DEQ_SLICES, STAGE, LNKV_SUB, DBG)
    if key not in _NC_CACHE:
        _NC_CACHE[key] = build()
    return _NC_CACHE[key]


def _rope_tables():
    i = np.arange(1, 65, dtype=np.float32)
    thetas = (np.float32(10000.0) ** (np.float32(-2.0) * (i - np.float32(1.0)) / np.float32(128.0))).astype(np.float32)
    pos = np.arange(1, TKV + 1, dtype=np.float32)
    ang = pos[:, None] * thetas[None, :]
    c = np.cos(ang).astype(np.float32)
    s = np.sin(ang).astype(np.float32)
    cosF = np.ascontiguousarray(np.concatenate([c, c], axis=1).T)
    sinH = np.ascontiguousarray(np.concatenate([s, -s], axis=1).T)
    return cosF, sinH


def prepare_in_maps(L0_attn_x, L0_attn_Wq, L0_attn_bq, L0_attn_Wk, L0_attn_bk, L0_attn_Wv,
                    L0_attn_bv, L0_attn_Wo, L0_attn_bo, L0_ffn_gate_W_packed, L0_ffn_gate_scales,
                    L0_ffn_up_W_packed, L0_ffn_up_scales, L0_ffn_down_W_packed, L0_ffn_down_scales):
    x = np.asarray(L0_attn_x, dtype=np.float32)
    wq = np.ascontiguousarray(np.asarray(L0_attn_Wq, np.float32).T)
    wk = np.ascontiguousarray(np.asarray(L0_attn_Wk, np.float32).T)
    wv = np.ascontiguousarray(np.asarray(L0_attn_Wv, np.float32).T)
    wo = np.ascontiguousarray(np.asarray(L0_attn_Wo, np.float32).T)
    bqv = np.ascontiguousarray(np.asarray(L0_attn_bq, np.float32).reshape(16, 128).T)
    bkv = np.ascontiguousarray(np.asarray(L0_attn_bk, np.float32).reshape(4, 128).T)
    bvv = np.ascontiguousarray(np.asarray(L0_attn_bv, np.float32).reshape(4, 128).T)
    bov = np.ascontiguousarray(np.asarray(L0_attn_bo, np.float32).reshape(16, 128).T)
    gpk = np.ascontiguousarray(np.asarray(L0_ffn_gate_W_packed, np.int32))
    gsc = np.ascontiguousarray(np.asarray(L0_ffn_gate_scales, np.float32))
    upkk = np.ascontiguousarray(np.asarray(L0_ffn_up_W_packed, np.int32))
    upsc = np.ascontiguousarray(np.asarray(L0_ffn_up_scales, np.float32))
    dpk = np.ascontiguousarray(np.asarray(L0_ffn_down_W_packed, np.int32))
    dssc = np.ascontiguousarray(np.asarray(L0_ffn_down_scales, np.float32))

    cosF, sinH = _rope_tables()
    iden = np.eye(128, dtype=np.float32).astype(ml_dtypes.bfloat16)
    idenf32 = np.eye(128, dtype=np.float32)
    onescol = np.ones((128, 1), np.float32).astype(ml_dtypes.bfloat16)
    onesf32 = np.ones((1, 128), np.float32)

    in_maps = []
    for c in range(8):
        bsel = c // 2
        h = c % 2
        own0 = 1024 * h
        oth0 = 1024 * (1 - h)
        idx = np.concatenate([np.arange(own0, own0 + 1024), np.arange(oth0, oth0 + 1024)])
        xb = x[bsel]
        in_maps.append({
            "xT": np.ascontiguousarray(xb.T[:, idx]),
            "xr": np.ascontiguousarray(xb[own0:own0 + 1024]),
            "wqT": wq, "wkT": wk, "wvT": wv, "woT": wo,
            "bq": bqv, "bk": bkv, "bv": bvv, "bo": bov,
            "gp": np.ascontiguousarray(gpk[2048 * c:2048 * (c + 1)]),
            "gs": np.ascontiguousarray(gsc[2048 * c:2048 * (c + 1)]),
            "up": np.ascontiguousarray(upkk[2048 * c:2048 * (c + 1)]),
            "us": np.ascontiguousarray(upsc[2048 * c:2048 * (c + 1)]),
            "dp": np.ascontiguousarray(dpk[256 * c:256 * (c + 1)]),
            "ds": np.ascontiguousarray(dssc[256 * c:256 * (c + 1)]),
            "qcos": np.ascontiguousarray(cosF[:, own0:own0 + 1024]),
            "qsin": np.ascontiguousarray(sinH[:, own0:own0 + 1024]),
            "kcos": np.ascontiguousarray(cosF[:, idx]),
            "ksin": np.ascontiguousarray(sinH[:, idx]),
            "posq": np.ascontiguousarray(np.arange(own0, own0 + 1024, dtype=np.float32).reshape(8, 128).T),
            "poskcol": np.ascontiguousarray(idx.astype(np.float32).reshape(16, 128).T),
            "poskk": idx.astype(np.float32).reshape(1, TKV),
            "iden": iden, "idenf": idenf32, "onescol": onescol, "onesf32": onesf32,
        })
    return in_maps


def kernel(**inputs):
    in_maps = prepare_in_maps(**inputs)
    nc = _get_nc()
    res = run_bass_kernel_spmd(nc, in_maps, core_ids=list(range(8)))
    outp = np.empty((4, 2048, 2048), np.float32)
    for c in range(8):
        bsel = c // 2
        h = c % 2
        outp[bsel, 1024 * h:1024 * (h + 1)] = res.results[c]["out"]
    return outp


# version 11
# speedup vs baseline: 1.3586x; 1.0004x over previous
import contextlib
import os
import tempfile

os.environ["NEURON_COMPILE_CACHE_URL"] = tempfile.mkdtemp(prefix="ncc_")

import numpy as np
import ml_dtypes
import concourse.bass as bass
import concourse.mybir as mybir
from concourse.bass_utils import run_bass_kernel_spmd

F32 = mybir.dt.float32
BF16 = mybir.dt.bfloat16
I32 = mybir.dt.int32
AF = mybir.ActivationFunctionType
ALU = mybir.AluOpType
AX = mybir.AxisListType.X

D = 2048
DK = 512
HD = 128
TOWN = 1024
TKV = 2048
MLP = 16384
NEG = -1.0e30
GELU_C = 0.044715
GELU_S = 0.7978845608028654
ISQ = 0.08838834764831845
EPS = 1e-5

DEQ_SLICES = 1
STAGE = 5
LNKV_SUB = 7
DBG = ""


class Ctr:
    def __init__(self, es, nc, name):
        self.h = es.enter_context(nc.semaphore(name))
        self.n = 0

    def inc(self, instr, k=1):
        instr.then_inc(self.h, k)
        self.n += k
        return self.n


def build():
    nc = bass.Bass("TRN2", target_bir_lowering=False)

    def din(name, shape, dt):
        return nc.dram_tensor(name, shape, dt, kind="ExternalInput").ap()

    xT = din("xT", [D, TKV], F32)
    xr = din("xr", [TOWN, D], F32)
    wqT = din("wqT", [D, D], F32)
    wkT = din("wkT", [D, DK], F32)
    wvT = din("wvT", [D, DK], F32)
    woT = din("woT", [D, D], F32)
    bq = din("bq", [128, 16], F32)
    bk = din("bk", [128, 4], F32)
    bv = din("bv", [128, 4], F32)
    bo = din("bo", [128, 16], F32)
    gp = din("gp", [2048 * DEQ_SLICES, 1024], I32)
    gs = din("gs", [2048 * DEQ_SLICES, 64], F32)
    upk = din("up", [2048 * DEQ_SLICES, 1024], I32)
    ups = din("us", [2048 * DEQ_SLICES, 64], F32)
    dp = din("dp", [256 * DEQ_SLICES, 8192], I32)
    dsc = din("ds", [256 * DEQ_SLICES, 512], F32)
    qcos = din("qcos", [HD, TOWN], F32)
    qsin = din("qsin", [HD, TOWN], F32)
    kcos = din("kcos", [HD, TKV], F32)
    ksin = din("ksin", [HD, TKV], F32)
    posq = din("posq", [128, 8], F32)
    poskcol = din("poskcol", [128, 16], F32)
    poskk = din("poskk", [1, TKV], F32)
    iden = din("iden", [128, 128], BF16)
    idenf = din("idenf", [128, 128], F32)
    onescol = din("onescol", [128, 1], BF16)
    onesf32 = din("onesf32", [1, 128], F32)

    out = nc.dram_tensor("out", [TOWN, D], F32, kind="ExternalOutput").ap()

    wk_ = "ExternalOutput" if "w" in DBG else "Internal"
    if "k" in DBG:
        dbg_xn = nc.dram_tensor("dbg_xn", [128, 16, TOWN], BF16, kind="ExternalOutput").ap()
        dbg_kt = nc.dram_tensor("dbg_kt", [128, 4, TKV], BF16, kind="ExternalOutput").ap()
        dbg_vh = nc.dram_tensor("dbg_vh", [128, 4, 16, 128], BF16, kind="ExternalOutput").ap()
    if "a" in DBG:
        nc._dbg_a = dict(
            qt=nc.dram_tensor("dbg_qt", [128, TOWN], BF16, kind="ExternalOutput").ap(),
            pb=nc.dram_tensor("dbg_pb", [128, TKV], BF16, kind="ExternalOutput").ap(),
            ell=nc.dram_tensor("dbg_ell", [128, 1], F32, kind="ExternalOutput").ap(),
            s=nc.dram_tensor("dbg_s", [128, TKV], F32, kind="ExternalOutput").ap(),
            pb2=nc.dram_tensor("dbg_pb2", [128, TKV], BF16, kind="ExternalOutput").ap(),
            msk=nc.dram_tensor("dbg_msk", [128, 8, TKV], BF16, kind="ExternalOutput").ap(),
            yt=nc.dram_tensor("dbg_yt", [128, 16, TOWN], BF16, kind="ExternalOutput").ap(),
        )
    sk_ = "ExternalOutput" if "s" in DBG else "Internal"
    ggcF = nc.dram_tensor("ggc", [D, 2048], BF16, kind="Internal").ap()
    gucF = nc.dram_tensor("guc", [D, 2048], BF16, kind="Internal").ap()
    gdcF = nc.dram_tensor("gdc", [MLP, 256], BF16, kind="Internal").ap()
    ggc = ggcF.rearrange("(a k) n -> a k n", a=1)
    guc = gucF.rearrange("(a k) n -> a k n", a=1)
    gdc = gdcF.rearrange("(a k) n -> a k n", a=1)
    GgF = nc.dram_tensor("Gg", [8 * D, 2048], BF16, kind=wk_, addr_space="Shared").ap()
    GuF = nc.dram_tensor("Gu", [8 * D, 2048], BF16, kind=wk_, addr_space="Shared").ap()
    GdF = nc.dram_tensor("Gd", [8 * MLP, 256], BF16, kind=wk_, addr_space="Shared").ap()
    Gg = GgF.rearrange("(s k) n -> s k n", s=8)
    Gu = GuF.rearrange("(s k) n -> s k n", s=8)
    Gd = GdF.rearrange("(s k) n -> s k n", s=8)
    sDram = nc.dram_tensor("sDram", [D, TOWN], BF16, kind=sk_).ap()

    es = contextlib.ExitStack()
    with es:
        E = es.enter_context
        xnT = E(nc.sbuf_tensor("xnT", [128, 16, TOWN], BF16))
        identt = E(nc.sbuf_tensor("identt", [128, 128], BF16))
        idenft = E(nc.sbuf_tensor("idenft", [128, 128], F32))
        onescolt = E(nc.sbuf_tensor("onescolt", [128, 1], BF16))
        onesf32t = E(nc.sbuf_tensor("onesf32t", [1, 128], F32))
        posqt = E(nc.sbuf_tensor("posqt", [128, 8], F32))
        poskcolt = E(nc.sbuf_tensor("poskcolt", [128, 16], F32))
        bqt = E(nc.sbuf_tensor("bqt", [128, 16], F32))
        bot = E(nc.sbuf_tensor("bot", [128, 16], F32))
        bkt = E(nc.sbuf_tensor("bkt", [128, 4], F32))
        bvt = E(nc.sbuf_tensor("bvt", [128, 4], F32))
        cneg4 = E(nc.sbuf_tensor("cneg4", [128, 1], F32))

        C = {}
        for name in ["gd", "pe", "dv", "ac", "ou", "cn", "x", "ms", "st"]:
            C[name] = Ctr(es, nc, name)
        for name in ["ws", "gg", "gt", "xr"]:
            C[name] = [Ctr(es, nc, name + "0"), Ctr(es, nc, name + "1")]
        C["pk"] = [Ctr(es, nc, f"pk{i}") for i in range(4)]

        gd, pe, dv, ac, ou = C["gd"], C["pe"], C["dv"], C["ac"], C["ou"]

        blk = es.enter_context(nc.Block())
        SQ = nc.sync
        WQ = nc.scalar
        V = nc.vector
        A = nc.scalar
        T = nc.tensor
        ALL_E = (SQ, V, T, WQ, nc.gpsimd)

        def flat_ctrs():
            r = []
            for v in C.values():
                r.extend(v if isinstance(v, list) else [v])
            return r

        def barrier():
            vals = [(c, c.n) for c in flat_ctrs()]
            for e in ALL_E:
                for c, v in vals:
                    if v > 0:
                        e.wait_ge(c.h, v)

        cn = C["cn"]
        cn.inc(SQ.dma_start(out=identt[:], in_=iden), 16)
        cn.inc(SQ.dma_start(out=idenft[:], in_=idenf), 16)
        cn.inc(SQ.dma_start(out=onescolt[:], in_=onescol), 16)
        cn.inc(SQ.dma_start(out=onesf32t[:], in_=onesf32), 16)
        cn.inc(SQ.dma_start(out=posqt[:], in_=posq), 16)
        cn.inc(SQ.dma_start(out=poskcolt[:], in_=poskcol), 16)
        cn.inc(SQ.dma_start(out=bqt[:], in_=bq), 16)
        cn.inc(SQ.dma_start(out=bot[:], in_=bo), 16)
        cn.inc(SQ.dma_start(out=bkt[:], in_=bk), 16)
        cn.inc(SQ.dma_start(out=bvt[:], in_=bv), 16)
        dv_ = C["dv"]
        dv_.inc(V.memset(cneg4[:], -4.0))
        for e in (V, T, WQ, nc.gpsimd):
            e.wait_ge(cn.h, cn.n)
            e.wait_ge(dv_.h, 1)

        cc = Ctr(es, nc, "ccs")
        nc._cc = cc
        if STAGE >= 1:
            G_ = nc.gpsimd
            rg = [[0, 1, 2, 3, 4, 5, 6, 7]]
            _dequant(nc, SQ, WQ, V, T, C, identt, gp, gs, ggc, 1.0, "g")
            _dequant(nc, SQ, WQ, V, T, C, identt, upk, ups, guc, 1.0, "u")
            _dequant(nc, SQ, WQ, V, T, C, identt, dp, dsc, gdc, 0.5, "d")

            def issue_gathers():
                G_.wait_ge(gd.h, gd.n)
                G_.wait_ge(C["x"].h, C["x"].n)
                cc.inc(G_.collective_compute("AllGather", ALU.bypass,
                                             replica_groups=rg,
                                             ins=[ggcF.opt()], outs=[GgF.opt()]), 1)
                cc.inc(G_.collective_compute("AllGather", ALU.bypass,
                                             replica_groups=rg,
                                             ins=[gucF.opt()], outs=[GuF.opt()]), 1)
                cc.inc(G_.collective_compute("AllGather", ALU.bypass,
                                             replica_groups=rg,
                                             ins=[gdcF.opt()], outs=[GdF.opt()]), 1)
        else:
            def issue_gathers():
                pass
        SQ.wait_ge(gd.h, gd.n)

        if STAGE >= 2:
            with contextlib.ExitStack() as es23:
                E23 = es23.enter_context
                poskkb = E23(nc.sbuf_tensor("poskkb", [128, TKV], F32))
                qcost = E23(nc.sbuf_tensor("qcost", [128, TOWN], F32))
                qsint = E23(nc.sbuf_tensor("qsint", [128, TOWN], F32))
                ms = C["ms"]
                ms.inc(SQ.dma_start(out=poskkb[:], in_=poskk.partition_broadcast(128)), 16)
                ms.inc(SQ.dma_start(out=qcost[:], in_=qcos), 16)
                ms.inc(SQ.dma_start(out=qsint[:], in_=qsin), 16)
                kvt = _ln_kv(nc, es23, SQ, V, A, T, C,
                             xT, wkT, wvT, bkt, bvt, kcos, ksin,
                             xnT, onescolt, onesf32t, identt)
                issue_gathers()
                barrier()
                if "k" in DBG:
                    ou.inc(WQ.dma_start(out=dbg_xn, in_=xnT[:]), 16)
                    ou.inc(WQ.dma_start(out=dbg_kt, in_=kvt["KT"][:]), 16)
                    ou.inc(WQ.dma_start(out=dbg_vh, in_=kvt["Vh"][:]), 16)
                    for e in ALL_E:
                        e.wait_ge(ou.h, ou.n)
                if STAGE >= 3:
                    _attention(nc, es23, SQ, WQ, V, A, T, C,
                               wqT, woT, bqt, bot, qcost, qsint, poskcolt, poskkb,
                               xnT, sDram, kvt, identt, cneg4, onescolt, onesf32t)
                barrier()

        if STAGE >= 4:
            with contextlib.ExitStack() as es45:
                gacc = es45.enter_context(nc.sbuf_tensor("gacc", [128, 16, TOWN], F32))
                _ffn(nc, SQ, V, A, T, C, Gg, Gu, Gd, xnT, gacc, nc._cc)
                barrier()
                if STAGE >= 5:
                    _combine(nc, SQ, WQ, V, A, T, C, xr, out, sDram, gacc, identt, idenft)

        SQ.wait_ge(ou.h, ou.n)

    return nc


def _dequant(nc, SQ, WQ, V, T, C, identt, packed, scales, Gout, smul, tag):
    gd, pe, dv = C["gd"], C["pe"], C["dv"]
    pk = C["pk"]
    KK = packed.shape[1]
    NB = scales.shape[1]
    n_loc = Gout.shape[2]
    n_rtiles = n_loc // 128
    n_cchunk = KK // 512
    nslices = packed.shape[0] // n_loc
    K_win = Gout.shape[1] // n_cchunk

    with contextlib.ExitStack() as es:
        SQ.wait_ge(gd.h, gd.n)
        E = es.enter_context
        pt = [E(nc.sbuf_tensor(f"dqp{tag}{i}", [128, 512], I32)) for i in range(4)]
        sct = [E(nc.sbuf_tensor(f"dqs{tag}{i}", [128, NB], F32)) for i in range(4)]
        sce = [E(nc.sbuf_tensor(f"dqe{tag}{i}", [128, NB], F32)) for i in range(4)]
        mtile = E(nc.sbuf_tensor(f"dqm{tag}", [128, 1024], I32))
        stile = E(nc.sbuf_tensor(f"dqt{tag}", [128, 1024], I32))
        btile = E(nc.sbuf_tensor(f"dqb{tag}", [128, 1024], I32))
        wtile = E(nc.sbuf_tensor(f"dqw{tag}", [128, 1024], BF16))
        rb = E(nc.sbuf_tensor(f"dqr{tag}", [128, 8, n_loc], BF16))
        tp = [E(nc.psum_tensor(f"dqx{tag}{i}", [128, 512], BF16)) for i in range(2)]

        dv_decode = {}
        dv_evac = {}
        pe_tp = {}
        gd_grp = {}

        u = 0
        for sl in range(nslices):
            for c in range(n_cchunk):
                for r in range(n_rtiles):
                    b = u % 4
                    bc = u % 4
                    if u >= 4:
                        SQ.wait_ge(dv.h, dv_decode[u - 4])
                    row0 = sl * n_loc + r * 128
                    pk[bc].inc(SQ.dma_start(out=pt[b][:], in_=packed[row0:row0 + 128, c * 512:(c + 1) * 512]), 16)
                    pk[bc].inc(SQ.dma_start(out=sct[b][:], in_=scales[row0:row0 + 128, :]), 16)

                    V.wait_ge(pk[bc].h, pk[bc].n)
                    if u >= 1:
                        V.wait_ge(pe.h, pe_tp[u - 1])
                    if r == 0:
                        g_prev = (sl * n_cchunk + c) - 1
                        if g_prev >= 0:
                            V.wait_ge(gd.h, gd_grp[g_prev])
                    V.tensor_scalar(sce[b][:], sct[b][:], float(smul * (2.0 ** 126)), None, ALU.mult)
                    p = pt[b][:]
                    V.tensor_scalar(mtile[:, 0:512], p, 7, 22, ALU.bitwise_and, ALU.logical_shift_left)
                    V.tensor_scalar(mtile[:, 512:1024], p, 0x70, 18, ALU.bitwise_and, ALU.logical_shift_left)
                    V.tensor_scalar(stile[:, 0:512], p, 8, 28, ALU.bitwise_and, ALU.logical_shift_left)
                    V.tensor_scalar(stile[:, 512:1024], p, 0x80, 24, ALU.bitwise_and, ALU.logical_shift_left)
                    V.tensor_tensor(btile[:, 0:512], mtile[:, 0:512], stile[:, 0:512], ALU.bitwise_or)
                    V.tensor_tensor(btile[:, 512:1024], mtile[:, 512:1024], stile[:, 512:1024], ALU.bitwise_or)
                    scb = sce[b][:, 32 * c:32 * c + 32].unsqueeze(2).broadcast_to([128, 32, 16])
                    bl = btile[:, 0:512].bitcast(F32).rearrange("p (b j) -> p b j", b=32)
                    bh = btile[:, 512:1024].bitcast(F32).rearrange("p (b j) -> p b j", b=32)
                    V.tensor_tensor(wtile[:, 0:512].rearrange("p (b j) -> p b j", b=32), bl, scb, ALU.mult)
                    dv.inc(V.tensor_tensor(wtile[:, 512:1024].rearrange("p (b j) -> p b j", b=32), bh, scb, ALU.mult))
                    dv_decode[u] = dv.n

                    T.wait_ge(dv.h, dv_decode[u])
                    if u >= 1:
                        T.wait_ge(dv.h, dv_evac[u - 1])
                    for plane in range(2):
                        for s4 in range(4):
                            ins = T.transpose(tp[plane][:, 128 * s4:128 * (s4 + 1)],
                                              wtile[:, 512 * plane + 128 * s4:512 * plane + 128 * (s4 + 1)],
                                              identt[:])
                        pe.inc(ins)
                    pe_tp[u] = pe.n

                    V.wait_ge(pe.h, pe_tp[u])
                    V.tensor_copy(rb[:, 0:4, r * 128:(r + 1) * 128],
                                  tp[0][:].rearrange("p (s j) -> p s j", s=4))
                    dv.inc(V.tensor_copy(rb[:, 4:8, r * 128:(r + 1) * 128],
                                         tp[1][:].rearrange("p (s j) -> p s j", s=4)))
                    dv_evac[u] = dv.n
                    u += 1

                WQ.wait_ge(dv.h, dv_evac[u - 1])
                for plane in range(2):
                    for s4 in range(4):
                        base = K_win * c + 256 * s4 + plane
                        gd.inc(WQ.dma_start(out=Gout[sl, base:base + 255:2, :],
                                            in_=rb[:, plane * 4 + s4, :]), 16)
                gd_grp[sl * n_cchunk + c] = gd.n


def _ln_kv(nc, es, SQ, V, A, T, C, xT, wkT, wvT, bkt, bvt, kcos, ksin,
           xnT, onescolt, onesf32t, identt):
    pe, dv, ac, ms, sx, ws = C["pe"], C["dv"], C["ac"], C["ms"], C["x"], C["ws"]
    E = es.enter_context
    KT = E(nc.sbuf_tensor("KT", [128, 4, TKV], BF16))
    Vh = E(nc.sbuf_tensor("Vh", [128, 4, 16, 128], BF16))

    with contextlib.ExitStack() as es2:
        E2 = es2.enter_context
        kcost = E2(nc.sbuf_tensor("kcost", [128, TKV], F32))
        ksint = E2(nc.sbuf_tensor("ksint", [128, TKV], F32))
        wk16 = E2(nc.sbuf_tensor("wk16", [128, 16, DK], BF16))
        wv16 = E2(nc.sbuf_tensor("wv16", [128, 16, DK], BF16))
        xtc = E2(nc.sbuf_tensor("xtc", [128, 16, 512], F32))
        xn_t = E2(nc.sbuf_tensor("xn_t", [128, 16, 512], BF16))
        x16 = [E2(nc.sbuf_tensor(f"x16{i}", [128, 512], BF16)) for i in range(2)]
        x2 = [E2(nc.sbuf_tensor(f"x2{i}", [128, 512], BF16)) for i in range(2)]
        wstage = [E2(nc.sbuf_tensor(f"wst{i}", [128, 512], F32)) for i in range(2)]
        srow = E2(nc.sbuf_tensor("srow", [1, 512], F32))
        qrow = E2(nc.sbuf_tensor("qrow", [1, 512], F32))
        trow = E2(nc.sbuf_tensor("trow", [1, 512], F32))
        vrow = E2(nc.sbuf_tensor("vrow", [1, 512], F32))
        t1 = E2(nc.sbuf_tensor("lnt1", [128, 512], F32))
        kraw = E2(nc.sbuf_tensor("kraw", [128, 512], F32))
        krot = E2(nc.sbuf_tensor("krot", [128, 512], F32))
        kcs = E2(nc.sbuf_tensor("kcs", [128, 512], F32))
        vtmp = E2(nc.sbuf_tensor("vtmp", [128, 512], BF16))
        sum_p = E2(nc.psum_tensor("sum_p", [1, 512], F32))
        sq_p = E2(nc.psum_tensor("sq_p", [1, 512], F32))
        mb_p = E2(nc.psum_tensor("mb_p", [128, 512], F32))
        rb_p = E2(nc.psum_tensor("rb_p", [128, 512], F32))
        kv_p = E2(nc.psum_tensor("kv_p", [128, 512], F32))
        vv_p = E2(nc.psum_tensor("vv_p", [128, 512], F32))
        vt_p = E2(nc.psum_tensor("vt_p", [128, 512], BF16))

        ms.inc(SQ.dma_start(out=kcost[:], in_=kcos), 16)
        ms.inc(SQ.dma_start(out=ksint[:], in_=ksin), 16)
        MS_ALL = ms.n

        casts = {}
        for w in range(2):
            src = wkT if w == 0 else wvT
            dst = wk16 if w == 0 else wv16
            for kc in range(16):
                i = w * 16 + kc
                b = i % 2
                if i >= 2:
                    SQ.wait_ge(dv.h, casts[i - 2])
                ws[b].inc(SQ.dma_start(out=wstage[b][:], in_=src[128 * kc:128 * (kc + 1), :]), 16)
                V.wait_ge(ws[b].h, ws[b].n)
                dv.inc(V.tensor_copy(dst[:, kc, :], wstage[b][:]))
                casts[i] = dv.n
        V.wait_ge(ms.h, MS_ALL)

        dv_norm = {}
        pe_kv = {}
        dv_kv = dv.n
        for tc in range(4 if LNKV_SUB >= 2 else 0):
            own = tc < 2
            if tc >= 1:
                SQ.wait_ge(dv.h, dv_norm[tc - 1])
            for kc in range(16):
                sx.inc(SQ.dma_start(out=xtc[:, kc, :],
                                    in_=xT[128 * kc:128 * (kc + 1), 512 * tc:512 * (tc + 1)]), 16)
            sx_all = sx.n

            pe_stat = {}
            for kc in range(16):
                bb = kc % 2
                V.wait_ge(sx.h, sx_all)
                A.wait_ge(sx.h, sx_all)
                if kc >= 2:
                    V.wait_ge(pe.h, pe_stat[kc - 2])
                    A.wait_ge(pe.h, pe_stat[kc - 2])
                dv.inc(V.tensor_copy(x16[bb][:], xtc[:, kc, :]))
                ac.inc(A.activation(out=x2[bb][:], in_=xtc[:, kc, :], func=AF.Square))
                T.wait_ge(dv.h, dv.n)
                T.wait_ge(ac.h, ac.n)
                T.matmul(sum_p[:], onescolt[:], x16[bb][:], start=(kc == 0), stop=(kc == 15))
                pe.inc(T.matmul(sq_p[:], onescolt[:], x2[bb][:], start=(kc == 0), stop=(kc == 15)))
                pe_stat[kc] = pe.n
            pe_stats = pe.n

            if LNKV_SUB < 3:
                dv_norm[tc] = dv.n
                pe_kv[tc] = pe.n
                continue
            V.wait_ge(pe.h, pe_stats)
            V.tensor_scalar(srow[:], sum_p[:], 1.0 / D, None, ALU.mult)
            V.tensor_scalar(qrow[:], sq_p[:], 1.0 / D, None, ALU.mult)
            V.tensor_copy(t1[0:1, :], qrow[:])
            V.tensor_tensor(trow[:], srow[:], srow[:], ALU.mult)
            V.tensor_copy(t1[0:1, :], qrow[:])
            V.scalar_tensor_tensor(vrow[:], qrow[:], EPS, trow[:], ALU.add, ALU.subtract)
            dv.inc(V.tensor_copy(t1[0:1, :], qrow[:]))
            A.wait_ge(dv.h, dv.n)
            ac.inc(A.activation(out=trow[:], in_=vrow[:], func=AF.Sqrt))
            V.wait_ge(ac.h, ac.n)
            V.reciprocal(vrow[:], trow[:])
            dv.inc(V.tensor_copy(t1[0:1, :], qrow[:]))
            T.wait_ge(dv.h, dv.n)
            T.matmul(mb_p[:], onesf32t[:], srow[:], start=True, stop=True)
            pe.inc(T.matmul(rb_p[:], onesf32t[:], vrow[:], start=True, stop=True))
            pe_bc = pe.n

            if LNKV_SUB < 4:
                dv_norm[tc] = dv.n
                pe_kv[tc] = pe.n
                continue
            V.wait_ge(pe.h, pe_bc)
            if tc >= 3:
                V.wait_ge(pe.h, pe_kv[tc - 1])
            for kc in range(16):
                V.tensor_tensor(t1[:], xtc[:, kc, :], mb_p[:], ALU.subtract)
                dst = xnT[:, kc, 512 * tc:512 * (tc + 1)] if own else xn_t[:, kc, :]
                dv.inc(V.tensor_tensor(dst, t1[:], rb_p[:], ALU.mult))
            dv_norm[tc] = dv.n

            def xsrc(kc, tc=tc, own=own):
                return xnT[:, kc, 512 * tc:512 * (tc + 1)] if own else xn_t[:, kc, :]

            for kvh in range(4 if LNKV_SUB >= 5 else 0):
                T.wait_ge(dv.h, dv_norm[tc])
                T.wait_ge(dv.h, dv_kv)
                for kc in range(16):
                    T.matmul(kv_p[:], wk16[:, kc, 128 * kvh:128 * (kvh + 1)], xsrc(kc),
                             start=(kc == 0), stop=(kc == 15))
                for kc in range(16):
                    ins = T.matmul(vv_p[:], wv16[:, kc, 128 * kvh:128 * (kvh + 1)], xsrc(kc),
                                   start=(kc == 0), stop=(kc == 15))
                pe.inc(ins)
                pe_proj = pe.n
                if LNKV_SUB < 6:
                    dv_kv = dv.n
                    continue

                V.wait_ge(pe.h, pe_proj)
                cs = slice(512 * tc, 512 * (tc + 1))
                nops = (LNKV_SUB - 60) if 60 <= LNKV_SUB < 70 else 6
                ops = [
                    lambda: V.tensor_scalar(kraw[:], kv_p[:], bkt[:, kvh:kvh + 1], None, ALU.add),
                    lambda: V.tensor_tensor(krot[0:64, :], kraw[64:128, :], ksint[64:128, cs], ALU.mult),
                    lambda: V.tensor_tensor(krot[64:128, :], kraw[0:64, :], ksint[0:64, cs], ALU.mult),
                    lambda: V.tensor_tensor(kcs[:], kraw[:], kcost[:, cs], ALU.mult),
                    lambda: V.tensor_tensor(KT[:, kvh, cs], kcs[:], krot[:], ALU.add),
                    lambda: V.tensor_scalar(vtmp[:], vv_p[:], bvt[:, kvh:kvh + 1], None, ALU.add),
                ]
                last = None
                for _i in range(nops):
                    last = ops[_i]()
                if last is None:
                    last = V.engine_nop() if hasattr(V, "engine_nop") else V.nop()
                dv.inc(last)
                dv_kv = dv.n
                if LNKV_SUB < 7:
                    continue

                T.wait_ge(dv.h, dv_kv)
                for j in range(4):
                    ins = T.transpose(vt_p[:, 128 * j:128 * (j + 1)],
                                      vtmp[:, 128 * j:128 * (j + 1)], identt[:])
                pe.inc(ins)
                V.wait_ge(pe.h, pe.n)
                dv.inc(V.tensor_copy(Vh[:, kvh, 4 * tc:4 * tc + 4, :],
                                     vt_p[:].rearrange("p (s j) -> p s j", s=4)))
                dv_kv = dv.n
            pe_kv[tc] = pe.n
    return dict(KT=KT, Vh=Vh)


def _attention(nc, es, SQ, WQ, V, A, T, C, wqT, woT, bqt, bot,
               qcost, qsint, poskcolt, poskkb, xnT, sDram, kvt, identt, cneg4,
               onescolt, onesf32t):
    gd, pe, dv, ac, ws = C["gd"], C["pe"], C["dv"], C["ac"], C["ws"]
    KT, Vh = kvt["KT"], kvt["Vh"]
    E = es.enter_context
    YT = E(nc.sbuf_tensor("YT", [128, 16, TOWN], BF16))
    mask01 = E(nc.sbuf_tensor("mask01", [128, 16, TOWN], BF16))

    with contextlib.ExitStack() as es2:
        E2 = es2.enter_context
        wstage = [E2(nc.sbuf_tensor(f"aws{i}", [128, 16, 128], F32)) for i in range(2)]
        w16 = [E2(nc.sbuf_tensor(f"aw16{i}", [128, 16, 128], BF16)) for i in range(2)]
        qraw = E2(nc.sbuf_tensor("qraw", [128, TOWN], F32))
        qrot = E2(nc.sbuf_tensor("qrot", [128, TOWN], F32))
        qcs = E2(nc.sbuf_tensor("qcs", [128, TOWN], F32))
        QT = E2(nc.sbuf_tensor("QT", [128, 2, TOWN], BF16))
        NSL = 6
        Pe = E2(nc.sbuf_tensor("Pe", [128, NSL, 512], BF16))
        Pm = E2(nc.sbuf_tensor("Pm", [128, NSL, 512], BF16))
        elrec = E2(nc.sbuf_tensor("elrec", [1, TOWN], F32))
        ebs = E2(nc.sbuf_tensor("ebs", [128, 2, 512], F32))
        stmp = E2(nc.sbuf_tensor("stmp", [128, TOWN], BF16))
        st_p = [E2(nc.psum_tensor(f"st_p{i}", [128, 512], F32)) for i in range(3)]
        y_p = [E2(nc.psum_tensor(f"y_p{i}", [128, 512], F32)) for i in range(2)]
        eb_p = E2(nc.psum_tensor("eb_p", [128, 512], F32))
        qp_p = E2(nc.psum_tensor("qp_p", [128, 1024], F32))

        for s in range(16):
            dv.inc(V.tensor_scalar(mask01[:, s, :], poskkb[:, 0:TOWN],
                                   poskcolt[:, s:s + 1], None, ALU.is_ge))
        dv_mask_built = dv.n

        ac_exp = {}
        dv_mask = {}
        pe_st = {}
        pe_yell = {}
        dv_qe = {}
        st_last_of_qh = {}
        dv_ynorm = {0: 0, 1: 0}
        dv_tail = [0]
        pe_rank1 = {0: 0, 1: 0}
        pe_w = {}
        from collections import deque
        pending = deque()

        def flush_pending():
            pg, pqh, phalf, ps, pslot = pending.popleft()
            pqs = slice(512 * phalf, 512 * (phalf + 1))
            pkvh = pqh // 4
            T.wait_ge(dv.h, dv_mask[pg])
            if ps == 0:
                T.wait_ge(dv.h, dv_tail[0])
            T.matmul(y_p[phalf][:], Vh[:, pkvh, ps, :], Pm[:, pslot, :],
                     start=(ps == 0), stop=(ps == 15))
            pe.inc(T.matmul(eb_p[0:1, :], onescolt[:], Pm[:, pslot, :],
                            start=(ps == 0), stop=(ps == 15)))
            pe_yell[pg] = pe.n
            if ps == 15:
                V.wait_ge(pe.h, pe_yell[pg])
                V.wait_ge(pe.h, pe_rank1[phalf])
                dv.inc(V.reciprocal(elrec[0:1, pqs], eb_p[0:1, :]))
                dv_rec = dv.n
                T.wait_ge(dv.h, dv_rec)
                pe.inc(T.matmul(eb_p[:], onesf32t[:], elrec[0:1, pqs],
                                start=True, stop=True))
                pe_rank1[phalf] = pe.n
                V.wait_ge(pe.h, pe_rank1[phalf])
                V.tensor_copy(ebs[:, phalf, :], eb_p[:])
                dv.inc(V.tensor_tensor(YT[:, pqh, pqs], y_p[phalf][:],
                                       ebs[:, phalf, :], ALU.mult))
                dv_ynorm[phalf] = dv.n
                dv_tail[0] = dv.n

        g = 0
        for qh in range(16):
            kvh = qh // 4
            b = qh % 2
            if qh >= 2:
                SQ.wait_ge(pe.h, pe_w[qh - 2])
            ws[b].inc(SQ.dma_start(out=wstage[b][:],
                                   in_=wqT[:, 128 * qh:128 * (qh + 1)].rearrange("(c p) e -> p c e", p=128)), 16)
            V.wait_ge(ws[b].h, ws[b].n)
            if qh >= 2:
                V.wait_ge(pe.h, pe_w[qh - 2])
            dv.inc(V.tensor_copy(w16[b][:], wstage[b][:]))
            dv_w = dv.n

            T.wait_ge(dv.h, dv_w)
            T.wait_ge(dv.h, dv_qe.get(qh - 1, 0))
            for th in range(2):
                for kc in range(16):
                    ins = T.matmul(qp_p[:, 512 * th:512 * (th + 1)], w16[b][:, kc, :],
                                   xnT[:, kc, 512 * th:512 * (th + 1)], start=(kc == 0), stop=(kc == 15))
            pe.inc(ins)
            pe_w[qh] = pe.n

            qb = qh % 2
            V.wait_ge(pe.h, pe_w[qh])
            V.wait_ge(pe.h, st_last_of_qh.get(qh - 2, 0))
            V.tensor_scalar(qraw[:], qp_p[:], bqt[:, qh:qh + 1], None, ALU.add)
            V.tensor_tensor(qrot[0:64, :], qraw[64:128, :], qsint[64:128, :], ALU.mult)
            V.tensor_tensor(qrot[64:128, :], qraw[0:64, :], qsint[0:64, :], ALU.mult)
            V.tensor_tensor(qcs[:], qraw[:], qcost[:], ALU.mult)
            dv.inc(V.tensor_tensor(QT[:, qb, :], qcs[:], qrot[:], ALU.add))
            dv_qe[qh] = dv.n

            for half in range(2):
                qs = slice(512 * half, 512 * (half + 1))
                for s in range(16):
                    par = g % 3
                    slot = g % NSL
                    T.wait_ge(dv.h, dv_qe[qh])
                    if s == 0:
                        T.wait_ge(dv.h, max(dv_mask_built, dv_ynorm[half]))
                    if g >= 3:
                        T.wait_ge(ac.h, ac_exp[g - 3])
                    pe.inc(T.matmul(st_p[par][:], KT[:, kvh, 128 * s:128 * (s + 1)],
                                    QT[:, qb, qs], start=True, stop=True))
                    pe_st[g] = pe.n
                    if half == 1 and s == 15:
                        st_last_of_qh[qh] = pe.n

                    A.wait_ge(pe.h, pe_st[g])
                    if g >= NSL:
                        A.wait_ge(dv.h, dv_mask[g - NSL])
                    ac.inc(A.activation(out=Pe[:, slot, :], in_=st_p[par][:],
                                        func=AF.Exp, scale=ISQ, bias=cneg4[:]))
                    ac_exp[g] = ac.n

                    V.wait_ge(ac.h, ac_exp[g])
                    if g >= NSL:
                        V.wait_ge(pe.h, pe_yell.get(g - NSL, 0))
                    dv.inc(V.tensor_tensor(Pm[:, slot, :], Pe[:, slot, :],
                                           mask01[:, s, qs], ALU.mult))
                    dv_mask[g] = dv.n

                    if len(pending) >= 3:
                        flush_pending()
                    pending.append((g, qh, half, s, slot))
                    g += 1
        while pending:
            flush_pending()

        dv_yt = dv.n
        gd_s = 0
        dv_se = 0
        for ec in range(16):
            b = ec % 2
            last_use = pe_w[16 + ec - 2] if ec >= 2 else pe_w[14 + ec]
            SQ.wait_ge(pe.h, last_use)
            ws[b].inc(SQ.dma_start(out=wstage[b][:],
                                   in_=woT[:, 128 * ec:128 * (ec + 1)].rearrange("(c p) e -> p c e", p=128)), 16)
            V.wait_ge(ws[b].h, ws[b].n)
            V.wait_ge(pe.h, last_use)
            dv.inc(V.tensor_copy(w16[b][:], wstage[b][:]))
            dv_w2 = dv.n

            T.wait_ge(dv.h, dv_w2)
            T.wait_ge(dv.h, dv_yt)
            if dv_se:
                T.wait_ge(dv.h, dv_se)
            for th in range(2):
                for kc in range(16):
                    ins = T.matmul(qp_p[:, 512 * th:512 * (th + 1)], w16[b][:, kc, :],
                                   YT[:, kc, 512 * th:512 * (th + 1)], start=(kc == 0), stop=(kc == 15))
            pe.inc(ins)
            pe_w[16 + ec] = pe.n

            V.wait_ge(pe.h, pe_w[16 + ec])
            if gd_s:
                V.wait_ge(gd.h, gd_s)
            dv.inc(V.tensor_scalar(stmp[:], qp_p[:], bot[:, ec:ec + 1], None, ALU.add))
            dv_se = dv.n
            WQ.wait_ge(dv.h, dv_se)
            gd_s = gd.inc(WQ.dma_start(out=sDram[128 * ec:128 * (ec + 1), :], in_=stmp[:]), 16)


def _ffn(nc, SQ, V, A, T, C, Gg, Gu, Gd, xnT, gacc, cc):
    pe, dv, ac = C["pe"], C["dv"], C["ac"]
    sgg, sgt = C["gg"], C["gt"]
    NRING = 18
    with contextlib.ExitStack() as es2:
        E2 = es2.enter_context
        grng = E2(nc.sbuf_tensor("grng", [128, NRING, 512], BF16))
        urng = E2(nc.sbuf_tensor("urng", [128, NRING, 512], BF16))
        gdt = [E2(nc.sbuf_tensor(f"gdt{i}", [128, 16, 256], BF16)) for i in range(2)]
        interT = E2(nc.sbuf_tensor("interT", [128, 16, TOWN], BF16))
        ga = E2(nc.sbuf_tensor("ffa", [128, 1024], F32))
        gb = E2(nc.sbuf_tensor("ffb", [128, 1024], F32))
        gth = E2(nc.sbuf_tensor("ffth", [128, 1024], F32))
        gp_p = [E2(nc.psum_tensor(f"gp_p{i}", [128, 1024], F32)) for i in range(2)]
        up_p = [E2(nc.psum_tensor(f"up_p{i}", [128, 1024], F32)) for i in range(2)]

        dv.inc(V.memset(gacc[:], 0.0))

        group_pe = {}
        emitted = [0]
        n_strips = 8 * 4 * 16

        def strip_idx(seg, ng, kc):
            return (seg * 4 + ng) * 16 + kc

        SQ.wait_ge(cc.h, 2)
        cc_d = [False]

        def emit_loads(upto):
            while emitted[0] < min(upto, n_strips):
                s = emitted[0]
                seg, rem = divmod(s, 64)
                ng, kc = divmod(rem, 16)
                slot = s % NRING
                gidx = s // 16
                if s >= NRING:
                    SQ.wait_ge(pe.h, group_pe[(s - NRING) // 16])
                sgg[gidx % 2].inc(SQ.dma_start(out=grng[:, slot, :],
                                               in_=Gg[seg, 128 * kc:128 * (kc + 1), 512 * ng:512 * (ng + 1)]), 16)
                sgg[gidx % 2].inc(SQ.dma_start(out=urng[:, slot, :],
                                               in_=Gu[seg, 128 * kc:128 * (kc + 1), 512 * ng:512 * (ng + 1)]), 16)
                emitted[0] += 1

        dv_ep = {}
        dv_dacc = 0
        pe_dw = {}
        for seg in range(8):
            for ng in range(4):
                g = seg * 4 + ng
                emit_loads(strip_idx(seg, ng, 0) + NRING)
                sg_all = 32 * 16 * (g // 2 + 1)
                for ns in range(4):
                    nsub = 4 * ng + ns
                    gidx2 = seg * 16 + nsub
                    par = nsub % 2
                    T.wait_ge(sgg[g % 2].h, sg_all)
                    if gidx2 >= 2 and (gidx2 - 2) in dv_ep:
                        T.wait_ge(dv.h, dv_ep[gidx2 - 2])
                    if ng == 0 and ns < 2 and dv_dacc:
                        T.wait_ge(dv.h, dv_dacc)
                    for kc in range(16):
                        s = strip_idx(seg, ng, kc)
                        slot = s % NRING
                        for th in range(2):
                            T.matmul(gp_p[par][:, 512 * th:512 * (th + 1)],
                                     grng[:, slot, 128 * ns:128 * (ns + 1)],
                                     xnT[:, kc, 512 * th:512 * (th + 1)],
                                     start=(kc == 0), stop=(kc == 15))
                        for th in range(2):
                            ins = T.matmul(up_p[par][:, 512 * th:512 * (th + 1)],
                                           urng[:, slot, 128 * ns:128 * (ns + 1)],
                                           xnT[:, kc, 512 * th:512 * (th + 1)],
                                           start=(kc == 0), stop=(kc == 15))
                    pe.inc(ins)
                    pe_gu = pe.n

                    gpp = gp_p[par][:]
                    upp = up_p[par][:]
                    A.wait_ge(pe.h, pe_gu)
                    ac.inc(A.activation(out=ga[:], in_=gpp, func=AF.Square))
                    V.wait_ge(ac.h, ac.n)
                    V.tensor_scalar(ga[:], ga[:], GELU_C, 1.0, ALU.mult, ALU.add)
                    dv.inc(V.tensor_tensor(gb[:], ga[:], gpp, ALU.mult))
                    A.wait_ge(dv.h, dv.n)
                    ac.inc(A.activation(out=gth[:], in_=gb[:], func=AF.Tanh, scale=GELU_S))
                    V.wait_ge(ac.h, ac.n)
                    V.scalar_tensor_tensor(gb[:], gth[:], 1.0, gpp, ALU.add, ALU.mult)
                    dv.inc(V.tensor_tensor(interT[:, nsub, :], gb[:], upp, ALU.mult))
                    dv_ep[gidx2] = dv.n
                group_pe[g] = pe.n

            for sl in range(8):
                b = sl % 2
                di = seg * 8 + sl
                if di >= 2:
                    SQ.wait_ge(pe.h, pe_dw[di - 2])
                if not cc_d[0]:
                    SQ.wait_ge(cc.h, 3)
                    cc_d[0] = True
                sgt[b].inc(SQ.dma_start(out=gdt[b][:],
                                        in_=Gd[sl, 2048 * seg:2048 * (seg + 1), :].rearrange("(c p) h -> p c h", p=128)), 16)
                T.wait_ge(sgt[b].h, sgt[b].n)
                T.wait_ge(dv.h, dv_ep[seg * 16 + 15])
                for j in range(2):
                    if dv_dacc:
                        T.wait_ge(dv.h, dv_dacc)
                    for nci in range(16):
                        for th in range(2):
                            ins = T.matmul(gp_p[j][:, 512 * th:512 * (th + 1)],
                                           gdt[b][:, nci, 128 * j:128 * (j + 1)],
                                           interT[:, nci, 512 * th:512 * (th + 1)],
                                           start=(nci == 0), stop=(nci == 15))
                    pe.inc(ins)
                    V.wait_ge(pe.h, pe.n)
                    dv.inc(V.tensor_tensor(gacc[:, 2 * sl + j, :], gp_p[j][:],
                                           gacc[:, 2 * sl + j, :], ALU.add))
                    dv_dacc = dv.n
                pe_dw[di] = pe.n


def _combine(nc, SQ, WQ, V, A, T, C, xr, out, sDram, gacc, identt, idenft):
    pe, dv, ac, ou, st, sxr = C["pe"], C["dv"], C["ac"], C["ou"], C["st"], C["xr"]
    with contextlib.ExitStack() as es2:
        E2 = es2.enter_context
        outst = E2(nc.sbuf_tensor("outst", [128, 8, D], F32))
        stile = [E2(nc.sbuf_tensor(f"cst{i}", [128, TOWN], BF16)) for i in range(2)]
        silt = E2(nc.sbuf_tensor("silt", [128, TOWN], F32))
        xrt = [E2(nc.sbuf_tensor(f"xrt{i}", [128, D], F32)) for i in range(2)]
        cpa = E2(nc.sbuf_tensor("cpa", [128, 128], BF16))
        ta_p = E2(nc.psum_tensor("ta_p", [128, 128], BF16))
        tb_p = E2(nc.psum_tensor("tb_p", [128, 128], F32))

        dv_c = 0
        pe_tp2 = 0
        pe_ec = {}
        for ec in range(16):
            b = ec % 2
            if ec >= 2:
                SQ.wait_ge(pe.h, pe_ec[ec - 2])
            st.inc(SQ.dma_start(out=stile[b][:], in_=sDram[128 * ec:128 * (ec + 1), :]), 16)
            st_all = st.n
            if dv_c:
                A.wait_ge(dv.h, dv_c)
            if pe_tp2:
                A.wait_ge(pe.h, pe_tp2)
            ac.inc(A.activation(out=silt[:], in_=gacc[:, ec, :], func=AF.Silu))
            ac_sil = ac.n
            for t in range(8):
                T.wait_ge(st.h, st_all)
                T.wait_ge(ac.h, ac_sil)
                if dv_c:
                    T.wait_ge(dv.h, dv_c)
                T.transpose(ta_p[:], stile[b][:, 128 * t:128 * (t + 1)], identt[:])
                pe.inc(T.transpose(tb_p[:], silt[:, 128 * t:128 * (t + 1)], idenft[:]))
                pe_tp2 = pe.n
                V.wait_ge(pe.h, pe_tp2)
                V.tensor_copy(cpa[:], ta_p[:])
                dv.inc(V.tensor_tensor(outst[:, t, 128 * ec:128 * (ec + 1)], tb_p[:], cpa[:], ALU.mult))
                dv_c = dv.n
            pe_ec[ec] = pe.n

        dv_xr = {}
        for t in range(8):
            b = t % 2
            if t >= 2:
                SQ.wait_ge(dv.h, dv_xr[t - 2])
            sxr[b].inc(SQ.dma_start(out=xrt[b][:], in_=xr[128 * t:128 * (t + 1), :]), 16)
            V.wait_ge(sxr[b].h, sxr[b].n)
            dv.inc(V.tensor_tensor(outst[:, t, :], outst[:, t, :], xrt[b][:], ALU.add))
            dv_xr[t] = dv.n
            WQ.wait_ge(dv.h, dv.n)
            C["ou"].inc(WQ.dma_start(out=out[128 * t:128 * (t + 1), :], in_=outst[:, t, :]), 16)


_NC_CACHE = {}


def _get_nc():
    key = (DEQ_SLICES, STAGE, LNKV_SUB, DBG)
    if key not in _NC_CACHE:
        _NC_CACHE[key] = build()
    return _NC_CACHE[key]


def _rope_tables():
    i = np.arange(1, 65, dtype=np.float32)
    thetas = (np.float32(10000.0) ** (np.float32(-2.0) * (i - np.float32(1.0)) / np.float32(128.0))).astype(np.float32)
    pos = np.arange(1, TKV + 1, dtype=np.float32)
    ang = pos[:, None] * thetas[None, :]
    c = np.cos(ang).astype(np.float32)
    s = np.sin(ang).astype(np.float32)
    cosF = np.ascontiguousarray(np.concatenate([c, c], axis=1).T)
    sinH = np.ascontiguousarray(np.concatenate([s, -s], axis=1).T)
    return cosF, sinH


def prepare_in_maps(L0_attn_x, L0_attn_Wq, L0_attn_bq, L0_attn_Wk, L0_attn_bk, L0_attn_Wv,
                    L0_attn_bv, L0_attn_Wo, L0_attn_bo, L0_ffn_gate_W_packed, L0_ffn_gate_scales,
                    L0_ffn_up_W_packed, L0_ffn_up_scales, L0_ffn_down_W_packed, L0_ffn_down_scales):
    x = np.asarray(L0_attn_x, dtype=np.float32)
    wq = np.ascontiguousarray(np.asarray(L0_attn_Wq, np.float32).T)
    wk = np.ascontiguousarray(np.asarray(L0_attn_Wk, np.float32).T)
    wv = np.ascontiguousarray(np.asarray(L0_attn_Wv, np.float32).T)
    wo = np.ascontiguousarray(np.asarray(L0_attn_Wo, np.float32).T)
    bqv = np.ascontiguousarray(np.asarray(L0_attn_bq, np.float32).reshape(16, 128).T)
    bkv = np.ascontiguousarray(np.asarray(L0_attn_bk, np.float32).reshape(4, 128).T)
    bvv = np.ascontiguousarray(np.asarray(L0_attn_bv, np.float32).reshape(4, 128).T)
    bov = np.ascontiguousarray(np.asarray(L0_attn_bo, np.float32).reshape(16, 128).T)
    gpk = np.ascontiguousarray(np.asarray(L0_ffn_gate_W_packed, np.int32))
    gsc = np.ascontiguousarray(np.asarray(L0_ffn_gate_scales, np.float32))
    upkk = np.ascontiguousarray(np.asarray(L0_ffn_up_W_packed, np.int32))
    upsc = np.ascontiguousarray(np.asarray(L0_ffn_up_scales, np.float32))
    dpk = np.ascontiguousarray(np.asarray(L0_ffn_down_W_packed, np.int32))
    dssc = np.ascontiguousarray(np.asarray(L0_ffn_down_scales, np.float32))

    cosF, sinH = _rope_tables()
    iden = np.eye(128, dtype=np.float32).astype(ml_dtypes.bfloat16)
    idenf32 = np.eye(128, dtype=np.float32)
    onescol = np.ones((128, 1), np.float32).astype(ml_dtypes.bfloat16)
    onesf32 = np.ones((1, 128), np.float32)

    in_maps = []
    for c in range(8):
        bsel = c // 2
        h = c % 2
        own0 = 1024 * h
        oth0 = 1024 * (1 - h)
        idx = np.concatenate([np.arange(own0, own0 + 1024), np.arange(oth0, oth0 + 1024)])
        xb = x[bsel]
        in_maps.append({
            "xT": np.ascontiguousarray(xb.T[:, idx]),
            "xr": np.ascontiguousarray(xb[own0:own0 + 1024]),
            "wqT": wq, "wkT": wk, "wvT": wv, "woT": wo,
            "bq": bqv, "bk": bkv, "bv": bvv, "bo": bov,
            "gp": np.ascontiguousarray(gpk[2048 * c:2048 * (c + 1)]),
            "gs": np.ascontiguousarray(gsc[2048 * c:2048 * (c + 1)]),
            "up": np.ascontiguousarray(upkk[2048 * c:2048 * (c + 1)]),
            "us": np.ascontiguousarray(upsc[2048 * c:2048 * (c + 1)]),
            "dp": np.ascontiguousarray(dpk[256 * c:256 * (c + 1)]),
            "ds": np.ascontiguousarray(dssc[256 * c:256 * (c + 1)]),
            "qcos": np.ascontiguousarray(cosF[:, own0:own0 + 1024]),
            "qsin": np.ascontiguousarray(sinH[:, own0:own0 + 1024]),
            "kcos": np.ascontiguousarray(cosF[:, idx]),
            "ksin": np.ascontiguousarray(sinH[:, idx]),
            "posq": np.ascontiguousarray(np.arange(own0, own0 + 1024, dtype=np.float32).reshape(8, 128).T),
            "poskcol": np.ascontiguousarray(idx.astype(np.float32).reshape(16, 128).T),
            "poskk": idx.astype(np.float32).reshape(1, TKV),
            "iden": iden, "idenf": idenf32, "onescol": onescol, "onesf32": onesf32,
        })
    return in_maps


def kernel(**inputs):
    in_maps = prepare_in_maps(**inputs)
    nc = _get_nc()
    res = run_bass_kernel_spmd(nc, in_maps, core_ids=list(range(8)))
    outp = np.empty((4, 2048, 2048), np.float32)
    for c in range(8):
        bsel = c // 2
        h = c % 2
        outp[bsel, 1024 * h:1024 * (h + 1)] = res.results[c]["out"]
    return outp


# version 12 (final)
# speedup vs baseline: 1.3668x; 1.0060x over previous
import contextlib
import os
import tempfile

os.environ["NEURON_COMPILE_CACHE_URL"] = tempfile.mkdtemp(prefix="ncc_")

import numpy as np
import ml_dtypes
import concourse.bass as bass
import concourse.mybir as mybir
from concourse.bass_utils import run_bass_kernel_spmd

F32 = mybir.dt.float32
BF16 = mybir.dt.bfloat16
I32 = mybir.dt.int32
AF = mybir.ActivationFunctionType
ALU = mybir.AluOpType
AX = mybir.AxisListType.X

D = 2048
DK = 512
HD = 128
TOWN = 1024
TKV = 2048
MLP = 16384
NEG = -1.0e30
GELU_C = 0.044715
GELU_S = 0.7978845608028654
ISQ = 0.08838834764831845
EPS = 1e-5

DEQ_SLICES = 1
STAGE = 5
LNKV_SUB = 7
DBG = ""


class Ctr:
    def __init__(self, es, nc, name):
        self.h = es.enter_context(nc.semaphore(name))
        self.n = 0

    def inc(self, instr, k=1):
        instr.then_inc(self.h, k)
        self.n += k
        return self.n


def build():
    nc = bass.Bass("TRN2", target_bir_lowering=False)

    def din(name, shape, dt):
        return nc.dram_tensor(name, shape, dt, kind="ExternalInput").ap()

    xT = din("xT", [D, TKV], F32)
    xr = din("xr", [TOWN, D], F32)
    wqT = din("wqT", [D, D], F32)
    wkT = din("wkT", [D, DK], F32)
    wvT = din("wvT", [D, DK], F32)
    woT = din("woT", [D, D], F32)
    bq = din("bq", [128, 16], F32)
    bk = din("bk", [128, 4], F32)
    bv = din("bv", [128, 4], F32)
    bo = din("bo", [128, 16], F32)
    gp = din("gp", [2048 * DEQ_SLICES, 1024], I32)
    gs = din("gs", [2048 * DEQ_SLICES, 64], F32)
    upk = din("up", [2048 * DEQ_SLICES, 1024], I32)
    ups = din("us", [2048 * DEQ_SLICES, 64], F32)
    dp = din("dp", [256 * DEQ_SLICES, 8192], I32)
    dsc = din("ds", [256 * DEQ_SLICES, 512], F32)
    qcos = din("qcos", [HD, TOWN], F32)
    qsin = din("qsin", [HD, TOWN], F32)
    kcos = din("kcos", [HD, TKV], F32)
    ksin = din("ksin", [HD, TKV], F32)
    posq = din("posq", [128, 8], F32)
    poskcol = din("poskcol", [128, 16], F32)
    poskk = din("poskk", [1, TKV], F32)
    iden = din("iden", [128, 128], BF16)
    idenf = din("idenf", [128, 128], F32)
    onescol = din("onescol", [128, 1], BF16)
    onesf32 = din("onesf32", [1, 128], F32)

    out = nc.dram_tensor("out", [TOWN, D], F32, kind="ExternalOutput").ap()

    wk_ = "ExternalOutput" if "w" in DBG else "Internal"
    if "k" in DBG:
        dbg_xn = nc.dram_tensor("dbg_xn", [128, 16, TOWN], BF16, kind="ExternalOutput").ap()
        dbg_kt = nc.dram_tensor("dbg_kt", [128, 4, TKV], BF16, kind="ExternalOutput").ap()
        dbg_vh = nc.dram_tensor("dbg_vh", [128, 4, 16, 128], BF16, kind="ExternalOutput").ap()
    if "a" in DBG:
        nc._dbg_a = dict(
            qt=nc.dram_tensor("dbg_qt", [128, TOWN], BF16, kind="ExternalOutput").ap(),
            pb=nc.dram_tensor("dbg_pb", [128, TKV], BF16, kind="ExternalOutput").ap(),
            ell=nc.dram_tensor("dbg_ell", [128, 1], F32, kind="ExternalOutput").ap(),
            s=nc.dram_tensor("dbg_s", [128, TKV], F32, kind="ExternalOutput").ap(),
            pb2=nc.dram_tensor("dbg_pb2", [128, TKV], BF16, kind="ExternalOutput").ap(),
            msk=nc.dram_tensor("dbg_msk", [128, 8, TKV], BF16, kind="ExternalOutput").ap(),
            yt=nc.dram_tensor("dbg_yt", [128, 16, TOWN], BF16, kind="ExternalOutput").ap(),
        )
    sk_ = "ExternalOutput" if "s" in DBG else "Internal"
    ggcF = nc.dram_tensor("ggc", [D, 2048], BF16, kind="Internal").ap()
    gucF = nc.dram_tensor("guc", [D, 2048], BF16, kind="Internal").ap()
    gdcF = nc.dram_tensor("gdc", [MLP, 256], BF16, kind="Internal").ap()
    ggc = ggcF.rearrange("(a k) n -> a k n", a=1)
    guc = gucF.rearrange("(a k) n -> a k n", a=1)
    gdc = gdcF.rearrange("(a k) n -> a k n", a=1)
    GgF = nc.dram_tensor("Gg", [8 * D, 2048], BF16, kind=wk_, addr_space="Shared").ap()
    GuF = nc.dram_tensor("Gu", [8 * D, 2048], BF16, kind=wk_, addr_space="Shared").ap()
    GdF = nc.dram_tensor("Gd", [8 * MLP, 256], BF16, kind=wk_, addr_space="Shared").ap()
    Gg = GgF.rearrange("(s k) n -> s k n", s=8)
    Gu = GuF.rearrange("(s k) n -> s k n", s=8)
    Gd = GdF.rearrange("(s k) n -> s k n", s=8)
    sDram = nc.dram_tensor("sDram", [D, TOWN], BF16, kind=sk_).ap()

    es = contextlib.ExitStack()
    with es:
        E = es.enter_context
        xnT = E(nc.sbuf_tensor("xnT", [128, 16, TOWN], BF16))
        identt = E(nc.sbuf_tensor("identt", [128, 128], BF16))
        idenft = E(nc.sbuf_tensor("idenft", [128, 128], F32))
        onescolt = E(nc.sbuf_tensor("onescolt", [128, 1], BF16))
        onesf32t = E(nc.sbuf_tensor("onesf32t", [1, 128], F32))
        posqt = E(nc.sbuf_tensor("posqt", [128, 8], F32))
        poskcolt = E(nc.sbuf_tensor("poskcolt", [128, 16], F32))
        bqt = E(nc.sbuf_tensor("bqt", [128, 16], F32))
        bot = E(nc.sbuf_tensor("bot", [128, 16], F32))
        bkt = E(nc.sbuf_tensor("bkt", [128, 4], F32))
        bvt = E(nc.sbuf_tensor("bvt", [128, 4], F32))
        cneg4 = E(nc.sbuf_tensor("cneg4", [128, 1], F32))

        C = {}
        for name in ["gd", "pe", "dv", "ac", "ou", "cn", "x", "ms", "st"]:
            C[name] = Ctr(es, nc, name)
        for name in ["ws", "gg", "gt", "xr"]:
            C[name] = [Ctr(es, nc, name + "0"), Ctr(es, nc, name + "1")]
        C["pk"] = [Ctr(es, nc, f"pk{i}") for i in range(4)]

        gd, pe, dv, ac, ou = C["gd"], C["pe"], C["dv"], C["ac"], C["ou"]

        blk = es.enter_context(nc.Block())
        SQ = nc.sync
        WQ = nc.scalar
        V = nc.vector
        A = nc.scalar
        T = nc.tensor
        ALL_E = (SQ, V, T, WQ, nc.gpsimd)

        def flat_ctrs():
            r = []
            for v in C.values():
                r.extend(v if isinstance(v, list) else [v])
            return r

        def barrier():
            vals = [(c, c.n) for c in flat_ctrs()]
            for e in ALL_E:
                for c, v in vals:
                    if v > 0:
                        e.wait_ge(c.h, v)

        cn = C["cn"]
        cn.inc(SQ.dma_start(out=identt[:], in_=iden), 16)
        cn.inc(SQ.dma_start(out=idenft[:], in_=idenf), 16)
        cn.inc(SQ.dma_start(out=onescolt[:], in_=onescol), 16)
        cn.inc(SQ.dma_start(out=onesf32t[:], in_=onesf32), 16)
        cn.inc(SQ.dma_start(out=posqt[:], in_=posq), 16)
        cn.inc(SQ.dma_start(out=poskcolt[:], in_=poskcol), 16)
        cn.inc(SQ.dma_start(out=bqt[:], in_=bq), 16)
        cn.inc(SQ.dma_start(out=bot[:], in_=bo), 16)
        cn.inc(SQ.dma_start(out=bkt[:], in_=bk), 16)
        cn.inc(SQ.dma_start(out=bvt[:], in_=bv), 16)
        dv_ = C["dv"]
        dv_.inc(V.memset(cneg4[:], -4.0))
        for e in (V, T, WQ, nc.gpsimd):
            e.wait_ge(cn.h, cn.n)
            e.wait_ge(dv_.h, 1)

        cc = Ctr(es, nc, "ccs")
        nc._cc = cc
        if STAGE >= 1:
            G_ = nc.gpsimd
            rg = [[0, 1, 2, 3, 4, 5, 6, 7]]
            _dequant(nc, SQ, WQ, V, T, C, identt, gp, gs, ggc, 1.0, "g")
            _dequant(nc, SQ, WQ, V, T, C, identt, upk, ups, guc, 1.0, "u")
            _dequant(nc, SQ, WQ, V, T, C, identt, dp, dsc, gdc, 0.5, "d")

            def issue_gathers():
                G_.wait_ge(gd.h, gd.n)
                G_.wait_ge(C["x"].h, C["x"].n)
                cc.inc(G_.collective_compute("AllGather", ALU.bypass,
                                             replica_groups=rg,
                                             ins=[ggcF.opt()], outs=[GgF.opt()]), 1)
                cc.inc(G_.collective_compute("AllGather", ALU.bypass,
                                             replica_groups=rg,
                                             ins=[gucF.opt()], outs=[GuF.opt()]), 1)
                cc.inc(G_.collective_compute("AllGather", ALU.bypass,
                                             replica_groups=rg,
                                             ins=[gdcF.opt()], outs=[GdF.opt()]), 1)
        else:
            def issue_gathers():
                pass
        SQ.wait_ge(gd.h, gd.n)

        if STAGE >= 2:
            with contextlib.ExitStack() as es23:
                E23 = es23.enter_context
                poskkb = E23(nc.sbuf_tensor("poskkb", [128, TKV], F32))
                qcost = E23(nc.sbuf_tensor("qcost", [128, TOWN], F32))
                qsint = E23(nc.sbuf_tensor("qsint", [128, TOWN], F32))
                ms = C["ms"]
                ms.inc(SQ.dma_start(out=poskkb[:], in_=poskk.partition_broadcast(128)), 16)
                ms.inc(SQ.dma_start(out=qcost[:], in_=qcos), 16)
                ms.inc(SQ.dma_start(out=qsint[:], in_=qsin), 16)
                kvt = _ln_kv(nc, es23, SQ, V, A, T, C,
                             xT, wkT, wvT, bkt, bvt, kcos, ksin,
                             xnT, onescolt, onesf32t, identt)
                issue_gathers()
                barrier()
                if "k" in DBG:
                    ou.inc(WQ.dma_start(out=dbg_xn, in_=xnT[:]), 16)
                    ou.inc(WQ.dma_start(out=dbg_kt, in_=kvt["KT"][:]), 16)
                    ou.inc(WQ.dma_start(out=dbg_vh, in_=kvt["Vh"][:]), 16)
                    for e in ALL_E:
                        e.wait_ge(ou.h, ou.n)
                if STAGE >= 3:
                    _attention(nc, es23, SQ, WQ, V, A, T, C,
                               wqT, woT, bqt, bot, qcost, qsint, poskcolt, poskkb,
                               xnT, sDram, kvt, identt, cneg4, onescolt, onesf32t)
                barrier()

        if STAGE >= 4:
            with contextlib.ExitStack() as es45:
                gacc = es45.enter_context(nc.sbuf_tensor("gacc", [128, 16, TOWN], F32))
                _ffn(nc, SQ, V, A, T, C, Gg, Gu, Gd, xnT, gacc, nc._cc)
                barrier()
                if STAGE >= 5:
                    _combine(nc, SQ, WQ, V, A, T, C, xr, out, sDram, gacc, identt, idenft)

        SQ.wait_ge(ou.h, ou.n)

    return nc


def _dequant(nc, SQ, WQ, V, T, C, identt, packed, scales, Gout, smul, tag):
    gd, pe, dv = C["gd"], C["pe"], C["dv"]
    pk = C["pk"]
    KK = packed.shape[1]
    NB = scales.shape[1]
    n_loc = Gout.shape[2]
    n_rtiles = n_loc // 128
    n_cchunk = KK // 512
    nslices = packed.shape[0] // n_loc
    K_win = Gout.shape[1] // n_cchunk

    with contextlib.ExitStack() as es:
        SQ.wait_ge(gd.h, gd.n)
        E = es.enter_context
        pt = [E(nc.sbuf_tensor(f"dqp{tag}{i}", [128, 512], I32)) for i in range(4)]
        sct = [E(nc.sbuf_tensor(f"dqs{tag}{i}", [128, NB], F32)) for i in range(4)]
        sce = [E(nc.sbuf_tensor(f"dqe{tag}{i}", [128, NB], F32)) for i in range(4)]
        mtile = E(nc.sbuf_tensor(f"dqm{tag}", [128, 1024], I32))
        stile = E(nc.sbuf_tensor(f"dqt{tag}", [128, 1024], I32))
        btile = E(nc.sbuf_tensor(f"dqb{tag}", [128, 1024], I32))
        wtile = E(nc.sbuf_tensor(f"dqw{tag}", [128, 1024], BF16))
        rb = E(nc.sbuf_tensor(f"dqr{tag}", [128, 8, n_loc], BF16))
        tp = [E(nc.psum_tensor(f"dqx{tag}{i}", [128, 512], BF16)) for i in range(2)]

        dv_decode = {}
        dv_evac = {}
        pe_tp = {}
        gd_grp = {}

        u = 0
        for sl in range(nslices):
            for c in range(n_cchunk):
                for r in range(n_rtiles):
                    b = u % 4
                    bc = u % 4
                    if u >= 4:
                        SQ.wait_ge(dv.h, dv_decode[u - 4])
                    row0 = sl * n_loc + r * 128
                    pk[bc].inc(SQ.dma_start(out=pt[b][:], in_=packed[row0:row0 + 128, c * 512:(c + 1) * 512]), 16)
                    pk[bc].inc(SQ.dma_start(out=sct[b][:], in_=scales[row0:row0 + 128, :]), 16)

                    V.wait_ge(pk[bc].h, pk[bc].n)
                    if u >= 1:
                        V.wait_ge(pe.h, pe_tp[u - 1])
                    if r == 0:
                        g_prev = (sl * n_cchunk + c) - 1
                        if g_prev >= 0:
                            V.wait_ge(gd.h, gd_grp[g_prev])
                    V.tensor_scalar(sce[b][:], sct[b][:], float(smul * (2.0 ** 126)), None, ALU.mult)
                    p = pt[b][:]
                    V.tensor_scalar(mtile[:, 0:512], p, 7, 22, ALU.bitwise_and, ALU.logical_shift_left)
                    V.tensor_scalar(mtile[:, 512:1024], p, 0x70, 18, ALU.bitwise_and, ALU.logical_shift_left)
                    V.tensor_scalar(stile[:, 0:512], p, 8, 28, ALU.bitwise_and, ALU.logical_shift_left)
                    V.tensor_scalar(stile[:, 512:1024], p, 0x80, 24, ALU.bitwise_and, ALU.logical_shift_left)
                    V.tensor_tensor(btile[:, 0:512], mtile[:, 0:512], stile[:, 0:512], ALU.bitwise_or)
                    V.tensor_tensor(btile[:, 512:1024], mtile[:, 512:1024], stile[:, 512:1024], ALU.bitwise_or)
                    scb = sce[b][:, 32 * c:32 * c + 32].unsqueeze(2).broadcast_to([128, 32, 16])
                    bl = btile[:, 0:512].bitcast(F32).rearrange("p (b j) -> p b j", b=32)
                    bh = btile[:, 512:1024].bitcast(F32).rearrange("p (b j) -> p b j", b=32)
                    V.tensor_tensor(wtile[:, 0:512].rearrange("p (b j) -> p b j", b=32), bl, scb, ALU.mult)
                    dv.inc(V.tensor_tensor(wtile[:, 512:1024].rearrange("p (b j) -> p b j", b=32), bh, scb, ALU.mult))
                    dv_decode[u] = dv.n

                    T.wait_ge(dv.h, dv_decode[u])
                    if u >= 1:
                        T.wait_ge(dv.h, dv_evac[u - 1])
                    for plane in range(2):
                        for s4 in range(4):
                            ins = T.transpose(tp[plane][:, 128 * s4:128 * (s4 + 1)],
                                              wtile[:, 512 * plane + 128 * s4:512 * plane + 128 * (s4 + 1)],
                                              identt[:])
                        pe.inc(ins)
                    pe_tp[u] = pe.n

                    V.wait_ge(pe.h, pe_tp[u])
                    V.tensor_copy(rb[:, 0:4, r * 128:(r + 1) * 128],
                                  tp[0][:].rearrange("p (s j) -> p s j", s=4))
                    dv.inc(V.tensor_copy(rb[:, 4:8, r * 128:(r + 1) * 128],
                                         tp[1][:].rearrange("p (s j) -> p s j", s=4)))
                    dv_evac[u] = dv.n
                    u += 1

                WQ.wait_ge(dv.h, dv_evac[u - 1])
                for plane in range(2):
                    for s4 in range(4):
                        base = K_win * c + 256 * s4 + plane
                        gd.inc(WQ.dma_start(out=Gout[sl, base:base + 255:2, :],
                                            in_=rb[:, plane * 4 + s4, :]), 16)
                gd_grp[sl * n_cchunk + c] = gd.n


def _ln_kv(nc, es, SQ, V, A, T, C, xT, wkT, wvT, bkt, bvt, kcos, ksin,
           xnT, onescolt, onesf32t, identt):
    pe, dv, ac, ms, sx, ws = C["pe"], C["dv"], C["ac"], C["ms"], C["x"], C["ws"]
    E = es.enter_context
    KT = E(nc.sbuf_tensor("KT", [128, 4, TKV], BF16))
    Vh = E(nc.sbuf_tensor("Vh", [128, 4, 16, 128], BF16))

    with contextlib.ExitStack() as es2:
        E2 = es2.enter_context
        kcost = E2(nc.sbuf_tensor("kcost", [128, TKV], F32))
        ksint = E2(nc.sbuf_tensor("ksint", [128, TKV], F32))
        wk16 = E2(nc.sbuf_tensor("wk16", [128, 16, DK], BF16))
        wv16 = E2(nc.sbuf_tensor("wv16", [128, 16, DK], BF16))
        xtc = E2(nc.sbuf_tensor("xtc", [128, 16, 512], F32))
        xn_t = E2(nc.sbuf_tensor("xn_t", [128, 16, 512], BF16))
        x16 = [E2(nc.sbuf_tensor(f"x16{i}", [128, 512], BF16)) for i in range(2)]
        x2 = [E2(nc.sbuf_tensor(f"x2{i}", [128, 512], BF16)) for i in range(2)]
        wstage = [E2(nc.sbuf_tensor(f"wst{i}", [128, 512], F32)) for i in range(2)]
        srow = E2(nc.sbuf_tensor("srow", [1, 512], F32))
        qrow = E2(nc.sbuf_tensor("qrow", [1, 512], F32))
        trow = E2(nc.sbuf_tensor("trow", [1, 512], F32))
        vrow = E2(nc.sbuf_tensor("vrow", [1, 512], F32))
        t1 = E2(nc.sbuf_tensor("lnt1", [128, 512], F32))
        kraw = E2(nc.sbuf_tensor("kraw", [128, 512], F32))
        krot = E2(nc.sbuf_tensor("krot", [128, 512], F32))
        kcs = E2(nc.sbuf_tensor("kcs", [128, 512], F32))
        vtmp = E2(nc.sbuf_tensor("vtmp", [128, 512], BF16))
        sum_p = E2(nc.psum_tensor("sum_p", [1, 512], F32))
        sq_p = E2(nc.psum_tensor("sq_p", [1, 512], F32))
        mb_p = E2(nc.psum_tensor("mb_p", [128, 512], F32))
        rb_p = E2(nc.psum_tensor("rb_p", [128, 512], F32))
        kv_p = E2(nc.psum_tensor("kv_p", [128, 512], F32))
        vv_p = E2(nc.psum_tensor("vv_p", [128, 512], F32))
        vt_p = E2(nc.psum_tensor("vt_p", [128, 512], BF16))

        ms.inc(SQ.dma_start(out=kcost[:], in_=kcos), 16)
        ms.inc(SQ.dma_start(out=ksint[:], in_=ksin), 16)
        MS_ALL = ms.n

        casts = {}
        for w in range(2):
            src = wkT if w == 0 else wvT
            dst = wk16 if w == 0 else wv16
            for kc in range(16):
                i = w * 16 + kc
                b = i % 2
                if i >= 2:
                    SQ.wait_ge(dv.h, casts[i - 2])
                ws[b].inc(SQ.dma_start(out=wstage[b][:], in_=src[128 * kc:128 * (kc + 1), :]), 16)
                V.wait_ge(ws[b].h, ws[b].n)
                dv.inc(V.tensor_copy(dst[:, kc, :], wstage[b][:]))
                casts[i] = dv.n
        V.wait_ge(ms.h, MS_ALL)

        dv_norm = {}
        pe_kv = {}
        dv_kv = dv.n
        for tc in range(4 if LNKV_SUB >= 2 else 0):
            own = tc < 2
            if tc >= 1:
                SQ.wait_ge(dv.h, dv_norm[tc - 1])
            for kc in range(16):
                sx.inc(SQ.dma_start(out=xtc[:, kc, :],
                                    in_=xT[128 * kc:128 * (kc + 1), 512 * tc:512 * (tc + 1)]), 16)
            sx_all = sx.n

            pe_stat = {}
            for kc in range(16):
                bb = kc % 2
                V.wait_ge(sx.h, sx_all)
                A.wait_ge(sx.h, sx_all)
                if kc >= 2:
                    V.wait_ge(pe.h, pe_stat[kc - 2])
                    A.wait_ge(pe.h, pe_stat[kc - 2])
                dv.inc(V.tensor_copy(x16[bb][:], xtc[:, kc, :]))
                ac.inc(A.activation(out=x2[bb][:], in_=xtc[:, kc, :], func=AF.Square))
                T.wait_ge(dv.h, dv.n)
                T.wait_ge(ac.h, ac.n)
                T.matmul(sum_p[:], onescolt[:], x16[bb][:], start=(kc == 0), stop=(kc == 15))
                pe.inc(T.matmul(sq_p[:], onescolt[:], x2[bb][:], start=(kc == 0), stop=(kc == 15)))
                pe_stat[kc] = pe.n
            pe_stats = pe.n

            if LNKV_SUB < 3:
                dv_norm[tc] = dv.n
                pe_kv[tc] = pe.n
                continue
            V.wait_ge(pe.h, pe_stats)
            V.tensor_scalar(srow[:], sum_p[:], 1.0 / D, None, ALU.mult)
            V.tensor_scalar(qrow[:], sq_p[:], 1.0 / D, None, ALU.mult)
            V.tensor_copy(t1[0:1, :], qrow[:])
            V.tensor_tensor(trow[:], srow[:], srow[:], ALU.mult)
            V.tensor_copy(t1[0:1, :], qrow[:])
            V.scalar_tensor_tensor(vrow[:], qrow[:], EPS, trow[:], ALU.add, ALU.subtract)
            dv.inc(V.tensor_copy(t1[0:1, :], qrow[:]))
            A.wait_ge(dv.h, dv.n)
            ac.inc(A.activation(out=trow[:], in_=vrow[:], func=AF.Sqrt))
            V.wait_ge(ac.h, ac.n)
            V.reciprocal(vrow[:], trow[:])
            dv.inc(V.tensor_copy(t1[0:1, :], qrow[:]))
            T.wait_ge(dv.h, dv.n)
            T.matmul(mb_p[:], onesf32t[:], srow[:], start=True, stop=True)
            pe.inc(T.matmul(rb_p[:], onesf32t[:], vrow[:], start=True, stop=True))
            pe_bc = pe.n

            if LNKV_SUB < 4:
                dv_norm[tc] = dv.n
                pe_kv[tc] = pe.n
                continue
            V.wait_ge(pe.h, pe_bc)
            if tc >= 3:
                V.wait_ge(pe.h, pe_kv[tc - 1])
            for kc in range(16):
                V.tensor_tensor(t1[:], xtc[:, kc, :], mb_p[:], ALU.subtract)
                dst = xnT[:, kc, 512 * tc:512 * (tc + 1)] if own else xn_t[:, kc, :]
                dv.inc(V.tensor_tensor(dst, t1[:], rb_p[:], ALU.mult))
            dv_norm[tc] = dv.n

            def xsrc(kc, tc=tc, own=own):
                return xnT[:, kc, 512 * tc:512 * (tc + 1)] if own else xn_t[:, kc, :]

            for kvh in range(4 if LNKV_SUB >= 5 else 0):
                T.wait_ge(dv.h, dv_norm[tc])
                T.wait_ge(dv.h, dv_kv)
                for kc in range(16):
                    T.matmul(kv_p[:], wk16[:, kc, 128 * kvh:128 * (kvh + 1)], xsrc(kc),
                             start=(kc == 0), stop=(kc == 15))
                for kc in range(16):
                    ins = T.matmul(vv_p[:], wv16[:, kc, 128 * kvh:128 * (kvh + 1)], xsrc(kc),
                                   start=(kc == 0), stop=(kc == 15))
                pe.inc(ins)
                pe_proj = pe.n
                if LNKV_SUB < 6:
                    dv_kv = dv.n
                    continue

                V.wait_ge(pe.h, pe_proj)
                cs = slice(512 * tc, 512 * (tc + 1))
                nops = (LNKV_SUB - 60) if 60 <= LNKV_SUB < 70 else 6
                ops = [
                    lambda: V.tensor_scalar(kraw[:], kv_p[:], bkt[:, kvh:kvh + 1], None, ALU.add),
                    lambda: V.tensor_tensor(krot[0:64, :], kraw[64:128, :], ksint[64:128, cs], ALU.mult),
                    lambda: V.tensor_tensor(krot[64:128, :], kraw[0:64, :], ksint[0:64, cs], ALU.mult),
                    lambda: V.tensor_tensor(kcs[:], kraw[:], kcost[:, cs], ALU.mult),
                    lambda: V.tensor_tensor(KT[:, kvh, cs], kcs[:], krot[:], ALU.add),
                    lambda: V.tensor_scalar(vtmp[:], vv_p[:], bvt[:, kvh:kvh + 1], None, ALU.add),
                ]
                last = None
                for _i in range(nops):
                    last = ops[_i]()
                if last is None:
                    last = V.engine_nop() if hasattr(V, "engine_nop") else V.nop()
                dv.inc(last)
                dv_kv = dv.n
                if LNKV_SUB < 7:
                    continue

                T.wait_ge(dv.h, dv_kv)
                for j in range(4):
                    ins = T.transpose(vt_p[:, 128 * j:128 * (j + 1)],
                                      vtmp[:, 128 * j:128 * (j + 1)], identt[:])
                pe.inc(ins)
                V.wait_ge(pe.h, pe.n)
                dv.inc(V.tensor_copy(Vh[:, kvh, 4 * tc:4 * tc + 4, :],
                                     vt_p[:].rearrange("p (s j) -> p s j", s=4)))
                dv_kv = dv.n
            pe_kv[tc] = pe.n
    return dict(KT=KT, Vh=Vh)


def _attention(nc, es, SQ, WQ, V, A, T, C, wqT, woT, bqt, bot,
               qcost, qsint, poskcolt, poskkb, xnT, sDram, kvt, identt, cneg4,
               onescolt, onesf32t):
    gd, pe, dv, ac, ws = C["gd"], C["pe"], C["dv"], C["ac"], C["ws"]
    KT, Vh = kvt["KT"], kvt["Vh"]
    E = es.enter_context
    YT = E(nc.sbuf_tensor("YT", [128, 16, TOWN], BF16))
    mask01 = E(nc.sbuf_tensor("mask01", [128, 16, TOWN], BF16))

    with contextlib.ExitStack() as es2:
        E2 = es2.enter_context
        wstage = [E2(nc.sbuf_tensor(f"aws{i}", [128, 16, 128], F32)) for i in range(2)]
        w16 = [E2(nc.sbuf_tensor(f"aw16{i}", [128, 16, 128], BF16)) for i in range(2)]
        qraw = E2(nc.sbuf_tensor("qraw", [128, TOWN], F32))
        qrot = E2(nc.sbuf_tensor("qrot", [128, TOWN], F32))
        qcs = E2(nc.sbuf_tensor("qcs", [128, TOWN], F32))
        QT = E2(nc.sbuf_tensor("QT", [128, 2, TOWN], BF16))
        NSL = 6
        Pe = E2(nc.sbuf_tensor("Pe", [128, NSL, 512], BF16))
        Pm = E2(nc.sbuf_tensor("Pm", [128, NSL, 512], BF16))
        elrec = E2(nc.sbuf_tensor("elrec", [1, TOWN], F32))
        ebs = E2(nc.sbuf_tensor("ebs", [128, 2, 512], F32))
        stmp = E2(nc.sbuf_tensor("stmp", [128, TOWN], BF16))
        st_p = [E2(nc.psum_tensor(f"st_p{i}", [128, 512], F32)) for i in range(3)]
        y_p = [E2(nc.psum_tensor(f"y_p{i}", [128, 512], F32)) for i in range(2)]
        eb_p = E2(nc.psum_tensor("eb_p", [128, 512], F32))
        qp_p = E2(nc.psum_tensor("qp_p", [128, 1024], F32))

        for s in range(16):
            dv.inc(V.tensor_scalar(mask01[:, s, :], poskkb[:, 0:TOWN],
                                   poskcolt[:, s:s + 1], None, ALU.is_ge))
        dv_mask_built = dv.n

        ac_exp = {}
        dv_mask = {}
        pe_st = {}
        pe_yell = {}
        dv_qe = {}
        st_last_of_qh = {}
        dv_ynorm = {0: 0, 1: 0}
        dv_tail = [0]
        pe_rank1 = {0: 0, 1: 0}
        pe_w = {}
        from collections import deque
        pending = deque()

        def flush_pending():
            pg, pqh, phalf, ps, pslot = pending.popleft()
            pqs = slice(512 * phalf, 512 * (phalf + 1))
            pkvh = pqh // 4
            T.wait_ge(dv.h, dv_mask[pg])
            if ps == 0:
                T.wait_ge(dv.h, dv_tail[0])
            T.matmul(y_p[phalf][:], Vh[:, pkvh, ps, :], Pm[:, pslot, :],
                     start=(ps == 0), stop=(ps == 15))
            pe.inc(T.matmul(eb_p[0:1, :], onescolt[:], Pm[:, pslot, :],
                            start=(ps == 0), stop=(ps == 15)))
            pe_yell[pg] = pe.n
            if ps == 15:
                V.wait_ge(pe.h, pe_yell[pg])
                V.wait_ge(pe.h, pe_rank1[phalf])
                dv.inc(V.reciprocal(elrec[0:1, pqs], eb_p[0:1, :]))
                dv_rec = dv.n
                T.wait_ge(dv.h, dv_rec)
                pe.inc(T.matmul(eb_p[:], onesf32t[:], elrec[0:1, pqs],
                                start=True, stop=True))
                pe_rank1[phalf] = pe.n
                V.wait_ge(pe.h, pe_rank1[phalf])
                V.tensor_copy(ebs[:, phalf, :], eb_p[:])
                dv.inc(V.tensor_tensor(YT[:, pqh, pqs], y_p[phalf][:],
                                       ebs[:, phalf, :], ALU.mult))
                dv_ynorm[phalf] = dv.n
                dv_tail[0] = dv.n

        g = 0
        for qh in range(16):
            kvh = qh // 4
            b = qh % 2
            if qh >= 2:
                SQ.wait_ge(pe.h, pe_w[qh - 2])
            ws[b].inc(SQ.dma_start(out=wstage[b][:],
                                   in_=wqT[:, 128 * qh:128 * (qh + 1)].rearrange("(c p) e -> p c e", p=128)), 16)
            V.wait_ge(ws[b].h, ws[b].n)
            if qh >= 2:
                V.wait_ge(pe.h, pe_w[qh - 2])
            dv.inc(V.tensor_copy(w16[b][:], wstage[b][:]))
            dv_w = dv.n

            T.wait_ge(dv.h, dv_w)
            T.wait_ge(dv.h, dv_qe.get(qh - 1, 0))
            for th in range(2):
                for kc in range(16):
                    ins = T.matmul(qp_p[:, 512 * th:512 * (th + 1)], w16[b][:, kc, :],
                                   xnT[:, kc, 512 * th:512 * (th + 1)], start=(kc == 0), stop=(kc == 15))
            pe.inc(ins)
            pe_w[qh] = pe.n

            qb = qh % 2
            V.wait_ge(pe.h, pe_w[qh])
            V.wait_ge(pe.h, st_last_of_qh.get(qh - 2, 0))
            V.tensor_scalar(qraw[:], qp_p[:], bqt[:, qh:qh + 1], None, ALU.add)
            V.tensor_tensor(qrot[0:64, :], qraw[64:128, :], qsint[64:128, :], ALU.mult)
            V.tensor_tensor(qrot[64:128, :], qraw[0:64, :], qsint[0:64, :], ALU.mult)
            V.tensor_tensor(qcs[:], qraw[:], qcost[:], ALU.mult)
            dv.inc(V.tensor_tensor(QT[:, qb, :], qcs[:], qrot[:], ALU.add))
            dv_qe[qh] = dv.n

            for half in range(2):
                qs = slice(512 * half, 512 * (half + 1))
                for s in range(16):
                    par = g % 3
                    slot = g % NSL
                    T.wait_ge(dv.h, dv_qe[qh])
                    if s == 0:
                        T.wait_ge(dv.h, max(dv_mask_built, dv_ynorm[half]))
                    if g >= 3:
                        T.wait_ge(ac.h, ac_exp[g - 3])
                    pe.inc(T.matmul(st_p[par][:], KT[:, kvh, 128 * s:128 * (s + 1)],
                                    QT[:, qb, qs], start=True, stop=True))
                    pe_st[g] = pe.n
                    if half == 1 and s == 15:
                        st_last_of_qh[qh] = pe.n

                    A.wait_ge(pe.h, pe_st[g])
                    if g >= NSL:
                        A.wait_ge(dv.h, dv_mask[g - NSL])
                    ac.inc(A.activation(out=Pe[:, slot, :], in_=st_p[par][:],
                                        func=AF.Exp, scale=ISQ, bias=cneg4[:]))
                    ac_exp[g] = ac.n

                    V.wait_ge(ac.h, ac_exp[g])
                    if g >= NSL:
                        V.wait_ge(pe.h, pe_yell.get(g - NSL, 0))
                    dv.inc(V.tensor_tensor(Pm[:, slot, :], Pe[:, slot, :],
                                           mask01[:, s, qs], ALU.mult))
                    dv_mask[g] = dv.n

                    if len(pending) >= 3:
                        flush_pending()
                    pending.append((g, qh, half, s, slot))
                    g += 1
        while pending:
            flush_pending()

        dv_yt = dv.n
        gd_s = 0
        dv_se = 0
        for ec in range(16):
            b = ec % 2
            last_use = pe_w[16 + ec - 2] if ec >= 2 else pe_w[14 + ec]
            SQ.wait_ge(pe.h, last_use)
            ws[b].inc(SQ.dma_start(out=wstage[b][:],
                                   in_=woT[:, 128 * ec:128 * (ec + 1)].rearrange("(c p) e -> p c e", p=128)), 16)
            V.wait_ge(ws[b].h, ws[b].n)
            V.wait_ge(pe.h, last_use)
            dv.inc(V.tensor_copy(w16[b][:], wstage[b][:]))
            dv_w2 = dv.n

            T.wait_ge(dv.h, dv_w2)
            T.wait_ge(dv.h, dv_yt)
            if dv_se:
                T.wait_ge(dv.h, dv_se)
            for th in range(2):
                for kc in range(16):
                    ins = T.matmul(qp_p[:, 512 * th:512 * (th + 1)], w16[b][:, kc, :],
                                   YT[:, kc, 512 * th:512 * (th + 1)], start=(kc == 0), stop=(kc == 15))
            pe.inc(ins)
            pe_w[16 + ec] = pe.n

            V.wait_ge(pe.h, pe_w[16 + ec])
            if gd_s:
                V.wait_ge(gd.h, gd_s)
            dv.inc(V.tensor_scalar(stmp[:], qp_p[:], bot[:, ec:ec + 1], None, ALU.add))
            dv_se = dv.n
            WQ.wait_ge(dv.h, dv_se)
            gd_s = gd.inc(WQ.dma_start(out=sDram[128 * ec:128 * (ec + 1), :], in_=stmp[:]), 16)


def _ffn(nc, SQ, V, A, T, C, Gg, Gu, Gd, xnT, gacc, cc):
    pe, dv, ac = C["pe"], C["dv"], C["ac"]
    sgg, sgt = C["gg"], C["gt"]
    NRING = 22
    with contextlib.ExitStack() as es2:
        E2 = es2.enter_context
        grng = E2(nc.sbuf_tensor("grng", [128, NRING, 512], BF16))
        urng = E2(nc.sbuf_tensor("urng", [128, NRING, 512], BF16))
        gdt = [E2(nc.sbuf_tensor(f"gdt{i}", [128, 16, 256], BF16)) for i in range(2)]
        interT = E2(nc.sbuf_tensor("interT", [128, 16, TOWN], BF16))
        ga = E2(nc.sbuf_tensor("ffa", [128, 1024], F32))
        gb = E2(nc.sbuf_tensor("ffb", [128, 1024], F32))
        gth = E2(nc.sbuf_tensor("ffth", [128, 1024], F32))
        gp_p = [E2(nc.psum_tensor(f"gp_p{i}", [128, 1024], F32)) for i in range(2)]
        up_p = [E2(nc.psum_tensor(f"up_p{i}", [128, 1024], F32)) for i in range(2)]

        dv.inc(V.memset(gacc[:], 0.0))

        group_pe = {}
        emitted = [0]
        n_strips = 8 * 4 * 16

        def strip_idx(seg, ng, kc):
            return (seg * 4 + ng) * 16 + kc

        SQ.wait_ge(cc.h, 2)
        cc_d = [False]

        def emit_loads(upto):
            while emitted[0] < min(upto, n_strips):
                s = emitted[0]
                seg, rem = divmod(s, 64)
                ng, kc = divmod(rem, 16)
                slot = s % NRING
                gidx = s // 16
                if s >= NRING:
                    SQ.wait_ge(pe.h, group_pe[(s - NRING) // 16])
                sgg[gidx % 2].inc(SQ.dma_start(out=grng[:, slot, :],
                                               in_=Gg[seg, 128 * kc:128 * (kc + 1), 512 * ng:512 * (ng + 1)]), 16)
                sgg[gidx % 2].inc(SQ.dma_start(out=urng[:, slot, :],
                                               in_=Gu[seg, 128 * kc:128 * (kc + 1), 512 * ng:512 * (ng + 1)]), 16)
                emitted[0] += 1

        dv_ep = {}
        dv_dacc = 0
        pe_dw = {}
        for seg in range(8):
            for ng in range(4):
                g = seg * 4 + ng
                emit_loads(strip_idx(seg, ng, 0) + NRING)
                sg_all = 32 * 16 * (g // 2 + 1)
                for ns in range(4):
                    nsub = 4 * ng + ns
                    gidx2 = seg * 16 + nsub
                    par = nsub % 2
                    T.wait_ge(sgg[g % 2].h, sg_all)
                    if gidx2 >= 2 and (gidx2 - 2) in dv_ep:
                        T.wait_ge(dv.h, dv_ep[gidx2 - 2])
                    if ng == 0 and ns < 2 and dv_dacc:
                        T.wait_ge(dv.h, dv_dacc)
                    for kc in range(16):
                        s = strip_idx(seg, ng, kc)
                        slot = s % NRING
                        for th in range(2):
                            T.matmul(gp_p[par][:, 512 * th:512 * (th + 1)],
                                     grng[:, slot, 128 * ns:128 * (ns + 1)],
                                     xnT[:, kc, 512 * th:512 * (th + 1)],
                                     start=(kc == 0), stop=(kc == 15))
                        for th in range(2):
                            ins = T.matmul(up_p[par][:, 512 * th:512 * (th + 1)],
                                           urng[:, slot, 128 * ns:128 * (ns + 1)],
                                           xnT[:, kc, 512 * th:512 * (th + 1)],
                                           start=(kc == 0), stop=(kc == 15))
                    pe.inc(ins)
                    pe_gu = pe.n

                    gpp = gp_p[par][:]
                    upp = up_p[par][:]
                    A.wait_ge(pe.h, pe_gu)
                    ac.inc(A.activation(out=ga[:], in_=gpp, func=AF.Square))
                    V.wait_ge(ac.h, ac.n)
                    V.tensor_scalar(ga[:], ga[:], GELU_C, 1.0, ALU.mult, ALU.add)
                    dv.inc(V.tensor_tensor(gb[:], ga[:], gpp, ALU.mult))
                    A.wait_ge(dv.h, dv.n)
                    ac.inc(A.activation(out=gth[:], in_=gb[:], func=AF.Tanh, scale=GELU_S))
                    V.wait_ge(ac.h, ac.n)
                    V.scalar_tensor_tensor(gb[:], gth[:], 1.0, gpp, ALU.add, ALU.mult)
                    dv.inc(V.tensor_tensor(interT[:, nsub, :], gb[:], upp, ALU.mult))
                    dv_ep[gidx2] = dv.n
                group_pe[g] = pe.n

            for sl in range(8):
                b = sl % 2
                di = seg * 8 + sl
                if di >= 2:
                    SQ.wait_ge(pe.h, pe_dw[di - 2])
                if not cc_d[0]:
                    SQ.wait_ge(cc.h, 3)
                    cc_d[0] = True
                sgt[b].inc(SQ.dma_start(out=gdt[b][:],
                                        in_=Gd[sl, 2048 * seg:2048 * (seg + 1), :].rearrange("(c p) h -> p c h", p=128)), 16)
                T.wait_ge(sgt[b].h, sgt[b].n)
                T.wait_ge(dv.h, dv_ep[seg * 16 + 15])
                for j in range(2):
                    if dv_dacc:
                        T.wait_ge(dv.h, dv_dacc)
                    for nci in range(16):
                        for th in range(2):
                            ins = T.matmul(gp_p[j][:, 512 * th:512 * (th + 1)],
                                           gdt[b][:, nci, 128 * j:128 * (j + 1)],
                                           interT[:, nci, 512 * th:512 * (th + 1)],
                                           start=(nci == 0), stop=(nci == 15))
                    pe.inc(ins)
                    V.wait_ge(pe.h, pe.n)
                    dv.inc(V.tensor_tensor(gacc[:, 2 * sl + j, :], gp_p[j][:],
                                           gacc[:, 2 * sl + j, :], ALU.add))
                    dv_dacc = dv.n
                pe_dw[di] = pe.n


def _combine(nc, SQ, WQ, V, A, T, C, xr, out, sDram, gacc, identt, idenft):
    pe, dv, ac, ou, st, sxr = C["pe"], C["dv"], C["ac"], C["ou"], C["st"], C["xr"]
    with contextlib.ExitStack() as es2:
        E2 = es2.enter_context
        outst = E2(nc.sbuf_tensor("outst", [128, 8, D], F32))
        stile = [E2(nc.sbuf_tensor(f"cst{i}", [128, TOWN], BF16)) for i in range(2)]
        silt = E2(nc.sbuf_tensor("silt", [128, TOWN], F32))
        xrt = [E2(nc.sbuf_tensor(f"xrt{i}", [128, D], F32)) for i in range(2)]
        cpa = E2(nc.sbuf_tensor("cpa", [128, 128], BF16))
        ta_p = E2(nc.psum_tensor("ta_p", [128, 128], BF16))
        tb_p = E2(nc.psum_tensor("tb_p", [128, 128], F32))

        dv_c = 0
        pe_tp2 = 0
        pe_ec = {}
        for ec in range(16):
            b = ec % 2
            if ec >= 2:
                SQ.wait_ge(pe.h, pe_ec[ec - 2])
            st.inc(SQ.dma_start(out=stile[b][:], in_=sDram[128 * ec:128 * (ec + 1), :]), 16)
            st_all = st.n
            if dv_c:
                A.wait_ge(dv.h, dv_c)
            if pe_tp2:
                A.wait_ge(pe.h, pe_tp2)
            ac.inc(A.activation(out=silt[:], in_=gacc[:, ec, :], func=AF.Silu))
            ac_sil = ac.n
            for t in range(8):
                T.wait_ge(st.h, st_all)
                T.wait_ge(ac.h, ac_sil)
                if dv_c:
                    T.wait_ge(dv.h, dv_c)
                T.transpose(ta_p[:], stile[b][:, 128 * t:128 * (t + 1)], identt[:])
                pe.inc(T.transpose(tb_p[:], silt[:, 128 * t:128 * (t + 1)], idenft[:]))
                pe_tp2 = pe.n
                V.wait_ge(pe.h, pe_tp2)
                V.tensor_copy(cpa[:], ta_p[:])
                dv.inc(V.tensor_tensor(outst[:, t, 128 * ec:128 * (ec + 1)], tb_p[:], cpa[:], ALU.mult))
                dv_c = dv.n
            pe_ec[ec] = pe.n

        dv_xr = {}
        for t in range(8):
            b = t % 2
            if t >= 2:
                SQ.wait_ge(dv.h, dv_xr[t - 2])
            sxr[b].inc(SQ.dma_start(out=xrt[b][:], in_=xr[128 * t:128 * (t + 1), :]), 16)
            V.wait_ge(sxr[b].h, sxr[b].n)
            dv.inc(V.tensor_tensor(outst[:, t, :], outst[:, t, :], xrt[b][:], ALU.add))
            dv_xr[t] = dv.n
            WQ.wait_ge(dv.h, dv.n)
            C["ou"].inc(WQ.dma_start(out=out[128 * t:128 * (t + 1), :], in_=outst[:, t, :]), 16)


_NC_CACHE = {}


def _get_nc():
    key = (DEQ_SLICES, STAGE, LNKV_SUB, DBG)
    if key not in _NC_CACHE:
        _NC_CACHE[key] = build()
    return _NC_CACHE[key]


def _rope_tables():
    i = np.arange(1, 65, dtype=np.float32)
    thetas = (np.float32(10000.0) ** (np.float32(-2.0) * (i - np.float32(1.0)) / np.float32(128.0))).astype(np.float32)
    pos = np.arange(1, TKV + 1, dtype=np.float32)
    ang = pos[:, None] * thetas[None, :]
    c = np.cos(ang).astype(np.float32)
    s = np.sin(ang).astype(np.float32)
    cosF = np.ascontiguousarray(np.concatenate([c, c], axis=1).T)
    sinH = np.ascontiguousarray(np.concatenate([s, -s], axis=1).T)
    return cosF, sinH


def prepare_in_maps(L0_attn_x, L0_attn_Wq, L0_attn_bq, L0_attn_Wk, L0_attn_bk, L0_attn_Wv,
                    L0_attn_bv, L0_attn_Wo, L0_attn_bo, L0_ffn_gate_W_packed, L0_ffn_gate_scales,
                    L0_ffn_up_W_packed, L0_ffn_up_scales, L0_ffn_down_W_packed, L0_ffn_down_scales):
    x = np.asarray(L0_attn_x, dtype=np.float32)
    wq = np.ascontiguousarray(np.asarray(L0_attn_Wq, np.float32).T)
    wk = np.ascontiguousarray(np.asarray(L0_attn_Wk, np.float32).T)
    wv = np.ascontiguousarray(np.asarray(L0_attn_Wv, np.float32).T)
    wo = np.ascontiguousarray(np.asarray(L0_attn_Wo, np.float32).T)
    bqv = np.ascontiguousarray(np.asarray(L0_attn_bq, np.float32).reshape(16, 128).T)
    bkv = np.ascontiguousarray(np.asarray(L0_attn_bk, np.float32).reshape(4, 128).T)
    bvv = np.ascontiguousarray(np.asarray(L0_attn_bv, np.float32).reshape(4, 128).T)
    bov = np.ascontiguousarray(np.asarray(L0_attn_bo, np.float32).reshape(16, 128).T)
    gpk = np.ascontiguousarray(np.asarray(L0_ffn_gate_W_packed, np.int32))
    gsc = np.ascontiguousarray(np.asarray(L0_ffn_gate_scales, np.float32))
    upkk = np.ascontiguousarray(np.asarray(L0_ffn_up_W_packed, np.int32))
    upsc = np.ascontiguousarray(np.asarray(L0_ffn_up_scales, np.float32))
    dpk = np.ascontiguousarray(np.asarray(L0_ffn_down_W_packed, np.int32))
    dssc = np.ascontiguousarray(np.asarray(L0_ffn_down_scales, np.float32))

    cosF, sinH = _rope_tables()
    iden = np.eye(128, dtype=np.float32).astype(ml_dtypes.bfloat16)
    idenf32 = np.eye(128, dtype=np.float32)
    onescol = np.ones((128, 1), np.float32).astype(ml_dtypes.bfloat16)
    onesf32 = np.ones((1, 128), np.float32)

    in_maps = []
    for c in range(8):
        bsel = c // 2
        h = c % 2
        own0 = 1024 * h
        oth0 = 1024 * (1 - h)
        idx = np.concatenate([np.arange(own0, own0 + 1024), np.arange(oth0, oth0 + 1024)])
        xb = x[bsel]
        in_maps.append({
            "xT": np.ascontiguousarray(xb.T[:, idx]),
            "xr": np.ascontiguousarray(xb[own0:own0 + 1024]),
            "wqT": wq, "wkT": wk, "wvT": wv, "woT": wo,
            "bq": bqv, "bk": bkv, "bv": bvv, "bo": bov,
            "gp": np.ascontiguousarray(gpk[2048 * c:2048 * (c + 1)]),
            "gs": np.ascontiguousarray(gsc[2048 * c:2048 * (c + 1)]),
            "up": np.ascontiguousarray(upkk[2048 * c:2048 * (c + 1)]),
            "us": np.ascontiguousarray(upsc[2048 * c:2048 * (c + 1)]),
            "dp": np.ascontiguousarray(dpk[256 * c:256 * (c + 1)]),
            "ds": np.ascontiguousarray(dssc[256 * c:256 * (c + 1)]),
            "qcos": np.ascontiguousarray(cosF[:, own0:own0 + 1024]),
            "qsin": np.ascontiguousarray(sinH[:, own0:own0 + 1024]),
            "kcos": np.ascontiguousarray(cosF[:, idx]),
            "ksin": np.ascontiguousarray(sinH[:, idx]),
            "posq": np.ascontiguousarray(np.arange(own0, own0 + 1024, dtype=np.float32).reshape(8, 128).T),
            "poskcol": np.ascontiguousarray(idx.astype(np.float32).reshape(16, 128).T),
            "poskk": idx.astype(np.float32).reshape(1, TKV),
            "iden": iden, "idenf": idenf32, "onescol": onescol, "onesf32": onesf32,
        })
    return in_maps


def kernel(**inputs):
    in_maps = prepare_in_maps(**inputs)
    nc = _get_nc()
    res = run_bass_kernel_spmd(nc, in_maps, core_ids=list(range(8)))
    outp = np.empty((4, 2048, 2048), np.float32)
    for c in range(8):
        bsel = c // 2
        h = c % 2
        outp[bsel, 1024 * h:1024 * (h + 1)] = res.results[c]["out"]
    return outp


# version 13
# speedup vs baseline: 1.3849x; 1.0133x over previous
import contextlib
import os
import tempfile

os.environ["NEURON_COMPILE_CACHE_URL"] = tempfile.mkdtemp(prefix="ncc_")

import numpy as np
import ml_dtypes
import concourse.bass as bass
import concourse.mybir as mybir
from concourse.bass_utils import run_bass_kernel_spmd

F32 = mybir.dt.float32
BF16 = mybir.dt.bfloat16
I32 = mybir.dt.int32
AF = mybir.ActivationFunctionType
ALU = mybir.AluOpType
AX = mybir.AxisListType.X

D = 2048
DK = 512
HD = 128
TOWN = 1024
TKV = 2048
MLP = 16384
NEG = -1.0e30
GELU_C = 0.044715
GELU_S = 0.7978845608028654
ISQ = 0.08838834764831845
EPS = 1e-5

DEQ_SLICES = 1
STAGE = 5
LNKV_SUB = 7
DBG = ""


class Ctr:
    def __init__(self, es, nc, name):
        self.h = es.enter_context(nc.semaphore(name))
        self.n = 0

    def inc(self, instr, k=1):
        instr.then_inc(self.h, k)
        self.n += k
        return self.n


def build():
    nc = bass.Bass("TRN2", target_bir_lowering=False)

    def din(name, shape, dt):
        return nc.dram_tensor(name, shape, dt, kind="ExternalInput").ap()

    xT = din("xT", [D, TKV], F32)
    xr = din("xr", [TOWN, D], F32)
    wqT = din("wqT", [D, D], F32)
    wkT = din("wkT", [D, DK], F32)
    wvT = din("wvT", [D, DK], F32)
    woT = din("woT", [D, D], F32)
    bq = din("bq", [128, 16], F32)
    bk = din("bk", [128, 4], F32)
    bv = din("bv", [128, 4], F32)
    bo = din("bo", [128, 16], F32)
    gp = din("gp", [2048 * DEQ_SLICES, 1024], I32)
    gs = din("gs", [2048 * DEQ_SLICES, 64], F32)
    upk = din("up", [2048 * DEQ_SLICES, 1024], I32)
    ups = din("us", [2048 * DEQ_SLICES, 64], F32)
    dp = din("dp", [256 * DEQ_SLICES, 8192], I32)
    dsc = din("ds", [256 * DEQ_SLICES, 512], F32)
    qcos = din("qcos", [HD, TOWN], F32)
    qsin = din("qsin", [HD, TOWN], F32)
    kcos = din("kcos", [HD, TKV], F32)
    ksin = din("ksin", [HD, TKV], F32)
    posq = din("posq", [128, 8], F32)
    poskcol = din("poskcol", [128, 16], F32)
    poskk = din("poskk", [1, TKV], F32)
    iden = din("iden", [128, 128], BF16)
    idenf = din("idenf", [128, 128], F32)
    onescol = din("onescol", [128, 1], BF16)
    onesf32 = din("onesf32", [1, 128], F32)

    out = nc.dram_tensor("out", [TOWN, D], F32, kind="ExternalOutput").ap()

    wk_ = "ExternalOutput" if "w" in DBG else "Internal"
    if "k" in DBG:
        dbg_xn = nc.dram_tensor("dbg_xn", [128, 16, TOWN], BF16, kind="ExternalOutput").ap()
        dbg_kt = nc.dram_tensor("dbg_kt", [128, 4, TKV], BF16, kind="ExternalOutput").ap()
        dbg_vh = nc.dram_tensor("dbg_vh", [128, 4, 16, 128], BF16, kind="ExternalOutput").ap()
    if "a" in DBG:
        nc._dbg_a = dict(
            qt=nc.dram_tensor("dbg_qt", [128, TOWN], BF16, kind="ExternalOutput").ap(),
            pb=nc.dram_tensor("dbg_pb", [128, TKV], BF16, kind="ExternalOutput").ap(),
            ell=nc.dram_tensor("dbg_ell", [128, 1], F32, kind="ExternalOutput").ap(),
            s=nc.dram_tensor("dbg_s", [128, TKV], F32, kind="ExternalOutput").ap(),
            pb2=nc.dram_tensor("dbg_pb2", [128, TKV], BF16, kind="ExternalOutput").ap(),
            msk=nc.dram_tensor("dbg_msk", [128, 8, TKV], BF16, kind="ExternalOutput").ap(),
            yt=nc.dram_tensor("dbg_yt", [128, 16, TOWN], BF16, kind="ExternalOutput").ap(),
        )
    sk_ = "ExternalOutput" if "s" in DBG else "Internal"
    ggcF = nc.dram_tensor("ggc", [D, 2048], BF16, kind="Internal").ap()
    gucF = nc.dram_tensor("guc", [D, 2048], BF16, kind="Internal").ap()
    gdcF = nc.dram_tensor("gdc", [MLP, 256], BF16, kind="Internal").ap()
    ggc = ggcF.rearrange("(a k) n -> a k n", a=1)
    guc = gucF.rearrange("(a k) n -> a k n", a=1)
    gdc = gdcF.rearrange("(a k) n -> a k n", a=1)
    GgF = nc.dram_tensor("Gg", [8 * D, 2048], BF16, kind=wk_, addr_space="Shared").ap()
    GuF = nc.dram_tensor("Gu", [8 * D, 2048], BF16, kind=wk_, addr_space="Shared").ap()
    GdF = nc.dram_tensor("Gd", [8 * MLP, 256], BF16, kind=wk_, addr_space="Shared").ap()
    Gg = GgF.rearrange("(s k) n -> s k n", s=8)
    Gu = GuF.rearrange("(s k) n -> s k n", s=8)
    Gd = GdF.rearrange("(s k) n -> s k n", s=8)
    sDram = nc.dram_tensor("sDram", [D, TOWN], BF16, kind=sk_).ap()

    es = contextlib.ExitStack()
    with es:
        E = es.enter_context
        xnT = E(nc.sbuf_tensor("xnT", [128, 16, TOWN], BF16))
        identt = E(nc.sbuf_tensor("identt", [128, 128], BF16))
        idenft = E(nc.sbuf_tensor("idenft", [128, 128], F32))
        onescolt = E(nc.sbuf_tensor("onescolt", [128, 1], BF16))
        onesf32t = E(nc.sbuf_tensor("onesf32t", [1, 128], F32))
        posqt = E(nc.sbuf_tensor("posqt", [128, 8], F32))
        poskcolt = E(nc.sbuf_tensor("poskcolt", [128, 16], F32))
        bqt = E(nc.sbuf_tensor("bqt", [128, 16], F32))
        bot = E(nc.sbuf_tensor("bot", [128, 16], F32))
        bkt = E(nc.sbuf_tensor("bkt", [128, 4], F32))
        bvt = E(nc.sbuf_tensor("bvt", [128, 4], F32))
        cneg4 = E(nc.sbuf_tensor("cneg4", [128, 1], F32))

        C = {}
        for name in ["gd", "pe", "dv", "ac", "ou", "cn", "x", "ms", "st"]:
            C[name] = Ctr(es, nc, name)
        for name in ["ws", "gg", "gt", "xr"]:
            C[name] = [Ctr(es, nc, name + "0"), Ctr(es, nc, name + "1")]
        C["pk"] = [Ctr(es, nc, f"pk{i}") for i in range(4)]

        gd, pe, dv, ac, ou = C["gd"], C["pe"], C["dv"], C["ac"], C["ou"]

        blk = es.enter_context(nc.Block())
        SQ = nc.sync
        WQ = nc.scalar
        V = nc.vector
        A = nc.scalar
        T = nc.tensor
        ALL_E = (SQ, V, T, WQ, nc.gpsimd)

        def flat_ctrs():
            r = []
            for v in C.values():
                r.extend(v if isinstance(v, list) else [v])
            return r

        def barrier():
            vals = [(c, c.n) for c in flat_ctrs()]
            for e in ALL_E:
                for c, v in vals:
                    if v > 0:
                        e.wait_ge(c.h, v)

        cn = C["cn"]
        cn.inc(SQ.dma_start(out=identt[:], in_=iden), 16)
        cn.inc(SQ.dma_start(out=idenft[:], in_=idenf), 16)
        cn.inc(SQ.dma_start(out=onescolt[:], in_=onescol), 16)
        cn.inc(SQ.dma_start(out=onesf32t[:], in_=onesf32), 16)
        cn.inc(SQ.dma_start(out=posqt[:], in_=posq), 16)
        cn.inc(SQ.dma_start(out=poskcolt[:], in_=poskcol), 16)
        cn.inc(SQ.dma_start(out=bqt[:], in_=bq), 16)
        cn.inc(SQ.dma_start(out=bot[:], in_=bo), 16)
        cn.inc(SQ.dma_start(out=bkt[:], in_=bk), 16)
        cn.inc(SQ.dma_start(out=bvt[:], in_=bv), 16)
        dv_ = C["dv"]
        dv_.inc(V.memset(cneg4[:], -4.0))
        for e in (V, T, WQ, nc.gpsimd):
            e.wait_ge(cn.h, cn.n)
            e.wait_ge(dv_.h, 1)

        cc = Ctr(es, nc, "ccs")
        nc._cc = cc
        if STAGE >= 1:
            G_ = nc.gpsimd
            rg = [[0, 1, 2, 3, 4, 5, 6, 7]]
            _dequant(nc, SQ, WQ, V, T, C, identt, gp, gs, ggc, 1.0, "g")
            _dequant(nc, SQ, WQ, V, T, C, identt, upk, ups, guc, 1.0, "u")
            _dequant(nc, SQ, WQ, V, T, C, identt, dp, dsc, gdc, 0.5, "d")

            def issue_gathers():
                G_.wait_ge(gd.h, gd.n)
                G_.wait_ge(C["x"].h, C["x"].n)
                cc.inc(G_.collective_compute("AllGather", ALU.bypass,
                                             replica_groups=rg,
                                             ins=[ggcF.opt()], outs=[GgF.opt()]), 1)
                cc.inc(G_.collective_compute("AllGather", ALU.bypass,
                                             replica_groups=rg,
                                             ins=[gucF.opt()], outs=[GuF.opt()]), 1)
                cc.inc(G_.collective_compute("AllGather", ALU.bypass,
                                             replica_groups=rg,
                                             ins=[gdcF.opt()], outs=[GdF.opt()]), 1)
        else:
            def issue_gathers():
                pass
        SQ.wait_ge(gd.h, gd.n)

        if STAGE >= 2:
            with contextlib.ExitStack() as es23:
                E23 = es23.enter_context
                poskkb = E23(nc.sbuf_tensor("poskkb", [128, TKV], F32))
                qcost = E23(nc.sbuf_tensor("qcost", [128, TOWN], F32))
                qsint = E23(nc.sbuf_tensor("qsint", [128, TOWN], F32))
                ms = C["ms"]
                ms.inc(SQ.dma_start(out=poskkb[:], in_=poskk.partition_broadcast(128)), 16)
                ms.inc(SQ.dma_start(out=qcost[:], in_=qcos), 16)
                ms.inc(SQ.dma_start(out=qsint[:], in_=qsin), 16)
                kvt = _ln_kv(nc, es23, SQ, V, A, T, C,
                             xT, wkT, wvT, bkt, bvt, kcos, ksin,
                             xnT, onescolt, onesf32t, identt)
                issue_gathers()
                barrier()
                if "k" in DBG:
                    ou.inc(WQ.dma_start(out=dbg_xn, in_=xnT[:]), 16)
                    ou.inc(WQ.dma_start(out=dbg_kt, in_=kvt["KT"][:]), 16)
                    ou.inc(WQ.dma_start(out=dbg_vh, in_=kvt["Vh"][:]), 16)
                    for e in ALL_E:
                        e.wait_ge(ou.h, ou.n)
                if STAGE >= 3:
                    _attention(nc, es23, SQ, WQ, V, A, T, C,
                               wqT, woT, bqt, bot, qcost, qsint, poskcolt, poskkb,
                               xnT, sDram, kvt, identt, cneg4, onescolt, onesf32t)
                barrier()

        if STAGE >= 4:
            with contextlib.ExitStack() as es45:
                gacc = es45.enter_context(nc.sbuf_tensor("gacc", [128, 16, TOWN], F32))
                _ffn(nc, SQ, V, A, T, C, Gg, Gu, Gd, xnT, gacc, nc._cc)
                barrier()
                if STAGE >= 5:
                    _combine(nc, SQ, WQ, V, A, T, C, xr, out, sDram, gacc, identt, idenft)

        SQ.wait_ge(ou.h, ou.n)

    return nc


def _dequant(nc, SQ, WQ, V, T, C, identt, packed, scales, Gout, smul, tag):
    gd, pe, dv = C["gd"], C["pe"], C["dv"]
    pk = C["pk"]
    KK = packed.shape[1]
    NB = scales.shape[1]
    n_loc = Gout.shape[2]
    n_rtiles = n_loc // 128
    n_cchunk = KK // 512
    nslices = packed.shape[0] // n_loc
    K_win = Gout.shape[1] // n_cchunk

    with contextlib.ExitStack() as es:
        SQ.wait_ge(gd.h, gd.n)
        E = es.enter_context
        pt = [E(nc.sbuf_tensor(f"dqp{tag}{i}", [128, 512], I32)) for i in range(4)]
        sct = [E(nc.sbuf_tensor(f"dqs{tag}{i}", [128, NB], F32)) for i in range(4)]
        sce = [E(nc.sbuf_tensor(f"dqe{tag}{i}", [128, NB], F32)) for i in range(4)]
        mtile = E(nc.sbuf_tensor(f"dqm{tag}", [128, 1024], I32))
        stile = E(nc.sbuf_tensor(f"dqt{tag}", [128, 1024], I32))
        btile = E(nc.sbuf_tensor(f"dqb{tag}", [128, 1024], I32))
        wtile = E(nc.sbuf_tensor(f"dqw{tag}", [128, 1024], BF16))
        rb = E(nc.sbuf_tensor(f"dqr{tag}", [128, 8, n_loc], BF16))
        tp = [E(nc.psum_tensor(f"dqx{tag}{i}", [128, 512], BF16)) for i in range(2)]

        dv_decode = {}
        dv_evac = {}
        pe_tp = {}
        gd_grp = {}

        u = 0
        for sl in range(nslices):
            for c in range(n_cchunk):
                for r in range(n_rtiles):
                    b = u % 4
                    bc = u % 4
                    if u >= 4:
                        SQ.wait_ge(dv.h, dv_decode[u - 4])
                    row0 = sl * n_loc + r * 128
                    pk[bc].inc(SQ.dma_start(out=pt[b][:], in_=packed[row0:row0 + 128, c * 512:(c + 1) * 512]), 16)
                    pk[bc].inc(SQ.dma_start(out=sct[b][:], in_=scales[row0:row0 + 128, :]), 16)

                    V.wait_ge(pk[bc].h, pk[bc].n)
                    if u >= 1:
                        V.wait_ge(pe.h, pe_tp[u - 1])
                    if r == 0:
                        g_prev = (sl * n_cchunk + c) - 1
                        if g_prev >= 0:
                            V.wait_ge(gd.h, gd_grp[g_prev])
                    V.tensor_scalar(sce[b][:], sct[b][:], float(smul * (2.0 ** 126)), None, ALU.mult)
                    p = pt[b][:]
                    V.tensor_scalar(mtile[:, 0:512], p, 7, 22, ALU.bitwise_and, ALU.logical_shift_left)
                    V.tensor_scalar(mtile[:, 512:1024], p, 0x70, 18, ALU.bitwise_and, ALU.logical_shift_left)
                    V.tensor_scalar(stile[:, 0:512], p, 8, 28, ALU.bitwise_and, ALU.logical_shift_left)
                    V.tensor_scalar(stile[:, 512:1024], p, 0x80, 24, ALU.bitwise_and, ALU.logical_shift_left)
                    V.tensor_tensor(btile[:, 0:512], mtile[:, 0:512], stile[:, 0:512], ALU.bitwise_or)
                    V.tensor_tensor(btile[:, 512:1024], mtile[:, 512:1024], stile[:, 512:1024], ALU.bitwise_or)
                    scb = sce[b][:, 32 * c:32 * c + 32].unsqueeze(2).broadcast_to([128, 32, 16])
                    bl = btile[:, 0:512].bitcast(F32).rearrange("p (b j) -> p b j", b=32)
                    bh = btile[:, 512:1024].bitcast(F32).rearrange("p (b j) -> p b j", b=32)
                    V.tensor_tensor(wtile[:, 0:512].rearrange("p (b j) -> p b j", b=32), bl, scb, ALU.mult)
                    dv.inc(V.tensor_tensor(wtile[:, 512:1024].rearrange("p (b j) -> p b j", b=32), bh, scb, ALU.mult))
                    dv_decode[u] = dv.n

                    T.wait_ge(dv.h, dv_decode[u])
                    if u >= 1:
                        T.wait_ge(dv.h, dv_evac[u - 1])
                    for plane in range(2):
                        for s4 in range(4):
                            ins = T.transpose(tp[plane][:, 128 * s4:128 * (s4 + 1)],
                                              wtile[:, 512 * plane + 128 * s4:512 * plane + 128 * (s4 + 1)],
                                              identt[:])
                        pe.inc(ins)
                    pe_tp[u] = pe.n

                    V.wait_ge(pe.h, pe_tp[u])
                    V.tensor_copy(rb[:, 0:4, r * 128:(r + 1) * 128],
                                  tp[0][:].rearrange("p (s j) -> p s j", s=4))
                    dv.inc(V.tensor_copy(rb[:, 4:8, r * 128:(r + 1) * 128],
                                         tp[1][:].rearrange("p (s j) -> p s j", s=4)))
                    dv_evac[u] = dv.n
                    u += 1

                WQ.wait_ge(dv.h, dv_evac[u - 1])
                for plane in range(2):
                    for s4 in range(4):
                        base = K_win * c + 256 * s4 + plane
                        gd.inc(WQ.dma_start(out=Gout[sl, base:base + 255:2, :],
                                            in_=rb[:, plane * 4 + s4, :]), 16)
                gd_grp[sl * n_cchunk + c] = gd.n


def _ln_kv(nc, es, SQ, V, A, T, C, xT, wkT, wvT, bkt, bvt, kcos, ksin,
           xnT, onescolt, onesf32t, identt):
    pe, dv, ac, ms, sx, ws = C["pe"], C["dv"], C["ac"], C["ms"], C["x"], C["ws"]
    E = es.enter_context
    KT = E(nc.sbuf_tensor("KT", [128, 4, TKV], BF16))
    Vh = E(nc.sbuf_tensor("Vh", [128, 4, 16, 128], BF16))

    with contextlib.ExitStack() as es2:
        E2 = es2.enter_context
        kcost = E2(nc.sbuf_tensor("kcost", [128, TKV], F32))
        ksint = E2(nc.sbuf_tensor("ksint", [128, TKV], F32))
        wk16 = E2(nc.sbuf_tensor("wk16", [128, 16, DK], BF16))
        wv16 = E2(nc.sbuf_tensor("wv16", [128, 16, DK], BF16))
        xtc = E2(nc.sbuf_tensor("xtc", [128, 16, 512], F32))
        xn_t = E2(nc.sbuf_tensor("xn_t", [128, 16, 512], BF16))
        x16 = [E2(nc.sbuf_tensor(f"x16{i}", [128, 512], BF16)) for i in range(2)]
        x2 = [E2(nc.sbuf_tensor(f"x2{i}", [128, 512], BF16)) for i in range(2)]
        wstage = [E2(nc.sbuf_tensor(f"wst{i}", [128, 512], F32)) for i in range(2)]
        srow = E2(nc.sbuf_tensor("srow", [1, 512], F32))
        qrow = E2(nc.sbuf_tensor("qrow", [1, 512], F32))
        trow = E2(nc.sbuf_tensor("trow", [1, 512], F32))
        vrow = E2(nc.sbuf_tensor("vrow", [1, 512], F32))
        t1 = E2(nc.sbuf_tensor("lnt1", [128, 512], F32))
        kraw = E2(nc.sbuf_tensor("kraw", [128, 512], F32))
        krot = E2(nc.sbuf_tensor("krot", [128, 512], F32))
        kcs = E2(nc.sbuf_tensor("kcs", [128, 512], F32))
        vtmp = E2(nc.sbuf_tensor("vtmp", [128, 512], BF16))
        sum_p = E2(nc.psum_tensor("sum_p", [1, 512], F32))
        sq_p = E2(nc.psum_tensor("sq_p", [1, 512], F32))
        mb_p = E2(nc.psum_tensor("mb_p", [128, 512], F32))
        rb_p = E2(nc.psum_tensor("rb_p", [128, 512], F32))
        kv_p = E2(nc.psum_tensor("kv_p", [128, 512], F32))
        vv_p = E2(nc.psum_tensor("vv_p", [128, 512], F32))
        vt_p = E2(nc.psum_tensor("vt_p", [128, 512], BF16))

        ms.inc(SQ.dma_start(out=kcost[:], in_=kcos), 16)
        ms.inc(SQ.dma_start(out=ksint[:], in_=ksin), 16)
        MS_ALL = ms.n

        casts = {}
        for w in range(2):
            src = wkT if w == 0 else wvT
            dst = wk16 if w == 0 else wv16
            for kc in range(16):
                i = w * 16 + kc
                b = i % 2
                if i >= 2:
                    SQ.wait_ge(dv.h, casts[i - 2])
                ws[b].inc(SQ.dma_start(out=wstage[b][:], in_=src[128 * kc:128 * (kc + 1), :]), 16)
                V.wait_ge(ws[b].h, ws[b].n)
                dv.inc(V.tensor_copy(dst[:, kc, :], wstage[b][:]))
                casts[i] = dv.n
        V.wait_ge(ms.h, MS_ALL)

        dv_norm = {}
        pe_kv = {}
        dv_kv = dv.n
        for tc in range(4 if LNKV_SUB >= 2 else 0):
            own = tc < 2
            if tc >= 1:
                SQ.wait_ge(dv.h, dv_norm[tc - 1])
            for kc in range(16):
                sx.inc(SQ.dma_start(out=xtc[:, kc, :],
                                    in_=xT[128 * kc:128 * (kc + 1), 512 * tc:512 * (tc + 1)]), 16)
            sx_all = sx.n

            pe_stat = {}
            for kc in range(16):
                bb = kc % 2
                V.wait_ge(sx.h, sx_all)
                A.wait_ge(sx.h, sx_all)
                if kc >= 2:
                    V.wait_ge(pe.h, pe_stat[kc - 2])
                    A.wait_ge(pe.h, pe_stat[kc - 2])
                dv.inc(V.tensor_copy(x16[bb][:], xtc[:, kc, :]))
                ac.inc(A.activation(out=x2[bb][:], in_=xtc[:, kc, :], func=AF.Square))
                T.wait_ge(dv.h, dv.n)
                T.wait_ge(ac.h, ac.n)
                T.matmul(sum_p[:], onescolt[:], x16[bb][:], start=(kc == 0), stop=(kc == 15))
                pe.inc(T.matmul(sq_p[:], onescolt[:], x2[bb][:], start=(kc == 0), stop=(kc == 15)))
                pe_stat[kc] = pe.n
            pe_stats = pe.n

            if LNKV_SUB < 3:
                dv_norm[tc] = dv.n
                pe_kv[tc] = pe.n
                continue
            V.wait_ge(pe.h, pe_stats)
            V.tensor_scalar(srow[:], sum_p[:], 1.0 / D, None, ALU.mult)
            V.tensor_scalar(qrow[:], sq_p[:], 1.0 / D, None, ALU.mult)
            V.tensor_copy(t1[0:1, :], qrow[:])
            V.tensor_tensor(trow[:], srow[:], srow[:], ALU.mult)
            V.tensor_copy(t1[0:1, :], qrow[:])
            V.scalar_tensor_tensor(vrow[:], qrow[:], EPS, trow[:], ALU.add, ALU.subtract)
            dv.inc(V.tensor_copy(t1[0:1, :], qrow[:]))
            A.wait_ge(dv.h, dv.n)
            ac.inc(A.activation(out=trow[:], in_=vrow[:], func=AF.Sqrt))
            V.wait_ge(ac.h, ac.n)
            V.reciprocal(vrow[:], trow[:])
            dv.inc(V.tensor_copy(t1[0:1, :], qrow[:]))
            T.wait_ge(dv.h, dv.n)
            T.matmul(mb_p[:], onesf32t[:], srow[:], start=True, stop=True)
            pe.inc(T.matmul(rb_p[:], onesf32t[:], vrow[:], start=True, stop=True))
            pe_bc = pe.n

            if LNKV_SUB < 4:
                dv_norm[tc] = dv.n
                pe_kv[tc] = pe.n
                continue
            V.wait_ge(pe.h, pe_bc)
            if tc >= 3:
                V.wait_ge(pe.h, pe_kv[tc - 1])
            for kc in range(16):
                V.tensor_tensor(t1[:], xtc[:, kc, :], mb_p[:], ALU.subtract)
                dst = xnT[:, kc, 512 * tc:512 * (tc + 1)] if own else xn_t[:, kc, :]
                dv.inc(V.tensor_tensor(dst, t1[:], rb_p[:], ALU.mult))
            dv_norm[tc] = dv.n

            def xsrc(kc, tc=tc, own=own):
                return xnT[:, kc, 512 * tc:512 * (tc + 1)] if own else xn_t[:, kc, :]

            for kvh in range(4 if LNKV_SUB >= 5 else 0):
                T.wait_ge(dv.h, dv_norm[tc])
                T.wait_ge(dv.h, dv_kv)
                for kc in range(16):
                    T.matmul(kv_p[:], wk16[:, kc, 128 * kvh:128 * (kvh + 1)], xsrc(kc),
                             start=(kc == 0), stop=(kc == 15))
                for kc in range(16):
                    ins = T.matmul(vv_p[:], wv16[:, kc, 128 * kvh:128 * (kvh + 1)], xsrc(kc),
                                   start=(kc == 0), stop=(kc == 15))
                pe.inc(ins)
                pe_proj = pe.n
                if LNKV_SUB < 6:
                    dv_kv = dv.n
                    continue

                V.wait_ge(pe.h, pe_proj)
                cs = slice(512 * tc, 512 * (tc + 1))
                nops = (LNKV_SUB - 60) if 60 <= LNKV_SUB < 70 else 6
                ops = [
                    lambda: V.tensor_scalar(kraw[:], kv_p[:], bkt[:, kvh:kvh + 1], None, ALU.add),
                    lambda: V.tensor_tensor(krot[0:64, :], kraw[64:128, :], ksint[64:128, cs], ALU.mult),
                    lambda: V.tensor_tensor(krot[64:128, :], kraw[0:64, :], ksint[0:64, cs], ALU.mult),
                    lambda: V.tensor_tensor(kcs[:], kraw[:], kcost[:, cs], ALU.mult),
                    lambda: V.tensor_tensor(KT[:, kvh, cs], kcs[:], krot[:], ALU.add),
                    lambda: V.tensor_scalar(vtmp[:], vv_p[:], bvt[:, kvh:kvh + 1], None, ALU.add),
                ]
                last = None
                for _i in range(nops):
                    last = ops[_i]()
                if last is None:
                    last = V.engine_nop() if hasattr(V, "engine_nop") else V.nop()
                dv.inc(last)
                dv_kv = dv.n
                if LNKV_SUB < 7:
                    continue

                T.wait_ge(dv.h, dv_kv)
                for j in range(4):
                    ins = T.transpose(vt_p[:, 128 * j:128 * (j + 1)],
                                      vtmp[:, 128 * j:128 * (j + 1)], identt[:])
                pe.inc(ins)
                V.wait_ge(pe.h, pe.n)
                dv.inc(V.tensor_copy(Vh[:, kvh, 4 * tc:4 * tc + 4, :],
                                     vt_p[:].rearrange("p (s j) -> p s j", s=4)))
                dv_kv = dv.n
            pe_kv[tc] = pe.n
    return dict(KT=KT, Vh=Vh)


def _attention(nc, es, SQ, WQ, V, A, T, C, wqT, woT, bqt, bot,
               qcost, qsint, poskcolt, poskkb, xnT, sDram, kvt, identt, cneg4,
               onescolt, onesf32t):
    gd, pe, dv, ac, ws = C["gd"], C["pe"], C["dv"], C["ac"], C["ws"]
    KT, Vh = kvt["KT"], kvt["Vh"]
    E = es.enter_context
    YT = E(nc.sbuf_tensor("YT", [128, 16, TOWN], BF16))
    mask01 = E(nc.sbuf_tensor("mask01", [128, 16, TOWN], BF16))

    with contextlib.ExitStack() as es2:
        E2 = es2.enter_context
        wstage = [E2(nc.sbuf_tensor(f"aws{i}", [128, 16, 128], F32)) for i in range(2)]
        w16 = [E2(nc.sbuf_tensor(f"aw16{i}", [128, 16, 128], BF16)) for i in range(2)]
        qraw = E2(nc.sbuf_tensor("qraw", [128, TOWN], F32))
        qrot = E2(nc.sbuf_tensor("qrot", [128, TOWN], F32))
        qcs = E2(nc.sbuf_tensor("qcs", [128, TOWN], F32))
        QT = E2(nc.sbuf_tensor("QT", [128, 2, TOWN], BF16))
        NSL = 6
        Pe = E2(nc.sbuf_tensor("Pe", [128, NSL, 512], BF16))
        Pm = E2(nc.sbuf_tensor("Pm", [128, NSL, 512], BF16))
        elrec = E2(nc.sbuf_tensor("elrec", [1, TOWN], F32))
        ebs = E2(nc.sbuf_tensor("ebs", [128, 2, 512], F32))
        stmp = E2(nc.sbuf_tensor("stmp", [128, TOWN], BF16))
        st_p = [E2(nc.psum_tensor(f"st_p{i}", [128, 512], F32)) for i in range(3)]
        y_p = [E2(nc.psum_tensor(f"y_p{i}", [128, 512], F32)) for i in range(2)]
        eb_p = E2(nc.psum_tensor("eb_p", [128, 512], F32))
        qp_p = E2(nc.psum_tensor("qp_p", [128, 1024], F32))

        for s in range(16):
            dv.inc(V.tensor_scalar(mask01[:, s, :], poskkb[:, 0:TOWN],
                                   poskcolt[:, s:s + 1], None, ALU.is_ge))
        dv_mask_built = dv.n

        ac_exp = {}
        dv_mask = {}
        pe_st = {}
        pe_yell = {}
        dv_qe = {}
        st_last_of_qh = {}
        dv_ynorm = {0: 0, 1: 0}
        dv_tail = [0]
        pe_rank1 = {0: 0, 1: 0}
        pe_w = {}
        from collections import deque
        pending = deque()

        def flush_pending():
            pg, pqh, phalf, ps, pslot = pending.popleft()
            pqs = slice(512 * phalf, 512 * (phalf + 1))
            pkvh = pqh // 4
            T.wait_ge(dv.h, dv_mask[pg])
            if ps == 0:
                T.wait_ge(dv.h, dv_tail[0])
            T.matmul(y_p[phalf][:], Vh[:, pkvh, ps, :], Pm[:, pslot, :],
                     start=(ps == 0), stop=(ps == 15))
            pe.inc(T.matmul(eb_p[0:1, :], onescolt[:], Pm[:, pslot, :],
                            start=(ps == 0), stop=(ps == 15)))
            pe_yell[pg] = pe.n
            if ps == 15:
                V.wait_ge(pe.h, pe_yell[pg])
                V.wait_ge(pe.h, pe_rank1[phalf])
                dv.inc(V.reciprocal(elrec[0:1, pqs], eb_p[0:1, :]))
                dv_rec = dv.n
                T.wait_ge(dv.h, dv_rec)
                pe.inc(T.matmul(eb_p[:], onesf32t[:], elrec[0:1, pqs],
                                start=True, stop=True))
                pe_rank1[phalf] = pe.n
                V.wait_ge(pe.h, pe_rank1[phalf])
                V.tensor_copy(ebs[:, phalf, :], eb_p[:])
                dv.inc(V.tensor_tensor(YT[:, pqh, pqs], y_p[phalf][:],
                                       ebs[:, phalf, :], ALU.mult))
                dv_ynorm[phalf] = dv.n
                dv_tail[0] = dv.n

        g = 0
        for qh in range(16):
            kvh = qh // 4
            b = qh % 2
            if qh >= 2:
                SQ.wait_ge(pe.h, pe_w[qh - 2])
            ws[b].inc(SQ.dma_start(out=wstage[b][:],
                                   in_=wqT[:, 128 * qh:128 * (qh + 1)].rearrange("(c p) e -> p c e", p=128)), 16)
            V.wait_ge(ws[b].h, ws[b].n)
            if qh >= 2:
                V.wait_ge(pe.h, pe_w[qh - 2])
            dv.inc(V.tensor_copy(w16[b][:], wstage[b][:]))
            dv_w = dv.n

            T.wait_ge(dv.h, dv_w)
            T.wait_ge(dv.h, dv_qe.get(qh - 1, 0))
            for th in range(2):
                for kc in range(16):
                    ins = T.matmul(qp_p[:, 512 * th:512 * (th + 1)], w16[b][:, kc, :],
                                   xnT[:, kc, 512 * th:512 * (th + 1)], start=(kc == 0), stop=(kc == 15))
            pe.inc(ins)
            pe_w[qh] = pe.n

            qb = qh % 2
            V.wait_ge(pe.h, pe_w[qh])
            V.wait_ge(pe.h, st_last_of_qh.get(qh - 2, 0))
            V.tensor_scalar(qraw[:], qp_p[:], bqt[:, qh:qh + 1], None, ALU.add)
            V.tensor_tensor(qrot[0:64, :], qraw[64:128, :], qsint[64:128, :], ALU.mult)
            V.tensor_tensor(qrot[64:128, :], qraw[0:64, :], qsint[0:64, :], ALU.mult)
            V.tensor_tensor(qcs[:], qraw[:], qcost[:], ALU.mult)
            dv.inc(V.tensor_tensor(QT[:, qb, :], qcs[:], qrot[:], ALU.add))
            dv_qe[qh] = dv.n

            for half in range(2):
                qs = slice(512 * half, 512 * (half + 1))
                for s in range(16):
                    par = g % 3
                    slot = g % NSL
                    T.wait_ge(dv.h, dv_qe[qh])
                    if s == 0:
                        T.wait_ge(dv.h, max(dv_mask_built, dv_ynorm[half]))
                    if g >= 3:
                        T.wait_ge(ac.h, ac_exp[g - 3])
                    pe.inc(T.matmul(st_p[par][:], KT[:, kvh, 128 * s:128 * (s + 1)],
                                    QT[:, qb, qs], start=True, stop=True))
                    pe_st[g] = pe.n
                    if half == 1 and s == 15:
                        st_last_of_qh[qh] = pe.n

                    A.wait_ge(pe.h, pe_st[g])
                    if g >= NSL:
                        A.wait_ge(dv.h, dv_mask[g - NSL])
                    ac.inc(A.activation(out=Pe[:, slot, :], in_=st_p[par][:],
                                        func=AF.Exp, scale=ISQ, bias=cneg4[:]))
                    ac_exp[g] = ac.n

                    V.wait_ge(ac.h, ac_exp[g])
                    if g >= NSL:
                        V.wait_ge(pe.h, pe_yell.get(g - NSL, 0))
                    dv.inc(V.tensor_tensor(Pm[:, slot, :], Pe[:, slot, :],
                                           mask01[:, s, qs], ALU.mult))
                    dv_mask[g] = dv.n

                    if len(pending) >= 3:
                        flush_pending()
                    pending.append((g, qh, half, s, slot))
                    g += 1
        while pending:
            flush_pending()

        dv_yt = dv.n
        gd_s = 0
        dv_se = 0
        for ec in range(16):
            b = ec % 2
            last_use = pe_w[16 + ec - 2] if ec >= 2 else pe_w[14 + ec]
            SQ.wait_ge(pe.h, last_use)
            ws[b].inc(SQ.dma_start(out=wstage[b][:],
                                   in_=woT[:, 128 * ec:128 * (ec + 1)].rearrange("(c p) e -> p c e", p=128)), 16)
            V.wait_ge(ws[b].h, ws[b].n)
            V.wait_ge(pe.h, last_use)
            dv.inc(V.tensor_copy(w16[b][:], wstage[b][:]))
            dv_w2 = dv.n

            T.wait_ge(dv.h, dv_w2)
            T.wait_ge(dv.h, dv_yt)
            if dv_se:
                T.wait_ge(dv.h, dv_se)
            for th in range(2):
                for kc in range(16):
                    ins = T.matmul(qp_p[:, 512 * th:512 * (th + 1)], w16[b][:, kc, :],
                                   YT[:, kc, 512 * th:512 * (th + 1)], start=(kc == 0), stop=(kc == 15))
            pe.inc(ins)
            pe_w[16 + ec] = pe.n

            V.wait_ge(pe.h, pe_w[16 + ec])
            if gd_s:
                V.wait_ge(gd.h, gd_s)
            dv.inc(V.tensor_scalar(stmp[:], qp_p[:], bot[:, ec:ec + 1], None, ALU.add))
            dv_se = dv.n
            WQ.wait_ge(dv.h, dv_se)
            gd_s = gd.inc(WQ.dma_start(out=sDram[128 * ec:128 * (ec + 1), :], in_=stmp[:]), 16)


def _ffn(nc, SQ, V, A, T, C, Gg, Gu, Gd, xnT, gacc, cc):
    pe, dv, ac = C["pe"], C["dv"], C["ac"]
    sgg, sgt = C["gg"], C["gt"]
    NRING = 24
    with contextlib.ExitStack() as es2:
        E2 = es2.enter_context
        grng = E2(nc.sbuf_tensor("grng", [128, NRING, 512], BF16))
        urng = E2(nc.sbuf_tensor("urng", [128, NRING, 512], BF16))
        gdt = [E2(nc.sbuf_tensor(f"gdt{i}", [128, 16, 256], BF16)) for i in range(2)]
        interT = E2(nc.sbuf_tensor("interT", [128, 16, TOWN], BF16))
        ga = E2(nc.sbuf_tensor("ffa", [128, 1024], F32))
        gb = E2(nc.sbuf_tensor("ffb", [128, 1024], F32))
        gth = E2(nc.sbuf_tensor("ffth", [128, 1024], F32))
        gp_p = [E2(nc.psum_tensor(f"gp_p{i}", [128, 1024], F32)) for i in range(2)]
        up_p = [E2(nc.psum_tensor(f"up_p{i}", [128, 1024], F32)) for i in range(2)]

        dv.inc(V.memset(gacc[:], 0.0))

        group_pe = {}
        emitted = [0]
        n_strips = 8 * 4 * 16

        def strip_idx(seg, ng, kc):
            return (seg * 4 + ng) * 16 + kc

        SQ.wait_ge(cc.h, 2)
        cc_d = [False]

        def emit_loads(upto):
            while emitted[0] < min(upto, n_strips):
                s = emitted[0]
                seg, rem = divmod(s, 64)
                ng, kc = divmod(rem, 16)
                slot = s % NRING
                gidx = s // 16
                if s >= NRING:
                    SQ.wait_ge(pe.h, group_pe[(s - NRING) // 16])
                sgg[gidx % 2].inc(SQ.dma_start(out=grng[:, slot, :],
                                               in_=Gg[seg, 128 * kc:128 * (kc + 1), 512 * ng:512 * (ng + 1)]), 16)
                sgg[gidx % 2].inc(SQ.dma_start(out=urng[:, slot, :],
                                               in_=Gu[seg, 128 * kc:128 * (kc + 1), 512 * ng:512 * (ng + 1)]), 16)
                emitted[0] += 1

        dv_ep = {}
        dv_dacc = 0
        pe_dw = {}
        for seg in range(8):
            for ng in range(4):
                g = seg * 4 + ng
                emit_loads(strip_idx(seg, ng, 0) + NRING)
                sg_all = 32 * 16 * (g // 2 + 1)
                for ns in range(4):
                    nsub = 4 * ng + ns
                    gidx2 = seg * 16 + nsub
                    par = nsub % 2
                    T.wait_ge(sgg[g % 2].h, sg_all)
                    if gidx2 >= 2 and (gidx2 - 2) in dv_ep:
                        T.wait_ge(dv.h, dv_ep[gidx2 - 2])
                    if ng == 0 and ns < 2 and dv_dacc:
                        T.wait_ge(dv.h, dv_dacc)
                    for kc in range(16):
                        s = strip_idx(seg, ng, kc)
                        slot = s % NRING
                        for th in range(2):
                            T.matmul(gp_p[par][:, 512 * th:512 * (th + 1)],
                                     grng[:, slot, 128 * ns:128 * (ns + 1)],
                                     xnT[:, kc, 512 * th:512 * (th + 1)],
                                     start=(kc == 0), stop=(kc == 15))
                        for th in range(2):
                            ins = T.matmul(up_p[par][:, 512 * th:512 * (th + 1)],
                                           urng[:, slot, 128 * ns:128 * (ns + 1)],
                                           xnT[:, kc, 512 * th:512 * (th + 1)],
                                           start=(kc == 0), stop=(kc == 15))
                    pe.inc(ins)
                    pe_gu = pe.n

                    gpp = gp_p[par][:]
                    upp = up_p[par][:]
                    A.wait_ge(pe.h, pe_gu)
                    ac.inc(A.activation(out=ga[:], in_=gpp, func=AF.Square))
                    V.wait_ge(ac.h, ac.n)
                    V.tensor_scalar(ga[:], ga[:], GELU_C, 1.0, ALU.mult, ALU.add)
                    dv.inc(V.tensor_tensor(gb[:], ga[:], gpp, ALU.mult))
                    A.wait_ge(dv.h, dv.n)
                    ac.inc(A.activation(out=gth[:], in_=gb[:], func=AF.Tanh, scale=GELU_S))
                    V.wait_ge(ac.h, ac.n)
                    V.scalar_tensor_tensor(gb[:], gth[:], 1.0, gpp, ALU.add, ALU.mult)
                    dv.inc(V.tensor_tensor(interT[:, nsub, :], gb[:], upp, ALU.mult))
                    dv_ep[gidx2] = dv.n
                group_pe[g] = pe.n

            for sl in range(8):
                b = sl % 2
                di = seg * 8 + sl
                if di >= 2:
                    SQ.wait_ge(pe.h, pe_dw[di - 2])
                if not cc_d[0]:
                    SQ.wait_ge(cc.h, 3)
                    cc_d[0] = True
                sgt[b].inc(SQ.dma_start(out=gdt[b][:],
                                        in_=Gd[sl, 2048 * seg:2048 * (seg + 1), :].rearrange("(c p) h -> p c h", p=128)), 16)
                T.wait_ge(sgt[b].h, sgt[b].n)
                T.wait_ge(dv.h, dv_ep[seg * 16 + 15])
                for j in range(2):
                    if dv_dacc:
                        T.wait_ge(dv.h, dv_dacc)
                    for nci in range(16):
                        for th in range(2):
                            ins = T.matmul(gp_p[j][:, 512 * th:512 * (th + 1)],
                                           gdt[b][:, nci, 128 * j:128 * (j + 1)],
                                           interT[:, nci, 512 * th:512 * (th + 1)],
                                           start=(nci == 0), stop=(nci == 15))
                    pe.inc(ins)
                    V.wait_ge(pe.h, pe.n)
                    dv.inc(V.tensor_tensor(gacc[:, 2 * sl + j, :], gp_p[j][:],
                                           gacc[:, 2 * sl + j, :], ALU.add))
                    dv_dacc = dv.n
                pe_dw[di] = pe.n


def _combine(nc, SQ, WQ, V, A, T, C, xr, out, sDram, gacc, identt, idenft):
    pe, dv, ac, ou, st, sxr = C["pe"], C["dv"], C["ac"], C["ou"], C["st"], C["xr"]
    with contextlib.ExitStack() as es2:
        E2 = es2.enter_context
        outst = E2(nc.sbuf_tensor("outst", [128, 8, D], F32))
        stile = [E2(nc.sbuf_tensor(f"cst{i}", [128, TOWN], BF16)) for i in range(2)]
        silt = E2(nc.sbuf_tensor("silt", [128, TOWN], F32))
        xrt = [E2(nc.sbuf_tensor(f"xrt{i}", [128, D], F32)) for i in range(2)]
        cpa = E2(nc.sbuf_tensor("cpa", [128, 128], BF16))
        ta_p = E2(nc.psum_tensor("ta_p", [128, 128], BF16))
        tb_p = E2(nc.psum_tensor("tb_p", [128, 128], F32))

        dv_c = 0
        pe_tp2 = 0
        pe_ec = {}
        for ec in range(16):
            b = ec % 2
            if ec >= 2:
                SQ.wait_ge(pe.h, pe_ec[ec - 2])
            st.inc(SQ.dma_start(out=stile[b][:], in_=sDram[128 * ec:128 * (ec + 1), :]), 16)
            st_all = st.n
            if dv_c:
                A.wait_ge(dv.h, dv_c)
            if pe_tp2:
                A.wait_ge(pe.h, pe_tp2)
            ac.inc(A.activation(out=silt[:], in_=gacc[:, ec, :], func=AF.Silu))
            ac_sil = ac.n
            for t in range(8):
                T.wait_ge(st.h, st_all)
                T.wait_ge(ac.h, ac_sil)
                if dv_c:
                    T.wait_ge(dv.h, dv_c)
                T.transpose(ta_p[:], stile[b][:, 128 * t:128 * (t + 1)], identt[:])
                pe.inc(T.transpose(tb_p[:], silt[:, 128 * t:128 * (t + 1)], idenft[:]))
                pe_tp2 = pe.n
                V.wait_ge(pe.h, pe_tp2)
                V.tensor_copy(cpa[:], ta_p[:])
                dv.inc(V.tensor_tensor(outst[:, t, 128 * ec:128 * (ec + 1)], tb_p[:], cpa[:], ALU.mult))
                dv_c = dv.n
            pe_ec[ec] = pe.n

        dv_xr = {}
        for t in range(8):
            b = t % 2
            if t >= 2:
                SQ.wait_ge(dv.h, dv_xr[t - 2])
            sxr[b].inc(SQ.dma_start(out=xrt[b][:], in_=xr[128 * t:128 * (t + 1), :]), 16)
            V.wait_ge(sxr[b].h, sxr[b].n)
            dv.inc(V.tensor_tensor(outst[:, t, :], outst[:, t, :], xrt[b][:], ALU.add))
            dv_xr[t] = dv.n
            WQ.wait_ge(dv.h, dv.n)
            C["ou"].inc(WQ.dma_start(out=out[128 * t:128 * (t + 1), :], in_=outst[:, t, :]), 16)


_NC_CACHE = {}


def _get_nc():
    key = (DEQ_SLICES, STAGE, LNKV_SUB, DBG)
    if key not in _NC_CACHE:
        _NC_CACHE[key] = build()
    return _NC_CACHE[key]


def _rope_tables():
    i = np.arange(1, 65, dtype=np.float32)
    thetas = (np.float32(10000.0) ** (np.float32(-2.0) * (i - np.float32(1.0)) / np.float32(128.0))).astype(np.float32)
    pos = np.arange(1, TKV + 1, dtype=np.float32)
    ang = pos[:, None] * thetas[None, :]
    c = np.cos(ang).astype(np.float32)
    s = np.sin(ang).astype(np.float32)
    cosF = np.ascontiguousarray(np.concatenate([c, c], axis=1).T)
    sinH = np.ascontiguousarray(np.concatenate([s, -s], axis=1).T)
    return cosF, sinH


def prepare_in_maps(L0_attn_x, L0_attn_Wq, L0_attn_bq, L0_attn_Wk, L0_attn_bk, L0_attn_Wv,
                    L0_attn_bv, L0_attn_Wo, L0_attn_bo, L0_ffn_gate_W_packed, L0_ffn_gate_scales,
                    L0_ffn_up_W_packed, L0_ffn_up_scales, L0_ffn_down_W_packed, L0_ffn_down_scales):
    x = np.asarray(L0_attn_x, dtype=np.float32)
    wq = np.ascontiguousarray(np.asarray(L0_attn_Wq, np.float32).T)
    wk = np.ascontiguousarray(np.asarray(L0_attn_Wk, np.float32).T)
    wv = np.ascontiguousarray(np.asarray(L0_attn_Wv, np.float32).T)
    wo = np.ascontiguousarray(np.asarray(L0_attn_Wo, np.float32).T)
    bqv = np.ascontiguousarray(np.asarray(L0_attn_bq, np.float32).reshape(16, 128).T)
    bkv = np.ascontiguousarray(np.asarray(L0_attn_bk, np.float32).reshape(4, 128).T)
    bvv = np.ascontiguousarray(np.asarray(L0_attn_bv, np.float32).reshape(4, 128).T)
    bov = np.ascontiguousarray(np.asarray(L0_attn_bo, np.float32).reshape(16, 128).T)
    gpk = np.ascontiguousarray(np.asarray(L0_ffn_gate_W_packed, np.int32))
    gsc = np.ascontiguousarray(np.asarray(L0_ffn_gate_scales, np.float32))
    upkk = np.ascontiguousarray(np.asarray(L0_ffn_up_W_packed, np.int32))
    upsc = np.ascontiguousarray(np.asarray(L0_ffn_up_scales, np.float32))
    dpk = np.ascontiguousarray(np.asarray(L0_ffn_down_W_packed, np.int32))
    dssc = np.ascontiguousarray(np.asarray(L0_ffn_down_scales, np.float32))

    cosF, sinH = _rope_tables()
    iden = np.eye(128, dtype=np.float32).astype(ml_dtypes.bfloat16)
    idenf32 = np.eye(128, dtype=np.float32)
    onescol = np.ones((128, 1), np.float32).astype(ml_dtypes.bfloat16)
    onesf32 = np.ones((1, 128), np.float32)

    in_maps = []
    for c in range(8):
        bsel = c // 2
        h = c % 2
        own0 = 1024 * h
        oth0 = 1024 * (1 - h)
        idx = np.concatenate([np.arange(own0, own0 + 1024), np.arange(oth0, oth0 + 1024)])
        xb = x[bsel]
        in_maps.append({
            "xT": np.ascontiguousarray(xb.T[:, idx]),
            "xr": np.ascontiguousarray(xb[own0:own0 + 1024]),
            "wqT": wq, "wkT": wk, "wvT": wv, "woT": wo,
            "bq": bqv, "bk": bkv, "bv": bvv, "bo": bov,
            "gp": np.ascontiguousarray(gpk[2048 * c:2048 * (c + 1)]),
            "gs": np.ascontiguousarray(gsc[2048 * c:2048 * (c + 1)]),
            "up": np.ascontiguousarray(upkk[2048 * c:2048 * (c + 1)]),
            "us": np.ascontiguousarray(upsc[2048 * c:2048 * (c + 1)]),
            "dp": np.ascontiguousarray(dpk[256 * c:256 * (c + 1)]),
            "ds": np.ascontiguousarray(dssc[256 * c:256 * (c + 1)]),
            "qcos": np.ascontiguousarray(cosF[:, own0:own0 + 1024]),
            "qsin": np.ascontiguousarray(sinH[:, own0:own0 + 1024]),
            "kcos": np.ascontiguousarray(cosF[:, idx]),
            "ksin": np.ascontiguousarray(sinH[:, idx]),
            "posq": np.ascontiguousarray(np.arange(own0, own0 + 1024, dtype=np.float32).reshape(8, 128).T),
            "poskcol": np.ascontiguousarray(idx.astype(np.float32).reshape(16, 128).T),
            "poskk": idx.astype(np.float32).reshape(1, TKV),
            "iden": iden, "idenf": idenf32, "onescol": onescol, "onesf32": onesf32,
        })
    return in_maps


def kernel(**inputs):
    in_maps = prepare_in_maps(**inputs)
    nc = _get_nc()
    res = run_bass_kernel_spmd(nc, in_maps, core_ids=list(range(8)))
    outp = np.empty((4, 2048, 2048), np.float32)
    for c in range(8):
        bsel = c // 2
        h = c % 2
        outp[bsel, 1024 * h:1024 * (h + 1)] = res.results[c]["out"]
    return outp
